# version 37
# baseline (speedup 1.0000x reference)
import math
from contextlib import ExitStack

import numpy as np
import ml_dtypes
import concourse.bass as bass
import concourse.mybir as mybir
from concourse.bass_utils import run_bass_kernel_spmd

F32 = mybir.dt.float32
BF16 = mybir.dt.bfloat16
BF = ml_dtypes.bfloat16
AF = mybir.ActivationFunctionType
OP = mybir.AluOpType
AX = mybir.AxisListType

ENGINES = ("sync", "tensor", "vector", "scalar", "gpsimd")
ESEMS = {
    "sync": (None, "d_sync"),
    "tensor": ("c_pe", None),
    "vector": ("c_dve", None),
    "scalar": ("c_act", "d_act"),
    "gpsimd": ("c_gp", "d_gp"),
}
BASE_SEM_NAMES = ["c_pe", "c_dve", "c_act", "c_gp", "c_cc"]

EPS_L2 = 1e-6
EPS_RMS = 1e-6
EPS_LN = 1e-5


class Ev:
    __slots__ = ("fn", "sem", "amt", "num", "eng", "idx")

    def __init__(self, eng, fn, idx):
        self.eng = eng
        self.fn = fn
        self.sem = None
        self.amt = 0
        self.num = None
        self.idx = idx


class Builder:

    def __init__(self, nc):
        self.nc = nc
        self.events = {e: [] for e in ENGINES}
        self.res = {}
        self.sem_handles = {}
        self.n_waits = 0
        self.n_events = 0
        self.used_sems = set(BASE_SEM_NAMES)
        self.last_dma = {}
        self.maxw = {}
        self.eidx = dict.fromkeys(ENGINES, 0)

    def _rec(self, key):
        r = self.res.get(key)
        if r is None:
            r = {"w": None, "r": []}
            self.res[key] = r
        return r

    def _wait(self, eng, d):
        psem = d.sem if d.sem is not None else ESEMS[d.eng][0]
        if self.maxw.get((eng, psem), -1) >= d.idx:
            return
        self._mark(d)
        self.events[eng].append(("w", d))
        self.maxw[(eng, psem)] = d.idx
        self.n_waits += 1

    def dma_sem(self, eng, tag):
        nm = f"d_{eng}_{tag}"
        self.used_sems.add(nm)
        return nm

    def emit(self, eng, fn, reads=(), writes=(), dma=False, collective=False):
        ev = Ev(eng, fn, self.eidx[eng])
        self.eidx[eng] += 1
        deps = []
        for k in reads:
            r = self._rec(k)
            if r["w"] is not None:
                deps.append(r["w"])
        for k in writes:
            r = self._rec(k)
            if r["w"] is not None:
                deps.append(r["w"])
            deps.extend(r["r"])
        seen = set()
        for d in deps:
            if id(d) in seen:
                continue
            seen.add(id(d))
            if d.eng == eng == "tensor":
                continue
            if d.amt == 16:
                d2 = self.last_dma[d.sem]
                if d2 is not d:
                    if id(d2) in seen:
                        continue
                    seen.add(id(d2))
                    d = d2
                if d.eng != eng:
                    self._wait(d.eng, d)
            self._wait(eng, d)
        if dma:
            tag = dma if isinstance(dma, str) else "main"
            ev.sem = self.dma_sem(eng, tag)
            ev.amt = 16
            self.last_dma[ev.sem] = ev
        elif collective:
            ev.sem = "c_cc"
            ev.amt = 1
        self.events[eng].append(("e", ev))
        self.n_events += 1
        for k in writes:
            r = self._rec(k)
            r["w"] = ev
            r["r"] = []
        for k in reads:
            self._rec(k)["r"].append(ev)
        return ev

    def _mark(self, ev):
        if ev.sem is None:
            ev.sem = ESEMS[ev.eng][0]
            assert ev.sem is not None
            ev.amt = 1

    def finalize(self):
        totals = dict.fromkeys(self.used_sems, 0)
        for e in ENGINES:
            for kind, item in self.events[e]:
                if kind == "e" and item.sem is not None:
                    totals[item.sem] += item.amt
                    item.num = totals[item.sem]
        return totals

    def replay_engine(self, name, E):
        for kind, item in self.events[name]:
            if kind == "w":
                assert item.num is not None, "wait on unfinalized event"
                E.wait_ge(self.sem_handles[item.sem], item.num)
            else:
                ins = item.fn(E)
                if item.sem is not None:
                    assert ins is not None, "milestone event produced no instruction"
                    ins.then_inc(self.sem_handles[item.sem], item.amt)


class ScanEmitter:

    def __init__(self, b: Builder, bufs: dict, H: int, C: int, NLEV: int = 7):
        self.b = b
        self.u = bufs
        self.H = H
        self.C = C
        self.NLEV = NLEV

    def h3(self, ap):
        return ap.rearrange("p (h c) -> p h c", h=self.H)

    def colb(self, ap_col):
        return ap_col.unsqueeze(2).broadcast_to([ap_col.shape[0], self.H, self.C])

    def maskb(self, ap_mask):
        return ap_mask.unsqueeze(1).broadcast_to([self.C, self.H, self.C])

    def p1_stages(self, khat, qhat, vhat, g_row, lnb_row, beta_row, par,
                  kq_keys=("kqv_in", "kqv_in", "kqv_in"),
                  row_keys=("rows_in", "rows_in", "rows_in")):
        b, u, H, C = self.b, self.u, self.H, self.C
        kk, qk, vk = kq_keys
        gk, lbk, bk_ = row_keys
        P = str(par)
        slab, slab2 = u["slab" + P], u["slab2" + P]
        kcb, gam, bkap = u["kcb" + P], u["gam" + P], u["bkap" + P]
        PST, pstk = u["PST" + P], "PST" + P
        bcol = slab[:, 0:H]
        betacol = slab[:, H:2 * H]
        brow3 = self.h3(u["brow"][:])
        browb3 = self.h3(u["browb"][:])
        knT3 = self.h3(u["knT" + P][:])
        qnT3 = self.h3(u["qnT" + P][:])

        def s0():
            b.emit("vector", lambda v, g=g_row: v.tensor_tensor_scan(
                u["b_scr"][:], g, u["zero_row"][:].broadcast_to([H, C]), 0.0, OP.add, OP.add),
                reads=[gk, "zero_row"], writes=["b_scr"])
            b.emit("vector", lambda v, l=lnb_row: v.tensor_add(u["blnb_scr"][:], u["b_scr"][:], l),
                reads=["b_scr", lbk], writes=["blnb_scr"])

            b.emit("tensor", lambda t: t.matmul(
                u["PSG"][:, 0:H], u["b_scr"][:], u["identH"][:], start=True, stop=False,
                is_transpose=True),
                reads=["b_scr", "identH", "PSG"], writes=["PSG"])
            b.emit("tensor", lambda t, br=beta_row: t.matmul(
                u["PSG"][:, H:2 * H], br, u["identH"][:], start=False, stop=True,
                is_transpose=True),
                reads=[bk_, "identH", "PSG"], writes=["PSG"])
            b.emit("vector", lambda v, slab=slab: v.tensor_copy(slab[:], u["PSG"][:, 0:2 * H]),
                reads=["PSG"], writes=["slab" + P])

            b.emit("scalar", lambda s, slab2=slab2, bcol=bcol: s.activation(
                slab2[:, H:2 * H], bcol, AF.Exp),
                reads=["slab" + P], writes=["slab2" + P])
            b.emit("tensor", lambda t: t.matmul(
                u["PSG"][0:1, 2 * H:3 * H], u["b_scr"][:, C - 1:C], u["identH"][:],
                start=True, stop=True, is_transpose=True),
                reads=["b_scr", "identH", "PSG"], writes=["PSG"])
            b.emit("vector", lambda v: v.tensor_copy(u["bc_row"][0:1, :], u["PSG"][0:1, 2 * H:3 * H]),
                reads=["PSG"], writes=["bc_row"])
            b.emit("tensor", lambda t: t.matmul(
                u["PSG"][:, 3 * H:4 * H], u["ones1"][0:1, :], u["bc_row"][0:1, :],
                start=True, stop=True),
                reads=["bc_row", "ones1", "PSG"], writes=["PSG"])
            b.emit("scalar", lambda s, kcb=kcb: s.activation(kcb[:], u["PSG"][:, 3 * H:4 * H], AF.Exp),
                reads=["PSG"], writes=["kcb" + P])
            b.emit("vector", lambda v, slab2=slab2, bcol=bcol: v.tensor_tensor(
                slab2[:, 0:H], u["PSG"][:, 3 * H:4 * H], bcol, OP.subtract),
                reads=["PSG", "slab" + P], writes=["slab2" + P])
            b.emit("scalar", lambda s, gam=gam, slab2=slab2: s.activation(
                gam[:], slab2[:, 0:H], AF.Exp),
                reads=["slab2" + P], writes=["gam" + P])
            b.emit("vector", lambda v, bkap=bkap, betacol=betacol, slab2=slab2: v.tensor_mul(
                bkap[:], betacol, slab2[:, H:2 * H]),
                reads=["slab" + P, "slab2" + P], writes=["bkap" + P])


        def s1():
            for nm, srcb, srck in (("brow", u["b_scr"], "b_scr"),
                                   ("browb", u["blnb_scr"], "blnb_scr")):
                for h in range(H):
                    b.emit("tensor", lambda t, h=h, srcb=srcb: t.matmul(
                        u["PSG"][:, h * C:(h + 1) * C], u["sel" + str(h)][:], srcb[:],
                        start=(h == 0), stop=(h == H - 1)),
                        reads=[srck, "sel" + str(h), "PSG"], writes=["PSG"])
                b.emit("vector", lambda v, nm=nm: v.tensor_copy(u[nm][:], u["PSG"][:, 0:H * C]),
                    reads=["PSG"], writes=[nm])

        def s2():
            b.emit("scalar", lambda s, k=khat: s.activation(self.h3(u["sq"][:]), k, AF.Square),
                reads=[kk], writes=["sq"])
            for h in range(H):
                b.emit("tensor", lambda t, h=h: t.matmul(
                    u["PSG"][:, 4 * H + h:4 * H + h + 1], self.h3(u["sq"][:])[:, h, :],
                    u["ones128"][:, 0:1], start=(h == 0), stop=False),
                    reads=["sq", "ones128", "PSG"], writes=["PSG"])
            b.emit("scalar", lambda s, q=qhat: s.activation(self.h3(u["sq"][:]), q, AF.Square),
                reads=[qk, "PSG"], writes=["sq"])
            for h in range(H):
                b.emit("tensor", lambda t, h=h: t.matmul(
                    u["PSG"][:, 5 * H + h:5 * H + h + 1], self.h3(u["sq"][:])[:, h, :],
                    u["ones128"][:, 0:1], start=False, stop=(h == H - 1)),
                    reads=["sq", "ones128", "PSG"], writes=["PSG"])
            b.emit("scalar", lambda s: s.activation(u["lncol"][:], u["PSG"][:, 4 * H:6 * H], AF.Ln,
                                                    bias=u["c_eps6"][:]),
                reads=["PSG", "c_eps6"], writes=["lncol"])
            b.emit("scalar", lambda s: s.activation(u["rcol"][:, 0:H], u["lncol"][:, 0:H],
                                                    AF.Exp, scale=-0.5),
                reads=["lncol"], writes=["rcol"])
            b.emit("scalar", lambda s: s.activation(u["rcol"][:, H:2 * H], u["lncol"][:, H:2 * H],
                                                    AF.Exp, scale=-0.5, bias=u["c_lnq"][:]),
                reads=["lncol", "c_lnq"], writes=["rcol"])


        def kq(nm, src, srck, rlo):
            for h in range(H):
                b.emit("tensor", lambda t, h=h, src=src, PST=PST: t.matmul(
                    PST[:, h * C:(h + 1) * C], src[:, h, :], u["identb"][:],
                    start=(h == 0), stop=(h == H - 1), is_transpose=True),
                    reads=[srck, "identb", pstk], writes=[pstk])
            tmk = (nm + "_tm" + P) if nm == "k" else "q_tm"
            tm = u[tmk]
            b.emit("vector", lambda v, tm=tm, rlo=rlo, PST=PST: v.tensor_tensor(
                self.h3(tm[:]), self.h3(PST[:, 0:H * C]),
                self.colb(u["rcol"][:, rlo:rlo + H]), OP.mult),
                reads=[pstk, "rcol"], writes=[tmk])
            for h in range(H):
                b.emit("tensor", lambda t, h=h, tm=tm, PST=PST: t.matmul(
                    PST[:, H * C + h * C:H * C + (h + 1) * C], self.h3(tm[:])[:, h, :],
                    u["identb"][:],
                    start=(h == 0), stop=(h == H - 1), is_transpose=True),
                    reads=[tmk, "identb", pstk], writes=[pstk])
            nt = u[nm + "nT" + P]
            b.emit("scalar", lambda s, nt=nt, PST=PST: s.copy(
                nt[:], PST[:, H * C:2 * H * C]),
                reads=[pstk], writes=[nm + "nT" + P])

        def s3():
            kq("k", khat, kk, 0)

        def s4():
            kq("q", qhat, qk, H)
            for h in range(H):
                b.emit("tensor", lambda t, h=h, v_=vhat, PST=PST: t.matmul(
                    PST[:, h * C:(h + 1) * C], v_[:, h, :], u["identb"][:],
                    start=(h == 0), stop=(h == H - 1), is_transpose=True),
                    reads=[vk, "identb", pstk], writes=[pstk])
            b.emit("scalar", lambda s, PST=PST: s.copy(u["v_tm"][:], PST[:, 0:H * C]),
                reads=[pstk], writes=["v_tm"])
            b.emit("vector", lambda v, betacol=betacol: v.tensor_tensor(
                self.h3(u["rhs1" + P][:]), self.h3(u["v_tm"][:]), self.colb(betacol), OP.mult),
                reads=["v_tm", "slab" + P], writes=["rhs1" + P])

        def s5():
            for h in range(H):
                b.emit("tensor", lambda t, h=h, knT3=knT3: t.matmul(
                    u["PSG"][:, h * C:(h + 1) * C], knT3[:, h, :], knT3[:, h, :],
                    start=(h == 0), stop=(h == H - 1)),
                    reads=["knT" + P, "PSG"], writes=["PSG"])

            b.emit("vector", lambda v, bcol=bcol: v.tensor_tensor(
                self.h3(u["e1"][:]), self.colb(bcol), brow3, OP.subtract),
                reads=["slab" + P, "brow"], writes=["e1"])
            b.emit("vector", lambda v: v.tensor_scalar_min(u["e1"][:], u["e1"][:], 0.0),
                reads=["e1"], writes=["e1"])
            b.emit("scalar", lambda s: s.activation(u["e1"][:], u["e1"][:], AF.Exp),
                reads=["e1"], writes=["e1"])
            b.emit("vector", lambda v: v.tensor_mul(u["t1"][:], u["e1"][:], u["PSG"][:, 0:H * C]),
                reads=["e1", "PSG"], writes=["t1"])
            b.emit("vector", lambda v: v.tensor_tensor(
                self.h3(u["t1"][:]), self.h3(u["t1"][:]), self.maskb(u["lmask_neg"][:]), OP.mult),
                reads=["t1", "lmask_neg"], writes=["t1"])
            b.emit("vector", lambda v, betacol=betacol: v.tensor_tensor(
                self.h3(u["Npow" + P][:]), self.h3(u["t1"][:]), self.colb(betacol), OP.mult),
                reads=["t1", "slab" + P], writes=["Npow" + P])

        def s6():
            b.emit("vector", lambda v, bcol=bcol: v.tensor_tensor(
                self.h3(u["e1"][:]), browb3, self.colb(bcol), OP.subtract),
                reads=["browb", "slab" + P], writes=["e1"])
            b.emit("vector", lambda v: v.tensor_scalar_min(u["e1"][:], u["e1"][:], 0.0),
                reads=["e1"], writes=["e1"])
            b.emit("scalar", lambda s: s.activation(u["e1"][:], u["e1"][:], AF.Exp),
                reads=["e1"], writes=["e1"])
            b.emit("vector", lambda v: v.tensor_mul(u["t1"][:], u["e1"][:], u["PSG"][:, 0:H * C]),
                reads=["e1", "PSG"], writes=["t1"])
            b.emit("vector", lambda v: v.tensor_tensor(
                self.h3(u["NTpow" + P][:]), self.h3(u["t1"][:]), self.maskb(u["umask_neg"][:]),
                OP.mult),
                reads=["t1", "umask_neg"], writes=["NTpow" + P])
            b.emit("vector", lambda v: v.tensor_tensor(
                self.h3(u["GT" + P][:]), self.h3(u["NTpow" + P][:]), self.maskb(u["identb"][:]),
                OP.add),
                reads=["NTpow" + P, "identb"], writes=["GT" + P])

            for h in range(H):
                b.emit("tensor", lambda t, h=h, knT3=knT3, qnT3=qnT3: t.matmul(
                    u["PSG"][:, h * C:(h + 1) * C], knT3[:, h, :], qnT3[:, h, :],
                    start=(h == 0), stop=(h == H - 1)),
                    reads=["knT" + P, "qnT" + P, "PSG"], writes=["PSG"])
            b.emit("vector", lambda v, bcol=bcol: v.tensor_tensor(
                self.h3(u["e1"][:]), brow3, self.colb(bcol), OP.subtract),
                reads=["brow", "slab" + P], writes=["e1"])
            b.emit("vector", lambda v: v.tensor_scalar_min(u["e1"][:], u["e1"][:], 0.0),
                reads=["e1"], writes=["e1"])
            b.emit("scalar", lambda s: s.activation(u["e1"][:], u["e1"][:], AF.Exp),
                reads=["e1"], writes=["e1"])
            b.emit("vector", lambda v: v.tensor_mul(u["t1"][:], u["e1"][:], u["PSG"][:, 0:H * C]),
                reads=["e1", "PSG"], writes=["t1"])
            b.emit("vector", lambda v: v.tensor_tensor(
                self.h3(u["Sup" + P][:]), self.h3(u["t1"][:]), self.maskb(u["umask_incl"][:]),
                OP.mult),
                reads=["t1", "umask_incl"], writes=["Sup" + P])

        return [s0, s1, s2, s3, s4, s5, s6]

    def p1b(self, par):
        b, u, H, C = self.b, self.u, self.H, self.C
        P = str(par)
        stages = []
        for lev in range(1, self.NLEV):
            def ql(lev=lev):
                for h in range(H):
                    b.emit("tensor", lambda t, h=h: t.matmul(
                        u["PSQ"][:, h * C:(h + 1) * C],
                        self.h3(u["NTpow" + P][:])[:, h, :],
                        self.h3(u["Npow" + P][:])[:, h, :],
                        start=(h == 0), stop=(h == H - 1)),
                        reads=["NTpow" + P, "Npow" + P, "PSQ"], writes=["PSQ"])
                if lev < self.NLEV - 1:
                    for h in range(H):
                        b.emit("tensor", lambda t, h=h: t.matmul(
                            u["PSR"][:, h * C:(h + 1) * C],
                            self.h3(u["Npow" + P][:])[:, h, :],
                            self.h3(u["NTpow" + P][:])[:, h, :],
                            start=(h == 0), stop=(h == H - 1)),
                            reads=["NTpow" + P, "Npow" + P, "PSR"], writes=["PSR"])
                b.emit("scalar", lambda s: s.copy(u["Npow" + P][:], u["PSQ"][:, 0:H * C]),
                    reads=["PSQ"], writes=["Npow" + P])
                if lev < self.NLEV - 1:
                    b.emit("scalar", lambda s: s.copy(u["NTpow" + P][:], u["PSR"][:, 0:H * C]),
                        reads=["PSR"], writes=["NTpow" + P])
                for h in range(H):
                    b.emit("tensor", lambda t, h=h: t.matmul(
                        u["PSA"][:, h * C:(h + 1) * C],
                        self.h3(u["Npow" + P][:])[:, h, :],
                        self.h3(u["GT" + P][:])[:, h, :],
                        start=(h == 0), stop=(h == H - 1)),
                        reads=["Npow" + P, "GT" + P, "PSA"], writes=["PSA"])
                b.emit("vector", lambda v: v.tensor_add(
                    u["GT" + P][:], u["GT" + P][:], u["PSA"][:, 0:H * C]),
                    reads=["GT" + P, "PSA"], writes=["GT" + P])
            stages.append(ql)

        def wind():
            for h in range(H):
                b.emit("tensor", lambda t, h=h: t.matmul(
                    u["PSR"][:, h * C:(h + 1) * C],
                    self.h3(u["GT" + P][:])[:, h, :],
                    self.h3(u["rhs1" + P][:])[:, h, :],
                    start=(h == 0), stop=(h == H - 1)),
                    reads=["GT" + P, "rhs1" + P, "PSR"], writes=["PSR"])
            b.emit("scalar", lambda s: s.copy(u["Wind"][:], u["PSR"][:, 0:H * C]),
                reads=["PSR"], writes=["Wind"])
        stages.append(wind)
        return stages

    def p23_chain(self, par):
        b, u, H, C = self.b, self.u, self.H, self.C
        P = str(par)
        kcb, gam, bkap = u["kcb" + P], u["gam" + P], u["bkap" + P]
        knT3 = self.h3(u["knT" + P][:])
        qnT3 = self.h3(u["qnT" + P][:])
        gt3 = self.h3(u["GT" + P][:])

        def c0():
            for h in range(H):
                b.emit("tensor", lambda t, h=h: t.matmul(
                    u["PSK"][:, h * C:(h + 1) * C], knT3[:, h, :],
                    self.h3(u["M_bf"][:])[:, h, :],
                    start=(h == 0), stop=(h == H - 1)),
                    reads=["knT" + P, "M_bf", "PSK"], writes=["PSK"])
            for h in range(H):
                b.emit("tensor", lambda t, h=h: t.matmul(
                    u["PSA"][:, h * C:(h + 1) * C], qnT3[:, h, :],
                    self.h3(u["M_bf"][:])[:, h, :],
                    start=(h == 0), stop=(h == H - 1)),
                    reads=["qnT" + P, "M_bf", "PSA"], writes=["PSA"])

        def c1():
            b.emit("vector", lambda v: v.tensor_tensor(
                self.h3(u["X"][:]), self.h3(u["PSK"][:, 0:H * C]),
                self.colb(bkap[:]), OP.mult),
                reads=["PSK", "bkap" + P], writes=["X"])

        def c2():
            for h in range(H):
                b.emit("tensor", lambda t, h=h: t.matmul(
                    u["PSK"][:, h * C:(h + 1) * C], gt3[:, h, :],
                    self.h3(u["X"][:])[:, h, :],
                    start=(h == 0), stop=(h == H - 1)),
                    reads=["GT" + P, "X", "PSK"], writes=["PSK"])

        def c3():
            b.emit("vector", lambda v: v.tensor_sub(
                u["Wf"][:], u["Wind"][:], u["PSK"][:, 0:H * C]),
                reads=["Wind", "PSK"], writes=["Wf"])

        def c4():
            b.emit("vector", lambda v: v.tensor_tensor(
                self.h3(u["Wg"][:]), self.h3(u["Wf"][:]), self.colb(gam[:]),
                OP.mult), reads=["Wf", "gam" + P], writes=["Wg"])

        def c5():
            for h in range(H):
                b.emit("tensor", lambda t, h=h: t.matmul(
                    u["PSK"][:, h * C:(h + 1) * C],
                    self.h3(u["k_tm" + P][:])[:, h, :],
                    self.h3(u["Wg"][:])[:, h, :],
                    start=(h == 0), stop=(h == H - 1)),
                    reads=["k_tm" + P, "Wg", "PSK"], writes=["PSK"])

        def c6():
            b.emit("vector", lambda v: v.tensor_tensor(
                self.h3(u["M_f32"][:]), self.h3(u["M_f32"][:]),
                self.colb(kcb[:]), OP.mult),
                reads=["M_f32", "kcb" + P], writes=["M_f32"])

        def c7():
            b.emit("vector", lambda v: v.tensor_add(
                u["M_f32"][:], u["M_f32"][:], u["PSK"][:, 0:H * C]),
                reads=["M_f32", "PSK"], writes=["M_f32"])

        def c8():
            b.emit("scalar", lambda s: s.copy(u["M_bf"][:], u["M_f32"][:]),
                reads=["M_f32"], writes=["M_bf"])

        return [c0, c1, c2, c3, c4, c5, c6, c7, c8]

    def p23_out(self, par, zsl, og_cb, z_key):
        b, u, H, C = self.b, self.u, self.H, self.C
        P = str(par)
        slab2 = u["slab2" + P]
        PST, pstk = u["PST" + P], "PST" + P
        kappa = slab2[:, H:2 * H]

        def o1a():
            b.emit("vector", lambda v: v.tensor_tensor(
                self.h3(u["t2"][:]), self.h3(u["PSA"][:, 0:H * C]),
                self.colb(kappa), OP.mult),
                reads=["PSA", "slab2" + P], writes=["t2"])

        def o1b():
            b.emit("scalar", lambda s: s.copy(u["W_bf"][:], u["Wf"][:]),
                reads=["Wf"], writes=["W_bf"])
            for h in range(H):
                b.emit("tensor", lambda t, h=h: t.matmul(
                    u["PSS"][:, h * C:(h + 1) * C],
                    self.h3(u["Sup" + P][:])[:, h, :],
                    self.h3(u["W_bf"][:])[:, h, :],
                    start=(h == 0), stop=(h == H - 1)),
                    reads=["Sup" + P, "W_bf", "PSS"], writes=["PSS"])
            b.emit("vector", lambda v: v.tensor_add(
                u["O"][:], u["t2"][:], u["PSS"][:, 0:H * C]),
                reads=["t2", "PSS"], writes=["O"])

        def o2():
            b.emit("scalar", lambda s: s.activation(u["t2"][:], u["O"][:], AF.Square),
                reads=["O"], writes=["t2"])
            b.emit("vector", lambda v: v.tensor_reduce(
                u["s4"][:].unsqueeze(2), self.h3(u["t2"][:]), AX.X, OP.add),
                reads=["t2"], writes=["s4"])
            b.emit("scalar", lambda s: s.activation(u["s4"][:], u["s4"][:], AF.Ln,
                                                    bias=u["c_eps6"][:], scale=1.0 / 128.0),
                reads=["s4", "c_eps6"], writes=["s4"])
            b.emit("scalar", lambda s: s.activation(u["s4"][:], u["s4"][:], AF.Exp, scale=-0.5),
                reads=["s4"], writes=["s4"])
            b.emit("vector", lambda v: v.tensor_tensor(
                self.h3(u["t2"][:]), self.h3(u["O"][:]), self.colb(u["s4"][:]), OP.mult),
                reads=["O", "s4"], writes=["t2"])
            b.emit("vector", lambda v, z=zsl: v.tensor_tensor(
                self.h3(u["og_tm"][:]), self.h3(u["t2"][:]), z, OP.mult),
                reads=["t2", z_key], writes=["og_tm"])

        def o3():
            for h in range(H):
                b.emit("tensor", lambda t, h=h, PST=PST: t.matmul(
                    PST[:, h * C:(h + 1) * C], self.h3(u["og_tm"][:])[:, h, :],
                    u["identb"][:], start=(h == 0), stop=(h == H - 1), is_transpose=True),
                    reads=["og_tm", "identb", pstk], writes=[pstk])
            b.emit("scalar", lambda s, PST=PST: s.copy(u["ogT"][:], PST[:, 0:H * C]),
                reads=[pstk], writes=["ogT"])
            og_cb()

        return [o1a, o1b, o2, o3]


def alloc_scan_bufs(nc, ctx, H, C):
    u = {}

    def sb(name, shape, dt=F32):
        u[name] = ctx.enter_context(nc.sbuf_tensor("scan_" + name, shape, dt))

    def ps(name, shape, dt=F32):
        u[name] = ctx.enter_context(nc.psum_tensor("scan_" + name, shape, dt))

    HC = H * C
    ps("PSG", [128, 512]); ps("PSS", [128, 512]); ps("PSK", [128, 512])
    ps("PSA", [128, 512]); ps("PSQ", [128, 512]); ps("PSR", [128, 512])
    ps("PST0", [128, 1024], BF16); ps("PST1", [128, 1024], BF16)
    sb("b_scr", [H, C]); sb("blnb_scr", [H, C]); sb("zero_row", [H, 1])
    sb("bc_row", [1, H])
    sb("brow", [128, HC]); sb("browb", [128, HC])
    sb("sq", [128, HC], BF16); sb("lncol", [128, 2 * H]); sb("rcol", [128, 2 * H], BF16)
    sb("q_tm", [128, HC], BF16); sb("v_tm", [128, HC], BF16)
    sb("e1", [128, HC]); sb("t1", [128, HC]); sb("t2", [128, HC])
    sb("X", [128, HC], BF16); sb("O", [128, HC])
    sb("s4", [128, H]); sb("og_tm", [128, HC], BF16); sb("ogT", [128, HC], BF16)
    sb("Wg", [128, HC], BF16)
    sb("Wf", [128, HC]); sb("W_bf", [128, HC], BF16)
    sb("M_f32", [128, HC]); sb("M_bf", [128, HC], BF16)
    for P in ("0", "1"):
        sb("slab" + P, [128, 2 * H]); sb("slab2" + P, [128, 2 * H])
        sb("kcb" + P, [128, H]); sb("gam" + P, [128, H], BF16); sb("bkap" + P, [128, H])
        sb("rhs1" + P, [128, HC], BF16)
        sb("knT" + P, [128, HC], BF16); sb("qnT" + P, [128, HC], BF16)
        sb("k_tm" + P, [128, HC], BF16)
        sb("Npow" + P, [128, HC], BF16); sb("NTpow" + P, [128, HC], BF16)
        sb("GT" + P, [128, HC], BF16); sb("Sup" + P, [128, HC], BF16)
    sb("Wind", [128, HC])
    sb("identH", [H, H]); sb("identb", [128, 128], BF16); sb("ident_f", [128, 128])
    sb("ones1", [1, 128]); sb("ones128", [128, 1], BF16)
    sb("c_eps6", [128, 1]); sb("c_eps5", [128, 1]); sb("c_lnq", [128, 1]); sb("c_one", [128, 1])
    sb("lmask_neg", [C, C]); sb("umask_neg", [C, C]); sb("umask_incl", [C, C])
    for h in range(H):
        sb("sel" + str(h), [H, 128])
    return u



def scan_const_arrays(H, C):
    c = {}
    c["identH"] = np.eye(H, dtype=np.float32)
    c["ident_f"] = np.eye(128, dtype=np.float32)
    c["identb"] = np.eye(128, dtype=np.float32).astype(BF)
    c["ones1"] = np.ones((1, 128), np.float32)
    c["ones128"] = np.ones((128, 1), np.float32).astype(BF)
    c["c_eps6"] = np.full((128, 1), 1e-6, np.float32)
    c["c_eps5"] = np.full((128, 1), 1e-5, np.float32)
    c["c_lnq"] = np.full((128, 1), math.log(1.0 / math.sqrt(128.0)), np.float32)
    c["c_one"] = np.ones((128, 1), np.float32)
    c["lmask_neg"] = -np.tril(np.ones((C, C), np.float32), -1)
    c["umask_neg"] = -np.triu(np.ones((C, C), np.float32), 1)
    c["umask_incl"] = np.triu(np.ones((C, C), np.float32), 0)
    for h in range(H):
        s = np.zeros((H, 128), np.float32)
        s[h, :] = 1.0
        c["sel" + str(h)] = s
    c["zero_row"] = np.zeros((H, 1), np.float32)
    return c


def emit_const_loads(b, u, consts, nc, dma_eng="sync"):
    for name, arr in consts.items():
        t = nc.inline_tensor(np.ascontiguousarray(arr), "const_" + name)
        b.emit(dma_eng, lambda E, t=t, name=name: E.dma_start(out=u[name][:], in_=t.ap()),
               writes=[name], dma="xload")



def default_cfg():
    return dict(D=2048, L=4096, TILE=512, H=4, HG=16, C=128, NLEV=7,
                IFF=1408, NE=8, TB=1024, KCONV=4, IBMAX=256)


def mini_cfg():
    return dict(D=512, L=512, TILE=256, H=1, HG=4, C=128, NLEV=7,
                IFF=256, NE=8, TB=128, KCONV=4, IBMAX=256)


class CoreProgram:

    def __init__(self, nc, cfg):
        self.nc = nc
        self.cfg = cfg
        self.b = Builder(nc)

    def const_into(self, u, stack, name, arr):
        arr = np.ascontiguousarray(arr)
        u[name] = stack.enter_context(
            self.nc.sbuf_tensor("c_" + name, list(arr.shape),
                                mybir.dt.from_np(arr.dtype)))
        t = self.nc.inline_tensor(arr, "ct_" + name)
        self.b.emit("sync", lambda E, t=t, name=name: E.dma_start(
            out=u[name][:], in_=t.ap()), writes=[name], dma="xload")

    def barrier(self):
        lasts = {}
        for e in ENGINES:
            for kind, item in reversed(self.b.events[e]):
                if kind == "e":
                    lasts[e] = item
                    break
        targets = list(lasts.values()) + list(self.b.last_dma.values())
        for ce in ENGINES:
            for ev in targets:
                if ev.eng != ce or ev.amt == 16:
                    self.b._wait(ce, ev)

    def build(self):
        nc, cfg, b = self.nc, self.cfg, self.b
        D, L, TILE, H, C = cfg["D"], cfg["L"], cfg["TILE"], cfg["H"], cfg["C"]
        HG, IFF, NE, TB, KC = cfg["HG"], cfg["IFF"], cfg["NE"], cfg["TB"], cfg["KCONV"]
        ND, NT, NG = D // 128, L // TILE, TILE // 128
        NQKV = 3 * H
        VD = HG * 128
        NVC = VD // 128
        NTB = TB // 128
        NIB = IFF // 128
        NHOB = D // 512
        NSPL = cfg.get("NSPL", 2 if TB >= 1024 else 1)
        THB = TB // NSPL

        SH = IFF // 8
        IFFT = IFF + SH
        NIC = (IFFT + 127) // 128
        IFFP = NIC * 128
        TBL = min(512, TB)
        NGB = TBL // 128
        NBR = TB // TBL
        NTBL = 8 * NBR
        NCHK = NBR

        dt_in = lambda n, s, dt=F32: nc.dram_tensor(n, s, dt, kind="ExternalInput").ap()
        x_attn = dt_in("x_attn", [L, D])
        x_res = dt_in("x_res", [TB, D])
        wqkvz = dt_in("wqkvz", [D, (NQKV + H) * 128])
        wba = dt_in("wba", [D, 2 * H])
        convw = dt_in("convw", [NQKV * 128, KC])
        alog4 = dt_in("alog4", [H, 1])
        dtb4 = dt_in("dtb4", [H, 1])
        normw = dt_in("normw", [128, 1])
        wout = dt_in("wout", [VD, D])
        routerw = dt_in("routerw", [NE, D])
        ewgT = dt_in("ewgT", [D, IFFT])
        ewuT = dt_in("ewuT", [D, IFFT])
        ewdT = dt_in("ewdT", [IFFP, D])
        y_out = nc.dram_tensor("y", [TB, D], F32, kind="ExternalOutput").ap()

        agin = nc.dram_tensor("agin", [4, H * 128, L // 4], BF16)
        agout = nc.dram_tensor("agout", [4, VD, L // 4], BF16)
        x1_dram = nc.dram_tensor("x1_dram", [TB, D], F32)
        agin2 = nc.dram_tensor("agin2", [2, D, TB // 2], BF16)
        agout2 = nc.dram_tensor("agout2", [2, 8 * D, TB // 2], BF16)
        ag3in = nc.dram_tensor("ag3in", [NE * NTB, 128], F32)
        ag3out = nc.dram_tensor("ag3out", [8 * NE * NTB, 128], F32)
        rs_in = [nc.dram_tensor(f"rs_in{k}", [8 * TBL, D], BF16)
                 for k in range(NCHK)]
        rs_out = [nc.dram_tensor(f"rs_out{k}", [TBL, D], BF16)
                  for k in range(NCHK)]

        QW = (NQKV + H) * 128 // 4
        wqkvz_bfd = nc.dram_tensor("wqkvz_bfd", [D, (NQKV + H) * 128], BF16)
        NCH_T = TILE // C
        with ExitStack() as actx:
            uA = {}
            sb = lambda n, s, dt=F32: uA.__setitem__(
                n, actx.enter_context(nc.sbuf_tensor(n, s, dt)))
            uA_scan = alloc_scan_bufs(nc, actx, H, C)
            uA.update(uA_scan)
            emit_const_loads(b, uA, scan_const_arrays(H, C), nc)

            sb("xt0", [128, D]); sb("xt1", [128, D])
            sb("xt2", [128, D]); sb("xt3", [128, D])
            sb("sq_ln", [128, D])
            sb("mu0", [128, 2]); sb("mu1", [128, 2])
            sb("rstd0", [128, 1]); sb("rstd1", [128, 1])
            sb("xn0", [128, D]); sb("xn1", [128, D])
            sb("lnxT", [128, ND, TILE], BF16)
            sb("wq0", [128, ND, QW], BF16)
            sb("wq1", [128, ND, QW], BF16)
            sb("wba_s", [128, ND, 2 * H], BF16)
            sb("qkv_raw", [128, NQKV, TILE + KC - 1], BF16)
            sb("silbuf", [128, NQKV, TILE], BF16)
            sb("z_sil0", [128, NCH_T, H * 128])
            sb("z_sil1", [128, NCH_T, H * 128])
            sb("diag4", [128, KC, 128], BF16)
            sb("convw_s", [128, NQKV, KC])
            sb("balog", [H, TILE]); sb("galog", [H, TILE]); sb("lnb_r", [H, TILE])
            sb("arows_scr", [H, TILE])
            sb("alog_c", [H, 1]); sb("dtb_c", [H, 1])

            CSTW = 128
            NSLOT = 4
            for _s in range(NSLOT):
                sb("cst_in" + str(_s), [128, CSTW])
                sb("cst_o" + str(_s), [128, CSTW], BF16)
            cast_seq = [0]

            def emit_cast(src_ap, dst_ap, dkey, sfx="", copy_eng="gpsimd"):
                i = cast_seq[0] % NSLOT
                cast_seq[0] += 1
                ncol = src_ap.shape[1]
                b.emit("sync", lambda E, i=i, src_ap=src_ap, ncol=ncol: E.dma_start(
                    out=uA["cst_in" + str(i)][:, 0:ncol], in_=src_ap),
                    reads=[("cst_in", i)], writes=[("cst_in", i)],
                    dma="wcst" + sfx + str(i))
                b.emit(copy_eng, lambda g, i=i, ncol=ncol, ce=copy_eng:
                    (g.copy if ce == "scalar" else g.tensor_copy)(
                        uA["cst_o" + str(i)][:, 0:ncol], uA["cst_in" + str(i)][:, 0:ncol]),
                    reads=[("cst_in", i), ("cst_o", i)], writes=[("cst_o", i)])
                b.emit("gpsimd", lambda E, i=i, dst_ap=dst_ap, ncol=ncol: E.dma_start(
                    out=dst_ap, in_=uA["cst_o" + str(i)][:, 0:ncol]),
                    reads=[("cst_o", i)], writes=[dkey], dma="wcso" + sfx + str(i))

            def emit_cast_matrix(srcT, dstT, nrows, ncols, dkey, sfx=""):
                for r0 in range(0, nrows, 128):
                    for c0 in range(0, ncols, CSTW):
                        c1 = min(ncols, c0 + CSTW)
                        emit_cast(srcT[r0:r0 + 128, c0:c1], dstT[r0:r0 + 128, c0:c1],
                                  dkey, sfx)

            for q4 in range(4):
                for r0 in range(0, D, 128):
                    for c0 in range(q4 * QW, (q4 + 1) * QW, CSTW):
                        i = cast_seq[0] % 2
                        emit_cast(wqkvz[r0:r0 + 128, c0:c0 + CSTW],
                                  wqkvz_bfd.ap()[r0:r0 + 128, c0:c0 + CSTW],
                                  ("wqkvz_bfd", q4), "A",
                                  copy_eng="scalar" if i == 0 else "vector")
            b.emit("gpsimd", lambda E: E.dma_start(
                out=uA["wba_s"][:],
                in_=wba.rearrange("(dc p) c -> p dc c", p=128)),
                writes=["wba_s"], dma="cast")
            b.emit("sync", lambda E: E.dma_start(
                out=uA["convw_s"][:],
                in_=convw.rearrange("(cc p) k -> p cc k", p=128)),
                writes=["convw_s"], dma="xload")
            b.emit("sync", lambda E: E.dma_start(out=uA["alog_c"][:], in_=alog4),
                   writes=["alog_c"], dma="xload")
            b.emit("sync", lambda E: E.dma_start(out=uA["dtb_c"][:], in_=dtb4),
                   writes=["dtb_c"], dma="xload")

            h3q = lambda nm, q: uA[nm][:]
            b.emit("scalar", lambda s: s.activation(uA["alog_c"][:], uA["alog_c"][:], AF.Exp),
                   reads=["alog_c"], writes=["alog_c"])
            b.emit("vector", lambda v: v.memset(uA["M_f32"][:], 0.0), writes=["M_f32"])
            b.emit("vector", lambda v: v.memset(uA["M_bf"][:], 0.0), writes=["M_bf"])
            b.emit("vector", lambda v: v.memset(
                h3q("qkv_raw", NQKV)[:, :, 0:KC - 1], 0.0), writes=["qkv_raw"])

            em = ScanEmitter(b, uA, H, C, cfg["NLEV"])

            def prep_xload(ti):
                t0 = ti * TILE
                for g in range(NG):
                    b.emit("sync", lambda E, t0=t0, g=g: E.dma_start(
                        out=uA["xt" + str(g)][:],
                        in_=x_attn[t0 + g * 128:t0 + (g + 1) * 128, :]),
                        reads=["xt" + str(g)], writes=["xt" + str(g)],
                        dma="xl" + str(g))

            def prep_ln(ti, pair):
                jobs = [(uA["xt" + str(g)][:], "xt" + str(g), g, str(g % 2))
                        for g in range(2 * pair, min(2 * pair + 2, NG))]
                self.emit_ln_pair(uA, jobs, "lnxT", ND, D, "PSG")

            def prep_q_load(ti, q):
                wpar = q % 2
                wq = uA["wq" + str(wpar)]
                half = QW // 2
                for hh in range(2):
                    b.emit("sync", lambda E, q=q, wq=wq, hh=hh: E.dma_start(
                        out=wq[:][:, :, hh * half:(hh + 1) * half],
                        in_=wqkvz_bfd.ap()[:, q * QW + hh * half:q * QW + (hh + 1) * half]
                        .rearrange("(dc p) c -> p dc c", p=128)),
                        reads=[("wqkvz_bfd", q), ("wq", wpar, hh)],
                        writes=[("wq", wpar, hh)],
                        dma="wq" + str(wpar) + str(hh))

            def prep_q_mm(ti, q):
                tp = str(ti % 2)
                zs = "z_sil" + tp
                lnxT3 = h3q("lnxT", ND)
                wpar = q % 2
                wq = uA["wq" + str(wpar)]
                for j in range(QW // 128):
                    cc = q * (QW // 128) + j
                    hh = j // ((QW // 128) // 2)
                    if cc < NQKV:
                        ps = "PSS"
                        for dc in range(ND):
                            b.emit("tensor", lambda t, j=j, dc=dc, ps=ps, wq=wq: t.matmul(
                                uA[ps][:, 0:TILE], wq[:][:, dc, j * 128:(j + 1) * 128],
                                lnxT3[:, dc, :], start=(dc == 0), stop=(dc == ND - 1)),
                                reads=[("wq", wpar, hh), "lnxT", ps], writes=[ps])
                        b.emit("scalar", lambda s, cc=cc, ps=ps: s.copy(
                            h3q("qkv_raw", NQKV)[:, cc, KC - 1:KC - 1 + TILE],
                            uA[ps][:, 0:TILE]),
                            reads=[ps], writes=["qkv_raw"])
                if q == 3:
                    for g in range(NG):
                        ps = "PSS"
                        for dc in range(ND):
                            b.emit("tensor", lambda t, g=g, dc=dc, ps=ps, wq=wq: t.matmul(
                                uA[ps][:, 0:H * 128], lnxT3[:, dc, g * 128:(g + 1) * 128],
                                wq[:][:, dc, QW - H * 128:QW],
                                start=(dc == 0), stop=(dc == ND - 1)),
                                reads=[("wq", wpar, 0), ("wq", wpar, 1), "lnxT", ps],
                                writes=[ps])
                        b.emit("scalar", lambda s, g=g, ps=ps, zs=zs: s.activation(
                            h3q(zs, NG)[:, g, :], uA[ps][:, 0:H * 128], AF.Silu),
                            reads=[ps], writes=[zs])
            def prep_s3(ti):
                lnxT3 = h3q("lnxT", ND)
                for which, lo in (("balog", 0), ("galog", H)):
                    for dc in range(ND):
                        b.emit("tensor", lambda t, dc=dc, lo=lo: t.matmul(
                            uA["PSS"][0:H, 0:TILE],
                            h3q("wba_s", ND)[:, dc, lo:lo + H], lnxT3[:, dc, :],
                            start=(dc == 0), stop=(dc == ND - 1)),
                            reads=["wba_s", "lnxT", "PSS"], writes=["PSS"])
                    b.emit("vector", lambda v, which=which: v.tensor_copy(
                        uA[which][:], uA["PSS"][0:H, 0:TILE]),
                        reads=["PSS"], writes=[which])
                for cc in range(NQKV):
                    ps = "PSS"
                    for j in range(KC):
                        b.emit("vector", lambda v, cc=cc, j=j: v.tensor_scalar_mul(
                            h3q("diag4", KC)[:, j, :], uA["identb"][:],
                            h3q("convw_s", NQKV)[:, cc, j:j + 1]),
                            reads=["identb", "convw_s", "diag4"], writes=["diag4"])
                    for j in range(KC):
                        b.emit("tensor", lambda t, cc=cc, j=j, ps=ps: t.matmul(
                            uA[ps][:, 0:TILE], h3q("diag4", KC)[:, j, :],
                            h3q("qkv_raw", NQKV)[:, cc, j:j + TILE],
                            start=(j == 0), stop=(j == KC - 1)),
                            reads=["diag4", "qkv_raw", ps], writes=[ps])
                    b.emit("scalar", lambda s, cc=cc, ps=ps: s.activation(
                        h3q("silbuf", NQKV)[:, cc, :], uA[ps][:, 0:TILE], AF.Silu),
                        reads=[ps], writes=["silbuf"])
                b.emit("vector", lambda v: v.tensor_copy(
                    h3q("qkv_raw", NQKV)[:, :, 0:KC - 1],
                    h3q("qkv_raw", NQKV)[:, :, TILE:TILE + KC - 1]),
                    reads=["qkv_raw"], writes=["qkv_raw"])
                bsl = uA["balog"][:]
                asl = uA["galog"][:]
                b.emit("scalar", lambda s, bsl=bsl: s.activation(
                    uA["arows_scr"][:], bsl, AF.Tanh, scale=0.5),
                    reads=["balog"], writes=["arows_scr"])
                b.emit("vector", lambda v, bsl=bsl: v.tensor_scalar(
                    bsl, uA["arows_scr"][:], 0.5, 0.5, OP.mult, OP.add),
                    reads=["arows_scr"], writes=["balog"])
                b.emit("vector", lambda v, asl=asl: v.tensor_scalar_add(
                    asl, asl, uA["dtb_c"][:]), reads=["galog", "dtb_c"], writes=["galog"])
                b.emit("scalar", lambda s, asl=asl: s.activation(
                    uA["arows_scr"][:], asl, AF.Abs), reads=["galog"], writes=["arows_scr"])
                b.emit("scalar", lambda s: s.activation(
                    uA["arows_scr"][:], uA["arows_scr"][:], AF.Exp, scale=-1.0),
                    reads=["arows_scr"], writes=["arows_scr"])
                b.emit("scalar", lambda s: s.activation(
                    uA["arows_scr"][:], uA["arows_scr"][:], AF.Ln, bias=uA["c_one"][0:H, :]),
                    reads=["arows_scr", "c_one"], writes=["arows_scr"])
                b.emit("vector", lambda v, asl=asl: v.tensor_scalar_max(
                    asl, asl, 0.0), reads=["galog"], writes=["galog"])
                b.emit("vector", lambda v, asl=asl: v.tensor_add(
                    asl, asl, uA["arows_scr"][:]), reads=["galog", "arows_scr"],
                    writes=["galog"])
                b.emit("vector", lambda v, asl=asl: v.tensor_scalar(
                    asl, asl, uA["alog_c"][:], -1.0, OP.mult, OP.mult),
                    reads=["galog", "alog_c"], writes=["galog"])
                b.emit("scalar", lambda s, bsl=bsl: s.activation(
                    uA["lnb_r"][:], bsl, AF.Ln),
                    reads=["balog"], writes=["lnb_r"])

            def prep_head(ti, ci):
                if ci == 0:
                    prep_xload(ti)
                    prep_q_load(ti, 0)
                    prep_q_load(ti, 1)
                elif ci == 2:
                    prep_q_load(ti, 2)
                    prep_q_load(ti, 3)

            def prep_tail(ti, ci):
                if ci == 0:
                    prep_ln(ti, 0)
                    prep_ln(ti, 1)
                elif ci == 1:
                    prep_q_mm(ti, 0)
                    prep_q_mm(ti, 1)
                elif ci == 2:
                    prep_q_mm(ti, 2)
                    prep_q_mm(ti, 3)
                else:
                    prep_s3(ti)

            def prep_all(ti):
                prep_xload(ti)
                prep_q_load(ti, 0); prep_q_load(ti, 1)
                prep_ln(ti, 0); prep_ln(ti, 1)
                prep_q_mm(ti, 0); prep_q_mm(ti, 1)
                prep_q_load(ti, 2); prep_q_load(ti, 3)
                prep_q_mm(ti, 2); prep_q_mm(ti, 3)
                prep_s3(ti)

            def scan_p1(ti, ci):
                par = (ti * NCH_T + ci) % 2
                return em.p1_stages(
                    h3q("silbuf", NQKV)[:, H:2 * H, ci * C:(ci + 1) * C],
                    h3q("silbuf", NQKV)[:, 0:H, ci * C:(ci + 1) * C],
                    h3q("silbuf", NQKV)[:, 2 * H:3 * H, ci * C:(ci + 1) * C],
                    uA["galog"][:, ci * C:(ci + 1) * C],
                    uA["lnb_r"][:, ci * C:(ci + 1) * C],
                    uA["balog"][:, ci * C:(ci + 1) * C],
                    par,
                    kq_keys=("silbuf", "silbuf", "silbuf"),
                    row_keys=("galog", "lnb_r", "balog"))

            def scan_p23_stages(ti, ci):
                if cfg.get("NOSCAN"):
                    return [], []
                par = (ti * NCH_T + ci) % 2
                tp = str(ti % 2)
                zs = "z_sil" + tp

                def og_cb(ti=ti, ci=ci):
                    tt = ti * TILE + ci * C
                    q, toff = tt // (L // 4), tt % (L // 4)
                    b.emit("scalar", lambda E, q=q, toff=toff: E.dma_start(
                        out=agin.ap()[q].rearrange("(h p) t -> p h t", p=128)
                        [:, :, toff:toff + C],
                        in_=uA["ogT"][:].rearrange("p (h c) -> p h c", h=H)),
                        reads=["ogT"], writes=[("agin_dram", (ti * NCH_T + ci) // 8)],
                        dma="og")
                ch = em.p23_chain(par)
                out = em.p23_out(
                    par, h3q(zs, NCH_T)[:, ci, :].rearrange("p (h c) -> p h c", h=H),
                    og_cb, zs)
                return ch, out

            NWIN = NT * NCH_T

            def fire_ag(q):
                b.emit("gpsimd", lambda E, q=q: E.collective_compute(
                    "AllGather", OP.bypass,
                    replica_groups=[[0, 1, 2, 3], [4, 5, 6, 7]],
                    ins=[agin.ap()[q]], outs=[agout.ap()[q]]),
                    reads=[("agin_dram", q)], writes=[("agout_dram", q)],
                    collective=True)

            if cfg.get("NOPREP"):
                def prep_all(ti):
                    pass
            prep_all(0)
            for w in range(NWIN + 1):
                ch, out, qs = [], [], []
                if w >= 1 and not cfg.get("NOSCAN"):
                    ch, out = scan_p23_stages((w - 1) // NCH_T, (w - 1) % NCH_T)
                    qs = em.p1b((w - 1) % 2)
                p1s = []
                if w < NWIN and not cfg.get("NOSCAN"):
                    p1s = scan_p1(w // NCH_T, w % NCH_T)
                if w < NWIN and (w // NCH_T) + 1 < NT and not cfg.get("NOPREP"):
                    prep_head((w // NCH_T) + 1, w % NCH_T)
                nop = lambda: None
                g = lambda lst, i: lst[i] if i < len(lst) else nop
                sched = [
                    g(ch, 0),
                    g(out, 0),
                    g(p1s, 0),
                    g(qs, 0),
                    g(ch, 1),
                    g(p1s, 1),
                    g(qs, 1),
                    g(p1s, 2),
                    g(qs, 2), g(qs, 3),
                    g(p1s, 3),
                    g(qs, 4),
                    g(p1s, 4),
                    g(qs, 5), g(qs, 6),
                    g(ch, 2),
                    g(p1s, 5),
                    g(ch, 3), g(ch, 4),
                    g(p1s, 6),
                    g(ch, 5),
                    g(ch, 6), g(ch, 7), g(ch, 8),
                    g(out, 1),
                    g(out, 2),
                    g(out, 3),
                ]
                for f in sched:
                    f()
                if w < NWIN:
                    cti, cci = w // NCH_T, w % NCH_T
                    if cti + 1 < NT and not cfg.get("NOPREP"):
                        prep_tail(cti + 1, cci)
                if not cfg.get("NOSCAN"):
                    if w == 24:
                        fire_ag(0)
                    elif w == 28:
                        fire_ag(1)
                    elif w == 30:
                        fire_ag(2)

            if not cfg.get("NOSCAN"):
                fire_ag(3)
            self.barrier()

        NVC2 = NVC // 2
        assert NVC % 2 == 0
        with ExitStack() as bctx:
            uB = {}
            for nm in ("PS0", "PS1", "PS2", "PS3", "PS4", "PS5"):
                uB[nm] = bctx.enter_context(nc.psum_tensor("b_" + nm, [128, 512], F32))

            with ExitStack() as b1ctx:
                u1 = dict(uB)
                sb1 = lambda n, s, dt=F32: u1.__setitem__(
                    n, b1ctx.enter_context(nc.sbuf_tensor("B1" + n, s, dt)))
                self.const_into(u1, b1ctx, "ident_f", np.eye(128, dtype=np.float32))
                self.const_into(u1, b1ctx, "identE", np.eye(NE, dtype=np.float32))
                self.const_into(u1, b1ctx, "c_eps5",
                                np.full((128, 1), EPS_LN, np.float32))
                sb1("x1", [128, NTB, D])
                sb1("ln2xT", [128, ND, TB], BF16)
                sb1("woutb", [128, NVC2, D], BF16)
                sb1("ogTg", [128, NVC2, THB], BF16)
                sb1("sq_ln", [128, D])
                sb1("mu0", [128, 2]); sb1("mu1", [128, 2])
                sb1("rstd0", [128, 1]); sb1("rstd1", [128, 1])
                sb1("xn0", [128, D]); sb1("xn1", [128, D])
                sb1("normw_c", [128, 1])
                sb1("routw_s", [NE, D])
                sb1("routT", [128, ND * NE], BF16)
                sb1("wts", [128, NTB, NE])
                sb1("comb", [128, NTB, NE])
                sb1("ctb", [NTB * NE, 128])
                sb1("m12", [128, 2])
                sb1("gpk", [1, 8])

                b.emit("sync", lambda E: E.dma_start(out=u1["normw_c"][:], in_=normw),
                       writes=["normw_c"], dma="xload")
                b.emit("sync", lambda E: E.dma_start(
                    out=u1["x1"][:],
                    in_=x_res.rearrange("(g p) d -> p g d", p=128)),
                    writes=["x1"], dma="xload")
                b.emit("sync", lambda E: E.dma_start(out=u1["routw_s"][:], in_=routerw),
                       writes=["routw_s"], dma="xload")
                assert ND * NE <= 512
                for dc in range(ND):
                    b.emit("tensor", lambda t, dc=dc: t.matmul(
                        uB["PS0"][:, dc * NE:(dc + 1) * NE],
                        u1["routw_s"][:, dc * 128:(dc + 1) * 128], u1["identE"][:],
                        start=(dc == 0), stop=(dc == ND - 1), is_transpose=True),
                        reads=["routw_s", "identE", "PS0"], writes=["PS0"])
                b.emit("scalar", lambda s: s.copy(u1["routT"][:], uB["PS0"][:, 0:ND * NE]),
                       reads=["PS0"], writes=["routT"])
                pid_cell = {}
                def rowbase_of(E, vh):
                    if vh not in pid_cell:
                        if "q" not in pid_cell:
                            pid_cell["q"] = (E.partition_id() % 4) * VD
                        pid_cell[vh] = pid_cell["q"] + vh * (NVC2 * 128)
                    return pid_cell[vh]
                for vh in range(2):
                    for vc in range(NVC2):
                        gvc = vh * NVC2 + vc
                        b.emit("gpsimd", lambda E, vc=vc, gvc=gvc: E.dma_start(
                            out=u1["woutb"][:][:, vc, :],
                            in_=wout[gvc * 128:(gvc + 1) * 128, :]),
                            reads=[("woutb", vc)], writes=[("woutb", vc)], dma="wout")
                        b.emit("vector", lambda v, vc=vc: v.tensor_scalar_mul(
                            u1["woutb"][:][:, vc, :],
                            u1["woutb"][:][:, vc, :],
                            u1["normw_c"][:]),
                            reads=[("woutb", vc), "normw_c"], writes=[("woutb", vc)])
                    for sp in range(NSPL):
                        b.emit("sync", lambda E, sp=sp, vh=vh: E.dma_start(
                            out=u1["ogTg"][:],
                            in_=agout.ap().rearrange("q v t -> (q v) t")
                            [bass.ds(rowbase_of(E, vh), NVC2 * 128),
                             sp * THB:(sp + 1) * THB]
                            .rearrange("(vc p) t -> p vc t", p=128)),
                            reads=[("agout_dram", 0), ("agout_dram", 1),
                                   ("agout_dram", 2), ("agout_dram", 3), "ogTg"],
                            writes=["ogTg"], dma="xload")
                        for tg in range(THB // 128):
                            gtg = sp * (THB // 128) + tg
                            for db in range(NHOB):
                                ps = ("PS0", "PS1", "PS2", "PS3")[db % 4]
                                for vc in range(NVC2):
                                    b.emit("tensor", lambda t, tg=tg, db=db, vc=vc,
                                           ps=ps: t.matmul(
                                        uB[ps][:],
                                        u1["ogTg"][:][:, vc, tg * 128:(tg + 1) * 128],
                                        u1["woutb"][:][:, vc, db * 512:(db + 1) * 512],
                                        start=(vc == 0), stop=(vc == NVC2 - 1)),
                                        reads=["ogTg", ("woutb", vc), ps], writes=[ps])
                                b.emit("vector", lambda v, gtg=gtg, db=db, ps=ps:
                                       v.tensor_add(
                                    u1["x1"][:][:, gtg, db * 512:(db + 1) * 512],
                                    u1["x1"][:][:, gtg, db * 512:(db + 1) * 512],
                                    uB[ps][:]),
                                    reads=[ps, "x1"], writes=["x1"])
                HT = NTB // 2
                for hf in range(2):
                    for gt0 in range(hf * HT, (hf + 1) * HT, 2):
                        self.emit_ln_pair(
                            u1, [(u1["x1"][:][:, g, :], "x1", g, str(g % 2))
                                 for g in range(gt0, min(gt0 + 2, NTB))],
                            "ln2xT", ND, D, psname="PS4", add_norm_to_src=True)
                    clo, chi = hf * HT * 128, (hf + 1) * HT * 128
                    b.emit("gpsimd", lambda E, hf=hf, clo=clo, chi=chi: E.dma_start(
                        out=agin2.ap()[hf].rearrange("(dc p) t -> p dc t", p=128),
                        in_=u1["ln2xT"][:][:, :, clo:chi]),
                        reads=["ln2xT"], writes=[("agin2_dram", hf)], dma="bout")
                    b.emit("gpsimd", lambda E, hf=hf: E.collective_compute(
                        "AllGather", OP.bypass,
                        replica_groups=[list(range(8))],
                        ins=[agin2.ap()[hf]], outs=[agout2.ap()[hf]]),
                        reads=[("agin2_dram", hf)], writes=[("agout2_dram", hf)],
                        collective=True)
                b.emit("gpsimd", lambda E: E.dma_start(
                    out=x1_dram.ap().rearrange("(g p) d -> p g d", p=128),
                    in_=u1["x1"][:]), reads=["x1"], writes=["x1_dram"], dma="bout")
                ln2xT3 = u1["ln2xT"][:]
                for gtg in range(NTB):
                    for dc in range(ND):
                        b.emit("tensor", lambda t, gtg=gtg, dc=dc: t.matmul(
                            uB["PS4"][:, 0:NE], ln2xT3[:, dc, gtg * 128:(gtg + 1) * 128],
                            u1["routT"][:, dc * NE:(dc + 1) * NE],
                            start=(dc == 0), stop=(dc == ND - 1)),
                            reads=["ln2xT", "routT", "PS4"], writes=["PS4"])
                    w3 = u1["wts"][:][:, gtg, :]
                    c3 = u1["comb"][:][:, gtg, :]
                    b.emit("vector", lambda v: v.reduce_max(
                        u1["m12"][:, 0:1], uB["PS4"][:, 0:NE], AX.X),
                        reads=["PS4"], writes=["m12"])
                    b.emit("vector", lambda v, w3=w3: v.tensor_scalar_sub(
                        w3, uB["PS4"][:, 0:NE], u1["m12"][:, 0:1]),
                        reads=["PS4", "m12"], writes=["wts"])
                    b.emit("scalar", lambda s, w3=w3: s.activation(w3, w3, AF.Exp),
                        reads=["wts"], writes=["wts"])
                    b.emit("vector", lambda v, w3=w3: v.reduce_sum(
                        u1["m12"][:, 0:1], w3, AX.X), reads=["wts"], writes=["m12"])
                    b.emit("vector", lambda v: v.reciprocal(
                        u1["m12"][:, 1:2], u1["m12"][:, 0:1]), reads=["m12"], writes=["m12"])
                    b.emit("vector", lambda v, w3=w3: v.tensor_scalar_mul(
                        w3, w3, u1["m12"][:, 1:2]), reads=["m12", "wts"], writes=["wts"])
                    b.emit("vector", lambda v, w3=w3: v.reduce_max(
                        u1["m12"][:, 0:1], w3, AX.X), reads=["wts"], writes=["m12"])
                    b.emit("vector", lambda v, w3=w3, c3=c3: v.tensor_scalar(
                        c3, w3, u1["m12"][:, 0:1], -1e30, OP.is_ge, OP.mult),
                        reads=["wts", "m12"], writes=["comb"])
                    b.emit("vector", lambda v, w3=w3, c3=c3: v.tensor_add(
                        c3, c3, w3), reads=["comb", "wts"], writes=["comb"])
                    b.emit("vector", lambda v, c3=c3: v.reduce_max(
                        u1["m12"][:, 1:2], c3, AX.X), reads=["comb"], writes=["m12"])
                    b.emit("vector", lambda v, w3=w3, c3=c3: v.tensor_scalar(
                        c3, w3, u1["m12"][:, 1:2], None, OP.is_ge),
                        reads=["wts", "m12"], writes=["comb"])
                    b.emit("vector", lambda v, w3=w3, c3=c3: v.tensor_mul(
                        c3, c3, w3), reads=["comb", "wts"], writes=["comb"])
                if cfg.get("DBG"):
                    comb_dbg = nc.dram_tensor("comb_dbg", [128, NTB, NE], F32,
                                              kind="ExternalOutput").ap()
                    b.emit("gpsimd", lambda E: E.dma_start(
                        out=comb_dbg, in_=u1["comb"][:]),
                        reads=["comb"], writes=["combdbg"], dma=True)
                b.emit("tensor", lambda t: t.matmul(
                    uB["PS4"][0:NTB * NE, 0:128],
                    u1["comb"][:].rearrange("p a b -> p (a b)"), u1["ident_f"][:],
                    start=True, stop=True, is_transpose=True),
                    reads=["comb", "ident_f", "PS4"], writes=["PS4"])
                b.emit("vector", lambda v: v.tensor_copy(
                    u1["ctb"][:], uB["PS4"][0:NTB * NE, 0:128]),
                    reads=["PS4"], writes=["ctb"])
                for g in range(NTB):
                    b.emit("gpsimd", lambda E, g=g: E.dma_start(
                        out=ag3in.ap().rearrange("(e g) c -> e g c", g=NTB)[:, g, :],
                        in_=u1["ctb"][g * NE:(g + 1) * NE, :]),
                        reads=["ctb"], writes=["ag3in_dram"], dma="bout")
                b.emit("gpsimd", lambda E: E.collective_compute(
                    "AllGather", OP.bypass,
                    replica_groups=[list(range(8))],
                    ins=[ag3in.ap()], outs=[ag3out.ap()]),
                    reads=["ag3in_dram"], writes=["ag3out_dram"], collective=True)
                b.emit("gpsimd", lambda E: E.memset(u1["gpk"][:], 0.0),
                       writes=["gpk"])
                self.barrier()

            with ExitStack() as b4ctx:
                u4 = dict(uB)
                sb4 = lambda n, s, dt=F32: u4.__setitem__(
                    n, b4ctx.enter_context(nc.sbuf_tensor("B4" + n, s, dt)))
                sb4("guTg", [128, ND, IFFP], BF16)
                sb4("guTu", [128, ND, IFFP], BF16)
                sb4("wdT", [128, NIC, D], BF16)
                sb4("strm0", [128, ND, TBL], BF16)
                sb4("strm1", [128, ND, TBL], BF16)
                sb4("hbuf", [128, NIC, TBL], BF16)
                sb4("sgs", [128, TBL])
                sb4("cbc", [128, TBL])
                sb4("rsst0", [128, 512], BF16)
                sb4("rsst1", [128, 512], BF16)
                if IFFP > IFFT:
                    for wnm in ("guTg", "guTu"):
                        b.emit("vector", lambda v, wnm=wnm: v.memset(
                            u4[wnm][:][:, :, IFFT:IFFP], 0.0),
                            writes=[wnm + "_s"])
                wl_seq = [0]

                def emit_wld(src_ap, dst_ap, dkey):
                    i = wl_seq[0] % 4
                    wl_seq[0] += 1
                    b.emit("gpsimd", lambda E, src_ap=src_ap, dst_ap=dst_ap:
                        E.dma_start(out=dst_ap, in_=src_ap),
                        writes=[dkey], dma="wld" + str(i))

                for dc in range(ND):
                    for src, wnm in ((ewgT, "guTg"), (ewuT, "guTu")):
                        emit_wld(src[dc * 128:(dc + 1) * 128, 0:IFFT],
                                 u4[wnm][:][:, dc, 0:IFFT], wnm + "_s")
                for ic in range(NIC):
                    for hh in range(2):
                        emit_wld(
                            ewdT[ic * 128:(ic + 1) * 128,
                                 hh * (D // 2):(hh + 1) * (D // 2)],
                            u4["wdT"][:][:, ic, hh * (D // 2):(hh + 1) * (D // 2)],
                            "wdT_s")
                pid4 = {}
                def rreg(E, r):
                    if "p" not in pid4:
                        pid4["p"] = E.partition_id() % 8
                    if r not in pid4:
                        pid4[r] = pid4["p"] + r * NE
                    return pid4[r]
                ev_seq = [0]
                for tb in range(NTBL):
                    par = tb % 2
                    strm = u4["strm" + str(par)]
                    hf, r = tb // 8, tb % 8
                    b.emit("sync", lambda E, r=r, hf=hf, strm=strm: E.dma_start(
                        out=strm[:],
                        in_=agout2.ap()[hf][r * D:(r + 1) * D, :]
                        .rearrange("(dc p) t -> p dc t", p=128)),
                        reads=[("agout2_dram", hf), ("strm", par)],
                        writes=[("strm", par)], dma="str")
                    b.emit("scalar", lambda E, r=r, hf=hf: E.dma_start(
                        out=u4["cbc"][:].rearrange("p (g c) -> p g c", g=NGB),
                        in_=ag3out.ap().rearrange("(q g) c -> q g c", g=NTB)
                        [bass.ds(rreg(E, r), 1), hf * NGB:hf * NGB + NGB, :]
                        .broadcast_to([128, NGB, 128])),
                        reads=["ag3out_dram", "cbc"], writes=["cbc"], dma="cmb")
                    for ic in range(NIC):
                        for wnm, psn in (("guTg", "PS0"), ("guTu", "PS1")):
                            for dc in range(ND):
                                b.emit("tensor", lambda t, wnm=wnm, psn=psn, dc=dc,
                                       ic=ic, strm=strm: t.matmul(
                                    uB[psn][:, 0:TBL],
                                    u4[wnm][:][:, dc, ic * 128:(ic + 1) * 128],
                                    strm[:][:, dc, :],
                                    start=(dc == 0), stop=(dc == ND - 1)),
                                    reads=[wnm + "_s", ("strm", par), psn],
                                    writes=[psn])
                        b.emit("scalar", lambda s: s.activation(
                            u4["sgs"][:], uB["PS0"][:, 0:TBL], AF.Sigmoid),
                            reads=["PS0", "sgs"], writes=["sgs"])
                        if ic * 128 < IFF:
                            b.emit("vector", lambda v: v.tensor_mul(
                                u4["sgs"][:], u4["sgs"][:], u4["cbc"][:]),
                                reads=["sgs", "cbc"], writes=["sgs"])
                        b.emit("vector", lambda v, ic=ic: v.tensor_mul(
                            u4["hbuf"][:][:, ic, :], u4["sgs"][:], uB["PS1"][:, 0:TBL]),
                            reads=["sgs", "PS1", ("hbuf", ic)], writes=[("hbuf", ic)])
                    for tg in range(NGB):
                        gt = (r * NBR + hf) * NGB + tg
                        for hob in range(NHOB):
                            ps = ("PS2", "PS3", "PS4", "PS5")[(tg * NHOB + hob) % 4]
                            for ic in range(NIC):
                                b.emit("tensor", lambda t, tg=tg, hob=hob, ic=ic,
                                       ps=ps: t.matmul(
                                    uB[ps][:],
                                    u4["hbuf"][:][:, ic, tg * 128:(tg + 1) * 128],
                                    u4["wdT"][:][:, ic, hob * 512:(hob + 1) * 512],
                                    start=(ic == 0), stop=(ic == NIC - 1)),
                                    reads=[("hbuf", ic), "wdT_s", ps], writes=[ps])
                            par2 = ev_seq[0] % 2
                            ev_seq[0] += 1
                            rsst = u4["rsst" + str(par2)]
                            b.emit("vector", lambda v, ps=ps, rsst=rsst: v.tensor_copy(
                                rsst[:], uB[ps][:]),
                                reads=[ps, ("rsst", par2)], writes=[("rsst", par2)])
                            gr = (r * NGB + tg) * 128
                            b.emit("gpsimd", lambda E, gr=gr, hob=hob, hf=hf,
                                   rsst=rsst: E.dma_start(
                                out=rs_in[hf].ap()[gr:gr + 128,
                                                   hob * 512:(hob + 1) * 512],
                                in_=rsst[:]),
                                reads=[("rsst", par2)], writes=["rs_dram"], dma="rso")
                    if r == 7:
                        b.emit("gpsimd", lambda E, hf=hf: E.collective_compute(
                            "ReduceScatter", OP.add,
                            replica_groups=[list(range(8))],
                            ins=[rs_in[hf].ap()], outs=[rs_out[hf].ap()]),
                            reads=["rs_dram"], writes=[("rsout", hf)],
                            collective=True)
                sb4("gpk2", [1, 8])
                b.emit("gpsimd", lambda E: E.memset(u4["gpk2"][:], 0.0),
                       writes=["gpk2"])
                self.barrier()

            with ExitStack() as b5ctx:
                u5 = dict(uB)
                sb5 = lambda n, s, dt=F32: u5.__setitem__(
                    n, b5ctx.enter_context(nc.sbuf_tensor("B5" + n, s, dt)))
                sb5("x1t", [128, NTB, D])
                sb5("rst", [128, NTB, D], BF16)
                b.emit("sync", lambda E: E.dma_start(
                    out=u5["x1t"][:],
                    in_=x1_dram.ap().rearrange("(g p) d -> p g d", p=128)),
                    reads=["x1_dram"], writes=["x1t"], dma="xload")
                gpc = NTB // NCHK
                st = None
                for k in range(NCHK):
                    b.emit("sync", lambda E, k=k: E.dma_start(
                        out=u5["rst"][:][:, k * gpc:(k + 1) * gpc, :],
                        in_=rs_out[k].ap().rearrange("(g p) d -> p g d", p=128)),
                        reads=[("rsout", k)], writes=[("rst", k)], dma="xload")
                    for g in range(k * gpc, (k + 1) * gpc):
                        b.emit("vector", lambda v, g=g: v.tensor_add(
                            u5["x1t"][:][:, g, :], u5["x1t"][:][:, g, :],
                            u5["rst"][:][:, g, :]),
                            reads=["x1t", ("rst", k)], writes=["x1t", ("yadd", k)])
                    st = b.emit("sync", lambda E, k=k: E.dma_start(
                        out=y_out[k * (TB // NCHK):(k + 1) * (TB // NCHK), :]
                        .rearrange("(g p) d -> p g d", p=128),
                        in_=u5["x1t"][:][:, k * gpc:(k + 1) * gpc, :]),
                        reads=[("yadd", k)], dma="yout")
                b._wait("sync", st)
                b._wait("gpsimd", st)
        return b

    def emit_ln_group(self, u, xg_ap, src_key, g, dst_name, ND, D, psname="PS6",
                      add_norm_to_src=False, par=""):
        self.emit_ln_pair(u, [(xg_ap, src_key, g, par)], dst_name, ND, D,
                          psname, add_norm_to_src)

    def emit_ln_pair(self, u, jobs, dst_name, ND, D, psname="PS6",
                     add_norm_to_src=False, ident="ident_f"):
        b = self.b
        J = [(xg_ap, sk, g, "mu" + par, "rstd" + par, "xn" + par)
             for (xg_ap, sk, g, par) in jobs]
        for xg_ap, sk, g, mu, rstd, xn in J:
            b.emit("vector", lambda v, xg_ap=xg_ap, mu=mu: v.reduce_sum(
                u[mu][:, 0:1], xg_ap, AX.X), reads=[sk], writes=[mu])
        for xg_ap, sk, g, mu, rstd, xn in J:
            b.emit("scalar", lambda s, xg_ap=xg_ap, mu=mu: s.activation(
                u["sq_ln"][:], xg_ap, AF.Square, accum_out=u[mu][:, 1:2]),
                reads=[sk, mu, "sq_ln"], writes=["sq_ln", mu])
        for xg_ap, sk, g, mu, rstd, xn in J:
            b.emit("vector", lambda v, mu=mu: v.tensor_scalar_mul(
                u[mu][:], u[mu][:], 1.0 / D), reads=[mu], writes=[mu])
        for xg_ap, sk, g, mu, rstd, xn in J:
            b.emit("vector", lambda v, mu=mu, rstd=rstd: v.tensor_tensor(
                u[rstd][:], u[mu][:, 0:1], u[mu][:, 0:1], OP.mult),
                reads=[mu], writes=[rstd])
        for xg_ap, sk, g, mu, rstd, xn in J:
            b.emit("vector", lambda v, mu=mu, rstd=rstd: v.tensor_sub(
                u[rstd][:], u[mu][:, 1:2], u[rstd][:]),
                reads=[mu, rstd], writes=[rstd])
        for xg_ap, sk, g, mu, rstd, xn in J:
            b.emit("scalar", lambda s, rstd=rstd: s.activation(
                u[rstd][:], u[rstd][:], AF.Ln, bias=u["c_eps5"][:]),
                reads=[rstd, "c_eps5"], writes=[rstd])
        for xg_ap, sk, g, mu, rstd, xn in J:
            b.emit("scalar", lambda s, rstd=rstd: s.activation(
                u[rstd][:], u[rstd][:], AF.Exp, scale=-0.5),
                reads=[rstd], writes=[rstd])
        for xg_ap, sk, g, mu, rstd, xn in J:
            b.emit("vector", lambda v, xg_ap=xg_ap, mu=mu, rstd=rstd, xn=xn:
                   v.tensor_scalar(
                u[xn][:], xg_ap, u[mu][:, 0:1], u[rstd][:],
                OP.subtract, OP.mult),
                reads=[sk, mu, rstd, xn], writes=[xn])
        if add_norm_to_src:
            for xg_ap, sk, g, mu, rstd, xn in J:
                b.emit("vector", lambda v, xg_ap=xg_ap, xn=xn: v.tensor_add(
                    xg_ap, xg_ap, u[xn][:]), reads=[sk, xn], writes=[sk])
        for xg_ap, sk, g, mu, rstd, xn in J:
            for dc in range(ND):
                b.emit("tensor", lambda t, dc=dc, xn=xn: t.matmul(
                    u[psname][:, (dc % 4) * 128:(dc % 4) * 128 + 128],
                    u[xn][:, dc * 128:(dc + 1) * 128], u[ident][:],
                    start=(dc % 4 == 0), stop=(dc % 4 == 3 or dc == ND - 1),
                    is_transpose=True),
                    reads=[xn, ident, psname], writes=[psname])
                if dc % 4 == 3 or dc == ND - 1:
                    lo = (dc // 4) * 4
                    n_in = dc - lo + 1
                    b.emit("scalar", lambda s, lo=lo, n_in=n_in, g=g: s.copy(
                        u[dst_name][:][:, lo:lo + n_in, g * 128:(g + 1) * 128],
                        u[psname][:, 0:n_in * 128]
                        .rearrange("p (dc t) -> p dc t", dc=n_in)),
                        reads=[psname], writes=[dst_name])



def make_core_inputs(cfg, c, x, Wqkv, Wz, Wb, Wa, conv_w, A_log, dt_bias, norm_w,
                     Wout, router_w, shg, shu, shd, wg, wu, wd):
    D, L, H, HG, TB = cfg["D"], cfg["L"], cfg["H"], cfg["HG"], cfg["TB"]
    KEY = HG * 128
    bc, hg = c // 4, c % 4
    h0 = hg * H
    cw = H * 128
    qcols = Wqkv[:, h0 * 128:h0 * 128 + cw]
    kcols = Wqkv[:, KEY + h0 * 128:KEY + h0 * 128 + cw]
    vcols = Wqkv[:, 2 * KEY + h0 * 128:2 * KEY + h0 * 128 + cw]
    zcols = Wz[:, h0 * 128:h0 * 128 + cw]
    wqkvz = np.ascontiguousarray(np.concatenate([qcols, kcols, vcols, zcols], axis=1))
    wba = np.ascontiguousarray(np.concatenate(
        [Wb[:, h0:h0 + H], Wa[:, h0:h0 + H]], axis=1))
    convw = np.ascontiguousarray(np.concatenate(
        [conv_w[h0 * 128:h0 * 128 + cw, 0, :],
         conv_w[KEY + h0 * 128:KEY + h0 * 128 + cw, 0, :],
         conv_w[2 * KEY + h0 * 128:2 * KEY + h0 * 128 + cw, 0, :]], axis=0))
    flat = x.reshape(-1, D)
    IFF = cfg["IFF"]
    SH = IFF // 8
    IFFT = IFF + SH
    NIC = (IFFT + 127) // 128
    IFFP = NIC * 128
    ewgT = np.concatenate([wg[c].T, shg[c * SH:(c + 1) * SH].T], axis=1)
    ewuT = np.concatenate([wu[c].T, shu[c * SH:(c + 1) * SH].T], axis=1)
    ewdT = np.zeros((IFFP, D), np.float32)
    ewdT[:IFF] = wd[c].T
    ewdT[IFF:IFFT] = shd[:, c * SH:(c + 1) * SH].T
    return {
        "x_attn": np.ascontiguousarray(x[bc]),
        "x_res": np.ascontiguousarray(flat[c * TB:(c + 1) * TB]),
        "wqkvz": wqkvz,
        "wba": wba,
        "convw": convw,
        "alog4": np.ascontiguousarray(A_log[h0:h0 + H].reshape(H, 1)),
        "dtb4": np.ascontiguousarray(dt_bias[h0:h0 + H].reshape(H, 1)),
        "normw": np.ascontiguousarray(norm_w.reshape(128, 1)),
        "wout": np.ascontiguousarray(Wout),
        "routerw": np.ascontiguousarray(router_w),
        "ewgT": np.ascontiguousarray(ewgT),
        "ewuT": np.ascontiguousarray(ewuT),
        "ewdT": ewdT,
    }


def build_program(cfg):
    nc = bass.Bass("TRN2", target_bir_lowering=False)
    prog = CoreProgram(nc, cfg)
    with ExitStack() as sems_ctx:
        with nc.Block() as block:
            b = prog.build()
            b.finalize()
            b.sem_handles = {s: sems_ctx.enter_context(nc.semaphore(s))
                             for s in sorted(b.used_sems)}
            block.sync(lambda E: b.replay_engine("sync", E))
            block.tensor(lambda E: b.replay_engine("tensor", E))
            block.vector(lambda E: b.replay_engine("vector", E))
            block.scalar(lambda E: b.replay_engine("scalar", E))
            block.gpsimd(lambda E: b.replay_engine("gpsimd", E))
    return nc


def kernel(L0_attn_x, L0_attn_Wqkv, L0_attn_Wz, L0_attn_Wb, L0_attn_Wa,
           L0_attn_conv_w, L0_attn_A_log, L0_attn_dt_bias, L0_attn_norm_w,
           L0_attn_Wout, L0_ffn_router_w, L0_ffn_shared_gate, L0_ffn_shared_up,
           L0_ffn_shared_down, L0_ffn_w_gate, L0_ffn_w_up, L0_ffn_w_down):
    cfg = default_cfg()
    args = [np.asarray(a) for a in
            (L0_attn_x, L0_attn_Wqkv, L0_attn_Wz, L0_attn_Wb, L0_attn_Wa,
             L0_attn_conv_w, L0_attn_A_log, L0_attn_dt_bias, L0_attn_norm_w,
             L0_attn_Wout, L0_ffn_router_w, L0_ffn_shared_gate, L0_ffn_shared_up,
             L0_ffn_shared_down, L0_ffn_w_gate, L0_ffn_w_up, L0_ffn_w_down)]
    in_maps = [make_core_inputs(cfg, c, *args) for c in range(8)]
    nc = build_program(cfg)
    res = run_bass_kernel_spmd(nc, in_maps, core_ids=list(range(8)))
    y = np.concatenate([np.asarray(r["y"]) for r in res.results], axis=0)
    return y.reshape(L0_attn_x.shape)


# version 38
# speedup vs baseline: 1.0242x; 1.0242x over previous
import math
from contextlib import ExitStack

import numpy as np
import ml_dtypes
import concourse.bass as bass
import concourse.mybir as mybir
from concourse.bass_utils import run_bass_kernel_spmd

F32 = mybir.dt.float32
BF16 = mybir.dt.bfloat16
BF = ml_dtypes.bfloat16
AF = mybir.ActivationFunctionType
OP = mybir.AluOpType
AX = mybir.AxisListType

ENGINES = ("sync", "tensor", "vector", "scalar", "gpsimd")
ESEMS = {
    "sync": (None, "d_sync"),
    "tensor": ("c_pe", None),
    "vector": ("c_dve", None),
    "scalar": ("c_act", "d_act"),
    "gpsimd": ("c_gp", "d_gp"),
}
BASE_SEM_NAMES = ["c_pe", "c_dve", "c_act", "c_gp", "c_cc"]

EPS_L2 = 1e-6
EPS_RMS = 1e-6
EPS_LN = 1e-5


class Ev:
    __slots__ = ("fn", "sem", "amt", "num", "eng", "idx")

    def __init__(self, eng, fn, idx):
        self.eng = eng
        self.fn = fn
        self.sem = None
        self.amt = 0
        self.num = None
        self.idx = idx


class Builder:

    def __init__(self, nc):
        self.nc = nc
        self.events = {e: [] for e in ENGINES}
        self.res = {}
        self.sem_handles = {}
        self.n_waits = 0
        self.n_events = 0
        self.used_sems = set(BASE_SEM_NAMES)
        self.last_dma = {}
        self.maxw = {}
        self.eidx = dict.fromkeys(ENGINES, 0)

    def _rec(self, key):
        r = self.res.get(key)
        if r is None:
            r = {"w": None, "r": []}
            self.res[key] = r
        return r

    def _wait(self, eng, d):
        psem = d.sem if d.sem is not None else ESEMS[d.eng][0]
        if self.maxw.get((eng, psem), -1) >= d.idx:
            return
        self._mark(d)
        self.events[eng].append(("w", d))
        self.maxw[(eng, psem)] = d.idx
        self.n_waits += 1

    def dma_sem(self, eng, tag):
        nm = f"d_{eng}_{tag}"
        self.used_sems.add(nm)
        return nm

    def emit(self, eng, fn, reads=(), writes=(), dma=False, collective=False):
        ev = Ev(eng, fn, self.eidx[eng])
        self.eidx[eng] += 1
        deps = []
        for k in reads:
            r = self._rec(k)
            if r["w"] is not None:
                deps.append(r["w"])
        for k in writes:
            r = self._rec(k)
            if r["w"] is not None:
                deps.append(r["w"])
            deps.extend(r["r"])
        seen = set()
        for d in deps:
            if id(d) in seen:
                continue
            seen.add(id(d))
            if d.eng == eng == "tensor":
                continue
            if d.amt == 16:
                d2 = self.last_dma[d.sem]
                if d2 is not d:
                    if id(d2) in seen:
                        continue
                    seen.add(id(d2))
                    d = d2
                if d.eng != eng:
                    self._wait(d.eng, d)
            self._wait(eng, d)
        if dma:
            tag = dma if isinstance(dma, str) else "main"
            ev.sem = self.dma_sem(eng, tag)
            ev.amt = 16
            self.last_dma[ev.sem] = ev
        elif collective:
            ev.sem = "c_cc"
            ev.amt = 1
        self.events[eng].append(("e", ev))
        self.n_events += 1
        for k in writes:
            r = self._rec(k)
            r["w"] = ev
            r["r"] = []
        for k in reads:
            self._rec(k)["r"].append(ev)
        return ev

    def _mark(self, ev):
        if ev.sem is None:
            ev.sem = ESEMS[ev.eng][0]
            assert ev.sem is not None
            ev.amt = 1

    def finalize(self):
        totals = dict.fromkeys(self.used_sems, 0)
        for e in ENGINES:
            for kind, item in self.events[e]:
                if kind == "e" and item.sem is not None:
                    totals[item.sem] += item.amt
                    item.num = totals[item.sem]
        return totals

    def replay_engine(self, name, E):
        for kind, item in self.events[name]:
            if kind == "w":
                assert item.num is not None, "wait on unfinalized event"
                E.wait_ge(self.sem_handles[item.sem], item.num)
            else:
                ins = item.fn(E)
                if item.sem is not None:
                    assert ins is not None, "milestone event produced no instruction"
                    ins.then_inc(self.sem_handles[item.sem], item.amt)


class ScanEmitter:

    def __init__(self, b: Builder, bufs: dict, H: int, C: int, NLEV: int = 7):
        self.b = b
        self.u = bufs
        self.H = H
        self.C = C
        self.NLEV = NLEV

    def h3(self, ap):
        return ap.rearrange("p (h c) -> p h c", h=self.H)

    def colb(self, ap_col):
        return ap_col.unsqueeze(2).broadcast_to([ap_col.shape[0], self.H, self.C])

    def maskb(self, ap_mask):
        return ap_mask.unsqueeze(1).broadcast_to([self.C, self.H, self.C])

    def p1_stages(self, khat, qhat, vhat, g_row, lnb_row, beta_row, par,
                  kq_keys=("kqv_in", "kqv_in", "kqv_in"),
                  row_keys=("rows_in", "rows_in", "rows_in")):
        b, u, H, C = self.b, self.u, self.H, self.C
        kk, qk, vk = kq_keys
        gk, lbk, bk_ = row_keys
        P = str(par)
        slab, slab2 = u["slab" + P], u["slab2" + P]
        kcb, gam, bkap = u["kcb" + P], u["gam" + P], u["bkap" + P]
        PST, pstk = u["PST" + P], "PST" + P
        bcol = slab[:, 0:H]
        betacol = slab[:, H:2 * H]
        brow3 = self.h3(u["brow"][:])
        browb3 = self.h3(u["browb"][:])
        knT3 = self.h3(u["knT" + P][:])
        qnT3 = self.h3(u["qnT" + P][:])

        def s0():
            b.emit("vector", lambda v, g=g_row: v.tensor_tensor_scan(
                u["b_scr"][:], g, u["zero_row"][:].broadcast_to([H, C]), 0.0, OP.add, OP.add),
                reads=[gk, "zero_row"], writes=["b_scr"])
            b.emit("vector", lambda v, l=lnb_row: v.tensor_add(u["blnb_scr"][:], u["b_scr"][:], l),
                reads=["b_scr", lbk], writes=["blnb_scr"])

            b.emit("tensor", lambda t: t.matmul(
                u["PSG"][:, 0:H], u["b_scr"][:], u["identH"][:], start=True, stop=False,
                is_transpose=True),
                reads=["b_scr", "identH", "PSG"], writes=["PSG"])
            b.emit("tensor", lambda t, br=beta_row: t.matmul(
                u["PSG"][:, H:2 * H], br, u["identH"][:], start=False, stop=True,
                is_transpose=True),
                reads=[bk_, "identH", "PSG"], writes=["PSG"])
            b.emit("vector", lambda v, slab=slab: v.tensor_copy(slab[:], u["PSG"][:, 0:2 * H]),
                reads=["PSG"], writes=["slab" + P])

            b.emit("scalar", lambda s, slab2=slab2, bcol=bcol: s.activation(
                slab2[:, H:2 * H], bcol, AF.Exp),
                reads=["slab" + P], writes=["slab2" + P])
            b.emit("tensor", lambda t: t.matmul(
                u["PSG"][0:1, 2 * H:3 * H], u["b_scr"][:, C - 1:C], u["identH"][:],
                start=True, stop=True, is_transpose=True),
                reads=["b_scr", "identH", "PSG"], writes=["PSG"])
            b.emit("vector", lambda v: v.tensor_copy(u["bc_row"][0:1, :], u["PSG"][0:1, 2 * H:3 * H]),
                reads=["PSG"], writes=["bc_row"])
            b.emit("tensor", lambda t: t.matmul(
                u["PSG"][:, 3 * H:4 * H], u["ones1"][0:1, :], u["bc_row"][0:1, :],
                start=True, stop=True),
                reads=["bc_row", "ones1", "PSG"], writes=["PSG"])
            b.emit("scalar", lambda s, kcb=kcb: s.activation(kcb[:], u["PSG"][:, 3 * H:4 * H], AF.Exp),
                reads=["PSG"], writes=["kcb" + P])
            b.emit("vector", lambda v, slab2=slab2, bcol=bcol: v.tensor_tensor(
                slab2[:, 0:H], u["PSG"][:, 3 * H:4 * H], bcol, OP.subtract),
                reads=["PSG", "slab" + P], writes=["slab2" + P])
            b.emit("scalar", lambda s, gam=gam, slab2=slab2: s.activation(
                gam[:], slab2[:, 0:H], AF.Exp),
                reads=["slab2" + P], writes=["gam" + P])
            b.emit("vector", lambda v, bkap=bkap, betacol=betacol, slab2=slab2: v.tensor_mul(
                bkap[:], betacol, slab2[:, H:2 * H]),
                reads=["slab" + P, "slab2" + P], writes=["bkap" + P])


        def s1():
            for nm, srcb, srck in (("brow", u["b_scr"], "b_scr"),
                                   ("browb", u["blnb_scr"], "blnb_scr")):
                for h in range(H):
                    b.emit("tensor", lambda t, h=h, srcb=srcb: t.matmul(
                        u["PSG"][:, h * C:(h + 1) * C], u["sel" + str(h)][:], srcb[:],
                        start=(h == 0), stop=(h == H - 1)),
                        reads=[srck, "sel" + str(h), "PSG"], writes=["PSG"])
                b.emit("vector", lambda v, nm=nm: v.tensor_copy(u[nm][:], u["PSG"][:, 0:H * C]),
                    reads=["PSG"], writes=[nm])

        def s2():
            b.emit("scalar", lambda s, k=khat: s.activation(self.h3(u["sq"][:]), k, AF.Square),
                reads=[kk], writes=["sq"])
            for h in range(H):
                b.emit("tensor", lambda t, h=h: t.matmul(
                    u["PSG"][:, 4 * H + h:4 * H + h + 1], self.h3(u["sq"][:])[:, h, :],
                    u["ones128"][:, 0:1], start=(h == 0), stop=False),
                    reads=["sq", "ones128", "PSG"], writes=["PSG"])
            b.emit("scalar", lambda s, q=qhat: s.activation(self.h3(u["sq"][:]), q, AF.Square),
                reads=[qk, "PSG"], writes=["sq"])
            for h in range(H):
                b.emit("tensor", lambda t, h=h: t.matmul(
                    u["PSG"][:, 5 * H + h:5 * H + h + 1], self.h3(u["sq"][:])[:, h, :],
                    u["ones128"][:, 0:1], start=False, stop=(h == H - 1)),
                    reads=["sq", "ones128", "PSG"], writes=["PSG"])
            b.emit("scalar", lambda s: s.activation(u["lncol"][:], u["PSG"][:, 4 * H:6 * H], AF.Ln,
                                                    bias=u["c_eps6"][:]),
                reads=["PSG", "c_eps6"], writes=["lncol"])
            b.emit("scalar", lambda s: s.activation(u["rcol"][:, 0:H], u["lncol"][:, 0:H],
                                                    AF.Exp, scale=-0.5),
                reads=["lncol"], writes=["rcol"])
            b.emit("scalar", lambda s: s.activation(u["rcol"][:, H:2 * H], u["lncol"][:, H:2 * H],
                                                    AF.Exp, scale=-0.5, bias=u["c_lnq"][:]),
                reads=["lncol", "c_lnq"], writes=["rcol"])


        def kq(nm, src, srck, rlo):
            for h in range(H):
                b.emit("tensor", lambda t, h=h, src=src, PST=PST: t.matmul(
                    PST[:, h * C:(h + 1) * C], src[:, h, :], u["identb"][:],
                    start=(h == 0), stop=(h == H - 1), is_transpose=True),
                    reads=[srck, "identb", pstk], writes=[pstk])
            tmk = (nm + "_tm" + P) if nm == "k" else "q_tm"
            tm = u[tmk]
            b.emit("vector", lambda v, tm=tm, rlo=rlo, PST=PST: v.tensor_tensor(
                self.h3(tm[:]), self.h3(PST[:, 0:H * C]),
                self.colb(u["rcol"][:, rlo:rlo + H]), OP.mult),
                reads=[pstk, "rcol"], writes=[tmk])
            for h in range(H):
                b.emit("tensor", lambda t, h=h, tm=tm, PST=PST: t.matmul(
                    PST[:, H * C + h * C:H * C + (h + 1) * C], self.h3(tm[:])[:, h, :],
                    u["identb"][:],
                    start=(h == 0), stop=(h == H - 1), is_transpose=True),
                    reads=[tmk, "identb", pstk], writes=[pstk])
            nt = u[nm + "nT" + P]
            b.emit("scalar", lambda s, nt=nt, PST=PST: s.copy(
                nt[:], PST[:, H * C:2 * H * C]),
                reads=[pstk], writes=[nm + "nT" + P])

        def s3():
            kq("k", khat, kk, 0)

        def s4():
            kq("q", qhat, qk, H)
            for h in range(H):
                b.emit("tensor", lambda t, h=h, v_=vhat, PST=PST: t.matmul(
                    PST[:, h * C:(h + 1) * C], v_[:, h, :], u["identb"][:],
                    start=(h == 0), stop=(h == H - 1), is_transpose=True),
                    reads=[vk, "identb", pstk], writes=[pstk])
            b.emit("scalar", lambda s, PST=PST: s.copy(u["v_tm"][:], PST[:, 0:H * C]),
                reads=[pstk], writes=["v_tm"])
            b.emit("vector", lambda v, betacol=betacol: v.tensor_tensor(
                self.h3(u["rhs1" + P][:]), self.h3(u["v_tm"][:]), self.colb(betacol), OP.mult),
                reads=["v_tm", "slab" + P], writes=["rhs1" + P])

        def s5():
            for h in range(H):
                b.emit("tensor", lambda t, h=h, knT3=knT3: t.matmul(
                    u["PSG"][:, h * C:(h + 1) * C], knT3[:, h, :], knT3[:, h, :],
                    start=(h == 0), stop=(h == H - 1)),
                    reads=["knT" + P, "PSG"], writes=["PSG"])

            b.emit("vector", lambda v, bcol=bcol: v.tensor_tensor(
                self.h3(u["e1"][:]), self.colb(bcol), brow3, OP.subtract),
                reads=["slab" + P, "brow"], writes=["e1"])
            b.emit("vector", lambda v: v.tensor_scalar_min(u["e1"][:], u["e1"][:], 0.0),
                reads=["e1"], writes=["e1"])
            b.emit("scalar", lambda s: s.activation(u["e1"][:], u["e1"][:], AF.Exp),
                reads=["e1"], writes=["e1"])
            b.emit("vector", lambda v: v.tensor_mul(u["t1"][:], u["e1"][:], u["PSG"][:, 0:H * C]),
                reads=["e1", "PSG"], writes=["t1"])
            b.emit("vector", lambda v: v.tensor_tensor(
                self.h3(u["t1"][:]), self.h3(u["t1"][:]), self.maskb(u["lmask_neg"][:]), OP.mult),
                reads=["t1", "lmask_neg"], writes=["t1"])
            b.emit("vector", lambda v, betacol=betacol: v.tensor_tensor(
                self.h3(u["Npow" + P][:]), self.h3(u["t1"][:]), self.colb(betacol), OP.mult),
                reads=["t1", "slab" + P], writes=["Npow" + P])

        def s6():
            b.emit("vector", lambda v, bcol=bcol: v.tensor_tensor(
                self.h3(u["e1"][:]), browb3, self.colb(bcol), OP.subtract),
                reads=["browb", "slab" + P], writes=["e1"])
            b.emit("vector", lambda v: v.tensor_scalar_min(u["e1"][:], u["e1"][:], 0.0),
                reads=["e1"], writes=["e1"])
            b.emit("scalar", lambda s: s.activation(u["e1"][:], u["e1"][:], AF.Exp),
                reads=["e1"], writes=["e1"])
            b.emit("vector", lambda v: v.tensor_mul(u["t1"][:], u["e1"][:], u["PSG"][:, 0:H * C]),
                reads=["e1", "PSG"], writes=["t1"])
            b.emit("vector", lambda v: v.tensor_tensor(
                self.h3(u["NTpow" + P][:]), self.h3(u["t1"][:]), self.maskb(u["umask_neg"][:]),
                OP.mult),
                reads=["t1", "umask_neg"], writes=["NTpow" + P])
            b.emit("vector", lambda v: v.tensor_tensor(
                self.h3(u["GT" + P][:]), self.h3(u["NTpow" + P][:]), self.maskb(u["identb"][:]),
                OP.add),
                reads=["NTpow" + P, "identb"], writes=["GT" + P])

            for h in range(H):
                b.emit("tensor", lambda t, h=h, knT3=knT3, qnT3=qnT3: t.matmul(
                    u["PSG"][:, h * C:(h + 1) * C], knT3[:, h, :], qnT3[:, h, :],
                    start=(h == 0), stop=(h == H - 1)),
                    reads=["knT" + P, "qnT" + P, "PSG"], writes=["PSG"])
            b.emit("vector", lambda v, bcol=bcol: v.tensor_tensor(
                self.h3(u["e1"][:]), brow3, self.colb(bcol), OP.subtract),
                reads=["brow", "slab" + P], writes=["e1"])
            b.emit("vector", lambda v: v.tensor_scalar_min(u["e1"][:], u["e1"][:], 0.0),
                reads=["e1"], writes=["e1"])
            b.emit("scalar", lambda s: s.activation(u["e1"][:], u["e1"][:], AF.Exp),
                reads=["e1"], writes=["e1"])
            b.emit("vector", lambda v: v.tensor_mul(u["t1"][:], u["e1"][:], u["PSG"][:, 0:H * C]),
                reads=["e1", "PSG"], writes=["t1"])
            b.emit("vector", lambda v: v.tensor_tensor(
                self.h3(u["Sup" + P][:]), self.h3(u["t1"][:]), self.maskb(u["umask_incl"][:]),
                OP.mult),
                reads=["t1", "umask_incl"], writes=["Sup" + P])

        return [s0, s1, s2, s3, s4, s5, s6]

    def p1b(self, par):
        b, u, H, C = self.b, self.u, self.H, self.C
        P = str(par)
        stages = []
        for lev in range(1, self.NLEV):
            def ql(lev=lev):
                for h in range(H):
                    b.emit("tensor", lambda t, h=h: t.matmul(
                        u["PSQ"][:, h * C:(h + 1) * C],
                        self.h3(u["NTpow" + P][:])[:, h, :],
                        self.h3(u["Npow" + P][:])[:, h, :],
                        start=(h == 0), stop=(h == H - 1)),
                        reads=["NTpow" + P, "Npow" + P, "PSQ"], writes=["PSQ"])
                if lev < self.NLEV - 1:
                    for h in range(H):
                        b.emit("tensor", lambda t, h=h: t.matmul(
                            u["PSR"][:, h * C:(h + 1) * C],
                            self.h3(u["Npow" + P][:])[:, h, :],
                            self.h3(u["NTpow" + P][:])[:, h, :],
                            start=(h == 0), stop=(h == H - 1)),
                            reads=["NTpow" + P, "Npow" + P, "PSR"], writes=["PSR"])
                b.emit("scalar", lambda s: s.copy(u["Npow" + P][:], u["PSQ"][:, 0:H * C]),
                    reads=["PSQ"], writes=["Npow" + P])
                if lev < self.NLEV - 1:
                    b.emit("scalar", lambda s: s.copy(u["NTpow" + P][:], u["PSR"][:, 0:H * C]),
                        reads=["PSR"], writes=["NTpow" + P])
                for h in range(H):
                    b.emit("tensor", lambda t, h=h: t.matmul(
                        u["PSA"][:, h * C:(h + 1) * C],
                        self.h3(u["Npow" + P][:])[:, h, :],
                        self.h3(u["GT" + P][:])[:, h, :],
                        start=(h == 0), stop=(h == H - 1)),
                        reads=["Npow" + P, "GT" + P, "PSA"], writes=["PSA"])
                b.emit("vector", lambda v: v.tensor_add(
                    u["GT" + P][:], u["GT" + P][:], u["PSA"][:, 0:H * C]),
                    reads=["GT" + P, "PSA"], writes=["GT" + P])
            stages.append(ql)

        def wind():
            for h in range(H):
                b.emit("tensor", lambda t, h=h: t.matmul(
                    u["PSR"][:, h * C:(h + 1) * C],
                    self.h3(u["GT" + P][:])[:, h, :],
                    self.h3(u["rhs1" + P][:])[:, h, :],
                    start=(h == 0), stop=(h == H - 1)),
                    reads=["GT" + P, "rhs1" + P, "PSR"], writes=["PSR"])
            b.emit("scalar", lambda s: s.copy(u["Wind"][:], u["PSR"][:, 0:H * C]),
                reads=["PSR"], writes=["Wind"])
        stages.append(wind)
        return stages

    def p23_chain(self, par):
        b, u, H, C = self.b, self.u, self.H, self.C
        P = str(par)
        kcb, gam, bkap = u["kcb" + P], u["gam" + P], u["bkap" + P]
        knT3 = self.h3(u["knT" + P][:])
        qnT3 = self.h3(u["qnT" + P][:])
        gt3 = self.h3(u["GT" + P][:])

        def c0():
            for h in range(H):
                b.emit("tensor", lambda t, h=h: t.matmul(
                    u["PSK"][:, h * C:(h + 1) * C], knT3[:, h, :],
                    self.h3(u["M_bf"][:])[:, h, :],
                    start=(h == 0), stop=(h == H - 1)),
                    reads=["knT" + P, "M_bf", "PSK"], writes=["PSK"])
            for h in range(H):
                b.emit("tensor", lambda t, h=h: t.matmul(
                    u["PSA"][:, h * C:(h + 1) * C], qnT3[:, h, :],
                    self.h3(u["M_bf"][:])[:, h, :],
                    start=(h == 0), stop=(h == H - 1)),
                    reads=["qnT" + P, "M_bf", "PSA"], writes=["PSA"])

        def c1():
            b.emit("vector", lambda v: v.tensor_tensor(
                self.h3(u["X"][:]), self.h3(u["PSK"][:, 0:H * C]),
                self.colb(bkap[:]), OP.mult),
                reads=["PSK", "bkap" + P], writes=["X"])

        def c2():
            for h in range(H):
                b.emit("tensor", lambda t, h=h: t.matmul(
                    u["PSK"][:, h * C:(h + 1) * C], gt3[:, h, :],
                    self.h3(u["X"][:])[:, h, :],
                    start=(h == 0), stop=(h == H - 1)),
                    reads=["GT" + P, "X", "PSK"], writes=["PSK"])

        def c3():
            b.emit("vector", lambda v: v.tensor_sub(
                u["Wf"][:], u["Wind"][:], u["PSK"][:, 0:H * C]),
                reads=["Wind", "PSK"], writes=["Wf"])

        def c4():
            b.emit("vector", lambda v: v.tensor_tensor(
                self.h3(u["Wg"][:]), self.h3(u["Wf"][:]), self.colb(gam[:]),
                OP.mult), reads=["Wf", "gam" + P], writes=["Wg"])

        def c5():
            for h in range(H):
                b.emit("tensor", lambda t, h=h: t.matmul(
                    u["PSK"][:, h * C:(h + 1) * C],
                    self.h3(u["k_tm" + P][:])[:, h, :],
                    self.h3(u["Wg"][:])[:, h, :],
                    start=(h == 0), stop=(h == H - 1)),
                    reads=["k_tm" + P, "Wg", "PSK"], writes=["PSK"])

        def c6():
            b.emit("vector", lambda v: v.tensor_tensor(
                self.h3(u["M_f32"][:]), self.h3(u["M_f32"][:]),
                self.colb(kcb[:]), OP.mult),
                reads=["M_f32", "kcb" + P], writes=["M_f32"])

        def c7():
            b.emit("vector", lambda v: v.tensor_add(
                u["M_f32"][:], u["M_f32"][:], u["PSK"][:, 0:H * C]),
                reads=["M_f32", "PSK"], writes=["M_f32"])

        def c8():
            b.emit("scalar", lambda s: s.copy(u["M_bf"][:], u["M_f32"][:]),
                reads=["M_f32"], writes=["M_bf"])

        return [c0, c1, c2, c3, c4, c5, c6, c7, c8]

    def p23_out(self, par, zsl, og_cb, z_key):
        b, u, H, C = self.b, self.u, self.H, self.C
        P = str(par)
        slab2 = u["slab2" + P]
        PST, pstk = u["PST" + P], "PST" + P
        kappa = slab2[:, H:2 * H]

        def o1a():
            b.emit("vector", lambda v: v.tensor_tensor(
                self.h3(u["t2"][:]), self.h3(u["PSA"][:, 0:H * C]),
                self.colb(kappa), OP.mult),
                reads=["PSA", "slab2" + P], writes=["t2"])

        def o1b():
            b.emit("scalar", lambda s: s.copy(u["W_bf"][:], u["Wf"][:]),
                reads=["Wf"], writes=["W_bf"])
            for h in range(H):
                b.emit("tensor", lambda t, h=h: t.matmul(
                    u["PSS"][:, h * C:(h + 1) * C],
                    self.h3(u["Sup" + P][:])[:, h, :],
                    self.h3(u["W_bf"][:])[:, h, :],
                    start=(h == 0), stop=(h == H - 1)),
                    reads=["Sup" + P, "W_bf", "PSS"], writes=["PSS"])
            b.emit("vector", lambda v: v.tensor_add(
                u["O"][:], u["t2"][:], u["PSS"][:, 0:H * C]),
                reads=["t2", "PSS"], writes=["O"])

        def o2():
            b.emit("scalar", lambda s: s.activation(u["t2"][:], u["O"][:], AF.Square),
                reads=["O"], writes=["t2"])
            b.emit("vector", lambda v: v.tensor_reduce(
                u["s4"][:].unsqueeze(2), self.h3(u["t2"][:]), AX.X, OP.add),
                reads=["t2"], writes=["s4"])
            b.emit("scalar", lambda s: s.activation(u["s4"][:], u["s4"][:], AF.Ln,
                                                    bias=u["c_eps6"][:], scale=1.0 / 128.0),
                reads=["s4", "c_eps6"], writes=["s4"])
            b.emit("scalar", lambda s: s.activation(u["s4"][:], u["s4"][:], AF.Exp, scale=-0.5),
                reads=["s4"], writes=["s4"])
            b.emit("vector", lambda v: v.tensor_tensor(
                self.h3(u["t2"][:]), self.h3(u["O"][:]), self.colb(u["s4"][:]), OP.mult),
                reads=["O", "s4"], writes=["t2"])
            b.emit("vector", lambda v, z=zsl: v.tensor_tensor(
                self.h3(u["og_tm"][:]), self.h3(u["t2"][:]), z, OP.mult),
                reads=["t2", z_key], writes=["og_tm"])

        def o3():
            for h in range(H):
                b.emit("tensor", lambda t, h=h, PST=PST: t.matmul(
                    PST[:, h * C:(h + 1) * C], self.h3(u["og_tm"][:])[:, h, :],
                    u["identb"][:], start=(h == 0), stop=(h == H - 1), is_transpose=True),
                    reads=["og_tm", "identb", pstk], writes=[pstk])
            b.emit("scalar", lambda s, PST=PST: s.copy(u["ogT"][:], PST[:, 0:H * C]),
                reads=[pstk], writes=["ogT"])
            og_cb()

        return [o1a, o1b, o2, o3]


def alloc_scan_bufs(nc, ctx, H, C):
    u = {}

    def sb(name, shape, dt=F32):
        u[name] = ctx.enter_context(nc.sbuf_tensor("scan_" + name, shape, dt))

    def ps(name, shape, dt=F32):
        u[name] = ctx.enter_context(nc.psum_tensor("scan_" + name, shape, dt))

    HC = H * C
    ps("PSG", [128, 512]); ps("PSS", [128, 512]); ps("PSK", [128, 512])
    ps("PSA", [128, 512]); ps("PSQ", [128, 512]); ps("PSR", [128, 512])
    ps("PST0", [128, 1024], BF16); ps("PST1", [128, 1024], BF16)
    sb("b_scr", [H, C]); sb("blnb_scr", [H, C]); sb("zero_row", [H, 1])
    sb("bc_row", [1, H])
    sb("brow", [128, HC]); sb("browb", [128, HC])
    sb("sq", [128, HC], BF16); sb("lncol", [128, 2 * H]); sb("rcol", [128, 2 * H], BF16)
    sb("q_tm", [128, HC], BF16); sb("v_tm", [128, HC], BF16)
    sb("e1", [128, HC]); sb("t1", [128, HC]); sb("t2", [128, HC])
    sb("X", [128, HC], BF16); sb("O", [128, HC])
    sb("s4", [128, H]); sb("og_tm", [128, HC], BF16); sb("ogT", [128, HC], BF16)
    sb("Wg", [128, HC], BF16)
    sb("Wf", [128, HC]); sb("W_bf", [128, HC], BF16)
    sb("M_f32", [128, HC]); sb("M_bf", [128, HC], BF16)
    for P in ("0", "1"):
        sb("slab" + P, [128, 2 * H]); sb("slab2" + P, [128, 2 * H])
        sb("kcb" + P, [128, H]); sb("gam" + P, [128, H], BF16); sb("bkap" + P, [128, H])
        sb("rhs1" + P, [128, HC], BF16)
        sb("knT" + P, [128, HC], BF16); sb("qnT" + P, [128, HC], BF16)
        sb("k_tm" + P, [128, HC], BF16)
        sb("Npow" + P, [128, HC], BF16); sb("NTpow" + P, [128, HC], BF16)
        sb("GT" + P, [128, HC], BF16); sb("Sup" + P, [128, HC], BF16)
    sb("Wind", [128, HC])
    sb("identH", [H, H]); sb("identb", [128, 128], BF16); sb("ident_f", [128, 128])
    sb("ones1", [1, 128]); sb("ones128", [128, 1], BF16)
    sb("c_eps6", [128, 1]); sb("c_eps5", [128, 1]); sb("c_lnq", [128, 1]); sb("c_one", [128, 1])
    sb("lmask_neg", [C, C]); sb("umask_neg", [C, C]); sb("umask_incl", [C, C])
    for h in range(H):
        sb("sel" + str(h), [H, 128])
    return u



def scan_const_arrays(H, C):
    c = {}
    c["identH"] = np.eye(H, dtype=np.float32)
    c["ident_f"] = np.eye(128, dtype=np.float32)
    c["identb"] = np.eye(128, dtype=np.float32).astype(BF)
    c["ones1"] = np.ones((1, 128), np.float32)
    c["ones128"] = np.ones((128, 1), np.float32).astype(BF)
    c["c_eps6"] = np.full((128, 1), 1e-6, np.float32)
    c["c_eps5"] = np.full((128, 1), 1e-5, np.float32)
    c["c_lnq"] = np.full((128, 1), math.log(1.0 / math.sqrt(128.0)), np.float32)
    c["c_one"] = np.ones((128, 1), np.float32)
    c["lmask_neg"] = -np.tril(np.ones((C, C), np.float32), -1)
    c["umask_neg"] = -np.triu(np.ones((C, C), np.float32), 1)
    c["umask_incl"] = np.triu(np.ones((C, C), np.float32), 0)
    for h in range(H):
        s = np.zeros((H, 128), np.float32)
        s[h, :] = 1.0
        c["sel" + str(h)] = s
    c["zero_row"] = np.zeros((H, 1), np.float32)
    return c


def emit_const_loads(b, u, consts, nc, dma_eng="sync"):
    for name, arr in consts.items():
        t = nc.inline_tensor(np.ascontiguousarray(arr), "const_" + name)
        b.emit(dma_eng, lambda E, t=t, name=name: E.dma_start(out=u[name][:], in_=t.ap()),
               writes=[name], dma="xload")



def default_cfg():
    return dict(D=2048, L=4096, TILE=512, H=4, HG=16, C=128, NLEV=7,
                IFF=1408, NE=8, TB=1024, KCONV=4, IBMAX=256)


def mini_cfg():
    return dict(D=512, L=512, TILE=256, H=1, HG=4, C=128, NLEV=7,
                IFF=256, NE=8, TB=128, KCONV=4, IBMAX=256)


class CoreProgram:

    def __init__(self, nc, cfg):
        self.nc = nc
        self.cfg = cfg
        self.b = Builder(nc)

    def const_into(self, u, stack, name, arr):
        arr = np.ascontiguousarray(arr)
        u[name] = stack.enter_context(
            self.nc.sbuf_tensor("c_" + name, list(arr.shape),
                                mybir.dt.from_np(arr.dtype)))
        t = self.nc.inline_tensor(arr, "ct_" + name)
        self.b.emit("sync", lambda E, t=t, name=name: E.dma_start(
            out=u[name][:], in_=t.ap()), writes=[name], dma="xload")

    def barrier(self):
        lasts = {}
        for e in ENGINES:
            for kind, item in reversed(self.b.events[e]):
                if kind == "e":
                    lasts[e] = item
                    break
        targets = list(lasts.values()) + list(self.b.last_dma.values())
        for ce in ENGINES:
            for ev in targets:
                if ev.eng != ce or ev.amt == 16:
                    self.b._wait(ce, ev)

    def build(self):
        nc, cfg, b = self.nc, self.cfg, self.b
        D, L, TILE, H, C = cfg["D"], cfg["L"], cfg["TILE"], cfg["H"], cfg["C"]
        HG, IFF, NE, TB, KC = cfg["HG"], cfg["IFF"], cfg["NE"], cfg["TB"], cfg["KCONV"]
        ND, NT, NG = D // 128, L // TILE, TILE // 128
        NQKV = 3 * H
        VD = HG * 128
        NVC = VD // 128
        NTB = TB // 128
        NIB = IFF // 128
        NHOB = D // 512
        NSPL = cfg.get("NSPL", 2 if TB >= 1024 else 1)
        THB = TB // NSPL

        SH = IFF // 8
        IFFT = IFF + SH
        NIC = (IFFT + 127) // 128
        IFFP = NIC * 128
        TBL = min(512, TB)
        NGB = TBL // 128
        NBR = TB // TBL
        NTBL = 8 * NBR
        NCHK = NBR

        dt_in = lambda n, s, dt=F32: nc.dram_tensor(n, s, dt, kind="ExternalInput").ap()
        x_attn = dt_in("x_attn", [L, D])
        x_res = dt_in("x_res", [TB, D])
        wqkvz = dt_in("wqkvz", [D, (NQKV + H) * 128])
        wba = dt_in("wba", [D, 2 * H])
        convw = dt_in("convw", [NQKV * 128, KC])
        alog4 = dt_in("alog4", [H, 1])
        dtb4 = dt_in("dtb4", [H, 1])
        normw = dt_in("normw", [128, 1])
        wout = dt_in("wout", [VD, D])
        routerw = dt_in("routerw", [NE, D])
        ewgT = dt_in("ewgT", [D, IFFT])
        ewuT = dt_in("ewuT", [D, IFFT])
        ewdT = dt_in("ewdT", [IFFP, D])
        y_out = nc.dram_tensor("y", [TB, D], F32, kind="ExternalOutput").ap()

        agin = nc.dram_tensor("agin", [4, H * 128, L // 4], BF16)
        agout = nc.dram_tensor("agout", [4, VD, L // 4], BF16)
        x1_dram = nc.dram_tensor("x1_dram", [TB, D], F32)
        agin2 = nc.dram_tensor("agin2", [2, D, TB // 2], BF16)
        agout2 = nc.dram_tensor("agout2", [2, 8 * D, TB // 2], BF16)
        ag3in = nc.dram_tensor("ag3in", [NE * NTB, 128], F32)
        ag3out = nc.dram_tensor("ag3out", [8 * NE * NTB, 128], F32)
        rs_in = [nc.dram_tensor(f"rs_in{k}", [8 * TBL, D], BF16)
                 for k in range(NCHK)]
        rs_out = [nc.dram_tensor(f"rs_out{k}", [TBL, D], BF16)
                  for k in range(NCHK)]

        QW = (NQKV + H) * 128 // 4
        wqkvz_bfd = nc.dram_tensor("wqkvz_bfd", [D, (NQKV + H) * 128], BF16)
        NCH_T = TILE // C
        with ExitStack() as actx:
            uA = {}
            sb = lambda n, s, dt=F32: uA.__setitem__(
                n, actx.enter_context(nc.sbuf_tensor(n, s, dt)))
            uA_scan = alloc_scan_bufs(nc, actx, H, C)
            uA.update(uA_scan)
            emit_const_loads(b, uA, scan_const_arrays(H, C), nc)

            sb("xt0", [128, D]); sb("xt1", [128, D])
            sb("xt2", [128, D]); sb("xt3", [128, D])
            sb("sq_ln", [128, D])
            sb("mu0", [128, 2]); sb("mu1", [128, 2])
            sb("rstd0", [128, 1]); sb("rstd1", [128, 1])
            sb("xn0", [128, D]); sb("xn1", [128, D])
            sb("lnxT", [128, ND, TILE], BF16)
            sb("wq0", [128, ND, QW], BF16)
            sb("wq1", [128, ND, QW], BF16)
            sb("wba_s", [128, ND, 2 * H], BF16)
            sb("qkv_raw", [128, NQKV, TILE + KC - 1], BF16)
            sb("silbuf", [128, NQKV, TILE], BF16)
            sb("z_sil0", [128, NCH_T, H * 128])
            sb("z_sil1", [128, NCH_T, H * 128])
            sb("diag8", [128, 2 * KC, 128], BF16)
            sb("convw_s", [128, NQKV, KC])
            sb("balog", [H, TILE]); sb("galog", [H, TILE]); sb("lnb_r", [H, TILE])
            sb("arows_scr", [H, TILE])
            sb("alog_c", [H, 1]); sb("dtb_c", [H, 1])

            CSTW = 128
            NSLOT = 4
            for _s in range(NSLOT):
                sb("cst_in" + str(_s), [128, CSTW])
                sb("cst_o" + str(_s), [128, CSTW], BF16)
            cast_seq = [0]

            def emit_cast(src_ap, dst_ap, dkey, sfx="", copy_eng="gpsimd"):
                i = cast_seq[0] % NSLOT
                cast_seq[0] += 1
                ncol = src_ap.shape[1]
                b.emit("sync", lambda E, i=i, src_ap=src_ap, ncol=ncol: E.dma_start(
                    out=uA["cst_in" + str(i)][:, 0:ncol], in_=src_ap),
                    reads=[("cst_in", i)], writes=[("cst_in", i)],
                    dma="wcst" + sfx + str(i))
                b.emit(copy_eng, lambda g, i=i, ncol=ncol, ce=copy_eng:
                    (g.copy if ce == "scalar" else g.tensor_copy)(
                        uA["cst_o" + str(i)][:, 0:ncol], uA["cst_in" + str(i)][:, 0:ncol]),
                    reads=[("cst_in", i), ("cst_o", i)], writes=[("cst_o", i)])
                b.emit("gpsimd", lambda E, i=i, dst_ap=dst_ap, ncol=ncol: E.dma_start(
                    out=dst_ap, in_=uA["cst_o" + str(i)][:, 0:ncol]),
                    reads=[("cst_o", i)], writes=[dkey], dma="wcso" + sfx + str(i))

            def emit_cast_matrix(srcT, dstT, nrows, ncols, dkey, sfx=""):
                for r0 in range(0, nrows, 128):
                    for c0 in range(0, ncols, CSTW):
                        c1 = min(ncols, c0 + CSTW)
                        emit_cast(srcT[r0:r0 + 128, c0:c1], dstT[r0:r0 + 128, c0:c1],
                                  dkey, sfx)

            for q4 in range(4):
                for r0 in range(0, D, 128):
                    for c0 in range(q4 * QW, (q4 + 1) * QW, CSTW):
                        i = cast_seq[0] % 2
                        emit_cast(wqkvz[r0:r0 + 128, c0:c0 + CSTW],
                                  wqkvz_bfd.ap()[r0:r0 + 128, c0:c0 + CSTW],
                                  ("wqkvz_bfd", q4), "A",
                                  copy_eng="scalar" if i == 0 else "vector")
            b.emit("gpsimd", lambda E: E.dma_start(
                out=uA["wba_s"][:],
                in_=wba.rearrange("(dc p) c -> p dc c", p=128)),
                writes=["wba_s"], dma="cast")
            b.emit("sync", lambda E: E.dma_start(
                out=uA["convw_s"][:],
                in_=convw.rearrange("(cc p) k -> p cc k", p=128)),
                writes=["convw_s"], dma="xload")
            b.emit("sync", lambda E: E.dma_start(out=uA["alog_c"][:], in_=alog4),
                   writes=["alog_c"], dma="xload")
            b.emit("sync", lambda E: E.dma_start(out=uA["dtb_c"][:], in_=dtb4),
                   writes=["dtb_c"], dma="xload")

            h3q = lambda nm, q: uA[nm][:]
            b.emit("scalar", lambda s: s.activation(uA["alog_c"][:], uA["alog_c"][:], AF.Exp),
                   reads=["alog_c"], writes=["alog_c"])
            b.emit("vector", lambda v: v.memset(uA["M_f32"][:], 0.0), writes=["M_f32"])
            b.emit("vector", lambda v: v.memset(uA["M_bf"][:], 0.0), writes=["M_bf"])
            b.emit("vector", lambda v: v.memset(
                h3q("qkv_raw", NQKV)[:, :, 0:KC - 1], 0.0), writes=["qkv_raw"])

            em = ScanEmitter(b, uA, H, C, cfg["NLEV"])

            def prep_xload(ti):
                t0 = ti * TILE
                for g in range(NG):
                    b.emit("sync", lambda E, t0=t0, g=g: E.dma_start(
                        out=uA["xt" + str(g)][:],
                        in_=x_attn[t0 + g * 128:t0 + (g + 1) * 128, :]),
                        reads=["xt" + str(g)], writes=["xt" + str(g)],
                        dma="xl" + str(g))

            def prep_ln(ti, pair):
                jobs = [(uA["xt" + str(g)][:], "xt" + str(g), g, str(g % 2))
                        for g in range(2 * pair, min(2 * pair + 2, NG))]
                self.emit_ln_pair(uA, jobs, "lnxT", ND, D, "PSG")

            def prep_q_load(ti, q):
                wpar = q % 2
                wq = uA["wq" + str(wpar)]
                half = QW // 2
                for hh in range(2):
                    b.emit("sync", lambda E, q=q, wq=wq, hh=hh: E.dma_start(
                        out=wq[:][:, :, hh * half:(hh + 1) * half],
                        in_=wqkvz_bfd.ap()[:, q * QW + hh * half:q * QW + (hh + 1) * half]
                        .rearrange("(dc p) c -> p dc c", p=128)),
                        reads=[("wqkvz_bfd", q), ("wq", wpar, hh)],
                        writes=[("wq", wpar, hh)],
                        dma="wq" + str(wpar) + str(hh))

            def prep_q_mm(ti, q):
                tp = str(ti % 2)
                zs = "z_sil" + tp
                lnxT3 = h3q("lnxT", ND)
                wpar = q % 2
                wq = uA["wq" + str(wpar)]
                for j in range(QW // 128):
                    cc = q * (QW // 128) + j
                    hh = j // ((QW // 128) // 2)
                    if cc < NQKV:
                        ps = "PSS"
                        for dc in range(ND):
                            b.emit("tensor", lambda t, j=j, dc=dc, ps=ps, wq=wq: t.matmul(
                                uA[ps][:, 0:TILE], wq[:][:, dc, j * 128:(j + 1) * 128],
                                lnxT3[:, dc, :], start=(dc == 0), stop=(dc == ND - 1)),
                                reads=[("wq", wpar, hh), "lnxT", ps], writes=[ps])
                        b.emit("scalar", lambda s, cc=cc, ps=ps: s.copy(
                            h3q("qkv_raw", NQKV)[:, cc, KC - 1:KC - 1 + TILE],
                            uA[ps][:, 0:TILE]),
                            reads=[ps], writes=["qkv_raw"])
                if q == 3:
                    for g in range(NG):
                        ps = "PSS"
                        for dc in range(ND):
                            b.emit("tensor", lambda t, g=g, dc=dc, ps=ps, wq=wq: t.matmul(
                                uA[ps][:, 0:H * 128], lnxT3[:, dc, g * 128:(g + 1) * 128],
                                wq[:][:, dc, QW - H * 128:QW],
                                start=(dc == 0), stop=(dc == ND - 1)),
                                reads=[("wq", wpar, 0), ("wq", wpar, 1), "lnxT", ps],
                                writes=[ps])
                        b.emit("scalar", lambda s, g=g, ps=ps, zs=zs: s.activation(
                            h3q(zs, NG)[:, g, :], uA[ps][:, 0:H * 128], AF.Silu),
                            reads=[ps], writes=[zs])
            def prep_s3(ti):
                lnxT3 = h3q("lnxT", ND)
                for which, lo in (("balog", 0), ("galog", H)):
                    for dc in range(ND):
                        b.emit("tensor", lambda t, dc=dc, lo=lo: t.matmul(
                            uA["PSS"][0:H, 0:TILE],
                            h3q("wba_s", ND)[:, dc, lo:lo + H], lnxT3[:, dc, :],
                            start=(dc == 0), stop=(dc == ND - 1)),
                            reads=["wba_s", "lnxT", "PSS"], writes=["PSS"])
                    b.emit("vector", lambda v, which=which: v.tensor_copy(
                        uA[which][:], uA["PSS"][0:H, 0:TILE]),
                        reads=["PSS"], writes=[which])
                for cc in range(NQKV):
                    pc = cc % 2
                    ps = "PSS" if pc == 0 else "PSG"
                    for j in range(KC):
                        b.emit("vector", lambda v, cc=cc, j=j, pc=pc: v.tensor_scalar_mul(
                            h3q("diag8", 2 * KC)[:, pc * KC + j, :], uA["identb"][:],
                            h3q("convw_s", NQKV)[:, cc, j:j + 1]),
                            reads=["identb", "convw_s", ("diag", pc)],
                            writes=[("diag", pc)])
                    for j in range(KC):
                        b.emit("tensor", lambda t, cc=cc, j=j, ps=ps, pc=pc: t.matmul(
                            uA[ps][:, 0:TILE], h3q("diag8", 2 * KC)[:, pc * KC + j, :],
                            h3q("qkv_raw", NQKV)[:, cc, j:j + TILE],
                            start=(j == 0), stop=(j == KC - 1)),
                            reads=[("diag", pc), "qkv_raw", ps], writes=[ps])
                    b.emit("scalar", lambda s, cc=cc, ps=ps: s.activation(
                        h3q("silbuf", NQKV)[:, cc, :], uA[ps][:, 0:TILE], AF.Silu),
                        reads=[ps], writes=["silbuf"])
                b.emit("vector", lambda v: v.tensor_copy(
                    h3q("qkv_raw", NQKV)[:, :, 0:KC - 1],
                    h3q("qkv_raw", NQKV)[:, :, TILE:TILE + KC - 1]),
                    reads=["qkv_raw"], writes=["qkv_raw"])
                bsl = uA["balog"][:]
                asl = uA["galog"][:]
                b.emit("scalar", lambda s, bsl=bsl: s.activation(
                    uA["arows_scr"][:], bsl, AF.Tanh, scale=0.5),
                    reads=["balog"], writes=["arows_scr"])
                b.emit("vector", lambda v, bsl=bsl: v.tensor_scalar(
                    bsl, uA["arows_scr"][:], 0.5, 0.5, OP.mult, OP.add),
                    reads=["arows_scr"], writes=["balog"])
                b.emit("vector", lambda v, asl=asl: v.tensor_scalar_add(
                    asl, asl, uA["dtb_c"][:]), reads=["galog", "dtb_c"], writes=["galog"])
                b.emit("scalar", lambda s, asl=asl: s.activation(
                    uA["arows_scr"][:], asl, AF.Abs), reads=["galog"], writes=["arows_scr"])
                b.emit("scalar", lambda s: s.activation(
                    uA["arows_scr"][:], uA["arows_scr"][:], AF.Exp, scale=-1.0),
                    reads=["arows_scr"], writes=["arows_scr"])
                b.emit("scalar", lambda s: s.activation(
                    uA["arows_scr"][:], uA["arows_scr"][:], AF.Ln, bias=uA["c_one"][0:H, :]),
                    reads=["arows_scr", "c_one"], writes=["arows_scr"])
                b.emit("vector", lambda v, asl=asl: v.tensor_scalar_max(
                    asl, asl, 0.0), reads=["galog"], writes=["galog"])
                b.emit("vector", lambda v, asl=asl: v.tensor_add(
                    asl, asl, uA["arows_scr"][:]), reads=["galog", "arows_scr"],
                    writes=["galog"])
                b.emit("vector", lambda v, asl=asl: v.tensor_scalar(
                    asl, asl, uA["alog_c"][:], -1.0, OP.mult, OP.mult),
                    reads=["galog", "alog_c"], writes=["galog"])
                b.emit("scalar", lambda s, bsl=bsl: s.activation(
                    uA["lnb_r"][:], bsl, AF.Ln),
                    reads=["balog"], writes=["lnb_r"])

            def prep_head(ti, ci):
                if ci == 0:
                    prep_xload(ti)
                    prep_q_load(ti, 0)
                    prep_q_load(ti, 1)
                elif ci == 2:
                    prep_q_load(ti, 2)
                    prep_q_load(ti, 3)

            def prep_tail(ti, ci):
                if ci == 0:
                    prep_ln(ti, 0)
                    prep_ln(ti, 1)
                elif ci == 1:
                    prep_q_mm(ti, 0)
                    prep_q_mm(ti, 1)
                elif ci == 2:
                    prep_q_mm(ti, 2)
                    prep_q_mm(ti, 3)
                else:
                    prep_s3(ti)

            def prep_all(ti):
                prep_xload(ti)
                prep_q_load(ti, 0); prep_q_load(ti, 1)
                prep_ln(ti, 0); prep_ln(ti, 1)
                prep_q_mm(ti, 0); prep_q_mm(ti, 1)
                prep_q_load(ti, 2); prep_q_load(ti, 3)
                prep_q_mm(ti, 2); prep_q_mm(ti, 3)
                prep_s3(ti)

            def scan_p1(ti, ci):
                par = (ti * NCH_T + ci) % 2
                return em.p1_stages(
                    h3q("silbuf", NQKV)[:, H:2 * H, ci * C:(ci + 1) * C],
                    h3q("silbuf", NQKV)[:, 0:H, ci * C:(ci + 1) * C],
                    h3q("silbuf", NQKV)[:, 2 * H:3 * H, ci * C:(ci + 1) * C],
                    uA["galog"][:, ci * C:(ci + 1) * C],
                    uA["lnb_r"][:, ci * C:(ci + 1) * C],
                    uA["balog"][:, ci * C:(ci + 1) * C],
                    par,
                    kq_keys=("silbuf", "silbuf", "silbuf"),
                    row_keys=("galog", "lnb_r", "balog"))

            def scan_p23_stages(ti, ci):
                if cfg.get("NOSCAN"):
                    return [], []
                par = (ti * NCH_T + ci) % 2
                tp = str(ti % 2)
                zs = "z_sil" + tp

                def og_cb(ti=ti, ci=ci):
                    tt = ti * TILE + ci * C
                    q, toff = tt // (L // 4), tt % (L // 4)
                    b.emit("scalar", lambda E, q=q, toff=toff: E.dma_start(
                        out=agin.ap()[q].rearrange("(h p) t -> p h t", p=128)
                        [:, :, toff:toff + C],
                        in_=uA["ogT"][:].rearrange("p (h c) -> p h c", h=H)),
                        reads=["ogT"], writes=[("agin_dram", (ti * NCH_T + ci) // 8)],
                        dma="og")
                ch = em.p23_chain(par)
                out = em.p23_out(
                    par, h3q(zs, NCH_T)[:, ci, :].rearrange("p (h c) -> p h c", h=H),
                    og_cb, zs)
                return ch, out

            NWIN = NT * NCH_T

            def fire_ag(q):
                b.emit("gpsimd", lambda E, q=q: E.collective_compute(
                    "AllGather", OP.bypass,
                    replica_groups=[[0, 1, 2, 3], [4, 5, 6, 7]],
                    ins=[agin.ap()[q]], outs=[agout.ap()[q]]),
                    reads=[("agin_dram", q)], writes=[("agout_dram", q)],
                    collective=True)

            if cfg.get("NOPREP"):
                def prep_all(ti):
                    pass
            prep_all(0)
            for w in range(NWIN + 1):
                ch, out, qs = [], [], []
                if w >= 1 and not cfg.get("NOSCAN"):
                    ch, out = scan_p23_stages((w - 1) // NCH_T, (w - 1) % NCH_T)
                    qs = em.p1b((w - 1) % 2)
                p1s = []
                if w < NWIN and not cfg.get("NOSCAN"):
                    p1s = scan_p1(w // NCH_T, w % NCH_T)
                if w < NWIN and (w // NCH_T) + 1 < NT and not cfg.get("NOPREP"):
                    prep_head((w // NCH_T) + 1, w % NCH_T)
                nop = lambda: None
                g = lambda lst, i: lst[i] if i < len(lst) else nop
                sched = [
                    g(ch, 0),
                    g(out, 0),
                    g(p1s, 0),
                    g(qs, 0),
                    g(ch, 1),
                    g(p1s, 1),
                    g(qs, 1),
                    g(p1s, 2),
                    g(qs, 2), g(qs, 3),
                    g(p1s, 3),
                    g(qs, 4),
                    g(p1s, 4),
                    g(qs, 5), g(qs, 6),
                    g(ch, 2),
                    g(p1s, 5),
                    g(ch, 3), g(ch, 4),
                    g(p1s, 6),
                    g(ch, 5),
                    g(ch, 6), g(ch, 7), g(ch, 8),
                    g(out, 1),
                    g(out, 2),
                    g(out, 3),
                ]
                for f in sched:
                    f()
                if w < NWIN:
                    cti, cci = w // NCH_T, w % NCH_T
                    if cti + 1 < NT and not cfg.get("NOPREP"):
                        prep_tail(cti + 1, cci)
                if not cfg.get("NOSCAN"):
                    if w == 24:
                        fire_ag(0)
                    elif w == 28:
                        fire_ag(1)
                    elif w == 30:
                        fire_ag(2)

            if not cfg.get("NOSCAN"):
                fire_ag(3)
            self.barrier()

        NVC2 = NVC // 2
        assert NVC % 2 == 0
        with ExitStack() as bctx:
            uB = {}
            for nm in ("PS0", "PS1", "PS2", "PS3", "PS4", "PS5"):
                uB[nm] = bctx.enter_context(nc.psum_tensor("b_" + nm, [128, 512], F32))

            with ExitStack() as b1ctx:
                u1 = dict(uB)
                sb1 = lambda n, s, dt=F32: u1.__setitem__(
                    n, b1ctx.enter_context(nc.sbuf_tensor("B1" + n, s, dt)))
                self.const_into(u1, b1ctx, "ident_f", np.eye(128, dtype=np.float32))
                self.const_into(u1, b1ctx, "identE", np.eye(NE, dtype=np.float32))
                self.const_into(u1, b1ctx, "c_eps5",
                                np.full((128, 1), EPS_LN, np.float32))
                sb1("x1", [128, NTB, D])
                sb1("ln2xT", [128, ND, TB], BF16)
                sb1("woutb", [128, NVC2, D], BF16)
                sb1("ogTg", [128, NVC2, THB], BF16)
                sb1("sq_ln", [128, D])
                sb1("mu0", [128, 2]); sb1("mu1", [128, 2])
                sb1("rstd0", [128, 1]); sb1("rstd1", [128, 1])
                sb1("xn0", [128, D]); sb1("xn1", [128, D])
                sb1("normw_c", [128, 1])
                sb1("routw_s", [NE, D])
                sb1("routT", [128, ND * NE], BF16)
                sb1("wts", [128, NTB, NE])
                sb1("comb", [128, NTB, NE])
                sb1("ctb", [NTB * NE, 128])
                sb1("m12", [128, 2])
                sb1("gpk", [1, 8])

                b.emit("sync", lambda E: E.dma_start(out=u1["normw_c"][:], in_=normw),
                       writes=["normw_c"], dma="xload")
                b.emit("sync", lambda E: E.dma_start(
                    out=u1["x1"][:],
                    in_=x_res.rearrange("(g p) d -> p g d", p=128)),
                    writes=["x1"], dma="xload")
                b.emit("sync", lambda E: E.dma_start(out=u1["routw_s"][:], in_=routerw),
                       writes=["routw_s"], dma="xload")
                assert ND * NE <= 512
                for dc in range(ND):
                    b.emit("tensor", lambda t, dc=dc: t.matmul(
                        uB["PS0"][:, dc * NE:(dc + 1) * NE],
                        u1["routw_s"][:, dc * 128:(dc + 1) * 128], u1["identE"][:],
                        start=(dc == 0), stop=(dc == ND - 1), is_transpose=True),
                        reads=["routw_s", "identE", "PS0"], writes=["PS0"])
                b.emit("scalar", lambda s: s.copy(u1["routT"][:], uB["PS0"][:, 0:ND * NE]),
                       reads=["PS0"], writes=["routT"])
                pid_cell = {}
                def rowbase_of(E, vh):
                    if vh not in pid_cell:
                        if "q" not in pid_cell:
                            pid_cell["q"] = (E.partition_id() % 4) * VD
                        pid_cell[vh] = pid_cell["q"] + vh * (NVC2 * 128)
                    return pid_cell[vh]
                for vh in range(2):
                    for vc in range(NVC2):
                        gvc = vh * NVC2 + vc
                        b.emit("gpsimd", lambda E, vc=vc, gvc=gvc: E.dma_start(
                            out=u1["woutb"][:][:, vc, :],
                            in_=wout[gvc * 128:(gvc + 1) * 128, :]),
                            reads=[("woutb", vc)], writes=[("woutb", vc)], dma="wout")
                        b.emit("vector", lambda v, vc=vc: v.tensor_scalar_mul(
                            u1["woutb"][:][:, vc, :],
                            u1["woutb"][:][:, vc, :],
                            u1["normw_c"][:]),
                            reads=[("woutb", vc), "normw_c"], writes=[("woutb", vc)])
                    for sp in range(NSPL):
                        b.emit("sync", lambda E, sp=sp, vh=vh: E.dma_start(
                            out=u1["ogTg"][:],
                            in_=agout.ap().rearrange("q v t -> (q v) t")
                            [bass.ds(rowbase_of(E, vh), NVC2 * 128),
                             sp * THB:(sp + 1) * THB]
                            .rearrange("(vc p) t -> p vc t", p=128)),
                            reads=[("agout_dram", 0), ("agout_dram", 1),
                                   ("agout_dram", 2), ("agout_dram", 3), "ogTg"],
                            writes=["ogTg"], dma="xload")
                        for tg in range(THB // 128):
                            gtg = sp * (THB // 128) + tg
                            for db in range(NHOB):
                                ps = ("PS0", "PS1", "PS2", "PS3")[db % 4]
                                for vc in range(NVC2):
                                    b.emit("tensor", lambda t, tg=tg, db=db, vc=vc,
                                           ps=ps: t.matmul(
                                        uB[ps][:],
                                        u1["ogTg"][:][:, vc, tg * 128:(tg + 1) * 128],
                                        u1["woutb"][:][:, vc, db * 512:(db + 1) * 512],
                                        start=(vc == 0), stop=(vc == NVC2 - 1)),
                                        reads=["ogTg", ("woutb", vc), ps], writes=[ps])
                                b.emit("vector", lambda v, gtg=gtg, db=db, ps=ps:
                                       v.tensor_add(
                                    u1["x1"][:][:, gtg, db * 512:(db + 1) * 512],
                                    u1["x1"][:][:, gtg, db * 512:(db + 1) * 512],
                                    uB[ps][:]),
                                    reads=[ps, "x1"], writes=["x1"])
                HT = NTB // 2
                for hf in range(2):
                    for gt0 in range(hf * HT, (hf + 1) * HT, 2):
                        self.emit_ln_pair(
                            u1, [(u1["x1"][:][:, g, :], "x1", g, str(g % 2))
                                 for g in range(gt0, min(gt0 + 2, NTB))],
                            "ln2xT", ND, D, psname="PS4", add_norm_to_src=True)
                    clo, chi = hf * HT * 128, (hf + 1) * HT * 128
                    b.emit("gpsimd", lambda E, hf=hf, clo=clo, chi=chi: E.dma_start(
                        out=agin2.ap()[hf].rearrange("(dc p) t -> p dc t", p=128),
                        in_=u1["ln2xT"][:][:, :, clo:chi]),
                        reads=["ln2xT"], writes=[("agin2_dram", hf)], dma="bout")
                    b.emit("gpsimd", lambda E, hf=hf: E.collective_compute(
                        "AllGather", OP.bypass,
                        replica_groups=[list(range(8))],
                        ins=[agin2.ap()[hf]], outs=[agout2.ap()[hf]]),
                        reads=[("agin2_dram", hf)], writes=[("agout2_dram", hf)],
                        collective=True)
                b.emit("gpsimd", lambda E: E.dma_start(
                    out=x1_dram.ap().rearrange("(g p) d -> p g d", p=128),
                    in_=u1["x1"][:]), reads=["x1"], writes=["x1_dram"], dma="bout")
                ln2xT3 = u1["ln2xT"][:]
                for gtg in range(NTB):
                    for dc in range(ND):
                        b.emit("tensor", lambda t, gtg=gtg, dc=dc: t.matmul(
                            uB["PS4"][:, 0:NE], ln2xT3[:, dc, gtg * 128:(gtg + 1) * 128],
                            u1["routT"][:, dc * NE:(dc + 1) * NE],
                            start=(dc == 0), stop=(dc == ND - 1)),
                            reads=["ln2xT", "routT", "PS4"], writes=["PS4"])
                    w3 = u1["wts"][:][:, gtg, :]
                    c3 = u1["comb"][:][:, gtg, :]
                    b.emit("vector", lambda v: v.reduce_max(
                        u1["m12"][:, 0:1], uB["PS4"][:, 0:NE], AX.X),
                        reads=["PS4"], writes=["m12"])
                    b.emit("vector", lambda v, w3=w3: v.tensor_scalar_sub(
                        w3, uB["PS4"][:, 0:NE], u1["m12"][:, 0:1]),
                        reads=["PS4", "m12"], writes=["wts"])
                    b.emit("scalar", lambda s, w3=w3: s.activation(w3, w3, AF.Exp),
                        reads=["wts"], writes=["wts"])
                    b.emit("vector", lambda v, w3=w3: v.reduce_sum(
                        u1["m12"][:, 0:1], w3, AX.X), reads=["wts"], writes=["m12"])
                    b.emit("vector", lambda v: v.reciprocal(
                        u1["m12"][:, 1:2], u1["m12"][:, 0:1]), reads=["m12"], writes=["m12"])
                    b.emit("vector", lambda v, w3=w3: v.tensor_scalar_mul(
                        w3, w3, u1["m12"][:, 1:2]), reads=["m12", "wts"], writes=["wts"])
                    b.emit("vector", lambda v, w3=w3: v.reduce_max(
                        u1["m12"][:, 0:1], w3, AX.X), reads=["wts"], writes=["m12"])
                    b.emit("vector", lambda v, w3=w3, c3=c3: v.tensor_scalar(
                        c3, w3, u1["m12"][:, 0:1], -1e30, OP.is_ge, OP.mult),
                        reads=["wts", "m12"], writes=["comb"])
                    b.emit("vector", lambda v, w3=w3, c3=c3: v.tensor_add(
                        c3, c3, w3), reads=["comb", "wts"], writes=["comb"])
                    b.emit("vector", lambda v, c3=c3: v.reduce_max(
                        u1["m12"][:, 1:2], c3, AX.X), reads=["comb"], writes=["m12"])
                    b.emit("vector", lambda v, w3=w3, c3=c3: v.tensor_scalar(
                        c3, w3, u1["m12"][:, 1:2], None, OP.is_ge),
                        reads=["wts", "m12"], writes=["comb"])
                    b.emit("vector", lambda v, w3=w3, c3=c3: v.tensor_mul(
                        c3, c3, w3), reads=["comb", "wts"], writes=["comb"])
                if cfg.get("DBG"):
                    comb_dbg = nc.dram_tensor("comb_dbg", [128, NTB, NE], F32,
                                              kind="ExternalOutput").ap()
                    b.emit("gpsimd", lambda E: E.dma_start(
                        out=comb_dbg, in_=u1["comb"][:]),
                        reads=["comb"], writes=["combdbg"], dma=True)
                b.emit("tensor", lambda t: t.matmul(
                    uB["PS4"][0:NTB * NE, 0:128],
                    u1["comb"][:].rearrange("p a b -> p (a b)"), u1["ident_f"][:],
                    start=True, stop=True, is_transpose=True),
                    reads=["comb", "ident_f", "PS4"], writes=["PS4"])
                b.emit("vector", lambda v: v.tensor_copy(
                    u1["ctb"][:], uB["PS4"][0:NTB * NE, 0:128]),
                    reads=["PS4"], writes=["ctb"])
                for g in range(NTB):
                    b.emit("gpsimd", lambda E, g=g: E.dma_start(
                        out=ag3in.ap().rearrange("(e g) c -> e g c", g=NTB)[:, g, :],
                        in_=u1["ctb"][g * NE:(g + 1) * NE, :]),
                        reads=["ctb"], writes=["ag3in_dram"], dma="bout")
                b.emit("gpsimd", lambda E: E.collective_compute(
                    "AllGather", OP.bypass,
                    replica_groups=[list(range(8))],
                    ins=[ag3in.ap()], outs=[ag3out.ap()]),
                    reads=["ag3in_dram"], writes=["ag3out_dram"], collective=True)
                b.emit("gpsimd", lambda E: E.memset(u1["gpk"][:], 0.0),
                       writes=["gpk"])
                self.barrier()

            with ExitStack() as b4ctx:
                u4 = dict(uB)
                sb4 = lambda n, s, dt=F32: u4.__setitem__(
                    n, b4ctx.enter_context(nc.sbuf_tensor("B4" + n, s, dt)))
                sb4("guTg", [128, ND, IFFP], BF16)
                sb4("guTu", [128, ND, IFFP], BF16)
                sb4("wdT", [128, NIC, D], BF16)
                sb4("strm0", [128, ND, TBL], BF16)
                sb4("strm1", [128, ND, TBL], BF16)
                sb4("hbuf", [128, NIC, TBL], BF16)
                sb4("sgs", [128, TBL])
                sb4("cbc", [128, TBL])
                sb4("rsst0", [128, 512], BF16)
                sb4("rsst1", [128, 512], BF16)
                if IFFP > IFFT:
                    for wnm in ("guTg", "guTu"):
                        b.emit("vector", lambda v, wnm=wnm: v.memset(
                            u4[wnm][:][:, :, IFFT:IFFP], 0.0),
                            writes=[wnm + "_s"])
                wl_seq = [0]

                def emit_wld(src_ap, dst_ap, dkey):
                    i = wl_seq[0] % 8
                    wl_seq[0] += 1
                    b.emit("gpsimd", lambda E, src_ap=src_ap, dst_ap=dst_ap:
                        E.dma_start(out=dst_ap, in_=src_ap),
                        writes=[dkey], dma="wld" + str(i))

                for dc in range(ND):
                    for src, wnm in ((ewgT, "guTg"), (ewuT, "guTu")):
                        emit_wld(src[dc * 128:(dc + 1) * 128, 0:IFFT],
                                 u4[wnm][:][:, dc, 0:IFFT], wnm + "_s")
                for ic in range(NIC):
                    for hh in range(2):
                        emit_wld(
                            ewdT[ic * 128:(ic + 1) * 128,
                                 hh * (D // 2):(hh + 1) * (D // 2)],
                            u4["wdT"][:][:, ic, hh * (D // 2):(hh + 1) * (D // 2)],
                            "wdT_s")
                pid4 = {}
                def rreg(E, r):
                    if "p" not in pid4:
                        pid4["p"] = E.partition_id() % 8
                    if r not in pid4:
                        pid4[r] = pid4["p"] + r * NE
                    return pid4[r]
                ev_seq = [0]
                for tb in range(NTBL):
                    par = tb % 2
                    strm = u4["strm" + str(par)]
                    hf, r = tb // 8, tb % 8
                    b.emit("sync", lambda E, r=r, hf=hf, strm=strm: E.dma_start(
                        out=strm[:],
                        in_=agout2.ap()[hf][r * D:(r + 1) * D, :]
                        .rearrange("(dc p) t -> p dc t", p=128)),
                        reads=[("agout2_dram", hf), ("strm", par)],
                        writes=[("strm", par)], dma="str")
                    b.emit("scalar", lambda E, r=r, hf=hf: E.dma_start(
                        out=u4["cbc"][:].rearrange("p (g c) -> p g c", g=NGB),
                        in_=ag3out.ap().rearrange("(q g) c -> q g c", g=NTB)
                        [bass.ds(rreg(E, r), 1), hf * NGB:hf * NGB + NGB, :]
                        .broadcast_to([128, NGB, 128])),
                        reads=["ag3out_dram", "cbc"], writes=["cbc"], dma="cmb")
                    for ic in range(NIC):
                        for wnm, psn in (("guTg", "PS0"), ("guTu", "PS1")):
                            for dc in range(ND):
                                b.emit("tensor", lambda t, wnm=wnm, psn=psn, dc=dc,
                                       ic=ic, strm=strm: t.matmul(
                                    uB[psn][:, 0:TBL],
                                    u4[wnm][:][:, dc, ic * 128:(ic + 1) * 128],
                                    strm[:][:, dc, :],
                                    start=(dc == 0), stop=(dc == ND - 1)),
                                    reads=[wnm + "_s", ("strm", par), psn],
                                    writes=[psn])
                        b.emit("scalar", lambda s: s.activation(
                            u4["sgs"][:], uB["PS0"][:, 0:TBL], AF.Sigmoid),
                            reads=["PS0", "sgs"], writes=["sgs"])
                        if ic * 128 < IFF:
                            b.emit("vector", lambda v: v.tensor_mul(
                                u4["sgs"][:], u4["sgs"][:], u4["cbc"][:]),
                                reads=["sgs", "cbc"], writes=["sgs"])
                        b.emit("vector", lambda v, ic=ic: v.tensor_mul(
                            u4["hbuf"][:][:, ic, :], u4["sgs"][:], uB["PS1"][:, 0:TBL]),
                            reads=["sgs", "PS1", ("hbuf", ic)], writes=[("hbuf", ic)])
                    for tg in range(NGB):
                        gt = (r * NBR + hf) * NGB + tg
                        for hob in range(NHOB):
                            ps = ("PS2", "PS3", "PS4", "PS5")[(tg * NHOB + hob) % 4]
                            for ic in range(NIC):
                                b.emit("tensor", lambda t, tg=tg, hob=hob, ic=ic,
                                       ps=ps: t.matmul(
                                    uB[ps][:],
                                    u4["hbuf"][:][:, ic, tg * 128:(tg + 1) * 128],
                                    u4["wdT"][:][:, ic, hob * 512:(hob + 1) * 512],
                                    start=(ic == 0), stop=(ic == NIC - 1)),
                                    reads=[("hbuf", ic), "wdT_s", ps], writes=[ps])
                            par2 = ev_seq[0] % 2
                            ev_seq[0] += 1
                            rsst = u4["rsst" + str(par2)]
                            b.emit("vector", lambda v, ps=ps, rsst=rsst: v.tensor_copy(
                                rsst[:], uB[ps][:]),
                                reads=[ps, ("rsst", par2)], writes=[("rsst", par2)])
                            gr = (r * NGB + tg) * 128
                            b.emit("gpsimd", lambda E, gr=gr, hob=hob, hf=hf,
                                   rsst=rsst: E.dma_start(
                                out=rs_in[hf].ap()[gr:gr + 128,
                                                   hob * 512:(hob + 1) * 512],
                                in_=rsst[:]),
                                reads=[("rsst", par2)], writes=["rs_dram"], dma="rso")
                    if r == 7:
                        b.emit("gpsimd", lambda E, hf=hf: E.collective_compute(
                            "ReduceScatter", OP.add,
                            replica_groups=[list(range(8))],
                            ins=[rs_in[hf].ap()], outs=[rs_out[hf].ap()]),
                            reads=["rs_dram"], writes=[("rsout", hf)],
                            collective=True)
                sb4("gpk2", [1, 8])
                b.emit("gpsimd", lambda E: E.memset(u4["gpk2"][:], 0.0),
                       writes=["gpk2"])
                self.barrier()

            with ExitStack() as b5ctx:
                u5 = dict(uB)
                sb5 = lambda n, s, dt=F32: u5.__setitem__(
                    n, b5ctx.enter_context(nc.sbuf_tensor("B5" + n, s, dt)))
                sb5("x1t", [128, NTB, D])
                sb5("rst", [128, NTB, D], BF16)
                b.emit("sync", lambda E: E.dma_start(
                    out=u5["x1t"][:],
                    in_=x1_dram.ap().rearrange("(g p) d -> p g d", p=128)),
                    reads=["x1_dram"], writes=["x1t"], dma="xload")
                gpc = NTB // NCHK
                st = None
                for k in range(NCHK):
                    b.emit("sync", lambda E, k=k: E.dma_start(
                        out=u5["rst"][:][:, k * gpc:(k + 1) * gpc, :],
                        in_=rs_out[k].ap().rearrange("(g p) d -> p g d", p=128)),
                        reads=[("rsout", k)], writes=[("rst", k)], dma="xload")
                    for g in range(k * gpc, (k + 1) * gpc):
                        b.emit("vector", lambda v, g=g: v.tensor_add(
                            u5["x1t"][:][:, g, :], u5["x1t"][:][:, g, :],
                            u5["rst"][:][:, g, :]),
                            reads=["x1t", ("rst", k)], writes=["x1t", ("yadd", k)])
                    st = b.emit("sync", lambda E, k=k: E.dma_start(
                        out=y_out[k * (TB // NCHK):(k + 1) * (TB // NCHK), :]
                        .rearrange("(g p) d -> p g d", p=128),
                        in_=u5["x1t"][:][:, k * gpc:(k + 1) * gpc, :]),
                        reads=[("yadd", k)], dma="yout")
                b._wait("sync", st)
                b._wait("gpsimd", st)
        return b

    def emit_ln_group(self, u, xg_ap, src_key, g, dst_name, ND, D, psname="PS6",
                      add_norm_to_src=False, par=""):
        self.emit_ln_pair(u, [(xg_ap, src_key, g, par)], dst_name, ND, D,
                          psname, add_norm_to_src)

    def emit_ln_pair(self, u, jobs, dst_name, ND, D, psname="PS6",
                     add_norm_to_src=False, ident="ident_f"):
        b = self.b
        J = [(xg_ap, sk, g, "mu" + par, "rstd" + par, "xn" + par)
             for (xg_ap, sk, g, par) in jobs]
        for xg_ap, sk, g, mu, rstd, xn in J:
            b.emit("vector", lambda v, xg_ap=xg_ap, mu=mu: v.reduce_sum(
                u[mu][:, 0:1], xg_ap, AX.X), reads=[sk], writes=[mu])
        for xg_ap, sk, g, mu, rstd, xn in J:
            b.emit("scalar", lambda s, xg_ap=xg_ap, mu=mu: s.activation(
                u["sq_ln"][:], xg_ap, AF.Square, accum_out=u[mu][:, 1:2]),
                reads=[sk, mu, "sq_ln"], writes=["sq_ln", mu])
        for xg_ap, sk, g, mu, rstd, xn in J:
            b.emit("vector", lambda v, mu=mu: v.tensor_scalar_mul(
                u[mu][:], u[mu][:], 1.0 / D), reads=[mu], writes=[mu])
        for xg_ap, sk, g, mu, rstd, xn in J:
            b.emit("vector", lambda v, mu=mu, rstd=rstd: v.tensor_tensor(
                u[rstd][:], u[mu][:, 0:1], u[mu][:, 0:1], OP.mult),
                reads=[mu], writes=[rstd])
        for xg_ap, sk, g, mu, rstd, xn in J:
            b.emit("vector", lambda v, mu=mu, rstd=rstd: v.tensor_sub(
                u[rstd][:], u[mu][:, 1:2], u[rstd][:]),
                reads=[mu, rstd], writes=[rstd])
        for xg_ap, sk, g, mu, rstd, xn in J:
            b.emit("scalar", lambda s, rstd=rstd: s.activation(
                u[rstd][:], u[rstd][:], AF.Ln, bias=u["c_eps5"][:]),
                reads=[rstd, "c_eps5"], writes=[rstd])
        for xg_ap, sk, g, mu, rstd, xn in J:
            b.emit("scalar", lambda s, rstd=rstd: s.activation(
                u[rstd][:], u[rstd][:], AF.Exp, scale=-0.5),
                reads=[rstd], writes=[rstd])
        for xg_ap, sk, g, mu, rstd, xn in J:
            b.emit("vector", lambda v, xg_ap=xg_ap, mu=mu, rstd=rstd, xn=xn:
                   v.tensor_scalar(
                u[xn][:], xg_ap, u[mu][:, 0:1], u[rstd][:],
                OP.subtract, OP.mult),
                reads=[sk, mu, rstd, xn], writes=[xn])
        if add_norm_to_src:
            for xg_ap, sk, g, mu, rstd, xn in J:
                b.emit("vector", lambda v, xg_ap=xg_ap, xn=xn: v.tensor_add(
                    xg_ap, xg_ap, u[xn][:]), reads=[sk, xn], writes=[sk])
        for xg_ap, sk, g, mu, rstd, xn in J:
            for dc in range(ND):
                b.emit("tensor", lambda t, dc=dc, xn=xn: t.matmul(
                    u[psname][:, (dc % 4) * 128:(dc % 4) * 128 + 128],
                    u[xn][:, dc * 128:(dc + 1) * 128], u[ident][:],
                    start=(dc % 4 == 0), stop=(dc % 4 == 3 or dc == ND - 1),
                    is_transpose=True),
                    reads=[xn, ident, psname], writes=[psname])
                if dc % 4 == 3 or dc == ND - 1:
                    lo = (dc // 4) * 4
                    n_in = dc - lo + 1
                    b.emit("scalar", lambda s, lo=lo, n_in=n_in, g=g: s.copy(
                        u[dst_name][:][:, lo:lo + n_in, g * 128:(g + 1) * 128],
                        u[psname][:, 0:n_in * 128]
                        .rearrange("p (dc t) -> p dc t", dc=n_in)),
                        reads=[psname], writes=[dst_name])



def make_core_inputs(cfg, c, x, Wqkv, Wz, Wb, Wa, conv_w, A_log, dt_bias, norm_w,
                     Wout, router_w, shg, shu, shd, wg, wu, wd):
    D, L, H, HG, TB = cfg["D"], cfg["L"], cfg["H"], cfg["HG"], cfg["TB"]
    KEY = HG * 128
    bc, hg = c // 4, c % 4
    h0 = hg * H
    cw = H * 128
    qcols = Wqkv[:, h0 * 128:h0 * 128 + cw]
    kcols = Wqkv[:, KEY + h0 * 128:KEY + h0 * 128 + cw]
    vcols = Wqkv[:, 2 * KEY + h0 * 128:2 * KEY + h0 * 128 + cw]
    zcols = Wz[:, h0 * 128:h0 * 128 + cw]
    wqkvz = np.ascontiguousarray(np.concatenate([qcols, kcols, vcols, zcols], axis=1))
    wba = np.ascontiguousarray(np.concatenate(
        [Wb[:, h0:h0 + H], Wa[:, h0:h0 + H]], axis=1))
    convw = np.ascontiguousarray(np.concatenate(
        [conv_w[h0 * 128:h0 * 128 + cw, 0, :],
         conv_w[KEY + h0 * 128:KEY + h0 * 128 + cw, 0, :],
         conv_w[2 * KEY + h0 * 128:2 * KEY + h0 * 128 + cw, 0, :]], axis=0))
    flat = x.reshape(-1, D)
    IFF = cfg["IFF"]
    SH = IFF // 8
    IFFT = IFF + SH
    NIC = (IFFT + 127) // 128
    IFFP = NIC * 128
    ewgT = np.concatenate([wg[c].T, shg[c * SH:(c + 1) * SH].T], axis=1)
    ewuT = np.concatenate([wu[c].T, shu[c * SH:(c + 1) * SH].T], axis=1)
    ewdT = np.zeros((IFFP, D), np.float32)
    ewdT[:IFF] = wd[c].T
    ewdT[IFF:IFFT] = shd[:, c * SH:(c + 1) * SH].T
    return {
        "x_attn": np.ascontiguousarray(x[bc]),
        "x_res": np.ascontiguousarray(flat[c * TB:(c + 1) * TB]),
        "wqkvz": wqkvz,
        "wba": wba,
        "convw": convw,
        "alog4": np.ascontiguousarray(A_log[h0:h0 + H].reshape(H, 1)),
        "dtb4": np.ascontiguousarray(dt_bias[h0:h0 + H].reshape(H, 1)),
        "normw": np.ascontiguousarray(norm_w.reshape(128, 1)),
        "wout": np.ascontiguousarray(Wout),
        "routerw": np.ascontiguousarray(router_w),
        "ewgT": np.ascontiguousarray(ewgT),
        "ewuT": np.ascontiguousarray(ewuT),
        "ewdT": ewdT,
    }


def build_program(cfg):
    nc = bass.Bass("TRN2", target_bir_lowering=False)
    prog = CoreProgram(nc, cfg)
    with ExitStack() as sems_ctx:
        with nc.Block() as block:
            b = prog.build()
            b.finalize()
            b.sem_handles = {s: sems_ctx.enter_context(nc.semaphore(s))
                             for s in sorted(b.used_sems)}
            block.sync(lambda E: b.replay_engine("sync", E))
            block.tensor(lambda E: b.replay_engine("tensor", E))
            block.vector(lambda E: b.replay_engine("vector", E))
            block.scalar(lambda E: b.replay_engine("scalar", E))
            block.gpsimd(lambda E: b.replay_engine("gpsimd", E))
    return nc


def kernel(L0_attn_x, L0_attn_Wqkv, L0_attn_Wz, L0_attn_Wb, L0_attn_Wa,
           L0_attn_conv_w, L0_attn_A_log, L0_attn_dt_bias, L0_attn_norm_w,
           L0_attn_Wout, L0_ffn_router_w, L0_ffn_shared_gate, L0_ffn_shared_up,
           L0_ffn_shared_down, L0_ffn_w_gate, L0_ffn_w_up, L0_ffn_w_down):
    cfg = default_cfg()
    args = [np.asarray(a) for a in
            (L0_attn_x, L0_attn_Wqkv, L0_attn_Wz, L0_attn_Wb, L0_attn_Wa,
             L0_attn_conv_w, L0_attn_A_log, L0_attn_dt_bias, L0_attn_norm_w,
             L0_attn_Wout, L0_ffn_router_w, L0_ffn_shared_gate, L0_ffn_shared_up,
             L0_ffn_shared_down, L0_ffn_w_gate, L0_ffn_w_up, L0_ffn_w_down)]
    in_maps = [make_core_inputs(cfg, c, *args) for c in range(8)]
    nc = build_program(cfg)
    res = run_bass_kernel_spmd(nc, in_maps, core_ids=list(range(8)))
    y = np.concatenate([np.asarray(r["y"]) for r in res.results], axis=0)
    return y.reshape(L0_attn_x.shape)


# version 40
# speedup vs baseline: 1.0374x; 1.0129x over previous
import math
from contextlib import ExitStack

import numpy as np
import ml_dtypes
import concourse.bass as bass
import concourse.mybir as mybir
from concourse.bass_utils import run_bass_kernel_spmd

F32 = mybir.dt.float32
BF16 = mybir.dt.bfloat16
BF = ml_dtypes.bfloat16
AF = mybir.ActivationFunctionType
OP = mybir.AluOpType
AX = mybir.AxisListType

ENGINES = ("sync", "tensor", "vector", "scalar", "gpsimd")
ESEMS = {
    "sync": (None, "d_sync"),
    "tensor": ("c_pe", None),
    "vector": ("c_dve", None),
    "scalar": ("c_act", "d_act"),
    "gpsimd": ("c_gp", "d_gp"),
}
BASE_SEM_NAMES = ["c_pe", "c_dve", "c_act", "c_gp", "c_cc"]

EPS_L2 = 1e-6
EPS_RMS = 1e-6
EPS_LN = 1e-5


class Ev:
    __slots__ = ("fn", "sem", "amt", "num", "eng", "idx")

    def __init__(self, eng, fn, idx):
        self.eng = eng
        self.fn = fn
        self.sem = None
        self.amt = 0
        self.num = None
        self.idx = idx


class Builder:

    def __init__(self, nc):
        self.nc = nc
        self.events = {e: [] for e in ENGINES}
        self.res = {}
        self.sem_handles = {}
        self.n_waits = 0
        self.n_events = 0
        self.used_sems = set(BASE_SEM_NAMES)
        self.last_dma = {}
        self.maxw = {}
        self.eidx = dict.fromkeys(ENGINES, 0)

    def _rec(self, key):
        r = self.res.get(key)
        if r is None:
            r = {"w": None, "r": []}
            self.res[key] = r
        return r

    def _wait(self, eng, d):
        psem = d.sem if d.sem is not None else ESEMS[d.eng][0]
        if self.maxw.get((eng, psem), -1) >= d.idx:
            return
        self._mark(d)
        self.events[eng].append(("w", d))
        self.maxw[(eng, psem)] = d.idx
        self.n_waits += 1

    def dma_sem(self, eng, tag):
        nm = f"d_{eng}_{tag}"
        self.used_sems.add(nm)
        return nm

    def emit(self, eng, fn, reads=(), writes=(), dma=False, collective=False):
        ev = Ev(eng, fn, self.eidx[eng])
        self.eidx[eng] += 1
        deps = []
        for k in reads:
            r = self._rec(k)
            if r["w"] is not None:
                deps.append(r["w"])
        for k in writes:
            r = self._rec(k)
            if r["w"] is not None:
                deps.append(r["w"])
            deps.extend(r["r"])
        seen = set()
        for d in deps:
            if id(d) in seen:
                continue
            seen.add(id(d))
            if d.eng == eng == "tensor":
                continue
            if d.amt == 16:
                d2 = self.last_dma[d.sem]
                if d2 is not d:
                    if id(d2) in seen:
                        continue
                    seen.add(id(d2))
                    d = d2
                if d.eng != eng:
                    self._wait(d.eng, d)
            self._wait(eng, d)
        if dma:
            tag = dma if isinstance(dma, str) else "main"
            ev.sem = self.dma_sem(eng, tag)
            ev.amt = 16
            self.last_dma[ev.sem] = ev
        elif collective:
            ev.sem = "c_cc"
            ev.amt = 1
        self.events[eng].append(("e", ev))
        self.n_events += 1
        for k in writes:
            r = self._rec(k)
            r["w"] = ev
            r["r"] = []
        for k in reads:
            self._rec(k)["r"].append(ev)
        return ev

    def _mark(self, ev):
        if ev.sem is None:
            ev.sem = ESEMS[ev.eng][0]
            assert ev.sem is not None
            ev.amt = 1

    def finalize(self):
        totals = dict.fromkeys(self.used_sems, 0)
        for e in ENGINES:
            for kind, item in self.events[e]:
                if kind == "e" and item.sem is not None:
                    totals[item.sem] += item.amt
                    item.num = totals[item.sem]
        return totals

    def replay_engine(self, name, E):
        for kind, item in self.events[name]:
            if kind == "w":
                assert item.num is not None, "wait on unfinalized event"
                E.wait_ge(self.sem_handles[item.sem], item.num)
            else:
                ins = item.fn(E)
                if item.sem is not None:
                    assert ins is not None, "milestone event produced no instruction"
                    ins.then_inc(self.sem_handles[item.sem], item.amt)


class ScanEmitter:

    def __init__(self, b: Builder, bufs: dict, H: int, C: int, NLEV: int = 7):
        self.b = b
        self.u = bufs
        self.H = H
        self.C = C
        self.NLEV = NLEV

    def h3(self, ap):
        return ap.rearrange("p (h c) -> p h c", h=self.H)

    def colb(self, ap_col):
        return ap_col.unsqueeze(2).broadcast_to([ap_col.shape[0], self.H, self.C])

    def maskb(self, ap_mask):
        return ap_mask.unsqueeze(1).broadcast_to([self.C, self.H, self.C])

    def p1_stages(self, khat, qhat, vhat, g_row, lnb_row, beta_row, par,
                  kq_keys=("kqv_in", "kqv_in", "kqv_in"),
                  row_keys=("rows_in", "rows_in", "rows_in")):
        b, u, H, C = self.b, self.u, self.H, self.C
        kk, qk, vk = kq_keys
        gk, lbk, bk_ = row_keys
        P = str(par)
        slab, slab2 = u["slab" + P], u["slab2" + P]
        kcb, gam, bkap = u["kcb" + P], u["gam" + P], u["bkap" + P]
        PST, pstk = u["PST" + P], "PST" + P
        bcol = slab[:, 0:H]
        betacol = slab[:, H:2 * H]
        brow3 = self.h3(u["brow"][:])
        browb3 = self.h3(u["browb"][:])
        knT3 = self.h3(u["knT" + P][:])
        qnT3 = self.h3(u["qnT" + P][:])

        def s0():
            b.emit("vector", lambda v, g=g_row: v.tensor_tensor_scan(
                u["b_scr"][:], g, u["zero_row"][:].broadcast_to([H, C]), 0.0, OP.add, OP.add),
                reads=[gk, "zero_row"], writes=["b_scr"])
            b.emit("vector", lambda v, l=lnb_row: v.tensor_add(u["blnb_scr"][:], u["b_scr"][:], l),
                reads=["b_scr", lbk], writes=["blnb_scr"])

            b.emit("tensor", lambda t: t.matmul(
                u["PSG"][:, 0:H], u["b_scr"][:], u["identH"][:], start=True, stop=False,
                is_transpose=True),
                reads=["b_scr", "identH", "PSG"], writes=["PSG"])
            b.emit("tensor", lambda t, br=beta_row: t.matmul(
                u["PSG"][:, H:2 * H], br, u["identH"][:], start=False, stop=True,
                is_transpose=True),
                reads=[bk_, "identH", "PSG"], writes=["PSG"])
            b.emit("vector", lambda v, slab=slab: v.tensor_copy(slab[:], u["PSG"][:, 0:2 * H]),
                reads=["PSG"], writes=["slab" + P])

            b.emit("scalar", lambda s, slab2=slab2, bcol=bcol: s.activation(
                slab2[:, H:2 * H], bcol, AF.Exp),
                reads=["slab" + P], writes=["slab2" + P])
            b.emit("tensor", lambda t: t.matmul(
                u["PSG"][0:1, 2 * H:3 * H], u["b_scr"][:, C - 1:C], u["identH"][:],
                start=True, stop=True, is_transpose=True),
                reads=["b_scr", "identH", "PSG"], writes=["PSG"])
            b.emit("vector", lambda v: v.tensor_copy(u["bc_row"][0:1, :], u["PSG"][0:1, 2 * H:3 * H]),
                reads=["PSG"], writes=["bc_row"])
            b.emit("tensor", lambda t: t.matmul(
                u["PSG"][:, 3 * H:4 * H], u["ones1"][0:1, :], u["bc_row"][0:1, :],
                start=True, stop=True),
                reads=["bc_row", "ones1", "PSG"], writes=["PSG"])
            b.emit("scalar", lambda s, kcb=kcb: s.activation(kcb[:], u["PSG"][:, 3 * H:4 * H], AF.Exp),
                reads=["PSG"], writes=["kcb" + P])
            b.emit("vector", lambda v, slab2=slab2, bcol=bcol: v.tensor_tensor(
                slab2[:, 0:H], u["PSG"][:, 3 * H:4 * H], bcol, OP.subtract),
                reads=["PSG", "slab" + P], writes=["slab2" + P])
            b.emit("scalar", lambda s, gam=gam, slab2=slab2: s.activation(
                gam[:], slab2[:, 0:H], AF.Exp),
                reads=["slab2" + P], writes=["gam" + P])
            b.emit("vector", lambda v, bkap=bkap, betacol=betacol, slab2=slab2: v.tensor_mul(
                bkap[:], betacol, slab2[:, H:2 * H]),
                reads=["slab" + P, "slab2" + P], writes=["bkap" + P])


        def s1():
            for nm, srcb, srck in (("brow", u["b_scr"], "b_scr"),
                                   ("browb", u["blnb_scr"], "blnb_scr")):
                for h in range(H):
                    b.emit("tensor", lambda t, h=h, srcb=srcb: t.matmul(
                        u["PSG"][:, h * C:(h + 1) * C], u["sel" + str(h)][:], srcb[:],
                        start=(h == 0), stop=(h == H - 1)),
                        reads=[srck, "sel" + str(h), "PSG"], writes=["PSG"])
                b.emit("vector", lambda v, nm=nm: v.tensor_copy(u[nm][:], u["PSG"][:, 0:H * C]),
                    reads=["PSG"], writes=[nm])

        def s2():
            b.emit("scalar", lambda s, k=khat: s.activation(self.h3(u["sq"][:]), k, AF.Square),
                reads=[kk], writes=["sq"])
            for h in range(H):
                b.emit("tensor", lambda t, h=h: t.matmul(
                    u["PSG"][:, 4 * H + h:4 * H + h + 1], self.h3(u["sq"][:])[:, h, :],
                    u["ones128"][:, 0:1], start=(h == 0), stop=False),
                    reads=["sq", "ones128", "PSG"], writes=["PSG"])
            b.emit("scalar", lambda s, q=qhat: s.activation(self.h3(u["sq"][:]), q, AF.Square),
                reads=[qk, "PSG"], writes=["sq"])
            for h in range(H):
                b.emit("tensor", lambda t, h=h: t.matmul(
                    u["PSG"][:, 5 * H + h:5 * H + h + 1], self.h3(u["sq"][:])[:, h, :],
                    u["ones128"][:, 0:1], start=False, stop=(h == H - 1)),
                    reads=["sq", "ones128", "PSG"], writes=["PSG"])
            b.emit("scalar", lambda s: s.activation(u["lncol"][:], u["PSG"][:, 4 * H:6 * H], AF.Ln,
                                                    bias=u["c_eps6"][:]),
                reads=["PSG", "c_eps6"], writes=["lncol"])
            b.emit("scalar", lambda s: s.activation(u["rcol"][:, 0:H], u["lncol"][:, 0:H],
                                                    AF.Exp, scale=-0.5),
                reads=["lncol"], writes=["rcol"])
            b.emit("scalar", lambda s: s.activation(u["rcol"][:, H:2 * H], u["lncol"][:, H:2 * H],
                                                    AF.Exp, scale=-0.5, bias=u["c_lnq"][:]),
                reads=["lncol", "c_lnq"], writes=["rcol"])


        def kq(nm, src, srck, rlo):
            for h in range(H):
                b.emit("tensor", lambda t, h=h, src=src, PST=PST: t.matmul(
                    PST[:, h * C:(h + 1) * C], src[:, h, :], u["identb"][:],
                    start=(h == 0), stop=(h == H - 1), is_transpose=True),
                    reads=[srck, "identb", pstk], writes=[pstk])
            tmk = (nm + "_tm" + P) if nm == "k" else "q_tm"
            tm = u[tmk]
            b.emit("vector", lambda v, tm=tm, rlo=rlo, PST=PST: v.tensor_tensor(
                self.h3(tm[:]), self.h3(PST[:, 0:H * C]),
                self.colb(u["rcol"][:, rlo:rlo + H]), OP.mult),
                reads=[pstk, "rcol"], writes=[tmk])
            for h in range(H):
                b.emit("tensor", lambda t, h=h, tm=tm, PST=PST: t.matmul(
                    PST[:, H * C + h * C:H * C + (h + 1) * C], self.h3(tm[:])[:, h, :],
                    u["identb"][:],
                    start=(h == 0), stop=(h == H - 1), is_transpose=True),
                    reads=[tmk, "identb", pstk], writes=[pstk])
            nt = u[nm + "nT" + P]
            b.emit("scalar", lambda s, nt=nt, PST=PST: s.copy(
                nt[:], PST[:, H * C:2 * H * C]),
                reads=[pstk], writes=[nm + "nT" + P])

        def s3():
            kq("k", khat, kk, 0)

        def s4():
            kq("q", qhat, qk, H)
            for h in range(H):
                b.emit("tensor", lambda t, h=h, v_=vhat, PST=PST: t.matmul(
                    PST[:, h * C:(h + 1) * C], v_[:, h, :], u["identb"][:],
                    start=(h == 0), stop=(h == H - 1), is_transpose=True),
                    reads=[vk, "identb", pstk], writes=[pstk])
            b.emit("scalar", lambda s, PST=PST: s.copy(u["v_tm"][:], PST[:, 0:H * C]),
                reads=[pstk], writes=["v_tm"])
            b.emit("vector", lambda v, betacol=betacol: v.tensor_tensor(
                self.h3(u["rhs1" + P][:]), self.h3(u["v_tm"][:]), self.colb(betacol), OP.mult),
                reads=["v_tm", "slab" + P], writes=["rhs1" + P])

        def s5():
            for h in range(H):
                b.emit("tensor", lambda t, h=h, knT3=knT3: t.matmul(
                    u["PSG"][:, h * C:(h + 1) * C], knT3[:, h, :], knT3[:, h, :],
                    start=(h == 0), stop=(h == H - 1)),
                    reads=["knT" + P, "PSG"], writes=["PSG"])

            b.emit("vector", lambda v, bcol=bcol: v.tensor_tensor(
                self.h3(u["e1"][:]), self.colb(bcol), brow3, OP.subtract),
                reads=["slab" + P, "brow"], writes=["e1"])
            b.emit("vector", lambda v: v.tensor_scalar_min(u["e1"][:], u["e1"][:], 0.0),
                reads=["e1"], writes=["e1"])
            b.emit("scalar", lambda s: s.activation(u["e1"][:], u["e1"][:], AF.Exp),
                reads=["e1"], writes=["e1"])
            b.emit("vector", lambda v: v.tensor_mul(u["t1"][:], u["e1"][:], u["PSG"][:, 0:H * C]),
                reads=["e1", "PSG"], writes=["t1"])
            b.emit("vector", lambda v: v.tensor_tensor(
                self.h3(u["t1"][:]), self.h3(u["t1"][:]), self.maskb(u["lmask_neg"][:]), OP.mult),
                reads=["t1", "lmask_neg"], writes=["t1"])
            b.emit("vector", lambda v, betacol=betacol: v.tensor_tensor(
                self.h3(u["Npow" + P][:]), self.h3(u["t1"][:]), self.colb(betacol), OP.mult),
                reads=["t1", "slab" + P], writes=["Npow" + P])

        def s6():
            b.emit("vector", lambda v, bcol=bcol: v.tensor_tensor(
                self.h3(u["e1"][:]), browb3, self.colb(bcol), OP.subtract),
                reads=["browb", "slab" + P], writes=["e1"])
            b.emit("vector", lambda v: v.tensor_scalar_min(u["e1"][:], u["e1"][:], 0.0),
                reads=["e1"], writes=["e1"])
            b.emit("scalar", lambda s: s.activation(u["e1"][:], u["e1"][:], AF.Exp),
                reads=["e1"], writes=["e1"])
            b.emit("vector", lambda v: v.tensor_mul(u["t1"][:], u["e1"][:], u["PSG"][:, 0:H * C]),
                reads=["e1", "PSG"], writes=["t1"])
            b.emit("vector", lambda v: v.tensor_tensor(
                self.h3(u["NTpow" + P][:]), self.h3(u["t1"][:]), self.maskb(u["umask_neg"][:]),
                OP.mult),
                reads=["t1", "umask_neg"], writes=["NTpow" + P])
            b.emit("vector", lambda v: v.tensor_tensor(
                self.h3(u["GT" + P][:]), self.h3(u["NTpow" + P][:]), self.maskb(u["identb"][:]),
                OP.add),
                reads=["NTpow" + P, "identb"], writes=["GT" + P])

            for h in range(H):
                b.emit("tensor", lambda t, h=h, knT3=knT3, qnT3=qnT3: t.matmul(
                    u["PSG"][:, h * C:(h + 1) * C], knT3[:, h, :], qnT3[:, h, :],
                    start=(h == 0), stop=(h == H - 1)),
                    reads=["knT" + P, "qnT" + P, "PSG"], writes=["PSG"])
            b.emit("vector", lambda v, bcol=bcol: v.tensor_tensor(
                self.h3(u["e1"][:]), brow3, self.colb(bcol), OP.subtract),
                reads=["brow", "slab" + P], writes=["e1"])
            b.emit("vector", lambda v: v.tensor_scalar_min(u["e1"][:], u["e1"][:], 0.0),
                reads=["e1"], writes=["e1"])
            b.emit("scalar", lambda s: s.activation(u["e1"][:], u["e1"][:], AF.Exp),
                reads=["e1"], writes=["e1"])
            b.emit("vector", lambda v: v.tensor_mul(u["t1"][:], u["e1"][:], u["PSG"][:, 0:H * C]),
                reads=["e1", "PSG"], writes=["t1"])
            b.emit("vector", lambda v: v.tensor_tensor(
                self.h3(u["Sup" + P][:]), self.h3(u["t1"][:]), self.maskb(u["umask_incl"][:]),
                OP.mult),
                reads=["t1", "umask_incl"], writes=["Sup" + P])

        return [s0, s1, s2, s3, s4, s5, s6]

    def p1b(self, par):
        b, u, H, C = self.b, self.u, self.H, self.C
        P = str(par)
        stages = []
        for lev in range(1, self.NLEV):
            def ql(lev=lev):
                for h in range(H):
                    b.emit("tensor", lambda t, h=h: t.matmul(
                        u["PSQ"][:, h * C:(h + 1) * C],
                        self.h3(u["NTpow" + P][:])[:, h, :],
                        self.h3(u["Npow" + P][:])[:, h, :],
                        start=(h == 0), stop=(h == H - 1)),
                        reads=["NTpow" + P, "Npow" + P, "PSQ"], writes=["PSQ"])
                if lev < self.NLEV - 1:
                    for h in range(H):
                        b.emit("tensor", lambda t, h=h: t.matmul(
                            u["PSR"][:, h * C:(h + 1) * C],
                            self.h3(u["Npow" + P][:])[:, h, :],
                            self.h3(u["NTpow" + P][:])[:, h, :],
                            start=(h == 0), stop=(h == H - 1)),
                            reads=["NTpow" + P, "Npow" + P, "PSR"], writes=["PSR"])
                b.emit("scalar", lambda s: s.copy(u["Npow" + P][:], u["PSQ"][:, 0:H * C]),
                    reads=["PSQ"], writes=["Npow" + P])
                if lev < self.NLEV - 1:
                    b.emit("scalar", lambda s: s.copy(u["NTpow" + P][:], u["PSR"][:, 0:H * C]),
                        reads=["PSR"], writes=["NTpow" + P])
                for h in range(H):
                    b.emit("tensor", lambda t, h=h: t.matmul(
                        u["PSA"][:, h * C:(h + 1) * C],
                        self.h3(u["Npow" + P][:])[:, h, :],
                        self.h3(u["GT" + P][:])[:, h, :],
                        start=(h == 0), stop=(h == H - 1)),
                        reads=["Npow" + P, "GT" + P, "PSA"], writes=["PSA"])
                b.emit("vector", lambda v: v.tensor_add(
                    u["GT" + P][:], u["GT" + P][:], u["PSA"][:, 0:H * C]),
                    reads=["GT" + P, "PSA"], writes=["GT" + P])
            stages.append(ql)

        def wind():
            for h in range(H):
                b.emit("tensor", lambda t, h=h: t.matmul(
                    u["PSR"][:, h * C:(h + 1) * C],
                    self.h3(u["GT" + P][:])[:, h, :],
                    self.h3(u["rhs1" + P][:])[:, h, :],
                    start=(h == 0), stop=(h == H - 1)),
                    reads=["GT" + P, "rhs1" + P, "PSR"], writes=["PSR"])
            b.emit("scalar", lambda s: s.copy(u["Wind"][:], u["PSR"][:, 0:H * C]),
                reads=["PSR"], writes=["Wind"])
        stages.append(wind)
        return stages

    def p23_chain(self, par):
        b, u, H, C = self.b, self.u, self.H, self.C
        P = str(par)
        kcb, gam, bkap = u["kcb" + P], u["gam" + P], u["bkap" + P]
        knT3 = self.h3(u["knT" + P][:])
        qnT3 = self.h3(u["qnT" + P][:])
        gt3 = self.h3(u["GT" + P][:])

        def c0():
            for h in range(H):
                b.emit("tensor", lambda t, h=h: t.matmul(
                    u["PSK"][:, h * C:(h + 1) * C], knT3[:, h, :],
                    self.h3(u["M_bf"][:])[:, h, :],
                    start=(h == 0), stop=(h == H - 1)),
                    reads=["knT" + P, "M_bf", "PSK"], writes=["PSK"])
            for h in range(H):
                b.emit("tensor", lambda t, h=h: t.matmul(
                    u["PSA"][:, h * C:(h + 1) * C], qnT3[:, h, :],
                    self.h3(u["M_bf"][:])[:, h, :],
                    start=(h == 0), stop=(h == H - 1)),
                    reads=["qnT" + P, "M_bf", "PSA"], writes=["PSA"])

        def c1():
            b.emit("vector", lambda v: v.tensor_tensor(
                self.h3(u["X"][:]), self.h3(u["PSK"][:, 0:H * C]),
                self.colb(bkap[:]), OP.mult),
                reads=["PSK", "bkap" + P], writes=["X"])

        def c2():
            for h in range(H):
                b.emit("tensor", lambda t, h=h: t.matmul(
                    u["PSK"][:, h * C:(h + 1) * C], gt3[:, h, :],
                    self.h3(u["X"][:])[:, h, :],
                    start=(h == 0), stop=(h == H - 1)),
                    reads=["GT" + P, "X", "PSK"], writes=["PSK"])

        def c3():
            b.emit("vector", lambda v: v.tensor_sub(
                u["Wf"][:], u["Wind"][:], u["PSK"][:, 0:H * C]),
                reads=["Wind", "PSK"], writes=["Wf"])

        def c4():
            b.emit("vector", lambda v: v.tensor_tensor(
                self.h3(u["Wg"][:]), self.h3(u["Wf"][:]), self.colb(gam[:]),
                OP.mult), reads=["Wf", "gam" + P], writes=["Wg"])

        def c5():
            for h in range(H):
                b.emit("tensor", lambda t, h=h: t.matmul(
                    u["PSK"][:, h * C:(h + 1) * C],
                    self.h3(u["k_tm" + P][:])[:, h, :],
                    self.h3(u["Wg"][:])[:, h, :],
                    start=(h == 0), stop=(h == H - 1)),
                    reads=["k_tm" + P, "Wg", "PSK"], writes=["PSK"])

        def c6():
            b.emit("vector", lambda v: v.tensor_tensor(
                self.h3(u["M_f32"][:]), self.h3(u["M_f32"][:]),
                self.colb(kcb[:]), OP.mult),
                reads=["M_f32", "kcb" + P], writes=["M_f32"])

        def c7():
            b.emit("vector", lambda v: v.tensor_add(
                u["M_f32"][:], u["M_f32"][:], u["PSK"][:, 0:H * C]),
                reads=["M_f32", "PSK"], writes=["M_f32"])

        def c8():
            b.emit("scalar", lambda s: s.copy(u["M_bf"][:], u["M_f32"][:]),
                reads=["M_f32"], writes=["M_bf"])

        return [c0, c1, c2, c3, c4, c5, c6, c7, c8]

    def p23_out(self, par, zsl, og_cb, z_key):
        b, u, H, C = self.b, self.u, self.H, self.C
        P = str(par)
        slab2 = u["slab2" + P]
        PST, pstk = u["PST" + P], "PST" + P
        kappa = slab2[:, H:2 * H]

        def o1a():
            b.emit("vector", lambda v: v.tensor_tensor(
                self.h3(u["t2"][:]), self.h3(u["PSA"][:, 0:H * C]),
                self.colb(kappa), OP.mult),
                reads=["PSA", "slab2" + P], writes=["t2"])

        def o1b():
            b.emit("scalar", lambda s: s.copy(u["W_bf"][:], u["Wf"][:]),
                reads=["Wf"], writes=["W_bf"])
            for h in range(H):
                b.emit("tensor", lambda t, h=h: t.matmul(
                    u["PSS"][:, h * C:(h + 1) * C],
                    self.h3(u["Sup" + P][:])[:, h, :],
                    self.h3(u["W_bf"][:])[:, h, :],
                    start=(h == 0), stop=(h == H - 1)),
                    reads=["Sup" + P, "W_bf", "PSS"], writes=["PSS"])
            b.emit("vector", lambda v: v.tensor_add(
                u["O"][:], u["t2"][:], u["PSS"][:, 0:H * C]),
                reads=["t2", "PSS"], writes=["O"])

        def o2():
            b.emit("scalar", lambda s: s.activation(u["t2"][:], u["O"][:], AF.Square),
                reads=["O"], writes=["t2"])
            b.emit("vector", lambda v: v.tensor_reduce(
                u["s4"][:].unsqueeze(2), self.h3(u["t2"][:]), AX.X, OP.add),
                reads=["t2"], writes=["s4"])
            b.emit("scalar", lambda s: s.activation(u["s4"][:], u["s4"][:], AF.Ln,
                                                    bias=u["c_eps6"][:], scale=1.0 / 128.0),
                reads=["s4", "c_eps6"], writes=["s4"])
            b.emit("scalar", lambda s: s.activation(u["s4"][:], u["s4"][:], AF.Exp, scale=-0.5),
                reads=["s4"], writes=["s4"])
            b.emit("vector", lambda v: v.tensor_tensor(
                self.h3(u["t2"][:]), self.h3(u["O"][:]), self.colb(u["s4"][:]), OP.mult),
                reads=["O", "s4"], writes=["t2"])
            b.emit("vector", lambda v, z=zsl: v.tensor_tensor(
                self.h3(u["og_tm"][:]), self.h3(u["t2"][:]), z, OP.mult),
                reads=["t2", z_key], writes=["og_tm"])

        def o3():
            for h in range(H):
                b.emit("tensor", lambda t, h=h, PST=PST: t.matmul(
                    PST[:, h * C:(h + 1) * C], self.h3(u["og_tm"][:])[:, h, :],
                    u["identb"][:], start=(h == 0), stop=(h == H - 1), is_transpose=True),
                    reads=["og_tm", "identb", pstk], writes=[pstk])
            b.emit("scalar", lambda s, PST=PST: s.copy(u["ogT"][:], PST[:, 0:H * C]),
                reads=[pstk], writes=["ogT"])
            og_cb()

        return [o1a, o1b, o2, o3]


def alloc_scan_bufs(nc, ctx, H, C):
    u = {}

    def sb(name, shape, dt=F32):
        u[name] = ctx.enter_context(nc.sbuf_tensor("scan_" + name, shape, dt))

    def ps(name, shape, dt=F32):
        u[name] = ctx.enter_context(nc.psum_tensor("scan_" + name, shape, dt))

    HC = H * C
    ps("PSG", [128, 512]); ps("PSS", [128, 512]); ps("PSK", [128, 512])
    ps("PSA", [128, 512]); ps("PSQ", [128, 512]); ps("PSR", [128, 512])
    ps("PST0", [128, 1024], BF16); ps("PST1", [128, 1024], BF16)
    sb("b_scr", [H, C]); sb("blnb_scr", [H, C]); sb("zero_row", [H, 1])
    sb("bc_row", [1, H])
    sb("brow", [128, HC]); sb("browb", [128, HC])
    sb("sq", [128, HC], BF16); sb("lncol", [128, 2 * H]); sb("rcol", [128, 2 * H], BF16)
    sb("q_tm", [128, HC], BF16); sb("v_tm", [128, HC], BF16)
    sb("e1", [128, HC]); sb("t1", [128, HC]); sb("t2", [128, HC])
    sb("X", [128, HC], BF16); sb("O", [128, HC])
    sb("s4", [128, H]); sb("og_tm", [128, HC], BF16); sb("ogT", [128, HC], BF16)
    sb("Wg", [128, HC], BF16)
    sb("Wf", [128, HC]); sb("W_bf", [128, HC], BF16)
    sb("M_f32", [128, HC]); sb("M_bf", [128, HC], BF16)
    for P in ("0", "1"):
        sb("slab" + P, [128, 2 * H]); sb("slab2" + P, [128, 2 * H])
        sb("kcb" + P, [128, H]); sb("gam" + P, [128, H], BF16); sb("bkap" + P, [128, H])
        sb("rhs1" + P, [128, HC], BF16)
        sb("knT" + P, [128, HC], BF16); sb("qnT" + P, [128, HC], BF16)
        sb("k_tm" + P, [128, HC], BF16)
        sb("Npow" + P, [128, HC], BF16); sb("NTpow" + P, [128, HC], BF16)
        sb("GT" + P, [128, HC], BF16); sb("Sup" + P, [128, HC], BF16)
    sb("Wind", [128, HC])
    sb("identH", [H, H]); sb("identb", [128, 128], BF16); sb("ident_f", [128, 128])
    sb("ones1", [1, 128]); sb("ones128", [128, 1], BF16)
    sb("c_eps6", [128, 1]); sb("c_eps5", [128, 1]); sb("c_lnq", [128, 1]); sb("c_one", [128, 1])
    sb("lmask_neg", [C, C]); sb("umask_neg", [C, C]); sb("umask_incl", [C, C])
    for h in range(H):
        sb("sel" + str(h), [H, 128])
    return u



def scan_const_arrays(H, C):
    c = {}
    c["identH"] = np.eye(H, dtype=np.float32)
    c["ident_f"] = np.eye(128, dtype=np.float32)
    c["identb"] = np.eye(128, dtype=np.float32).astype(BF)
    c["ones1"] = np.ones((1, 128), np.float32)
    c["ones128"] = np.ones((128, 1), np.float32).astype(BF)
    c["c_eps6"] = np.full((128, 1), 1e-6, np.float32)
    c["c_eps5"] = np.full((128, 1), 1e-5, np.float32)
    c["c_lnq"] = np.full((128, 1), math.log(1.0 / math.sqrt(128.0)), np.float32)
    c["c_one"] = np.ones((128, 1), np.float32)
    c["lmask_neg"] = -np.tril(np.ones((C, C), np.float32), -1)
    c["umask_neg"] = -np.triu(np.ones((C, C), np.float32), 1)
    c["umask_incl"] = np.triu(np.ones((C, C), np.float32), 0)
    for h in range(H):
        s = np.zeros((H, 128), np.float32)
        s[h, :] = 1.0
        c["sel" + str(h)] = s
    c["zero_row"] = np.zeros((H, 1), np.float32)
    return c


def emit_const_loads(b, u, consts, nc, dma_eng="sync"):
    for name, arr in consts.items():
        t = nc.inline_tensor(np.ascontiguousarray(arr), "const_" + name)
        b.emit(dma_eng, lambda E, t=t, name=name: E.dma_start(out=u[name][:], in_=t.ap()),
               writes=[name], dma="xload")



def default_cfg():
    return dict(D=2048, L=4096, TILE=512, H=4, HG=16, C=128, NLEV=7,
                IFF=1408, NE=8, TB=1024, KCONV=4, IBMAX=256)


def mini_cfg():
    return dict(D=512, L=512, TILE=256, H=1, HG=4, C=128, NLEV=7,
                IFF=256, NE=8, TB=128, KCONV=4, IBMAX=256)


class CoreProgram:

    def __init__(self, nc, cfg):
        self.nc = nc
        self.cfg = cfg
        self.b = Builder(nc)

    def const_into(self, u, stack, name, arr):
        arr = np.ascontiguousarray(arr)
        u[name] = stack.enter_context(
            self.nc.sbuf_tensor("c_" + name, list(arr.shape),
                                mybir.dt.from_np(arr.dtype)))
        t = self.nc.inline_tensor(arr, "ct_" + name)
        self.b.emit("sync", lambda E, t=t, name=name: E.dma_start(
            out=u[name][:], in_=t.ap()), writes=[name], dma="xload")

    def barrier(self):
        lasts = {}
        for e in ENGINES:
            for kind, item in reversed(self.b.events[e]):
                if kind == "e":
                    lasts[e] = item
                    break
        targets = list(lasts.values()) + list(self.b.last_dma.values())
        for ce in ENGINES:
            for ev in targets:
                if ev.eng != ce or ev.amt == 16:
                    self.b._wait(ce, ev)

    def build(self):
        nc, cfg, b = self.nc, self.cfg, self.b
        D, L, TILE, H, C = cfg["D"], cfg["L"], cfg["TILE"], cfg["H"], cfg["C"]
        HG, IFF, NE, TB, KC = cfg["HG"], cfg["IFF"], cfg["NE"], cfg["TB"], cfg["KCONV"]
        ND, NT, NG = D // 128, L // TILE, TILE // 128
        NQKV = 3 * H
        VD = HG * 128
        NVC = VD // 128
        NTB = TB // 128
        NIB = IFF // 128
        NHOB = D // 512
        NSPL = cfg.get("NSPL", 2 if TB >= 1024 else 1)
        THB = TB // NSPL

        SH = IFF // 8
        IFFT = IFF + SH
        NIC = (IFFT + 127) // 128
        IFFP = NIC * 128
        TBL = min(512, TB)
        NGB = TBL // 128
        NBR = TB // TBL
        NTBL = 8 * NBR
        NCHK = NBR

        dt_in = lambda n, s, dt=F32: nc.dram_tensor(n, s, dt, kind="ExternalInput").ap()
        x_attn = dt_in("x_attn", [L, D])
        x_res = dt_in("x_res", [TB, D])
        wqkvz = dt_in("wqkvz", [D, (NQKV + H) * 128])
        wba = dt_in("wba", [D, 2 * H])
        convw = dt_in("convw", [NQKV * 128, KC])
        alog4 = dt_in("alog4", [H, 1])
        dtb4 = dt_in("dtb4", [H, 1])
        normw = dt_in("normw", [128, 1])
        wout = dt_in("wout", [VD, D])
        routerw = dt_in("routerw", [NE, D])
        ewgT = dt_in("ewgT", [D, IFFT])
        ewuT = dt_in("ewuT", [D, IFFT])
        ewdT = dt_in("ewdT", [IFFP, D])
        y_out = nc.dram_tensor("y", [TB, D], F32, kind="ExternalOutput").ap()

        agin = nc.dram_tensor("agin", [4, H * 128, L // 4], BF16)
        agout = nc.dram_tensor("agout", [4, VD, L // 4], BF16)
        x1_dram = nc.dram_tensor("x1_dram", [TB, D], F32)
        agin2 = nc.dram_tensor("agin2", [2, D, TB // 2], BF16)
        agout2 = nc.dram_tensor("agout2", [2, 8 * D, TB // 2], BF16)
        ag3in = nc.dram_tensor("ag3in", [NE * NTB, 128], F32)
        ag3out = nc.dram_tensor("ag3out", [8 * NE * NTB, 128], F32)
        rs_in = [nc.dram_tensor(f"rs_in{k}", [8 * TBL, D], BF16)
                 for k in range(NCHK)]
        rs_out = [nc.dram_tensor(f"rs_out{k}", [TBL, D], BF16)
                  for k in range(NCHK)]

        QW = (NQKV + H) * 128 // 4
        wqkvz_bfd = nc.dram_tensor("wqkvz_bfd", [D, (NQKV + H) * 128], BF16)
        NCH_T = TILE // C
        with ExitStack() as actx:
            uA = {}
            sb = lambda n, s, dt=F32: uA.__setitem__(
                n, actx.enter_context(nc.sbuf_tensor(n, s, dt)))
            uA_scan = alloc_scan_bufs(nc, actx, H, C)
            uA.update(uA_scan)
            emit_const_loads(b, uA, scan_const_arrays(H, C), nc)

            sb("xt0", [128, D]); sb("xt1", [128, D])
            sb("xt2", [128, D]); sb("xt3", [128, D])
            sb("sq_ln", [128, D])
            sb("mu0", [128, 2]); sb("mu1", [128, 2])
            sb("rstd0", [128, 1]); sb("rstd1", [128, 1])
            sb("xn0", [128, D]); sb("xn1", [128, D])
            sb("lnxT", [128, ND, TILE], BF16)
            sb("wq0", [128, ND, QW], BF16)
            sb("wq1", [128, ND, QW], BF16)
            sb("wba_s", [128, ND, 2 * H], BF16)
            sb("qkv_raw", [128, NQKV, TILE + KC - 1], BF16)
            sb("silbuf", [128, NQKV, TILE], BF16)
            sb("z_sil0", [128, NCH_T, H * 128])
            sb("z_sil1", [128, NCH_T, H * 128])
            sb("diag8", [128, 2 * KC, 128], BF16)
            sb("convw_s", [128, NQKV, KC])
            sb("balog", [H, TILE]); sb("galog", [H, TILE]); sb("lnb_r", [H, TILE])
            sb("arows_scr", [H, TILE])
            sb("alog_c", [H, 1]); sb("dtb_c", [H, 1])

            CSTW = 128
            NSLOT = 4
            for _s in range(NSLOT):
                sb("cst_in" + str(_s), [128, CSTW])
                sb("cst_o" + str(_s), [128, CSTW], BF16)
            cast_seq = [0]

            def emit_cast(src_ap, dst_ap, dkey, sfx="", copy_eng="gpsimd"):
                i = cast_seq[0] % NSLOT
                cast_seq[0] += 1
                ncol = src_ap.shape[1]
                b.emit("sync", lambda E, i=i, src_ap=src_ap, ncol=ncol: E.dma_start(
                    out=uA["cst_in" + str(i)][:, 0:ncol], in_=src_ap),
                    reads=[("cst_in", i)], writes=[("cst_in", i)],
                    dma="wcst" + sfx + str(i))
                b.emit(copy_eng, lambda g, i=i, ncol=ncol, ce=copy_eng:
                    (g.copy if ce == "scalar" else g.tensor_copy)(
                        uA["cst_o" + str(i)][:, 0:ncol], uA["cst_in" + str(i)][:, 0:ncol]),
                    reads=[("cst_in", i), ("cst_o", i)], writes=[("cst_o", i)])
                b.emit("gpsimd", lambda E, i=i, dst_ap=dst_ap, ncol=ncol: E.dma_start(
                    out=dst_ap, in_=uA["cst_o" + str(i)][:, 0:ncol]),
                    reads=[("cst_o", i)], writes=[dkey], dma="wcso" + sfx + str(i))

            def emit_cast_matrix(srcT, dstT, nrows, ncols, dkey, sfx=""):
                for r0 in range(0, nrows, 128):
                    for c0 in range(0, ncols, CSTW):
                        c1 = min(ncols, c0 + CSTW)
                        emit_cast(srcT[r0:r0 + 128, c0:c1], dstT[r0:r0 + 128, c0:c1],
                                  dkey, sfx)

            for q4 in range(4):
                for r0 in range(0, D, 128):
                    for c0 in range(q4 * QW, (q4 + 1) * QW, CSTW):
                        i = cast_seq[0] % 2
                        emit_cast(wqkvz[r0:r0 + 128, c0:c0 + CSTW],
                                  wqkvz_bfd.ap()[r0:r0 + 128, c0:c0 + CSTW],
                                  ("wqkvz_bfd", q4), "A",
                                  copy_eng="scalar" if i == 0 else "vector")
            b.emit("gpsimd", lambda E: E.dma_start(
                out=uA["wba_s"][:],
                in_=wba.rearrange("(dc p) c -> p dc c", p=128)),
                writes=["wba_s"], dma="cast")
            b.emit("sync", lambda E: E.dma_start(
                out=uA["convw_s"][:],
                in_=convw.rearrange("(cc p) k -> p cc k", p=128)),
                writes=["convw_s"], dma="xload")
            b.emit("sync", lambda E: E.dma_start(out=uA["alog_c"][:], in_=alog4),
                   writes=["alog_c"], dma="xload")
            b.emit("sync", lambda E: E.dma_start(out=uA["dtb_c"][:], in_=dtb4),
                   writes=["dtb_c"], dma="xload")

            h3q = lambda nm, q: uA[nm][:]
            b.emit("scalar", lambda s: s.activation(uA["alog_c"][:], uA["alog_c"][:], AF.Exp),
                   reads=["alog_c"], writes=["alog_c"])
            b.emit("vector", lambda v: v.memset(uA["M_f32"][:], 0.0), writes=["M_f32"])
            b.emit("vector", lambda v: v.memset(uA["M_bf"][:], 0.0), writes=["M_bf"])
            b.emit("vector", lambda v: v.memset(
                h3q("qkv_raw", NQKV)[:, :, 0:KC - 1], 0.0), writes=["qkv_raw"])

            em = ScanEmitter(b, uA, H, C, cfg["NLEV"])

            def prep_xload(ti):
                t0 = ti * TILE
                for g in range(NG):
                    b.emit("sync", lambda E, t0=t0, g=g: E.dma_start(
                        out=uA["xt" + str(g)][:],
                        in_=x_attn[t0 + g * 128:t0 + (g + 1) * 128, :]),
                        reads=["xt" + str(g)], writes=["xt" + str(g)],
                        dma="xl" + str(g))

            def prep_ln(ti, pair):
                jobs = [(uA["xt" + str(g)][:], "xt" + str(g), g, str(g % 2))
                        for g in range(2 * pair, min(2 * pair + 2, NG))]
                self.emit_ln_pair(uA, jobs, "lnxT", ND, D, "PSG")

            def prep_q_load(ti, q):
                wpar = q % 2
                wq = uA["wq" + str(wpar)]
                half = QW // 2
                for hh in range(2):
                    b.emit("sync", lambda E, q=q, wq=wq, hh=hh: E.dma_start(
                        out=wq[:][:, :, hh * half:(hh + 1) * half],
                        in_=wqkvz_bfd.ap()[:, q * QW + hh * half:q * QW + (hh + 1) * half]
                        .rearrange("(dc p) c -> p dc c", p=128)),
                        reads=[("wqkvz_bfd", q), ("wq", wpar, hh)],
                        writes=[("wq", wpar, hh)],
                        dma="wq" + str(wpar) + str(hh))

            def prep_q_mm(ti, q):
                tp = str(ti % 2)
                zs = "z_sil" + tp
                lnxT3 = h3q("lnxT", ND)
                wpar = q % 2
                wq = uA["wq" + str(wpar)]
                for j in range(QW // 128):
                    cc = q * (QW // 128) + j
                    hh = j // ((QW // 128) // 2)
                    if cc < NQKV:
                        ps = "PSS"
                        for dc in range(ND):
                            b.emit("tensor", lambda t, j=j, dc=dc, ps=ps, wq=wq: t.matmul(
                                uA[ps][:, 0:TILE], wq[:][:, dc, j * 128:(j + 1) * 128],
                                lnxT3[:, dc, :], start=(dc == 0), stop=(dc == ND - 1)),
                                reads=[("wq", wpar, hh), "lnxT", ps], writes=[ps])
                        b.emit("scalar", lambda s, cc=cc, ps=ps: s.copy(
                            h3q("qkv_raw", NQKV)[:, cc, KC - 1:KC - 1 + TILE],
                            uA[ps][:, 0:TILE]),
                            reads=[ps], writes=["qkv_raw"])
                if q == 3:
                    for g in range(NG):
                        ps = "PSS"
                        for dc in range(ND):
                            b.emit("tensor", lambda t, g=g, dc=dc, ps=ps, wq=wq: t.matmul(
                                uA[ps][:, 0:H * 128], lnxT3[:, dc, g * 128:(g + 1) * 128],
                                wq[:][:, dc, QW - H * 128:QW],
                                start=(dc == 0), stop=(dc == ND - 1)),
                                reads=[("wq", wpar, 0), ("wq", wpar, 1), "lnxT", ps],
                                writes=[ps])
                        b.emit("scalar", lambda s, g=g, ps=ps, zs=zs: s.activation(
                            h3q(zs, NG)[:, g, :], uA[ps][:, 0:H * 128], AF.Silu),
                            reads=[ps], writes=[zs])
            def prep_s3(ti):
                lnxT3 = h3q("lnxT", ND)
                for which, lo in (("balog", 0), ("galog", H)):
                    for dc in range(ND):
                        b.emit("tensor", lambda t, dc=dc, lo=lo: t.matmul(
                            uA["PSS"][0:H, 0:TILE],
                            h3q("wba_s", ND)[:, dc, lo:lo + H], lnxT3[:, dc, :],
                            start=(dc == 0), stop=(dc == ND - 1)),
                            reads=["wba_s", "lnxT", "PSS"], writes=["PSS"])
                    b.emit("vector", lambda v, which=which: v.tensor_copy(
                        uA[which][:], uA["PSS"][0:H, 0:TILE]),
                        reads=["PSS"], writes=[which])
                for cc in range(NQKV):
                    pc = cc % 2
                    ps = "PSS" if pc == 0 else "PSG"
                    for j in range(KC):
                        b.emit("vector", lambda v, cc=cc, j=j, pc=pc: v.tensor_scalar_mul(
                            h3q("diag8", 2 * KC)[:, pc * KC + j, :], uA["identb"][:],
                            h3q("convw_s", NQKV)[:, cc, j:j + 1]),
                            reads=["identb", "convw_s", ("diag", pc)],
                            writes=[("diag", pc)])
                    for j in range(KC):
                        b.emit("tensor", lambda t, cc=cc, j=j, ps=ps, pc=pc: t.matmul(
                            uA[ps][:, 0:TILE], h3q("diag8", 2 * KC)[:, pc * KC + j, :],
                            h3q("qkv_raw", NQKV)[:, cc, j:j + TILE],
                            start=(j == 0), stop=(j == KC - 1)),
                            reads=[("diag", pc), "qkv_raw", ps], writes=[ps])
                    b.emit("scalar", lambda s, cc=cc, ps=ps: s.activation(
                        h3q("silbuf", NQKV)[:, cc, :], uA[ps][:, 0:TILE], AF.Silu),
                        reads=[ps], writes=["silbuf"])
                b.emit("vector", lambda v: v.tensor_copy(
                    h3q("qkv_raw", NQKV)[:, :, 0:KC - 1],
                    h3q("qkv_raw", NQKV)[:, :, TILE:TILE + KC - 1]),
                    reads=["qkv_raw"], writes=["qkv_raw"])
                bsl = uA["balog"][:]
                asl = uA["galog"][:]
                b.emit("scalar", lambda s, bsl=bsl: s.activation(
                    uA["arows_scr"][:], bsl, AF.Tanh, scale=0.5),
                    reads=["balog"], writes=["arows_scr"])
                b.emit("vector", lambda v, bsl=bsl: v.tensor_scalar(
                    bsl, uA["arows_scr"][:], 0.5, 0.5, OP.mult, OP.add),
                    reads=["arows_scr"], writes=["balog"])
                b.emit("vector", lambda v, asl=asl: v.tensor_scalar_add(
                    asl, asl, uA["dtb_c"][:]), reads=["galog", "dtb_c"], writes=["galog"])
                b.emit("scalar", lambda s, asl=asl: s.activation(
                    uA["arows_scr"][:], asl, AF.Abs), reads=["galog"], writes=["arows_scr"])
                b.emit("scalar", lambda s: s.activation(
                    uA["arows_scr"][:], uA["arows_scr"][:], AF.Exp, scale=-1.0),
                    reads=["arows_scr"], writes=["arows_scr"])
                b.emit("scalar", lambda s: s.activation(
                    uA["arows_scr"][:], uA["arows_scr"][:], AF.Ln, bias=uA["c_one"][0:H, :]),
                    reads=["arows_scr", "c_one"], writes=["arows_scr"])
                b.emit("vector", lambda v, asl=asl: v.tensor_scalar_max(
                    asl, asl, 0.0), reads=["galog"], writes=["galog"])
                b.emit("vector", lambda v, asl=asl: v.tensor_add(
                    asl, asl, uA["arows_scr"][:]), reads=["galog", "arows_scr"],
                    writes=["galog"])
                b.emit("vector", lambda v, asl=asl: v.tensor_scalar(
                    asl, asl, uA["alog_c"][:], -1.0, OP.mult, OP.mult),
                    reads=["galog", "alog_c"], writes=["galog"])
                b.emit("scalar", lambda s, bsl=bsl: s.activation(
                    uA["lnb_r"][:], bsl, AF.Ln),
                    reads=["balog"], writes=["lnb_r"])

            def prep_head(ti, ci):
                if ci == 0:
                    prep_xload(ti)
                    prep_q_load(ti, 0)
                    prep_q_load(ti, 1)
                elif ci == 2:
                    prep_q_load(ti, 2)
                    prep_q_load(ti, 3)

            def prep_tail(ti, ci):
                if ci == 0:
                    prep_ln(ti, 0)
                    prep_ln(ti, 1)
                elif ci == 1:
                    prep_q_mm(ti, 0)
                    prep_q_mm(ti, 1)
                elif ci == 2:
                    prep_q_mm(ti, 2)
                    prep_q_mm(ti, 3)
                else:
                    prep_s3(ti)

            def prep_all(ti):
                prep_xload(ti)
                prep_q_load(ti, 0); prep_q_load(ti, 1)
                prep_ln(ti, 0); prep_ln(ti, 1)
                prep_q_mm(ti, 0); prep_q_mm(ti, 1)
                prep_q_load(ti, 2); prep_q_load(ti, 3)
                prep_q_mm(ti, 2); prep_q_mm(ti, 3)
                prep_s3(ti)

            def scan_p1(ti, ci):
                par = (ti * NCH_T + ci) % 2
                return em.p1_stages(
                    h3q("silbuf", NQKV)[:, H:2 * H, ci * C:(ci + 1) * C],
                    h3q("silbuf", NQKV)[:, 0:H, ci * C:(ci + 1) * C],
                    h3q("silbuf", NQKV)[:, 2 * H:3 * H, ci * C:(ci + 1) * C],
                    uA["galog"][:, ci * C:(ci + 1) * C],
                    uA["lnb_r"][:, ci * C:(ci + 1) * C],
                    uA["balog"][:, ci * C:(ci + 1) * C],
                    par,
                    kq_keys=("silbuf", "silbuf", "silbuf"),
                    row_keys=("galog", "lnb_r", "balog"))

            def scan_p23_stages(ti, ci):
                if cfg.get("NOSCAN"):
                    return [], []
                par = (ti * NCH_T + ci) % 2
                tp = str(ti % 2)
                zs = "z_sil" + tp

                def og_cb(ti=ti, ci=ci):
                    tt = ti * TILE + ci * C
                    q, toff = tt // (L // 4), tt % (L // 4)
                    b.emit("scalar", lambda E, q=q, toff=toff: E.dma_start(
                        out=agin.ap()[q].rearrange("(h p) t -> p h t", p=128)
                        [:, :, toff:toff + C],
                        in_=uA["ogT"][:].rearrange("p (h c) -> p h c", h=H)),
                        reads=["ogT"], writes=[("agin_dram", (ti * NCH_T + ci) // 8)],
                        dma="og")
                ch = em.p23_chain(par)
                out = em.p23_out(
                    par, h3q(zs, NCH_T)[:, ci, :].rearrange("p (h c) -> p h c", h=H),
                    og_cb, zs)
                return ch, out

            NWIN = NT * NCH_T

            def fire_ag(q):
                b.emit("gpsimd", lambda E, q=q: E.collective_compute(
                    "AllGather", OP.bypass,
                    replica_groups=[[0, 1, 2, 3], [4, 5, 6, 7]],
                    ins=[agin.ap()[q]], outs=[agout.ap()[q]]),
                    reads=[("agin_dram", q)], writes=[("agout_dram", q)],
                    collective=True)

            if cfg.get("NOPREP"):
                def prep_all(ti):
                    pass
            prep_all(0)
            for w in range(NWIN + 1):
                ch, out, qs = [], [], []
                if w >= 1 and not cfg.get("NOSCAN"):
                    ch, out = scan_p23_stages((w - 1) // NCH_T, (w - 1) % NCH_T)
                    qs = em.p1b((w - 1) % 2)
                p1s = []
                if w < NWIN and not cfg.get("NOSCAN"):
                    p1s = scan_p1(w // NCH_T, w % NCH_T)
                if w < NWIN and (w // NCH_T) + 1 < NT and not cfg.get("NOPREP"):
                    prep_head((w // NCH_T) + 1, w % NCH_T)
                nop = lambda: None
                g = lambda lst, i: lst[i] if i < len(lst) else nop
                sched = [
                    g(ch, 0),
                    g(out, 0),
                    g(p1s, 0),
                    g(qs, 0),
                    g(ch, 1),
                    g(p1s, 1),
                    g(qs, 1),
                    g(p1s, 2),
                    g(qs, 2), g(qs, 3),
                    g(p1s, 3),
                    g(qs, 4),
                    g(p1s, 4),
                    g(qs, 5), g(qs, 6),
                    g(ch, 2),
                    g(p1s, 5),
                    g(ch, 3), g(ch, 4),
                    g(p1s, 6),
                    g(ch, 5),
                    g(ch, 6), g(ch, 7), g(ch, 8),
                    g(out, 1),
                    g(out, 2),
                    g(out, 3),
                ]
                for f in sched:
                    f()
                if w < NWIN:
                    cti, cci = w // NCH_T, w % NCH_T
                    if cti + 1 < NT and not cfg.get("NOPREP"):
                        prep_tail(cti + 1, cci)
                if not cfg.get("NOSCAN"):
                    if w == 24:
                        fire_ag(0)
                    elif w == 28:
                        fire_ag(1)
                    elif w == 30:
                        fire_ag(2)

            if not cfg.get("NOSCAN"):
                fire_ag(3)
            self.barrier()

        NVC2 = NVC // 2
        assert NVC % 2 == 0
        with ExitStack() as bctx:
            uB = {}
            for nm in ("PS0", "PS1", "PS2", "PS3", "PS4", "PS5"):
                uB[nm] = bctx.enter_context(nc.psum_tensor("b_" + nm, [128, 512], F32))

            with ExitStack() as b1ctx:
                u1 = dict(uB)
                sb1 = lambda n, s, dt=F32: u1.__setitem__(
                    n, b1ctx.enter_context(nc.sbuf_tensor("B1" + n, s, dt)))
                self.const_into(u1, b1ctx, "ident_f", np.eye(128, dtype=np.float32))
                self.const_into(u1, b1ctx, "identE", np.eye(NE, dtype=np.float32))
                self.const_into(u1, b1ctx, "c_eps5",
                                np.full((128, 1), EPS_LN, np.float32))
                sb1("x1", [128, NTB, D])
                sb1("ln2xT", [128, ND, TB], BF16)
                sb1("woutb0", [128, NVC2, D], BF16)
                sb1("woutb1", [128, NVC2, D], BF16)
                sb1("ogTg0", [128, NVC2, THB], BF16)
                sb1("sq_ln", [128, D])
                sb1("mu0", [128, 2]); sb1("mu1", [128, 2])
                sb1("rstd0", [128, 1]); sb1("rstd1", [128, 1])
                sb1("xn0", [128, D]); sb1("xn1", [128, D])
                sb1("normw_c", [128, 1])
                sb1("routw_s", [NE, D])
                sb1("routT", [128, ND * NE], BF16)
                sb1("wts", [128, NTB, NE])
                sb1("comb", [128, NTB, NE])
                sb1("ctb", [NTB * NE, 128])
                sb1("m12", [128, 2])
                sb1("gpk", [1, 8])

                b.emit("sync", lambda E: E.dma_start(out=u1["normw_c"][:], in_=normw),
                       writes=["normw_c"], dma="xload")
                b.emit("sync", lambda E: E.dma_start(
                    out=u1["x1"][:],
                    in_=x_res.rearrange("(g p) d -> p g d", p=128)),
                    writes=["x1"], dma="xload")
                b.emit("sync", lambda E: E.dma_start(out=u1["routw_s"][:], in_=routerw),
                       writes=["routw_s"], dma="xload")
                assert ND * NE <= 512
                for dc in range(ND):
                    b.emit("tensor", lambda t, dc=dc: t.matmul(
                        uB["PS0"][:, dc * NE:(dc + 1) * NE],
                        u1["routw_s"][:, dc * 128:(dc + 1) * 128], u1["identE"][:],
                        start=(dc == 0), stop=(dc == ND - 1), is_transpose=True),
                        reads=["routw_s", "identE", "PS0"], writes=["PS0"])
                b.emit("scalar", lambda s: s.copy(u1["routT"][:], uB["PS0"][:, 0:ND * NE]),
                       reads=["PS0"], writes=["routT"])
                pid_cell = {}
                def rowbase_of(E, vh):
                    if vh not in pid_cell:
                        if "q" not in pid_cell:
                            pid_cell["q"] = (E.partition_id() % 4) * VD
                        pid_cell[vh] = pid_cell["q"] + vh * (NVC2 * 128)
                    return pid_cell[vh]
                HT = NTB // 2
                for hf in range(NSPL):
                    for vh in range(2):
                        wb = "woutb" + str(vh)
                        og = "ogTg0"
                        if hf == 0:
                            for vc in range(NVC2):
                                gvc = vh * NVC2 + vc
                                b.emit("gpsimd", lambda E, vc=vc, gvc=gvc, wb=wb:
                                       E.dma_start(
                                    out=u1[wb][:][:, vc, :],
                                    in_=wout[gvc * 128:(gvc + 1) * 128, :]),
                                    reads=[(wb, vc)], writes=[(wb, vc)],
                                    dma="wout" + str(vh))
                                b.emit("vector", lambda v, vc=vc, wb=wb:
                                       v.tensor_scalar_mul(
                                    u1[wb][:][:, vc, :], u1[wb][:][:, vc, :],
                                    u1["normw_c"][:]),
                                    reads=[(wb, vc), "normw_c"], writes=[(wb, vc)])
                        b.emit("sync", lambda E, hf=hf, vh=vh, og=og: E.dma_start(
                            out=u1[og][:],
                            in_=agout.ap().rearrange("q v t -> (q v) t")
                            [bass.ds(rowbase_of(E, vh), NVC2 * 128),
                             hf * THB:(hf + 1) * THB]
                            .rearrange("(vc p) t -> p vc t", p=128)),
                            reads=[("agout_dram", 0), ("agout_dram", 1),
                                   ("agout_dram", 2), ("agout_dram", 3), og],
                            writes=[og], dma="xload")
                        for tg in range(THB // 128):
                            gtg = hf * (THB // 128) + tg
                            for db in range(NHOB):
                                ps = ("PS0", "PS1", "PS2", "PS3")[db % 4]
                                for vc in range(NVC2):
                                    b.emit("tensor", lambda t, tg=tg, db=db, vc=vc,
                                           ps=ps, wb=wb, og=og: t.matmul(
                                        uB[ps][:],
                                        u1[og][:][:, vc, tg * 128:(tg + 1) * 128],
                                        u1[wb][:][:, vc, db * 512:(db + 1) * 512],
                                        start=(vc == 0), stop=(vc == NVC2 - 1)),
                                        reads=[og] + [(wb, vc) for vc in range(NVC2)]
                                        + [ps], writes=[ps])
                                b.emit("vector", lambda v, gtg=gtg, db=db, ps=ps:
                                       v.tensor_add(
                                    u1["x1"][:][:, gtg, db * 512:(db + 1) * 512],
                                    u1["x1"][:][:, gtg, db * 512:(db + 1) * 512],
                                    uB[ps][:]),
                                    reads=[ps, "x1"], writes=["x1"])
                    for gt0 in range(hf * HT, (hf + 1) * HT, 2):
                        self.emit_ln_pair(
                            u1, [(u1["x1"][:][:, g, :], "x1", g, str(g % 2))
                                 for g in range(gt0, min(gt0 + 2, NTB))],
                            "ln2xT", ND, D, psname="PS4", add_norm_to_src=True)
                    clo, chi = hf * HT * 128, (hf + 1) * HT * 128
                    b.emit("gpsimd", lambda E, hf=hf, clo=clo, chi=chi: E.dma_start(
                        out=agin2.ap()[hf].rearrange("(dc p) t -> p dc t", p=128),
                        in_=u1["ln2xT"][:][:, :, clo:chi]),
                        reads=["ln2xT"], writes=[("agin2_dram", hf)], dma="bout")
                    b.emit("gpsimd", lambda E, hf=hf: E.collective_compute(
                        "AllGather", OP.bypass,
                        replica_groups=[list(range(8))],
                        ins=[agin2.ap()[hf]], outs=[agout2.ap()[hf]]),
                        reads=[("agin2_dram", hf)], writes=[("agout2_dram", hf)],
                        collective=True)
                b.emit("gpsimd", lambda E: E.dma_start(
                    out=x1_dram.ap().rearrange("(g p) d -> p g d", p=128),
                    in_=u1["x1"][:]), reads=["x1"], writes=["x1_dram"], dma="bout")
                ln2xT3 = u1["ln2xT"][:]
                for gtg in range(NTB):
                    for dc in range(ND):
                        b.emit("tensor", lambda t, gtg=gtg, dc=dc: t.matmul(
                            uB["PS4"][:, 0:NE], ln2xT3[:, dc, gtg * 128:(gtg + 1) * 128],
                            u1["routT"][:, dc * NE:(dc + 1) * NE],
                            start=(dc == 0), stop=(dc == ND - 1)),
                            reads=["ln2xT", "routT", "PS4"], writes=["PS4"])
                    w3 = u1["wts"][:][:, gtg, :]
                    c3 = u1["comb"][:][:, gtg, :]
                    b.emit("vector", lambda v: v.reduce_max(
                        u1["m12"][:, 0:1], uB["PS4"][:, 0:NE], AX.X),
                        reads=["PS4"], writes=["m12"])
                    b.emit("vector", lambda v, w3=w3: v.tensor_scalar_sub(
                        w3, uB["PS4"][:, 0:NE], u1["m12"][:, 0:1]),
                        reads=["PS4", "m12"], writes=["wts"])
                    b.emit("scalar", lambda s, w3=w3: s.activation(w3, w3, AF.Exp),
                        reads=["wts"], writes=["wts"])
                    b.emit("vector", lambda v, w3=w3: v.reduce_sum(
                        u1["m12"][:, 0:1], w3, AX.X), reads=["wts"], writes=["m12"])
                    b.emit("vector", lambda v: v.reciprocal(
                        u1["m12"][:, 1:2], u1["m12"][:, 0:1]), reads=["m12"], writes=["m12"])
                    b.emit("vector", lambda v, w3=w3: v.tensor_scalar_mul(
                        w3, w3, u1["m12"][:, 1:2]), reads=["m12", "wts"], writes=["wts"])
                    b.emit("vector", lambda v, w3=w3: v.reduce_max(
                        u1["m12"][:, 0:1], w3, AX.X), reads=["wts"], writes=["m12"])
                    b.emit("vector", lambda v, w3=w3, c3=c3: v.tensor_scalar(
                        c3, w3, u1["m12"][:, 0:1], -1e30, OP.is_ge, OP.mult),
                        reads=["wts", "m12"], writes=["comb"])
                    b.emit("vector", lambda v, w3=w3, c3=c3: v.tensor_add(
                        c3, c3, w3), reads=["comb", "wts"], writes=["comb"])
                    b.emit("vector", lambda v, c3=c3: v.reduce_max(
                        u1["m12"][:, 1:2], c3, AX.X), reads=["comb"], writes=["m12"])
                    b.emit("vector", lambda v, w3=w3, c3=c3: v.tensor_scalar(
                        c3, w3, u1["m12"][:, 1:2], None, OP.is_ge),
                        reads=["wts", "m12"], writes=["comb"])
                    b.emit("vector", lambda v, w3=w3, c3=c3: v.tensor_mul(
                        c3, c3, w3), reads=["comb", "wts"], writes=["comb"])
                if cfg.get("DBG"):
                    comb_dbg = nc.dram_tensor("comb_dbg", [128, NTB, NE], F32,
                                              kind="ExternalOutput").ap()
                    b.emit("gpsimd", lambda E: E.dma_start(
                        out=comb_dbg, in_=u1["comb"][:]),
                        reads=["comb"], writes=["combdbg"], dma=True)
                b.emit("tensor", lambda t: t.matmul(
                    uB["PS4"][0:NTB * NE, 0:128],
                    u1["comb"][:].rearrange("p a b -> p (a b)"), u1["ident_f"][:],
                    start=True, stop=True, is_transpose=True),
                    reads=["comb", "ident_f", "PS4"], writes=["PS4"])
                b.emit("vector", lambda v: v.tensor_copy(
                    u1["ctb"][:], uB["PS4"][0:NTB * NE, 0:128]),
                    reads=["PS4"], writes=["ctb"])
                for g in range(NTB):
                    b.emit("gpsimd", lambda E, g=g: E.dma_start(
                        out=ag3in.ap().rearrange("(e g) c -> e g c", g=NTB)[:, g, :],
                        in_=u1["ctb"][g * NE:(g + 1) * NE, :]),
                        reads=["ctb"], writes=["ag3in_dram"], dma="bout")
                b.emit("gpsimd", lambda E: E.collective_compute(
                    "AllGather", OP.bypass,
                    replica_groups=[list(range(8))],
                    ins=[ag3in.ap()], outs=[ag3out.ap()]),
                    reads=["ag3in_dram"], writes=["ag3out_dram"], collective=True)
                b.emit("gpsimd", lambda E: E.memset(u1["gpk"][:], 0.0),
                       writes=["gpk"])
                self.barrier()

            with ExitStack() as b4ctx:
                u4 = dict(uB)
                sb4 = lambda n, s, dt=F32: u4.__setitem__(
                    n, b4ctx.enter_context(nc.sbuf_tensor("B4" + n, s, dt)))
                sb4("guTg", [128, ND, IFFP], BF16)
                sb4("guTu", [128, ND, IFFP], BF16)
                sb4("wdT", [128, NIC, D], BF16)
                sb4("strm0", [128, ND, TBL], BF16)
                sb4("strm1", [128, ND, TBL], BF16)
                sb4("hbuf", [128, NIC, TBL], BF16)
                sb4("sgs", [128, TBL])
                sb4("cbc", [128, TBL])
                sb4("rsst0", [128, 512], BF16)
                sb4("rsst1", [128, 512], BF16)
                if IFFP > IFFT:
                    for wnm in ("guTg", "guTu"):
                        b.emit("vector", lambda v, wnm=wnm: v.memset(
                            u4[wnm][:][:, :, IFFT:IFFP], 0.0),
                            writes=[wnm + "_s"])
                wl_seq = [0]

                def emit_wld(src_ap, dst_ap, dkey):
                    i = wl_seq[0] % 8
                    wl_seq[0] += 1
                    b.emit("gpsimd", lambda E, src_ap=src_ap, dst_ap=dst_ap:
                        E.dma_start(out=dst_ap, in_=src_ap),
                        writes=[dkey], dma="wld" + str(i))

                for dc in range(ND):
                    for src, wnm in ((ewgT, "guTg"), (ewuT, "guTu")):
                        emit_wld(src[dc * 128:(dc + 1) * 128, 0:IFFT],
                                 u4[wnm][:][:, dc, 0:IFFT], wnm + "_s")
                for ic in range(NIC):
                    for hh in range(2):
                        emit_wld(
                            ewdT[ic * 128:(ic + 1) * 128,
                                 hh * (D // 2):(hh + 1) * (D // 2)],
                            u4["wdT"][:][:, ic, hh * (D // 2):(hh + 1) * (D // 2)],
                            "wdT_s")
                pid4 = {}
                def rreg(E, r):
                    if "p" not in pid4:
                        pid4["p"] = E.partition_id() % 8
                    if r not in pid4:
                        pid4[r] = pid4["p"] + r * NE
                    return pid4[r]
                ev_seq = [0]
                for tb in range(NTBL):
                    par = tb % 2
                    strm = u4["strm" + str(par)]
                    hf, r = tb // 8, tb % 8
                    b.emit("sync", lambda E, r=r, hf=hf, strm=strm: E.dma_start(
                        out=strm[:],
                        in_=agout2.ap()[hf][r * D:(r + 1) * D, :]
                        .rearrange("(dc p) t -> p dc t", p=128)),
                        reads=[("agout2_dram", hf), ("strm", par)],
                        writes=[("strm", par)], dma="str")
                    b.emit("scalar", lambda E, r=r, hf=hf: E.dma_start(
                        out=u4["cbc"][:].rearrange("p (g c) -> p g c", g=NGB),
                        in_=ag3out.ap().rearrange("(q g) c -> q g c", g=NTB)
                        [bass.ds(rreg(E, r), 1), hf * NGB:hf * NGB + NGB, :]
                        .broadcast_to([128, NGB, 128])),
                        reads=["ag3out_dram", "cbc"], writes=["cbc"], dma="cmb")
                    for ic in range(NIC):
                        for wnm, psn in (("guTg", "PS0"), ("guTu", "PS1")):
                            for dc in range(ND):
                                b.emit("tensor", lambda t, wnm=wnm, psn=psn, dc=dc,
                                       ic=ic, strm=strm: t.matmul(
                                    uB[psn][:, 0:TBL],
                                    u4[wnm][:][:, dc, ic * 128:(ic + 1) * 128],
                                    strm[:][:, dc, :],
                                    start=(dc == 0), stop=(dc == ND - 1)),
                                    reads=[wnm + "_s", ("strm", par), psn],
                                    writes=[psn])
                        b.emit("scalar", lambda s: s.activation(
                            u4["sgs"][:], uB["PS0"][:, 0:TBL], AF.Sigmoid),
                            reads=["PS0", "sgs"], writes=["sgs"])
                        if ic * 128 < IFF:
                            b.emit("vector", lambda v: v.tensor_mul(
                                u4["sgs"][:], u4["sgs"][:], u4["cbc"][:]),
                                reads=["sgs", "cbc"], writes=["sgs"])
                        b.emit("vector", lambda v, ic=ic: v.tensor_mul(
                            u4["hbuf"][:][:, ic, :], u4["sgs"][:], uB["PS1"][:, 0:TBL]),
                            reads=["sgs", "PS1", ("hbuf", ic)], writes=[("hbuf", ic)])
                    for tg in range(NGB):
                        gt = (r * NBR + hf) * NGB + tg
                        for hob in range(NHOB):
                            ps = ("PS2", "PS3", "PS4", "PS5")[(tg * NHOB + hob) % 4]
                            for ic in range(NIC):
                                b.emit("tensor", lambda t, tg=tg, hob=hob, ic=ic,
                                       ps=ps: t.matmul(
                                    uB[ps][:],
                                    u4["hbuf"][:][:, ic, tg * 128:(tg + 1) * 128],
                                    u4["wdT"][:][:, ic, hob * 512:(hob + 1) * 512],
                                    start=(ic == 0), stop=(ic == NIC - 1)),
                                    reads=[("hbuf", ic), "wdT_s", ps], writes=[ps])
                            par2 = ev_seq[0] % 2
                            ev_seq[0] += 1
                            rsst = u4["rsst" + str(par2)]
                            b.emit("vector", lambda v, ps=ps, rsst=rsst: v.tensor_copy(
                                rsst[:], uB[ps][:]),
                                reads=[ps, ("rsst", par2)], writes=[("rsst", par2)])
                            gr = (r * NGB + tg) * 128
                            b.emit("gpsimd", lambda E, gr=gr, hob=hob, hf=hf,
                                   rsst=rsst: E.dma_start(
                                out=rs_in[hf].ap()[gr:gr + 128,
                                                   hob * 512:(hob + 1) * 512],
                                in_=rsst[:]),
                                reads=[("rsst", par2)], writes=["rs_dram"], dma="rso")
                    if r == 7:
                        b.emit("gpsimd", lambda E, hf=hf: E.collective_compute(
                            "ReduceScatter", OP.add,
                            replica_groups=[list(range(8))],
                            ins=[rs_in[hf].ap()], outs=[rs_out[hf].ap()]),
                            reads=["rs_dram"], writes=[("rsout", hf)],
                            collective=True)
                sb4("gpk2", [1, 8])
                b.emit("gpsimd", lambda E: E.memset(u4["gpk2"][:], 0.0),
                       writes=["gpk2"])
                self.barrier()

            with ExitStack() as b5ctx:
                u5 = dict(uB)
                sb5 = lambda n, s, dt=F32: u5.__setitem__(
                    n, b5ctx.enter_context(nc.sbuf_tensor("B5" + n, s, dt)))
                sb5("x1t", [128, NTB, D])
                sb5("rst", [128, NTB, D], BF16)
                b.emit("sync", lambda E: E.dma_start(
                    out=u5["x1t"][:],
                    in_=x1_dram.ap().rearrange("(g p) d -> p g d", p=128)),
                    reads=["x1_dram"], writes=["x1t"], dma="xload")
                gpc = NTB // NCHK
                st = None
                for k in range(NCHK):
                    b.emit("sync", lambda E, k=k: E.dma_start(
                        out=u5["rst"][:][:, k * gpc:(k + 1) * gpc, :],
                        in_=rs_out[k].ap().rearrange("(g p) d -> p g d", p=128)),
                        reads=[("rsout", k)], writes=[("rst", k)], dma="xload")
                    for g in range(k * gpc, (k + 1) * gpc):
                        b.emit("vector", lambda v, g=g: v.tensor_add(
                            u5["x1t"][:][:, g, :], u5["x1t"][:][:, g, :],
                            u5["rst"][:][:, g, :]),
                            reads=["x1t", ("rst", k)], writes=["x1t", ("yadd", k)])
                    st = b.emit("sync", lambda E, k=k: E.dma_start(
                        out=y_out[k * (TB // NCHK):(k + 1) * (TB // NCHK), :]
                        .rearrange("(g p) d -> p g d", p=128),
                        in_=u5["x1t"][:][:, k * gpc:(k + 1) * gpc, :]),
                        reads=[("yadd", k)], dma="yout")
                b._wait("sync", st)
                b._wait("gpsimd", st)
        return b

    def emit_ln_group(self, u, xg_ap, src_key, g, dst_name, ND, D, psname="PS6",
                      add_norm_to_src=False, par=""):
        self.emit_ln_pair(u, [(xg_ap, src_key, g, par)], dst_name, ND, D,
                          psname, add_norm_to_src)

    def emit_ln_pair(self, u, jobs, dst_name, ND, D, psname="PS6",
                     add_norm_to_src=False, ident="ident_f"):
        b = self.b
        J = [(xg_ap, sk, g, "mu" + par, "rstd" + par, "xn" + par)
             for (xg_ap, sk, g, par) in jobs]
        for xg_ap, sk, g, mu, rstd, xn in J:
            b.emit("vector", lambda v, xg_ap=xg_ap, mu=mu: v.reduce_sum(
                u[mu][:, 0:1], xg_ap, AX.X), reads=[sk], writes=[mu])
        for xg_ap, sk, g, mu, rstd, xn in J:
            b.emit("scalar", lambda s, xg_ap=xg_ap, mu=mu: s.activation(
                u["sq_ln"][:], xg_ap, AF.Square, accum_out=u[mu][:, 1:2]),
                reads=[sk, mu, "sq_ln"], writes=["sq_ln", mu])
        for xg_ap, sk, g, mu, rstd, xn in J:
            b.emit("vector", lambda v, mu=mu: v.tensor_scalar_mul(
                u[mu][:], u[mu][:], 1.0 / D), reads=[mu], writes=[mu])
        for xg_ap, sk, g, mu, rstd, xn in J:
            b.emit("vector", lambda v, mu=mu, rstd=rstd: v.tensor_tensor(
                u[rstd][:], u[mu][:, 0:1], u[mu][:, 0:1], OP.mult),
                reads=[mu], writes=[rstd])
        for xg_ap, sk, g, mu, rstd, xn in J:
            b.emit("vector", lambda v, mu=mu, rstd=rstd: v.tensor_sub(
                u[rstd][:], u[mu][:, 1:2], u[rstd][:]),
                reads=[mu, rstd], writes=[rstd])
        for xg_ap, sk, g, mu, rstd, xn in J:
            b.emit("scalar", lambda s, rstd=rstd: s.activation(
                u[rstd][:], u[rstd][:], AF.Ln, bias=u["c_eps5"][:]),
                reads=[rstd, "c_eps5"], writes=[rstd])
        for xg_ap, sk, g, mu, rstd, xn in J:
            b.emit("scalar", lambda s, rstd=rstd: s.activation(
                u[rstd][:], u[rstd][:], AF.Exp, scale=-0.5),
                reads=[rstd], writes=[rstd])
        for xg_ap, sk, g, mu, rstd, xn in J:
            b.emit("vector", lambda v, xg_ap=xg_ap, mu=mu, rstd=rstd, xn=xn:
                   v.tensor_scalar(
                u[xn][:], xg_ap, u[mu][:, 0:1], u[rstd][:],
                OP.subtract, OP.mult),
                reads=[sk, mu, rstd, xn], writes=[xn])
        if add_norm_to_src:
            for xg_ap, sk, g, mu, rstd, xn in J:
                b.emit("vector", lambda v, xg_ap=xg_ap, xn=xn: v.tensor_add(
                    xg_ap, xg_ap, u[xn][:]), reads=[sk, xn], writes=[sk])
        for xg_ap, sk, g, mu, rstd, xn in J:
            for dc in range(ND):
                b.emit("tensor", lambda t, dc=dc, xn=xn: t.matmul(
                    u[psname][:, (dc % 4) * 128:(dc % 4) * 128 + 128],
                    u[xn][:, dc * 128:(dc + 1) * 128], u[ident][:],
                    start=(dc % 4 == 0), stop=(dc % 4 == 3 or dc == ND - 1),
                    is_transpose=True),
                    reads=[xn, ident, psname], writes=[psname])
                if dc % 4 == 3 or dc == ND - 1:
                    lo = (dc // 4) * 4
                    n_in = dc - lo + 1
                    b.emit("scalar", lambda s, lo=lo, n_in=n_in, g=g: s.copy(
                        u[dst_name][:][:, lo:lo + n_in, g * 128:(g + 1) * 128],
                        u[psname][:, 0:n_in * 128]
                        .rearrange("p (dc t) -> p dc t", dc=n_in)),
                        reads=[psname], writes=[dst_name])



def make_core_inputs(cfg, c, x, Wqkv, Wz, Wb, Wa, conv_w, A_log, dt_bias, norm_w,
                     Wout, router_w, shg, shu, shd, wg, wu, wd):
    D, L, H, HG, TB = cfg["D"], cfg["L"], cfg["H"], cfg["HG"], cfg["TB"]
    KEY = HG * 128
    bc, hg = c // 4, c % 4
    h0 = hg * H
    cw = H * 128
    qcols = Wqkv[:, h0 * 128:h0 * 128 + cw]
    kcols = Wqkv[:, KEY + h0 * 128:KEY + h0 * 128 + cw]
    vcols = Wqkv[:, 2 * KEY + h0 * 128:2 * KEY + h0 * 128 + cw]
    zcols = Wz[:, h0 * 128:h0 * 128 + cw]
    wqkvz = np.ascontiguousarray(np.concatenate([qcols, kcols, vcols, zcols], axis=1))
    wba = np.ascontiguousarray(np.concatenate(
        [Wb[:, h0:h0 + H], Wa[:, h0:h0 + H]], axis=1))
    convw = np.ascontiguousarray(np.concatenate(
        [conv_w[h0 * 128:h0 * 128 + cw, 0, :],
         conv_w[KEY + h0 * 128:KEY + h0 * 128 + cw, 0, :],
         conv_w[2 * KEY + h0 * 128:2 * KEY + h0 * 128 + cw, 0, :]], axis=0))
    flat = x.reshape(-1, D)
    IFF = cfg["IFF"]
    SH = IFF // 8
    IFFT = IFF + SH
    NIC = (IFFT + 127) // 128
    IFFP = NIC * 128
    ewgT = np.concatenate([wg[c].T, shg[c * SH:(c + 1) * SH].T], axis=1)
    ewuT = np.concatenate([wu[c].T, shu[c * SH:(c + 1) * SH].T], axis=1)
    ewdT = np.zeros((IFFP, D), np.float32)
    ewdT[:IFF] = wd[c].T
    ewdT[IFF:IFFT] = shd[:, c * SH:(c + 1) * SH].T
    return {
        "x_attn": np.ascontiguousarray(x[bc]),
        "x_res": np.ascontiguousarray(flat[c * TB:(c + 1) * TB]),
        "wqkvz": wqkvz,
        "wba": wba,
        "convw": convw,
        "alog4": np.ascontiguousarray(A_log[h0:h0 + H].reshape(H, 1)),
        "dtb4": np.ascontiguousarray(dt_bias[h0:h0 + H].reshape(H, 1)),
        "normw": np.ascontiguousarray(norm_w.reshape(128, 1)),
        "wout": np.ascontiguousarray(Wout),
        "routerw": np.ascontiguousarray(router_w),
        "ewgT": np.ascontiguousarray(ewgT),
        "ewuT": np.ascontiguousarray(ewuT),
        "ewdT": ewdT,
    }


def build_program(cfg):
    nc = bass.Bass("TRN2", target_bir_lowering=False)
    prog = CoreProgram(nc, cfg)
    with ExitStack() as sems_ctx:
        with nc.Block() as block:
            b = prog.build()
            b.finalize()
            b.sem_handles = {s: sems_ctx.enter_context(nc.semaphore(s))
                             for s in sorted(b.used_sems)}
            block.sync(lambda E: b.replay_engine("sync", E))
            block.tensor(lambda E: b.replay_engine("tensor", E))
            block.vector(lambda E: b.replay_engine("vector", E))
            block.scalar(lambda E: b.replay_engine("scalar", E))
            block.gpsimd(lambda E: b.replay_engine("gpsimd", E))
    return nc


def kernel(L0_attn_x, L0_attn_Wqkv, L0_attn_Wz, L0_attn_Wb, L0_attn_Wa,
           L0_attn_conv_w, L0_attn_A_log, L0_attn_dt_bias, L0_attn_norm_w,
           L0_attn_Wout, L0_ffn_router_w, L0_ffn_shared_gate, L0_ffn_shared_up,
           L0_ffn_shared_down, L0_ffn_w_gate, L0_ffn_w_up, L0_ffn_w_down):
    cfg = default_cfg()
    args = [np.asarray(a) for a in
            (L0_attn_x, L0_attn_Wqkv, L0_attn_Wz, L0_attn_Wb, L0_attn_Wa,
             L0_attn_conv_w, L0_attn_A_log, L0_attn_dt_bias, L0_attn_norm_w,
             L0_attn_Wout, L0_ffn_router_w, L0_ffn_shared_gate, L0_ffn_shared_up,
             L0_ffn_shared_down, L0_ffn_w_gate, L0_ffn_w_up, L0_ffn_w_down)]
    in_maps = [make_core_inputs(cfg, c, *args) for c in range(8)]
    nc = build_program(cfg)
    res = run_bass_kernel_spmd(nc, in_maps, core_ids=list(range(8)))
    y = np.concatenate([np.asarray(r["y"]) for r in res.results], axis=0)
    return y.reshape(L0_attn_x.shape)


# version 41 (final)
# speedup vs baseline: 1.1309x; 1.0901x over previous
import math
from contextlib import ExitStack

import numpy as np
import ml_dtypes
import concourse.bass as bass
import concourse.mybir as mybir
from concourse.bass_utils import run_bass_kernel_spmd

F32 = mybir.dt.float32
BF16 = mybir.dt.bfloat16
BF = ml_dtypes.bfloat16
AF = mybir.ActivationFunctionType
OP = mybir.AluOpType
AX = mybir.AxisListType

ENGINES = ("sync", "tensor", "vector", "scalar", "gpsimd")
ESEMS = {
    "sync": (None, "d_sync"),
    "tensor": ("c_pe", None),
    "vector": ("c_dve", None),
    "scalar": ("c_act", "d_act"),
    "gpsimd": ("c_gp", "d_gp"),
}
BASE_SEM_NAMES = ["c_pe", "c_dve", "c_act", "c_gp", "c_cc"]

EPS_L2 = 1e-6
EPS_RMS = 1e-6
EPS_LN = 1e-5


class Ev:
    __slots__ = ("fn", "sem", "amt", "num", "eng", "idx")

    def __init__(self, eng, fn, idx):
        self.eng = eng
        self.fn = fn
        self.sem = None
        self.amt = 0
        self.num = None
        self.idx = idx


class Builder:

    def __init__(self, nc):
        self.nc = nc
        self.events = {e: [] for e in ENGINES}
        self.res = {}
        self.sem_handles = {}
        self.n_waits = 0
        self.n_events = 0
        self.used_sems = set(BASE_SEM_NAMES)
        self.last_dma = {}
        self.maxw = {}
        self.eidx = dict.fromkeys(ENGINES, 0)

    def _rec(self, key):
        r = self.res.get(key)
        if r is None:
            r = {"w": None, "r": []}
            self.res[key] = r
        return r

    def _wait(self, eng, d):
        psem = d.sem if d.sem is not None else ESEMS[d.eng][0]
        if self.maxw.get((eng, psem), -1) >= d.idx:
            return
        self._mark(d)
        self.events[eng].append(("w", d))
        self.maxw[(eng, psem)] = d.idx
        self.n_waits += 1

    def dma_sem(self, eng, tag):
        nm = f"d_{eng}_{tag}"
        self.used_sems.add(nm)
        return nm

    def emit(self, eng, fn, reads=(), writes=(), dma=False, collective=False):
        ev = Ev(eng, fn, self.eidx[eng])
        self.eidx[eng] += 1
        deps = []
        for k in reads:
            r = self._rec(k)
            if r["w"] is not None:
                deps.append(r["w"])
        for k in writes:
            r = self._rec(k)
            if r["w"] is not None:
                deps.append(r["w"])
            deps.extend(r["r"])
        seen = set()
        for d in deps:
            if id(d) in seen:
                continue
            seen.add(id(d))
            if d.eng == eng == "tensor":
                continue
            if d.amt == 16:
                d2 = self.last_dma[d.sem]
                if d2 is not d:
                    if id(d2) in seen:
                        continue
                    seen.add(id(d2))
                    d = d2
                if d.eng != eng:
                    self._wait(d.eng, d)
            self._wait(eng, d)
        if dma:
            tag = dma if isinstance(dma, str) else "main"
            ev.sem = self.dma_sem(eng, tag)
            ev.amt = 16
            self.last_dma[ev.sem] = ev
        elif collective:
            ev.sem = "c_cc"
            ev.amt = 1
        self.events[eng].append(("e", ev))
        self.n_events += 1
        for k in writes:
            r = self._rec(k)
            r["w"] = ev
            r["r"] = []
        for k in reads:
            self._rec(k)["r"].append(ev)
        return ev

    def _mark(self, ev):
        if ev.sem is None:
            ev.sem = ESEMS[ev.eng][0]
            assert ev.sem is not None
            ev.amt = 1

    def finalize(self):
        totals = dict.fromkeys(self.used_sems, 0)
        for e in ENGINES:
            for kind, item in self.events[e]:
                if kind == "e" and item.sem is not None:
                    totals[item.sem] += item.amt
                    item.num = totals[item.sem]
        return totals

    def replay_engine(self, name, E):
        for kind, item in self.events[name]:
            if kind == "w":
                assert item.num is not None, "wait on unfinalized event"
                E.wait_ge(self.sem_handles[item.sem], item.num)
            else:
                ins = item.fn(E)
                if item.sem is not None:
                    assert ins is not None, "milestone event produced no instruction"
                    ins.then_inc(self.sem_handles[item.sem], item.amt)


class ScanEmitter:

    def __init__(self, b: Builder, bufs: dict, H: int, C: int, NLEV: int = 7):
        self.b = b
        self.u = bufs
        self.H = H
        self.C = C
        self.NLEV = NLEV

    def h3(self, ap):
        return ap.rearrange("p (h c) -> p h c", h=self.H)

    def colb(self, ap_col):
        return ap_col.unsqueeze(2).broadcast_to([ap_col.shape[0], self.H, self.C])

    def maskb(self, ap_mask):
        return ap_mask.unsqueeze(1).broadcast_to([self.C, self.H, self.C])

    def p1_stages(self, khat, qhat, vhat, g_row, lnb_row, beta_row, par,
                  kq_keys=("kqv_in", "kqv_in", "kqv_in"),
                  row_keys=("rows_in", "rows_in", "rows_in")):
        b, u, H, C = self.b, self.u, self.H, self.C
        kk, qk, vk = kq_keys
        gk, lbk, bk_ = row_keys
        P = str(par)
        slab, slab2 = u["slab" + P], u["slab2" + P]
        kcb, gam, bkap = u["kcb" + P], u["gam" + P], u["bkap" + P]
        PST, pstk = u["PST" + P], "PST" + P
        bcol = slab[:, 0:H]
        betacol = slab[:, H:2 * H]
        brow3 = self.h3(u["brow"][:])
        browb3 = self.h3(u["browb"][:])
        knT3 = self.h3(u["knT" + P][:])
        qnT3 = self.h3(u["qnT" + P][:])

        def s0():
            b.emit("vector", lambda v, g=g_row: v.tensor_tensor_scan(
                u["b_scr"][:], g, u["zero_row"][:].broadcast_to([H, C]), 0.0, OP.add, OP.add),
                reads=[gk, "zero_row"], writes=["b_scr"])
            b.emit("vector", lambda v, l=lnb_row: v.tensor_add(u["blnb_scr"][:], u["b_scr"][:], l),
                reads=["b_scr", lbk], writes=["blnb_scr"])

            b.emit("tensor", lambda t: t.matmul(
                u["PSG"][:, 0:H], u["b_scr"][:], u["identH"][:], start=True, stop=False,
                is_transpose=True),
                reads=["b_scr", "identH", "PSG"], writes=["PSG"])
            b.emit("tensor", lambda t, br=beta_row: t.matmul(
                u["PSG"][:, H:2 * H], br, u["identH"][:], start=False, stop=True,
                is_transpose=True),
                reads=[bk_, "identH", "PSG"], writes=["PSG"])
            b.emit("vector", lambda v, slab=slab: v.tensor_copy(slab[:], u["PSG"][:, 0:2 * H]),
                reads=["PSG"], writes=["slab" + P])

            b.emit("scalar", lambda s, slab2=slab2, bcol=bcol: s.activation(
                slab2[:, H:2 * H], bcol, AF.Exp),
                reads=["slab" + P], writes=["slab2" + P])
            b.emit("tensor", lambda t: t.matmul(
                u["PSG"][0:1, 2 * H:3 * H], u["b_scr"][:, C - 1:C], u["identH"][:],
                start=True, stop=True, is_transpose=True),
                reads=["b_scr", "identH", "PSG"], writes=["PSG"])
            b.emit("vector", lambda v: v.tensor_copy(u["bc_row"][0:1, :], u["PSG"][0:1, 2 * H:3 * H]),
                reads=["PSG"], writes=["bc_row"])
            b.emit("tensor", lambda t: t.matmul(
                u["PSG"][:, 3 * H:4 * H], u["ones1"][0:1, :], u["bc_row"][0:1, :],
                start=True, stop=True),
                reads=["bc_row", "ones1", "PSG"], writes=["PSG"])
            b.emit("scalar", lambda s, kcb=kcb: s.activation(kcb[:], u["PSG"][:, 3 * H:4 * H], AF.Exp),
                reads=["PSG"], writes=["kcb" + P])
            b.emit("vector", lambda v, slab2=slab2, bcol=bcol: v.tensor_tensor(
                slab2[:, 0:H], u["PSG"][:, 3 * H:4 * H], bcol, OP.subtract),
                reads=["PSG", "slab" + P], writes=["slab2" + P])
            b.emit("scalar", lambda s, gam=gam, slab2=slab2: s.activation(
                gam[:], slab2[:, 0:H], AF.Exp),
                reads=["slab2" + P], writes=["gam" + P])
            b.emit("vector", lambda v, bkap=bkap, betacol=betacol, slab2=slab2: v.tensor_mul(
                bkap[:], betacol, slab2[:, H:2 * H]),
                reads=["slab" + P, "slab2" + P], writes=["bkap" + P])


        def s1():
            for nm, srcb, srck in (("brow", u["b_scr"], "b_scr"),
                                   ("browb", u["blnb_scr"], "blnb_scr")):
                for h in range(H):
                    b.emit("tensor", lambda t, h=h, srcb=srcb: t.matmul(
                        u["PSG"][:, h * C:(h + 1) * C], u["sel" + str(h)][:], srcb[:],
                        start=(h == 0), stop=(h == H - 1)),
                        reads=[srck, "sel" + str(h), "PSG"], writes=["PSG"])
                b.emit("vector", lambda v, nm=nm: v.tensor_copy(u[nm][:], u["PSG"][:, 0:H * C]),
                    reads=["PSG"], writes=[nm])

        def s2():
            b.emit("scalar", lambda s, k=khat: s.activation(self.h3(u["sq"][:]), k, AF.Square),
                reads=[kk], writes=["sq"])
            for h in range(H):
                b.emit("tensor", lambda t, h=h: t.matmul(
                    u["PSG"][:, 4 * H + h:4 * H + h + 1], self.h3(u["sq"][:])[:, h, :],
                    u["ones128"][:, 0:1], start=(h == 0), stop=False),
                    reads=["sq", "ones128", "PSG"], writes=["PSG"])
            b.emit("scalar", lambda s, q=qhat: s.activation(self.h3(u["sq"][:]), q, AF.Square),
                reads=[qk, "PSG"], writes=["sq"])
            for h in range(H):
                b.emit("tensor", lambda t, h=h: t.matmul(
                    u["PSG"][:, 5 * H + h:5 * H + h + 1], self.h3(u["sq"][:])[:, h, :],
                    u["ones128"][:, 0:1], start=False, stop=(h == H - 1)),
                    reads=["sq", "ones128", "PSG"], writes=["PSG"])
            b.emit("scalar", lambda s: s.activation(u["lncol"][:], u["PSG"][:, 4 * H:6 * H], AF.Ln,
                                                    bias=u["c_eps6"][:]),
                reads=["PSG", "c_eps6"], writes=["lncol"])
            b.emit("scalar", lambda s: s.activation(u["rcol"][:, 0:H], u["lncol"][:, 0:H],
                                                    AF.Exp, scale=-0.5),
                reads=["lncol"], writes=["rcol"])
            b.emit("scalar", lambda s: s.activation(u["rcol"][:, H:2 * H], u["lncol"][:, H:2 * H],
                                                    AF.Exp, scale=-0.5, bias=u["c_lnq"][:]),
                reads=["lncol", "c_lnq"], writes=["rcol"])


        def kq(nm, src, srck, rlo):
            for h in range(H):
                b.emit("tensor", lambda t, h=h, src=src, PST=PST: t.matmul(
                    PST[:, h * C:(h + 1) * C], src[:, h, :], u["identb"][:],
                    start=(h == 0), stop=(h == H - 1), is_transpose=True),
                    reads=[srck, "identb", pstk], writes=[pstk])
            tmk = (nm + "_tm" + P) if nm == "k" else "q_tm"
            tm = u[tmk]
            b.emit("vector", lambda v, tm=tm, rlo=rlo, PST=PST: v.tensor_tensor(
                self.h3(tm[:]), self.h3(PST[:, 0:H * C]),
                self.colb(u["rcol"][:, rlo:rlo + H]), OP.mult),
                reads=[pstk, "rcol"], writes=[tmk])
            for h in range(H):
                b.emit("tensor", lambda t, h=h, tm=tm, PST=PST: t.matmul(
                    PST[:, H * C + h * C:H * C + (h + 1) * C], self.h3(tm[:])[:, h, :],
                    u["identb"][:],
                    start=(h == 0), stop=(h == H - 1), is_transpose=True),
                    reads=[tmk, "identb", pstk], writes=[pstk])
            nt = u[nm + "nT" + P]
            b.emit("scalar", lambda s, nt=nt, PST=PST: s.copy(
                nt[:], PST[:, H * C:2 * H * C]),
                reads=[pstk], writes=[nm + "nT" + P])

        def s3():
            kq("k", khat, kk, 0)

        def s4():
            kq("q", qhat, qk, H)
            for h in range(H):
                b.emit("tensor", lambda t, h=h, v_=vhat, PST=PST: t.matmul(
                    PST[:, h * C:(h + 1) * C], v_[:, h, :], u["identb"][:],
                    start=(h == 0), stop=(h == H - 1), is_transpose=True),
                    reads=[vk, "identb", pstk], writes=[pstk])
            b.emit("scalar", lambda s, PST=PST: s.copy(u["v_tm"][:], PST[:, 0:H * C]),
                reads=[pstk], writes=["v_tm"])
            b.emit("vector", lambda v, betacol=betacol: v.tensor_tensor(
                self.h3(u["rhs1" + P][:]), self.h3(u["v_tm"][:]), self.colb(betacol), OP.mult),
                reads=["v_tm", "slab" + P], writes=["rhs1" + P])

        def s5():
            for h in range(H):
                b.emit("tensor", lambda t, h=h, knT3=knT3: t.matmul(
                    u["PSG"][:, h * C:(h + 1) * C], knT3[:, h, :], knT3[:, h, :],
                    start=(h == 0), stop=(h == H - 1)),
                    reads=["knT" + P, "PSG"], writes=["PSG"])

            b.emit("vector", lambda v, bcol=bcol: v.tensor_tensor(
                self.h3(u["e1"][:]), self.colb(bcol), brow3, OP.subtract),
                reads=["slab" + P, "brow"], writes=["e1"])
            b.emit("vector", lambda v: v.tensor_scalar_min(u["e1"][:], u["e1"][:], 0.0),
                reads=["e1"], writes=["e1"])
            b.emit("scalar", lambda s: s.activation(u["e1"][:], u["e1"][:], AF.Exp),
                reads=["e1"], writes=["e1"])
            b.emit("vector", lambda v: v.tensor_mul(u["t1"][:], u["e1"][:], u["PSG"][:, 0:H * C]),
                reads=["e1", "PSG"], writes=["t1"])
            b.emit("vector", lambda v: v.tensor_tensor(
                self.h3(u["t1"][:]), self.h3(u["t1"][:]), self.maskb(u["lmask_neg"][:]), OP.mult),
                reads=["t1", "lmask_neg"], writes=["t1"])
            b.emit("vector", lambda v, betacol=betacol: v.tensor_tensor(
                self.h3(u["Npow" + P][:]), self.h3(u["t1"][:]), self.colb(betacol), OP.mult),
                reads=["t1", "slab" + P], writes=["Npow" + P])

        def s6():
            b.emit("vector", lambda v, bcol=bcol: v.tensor_tensor(
                self.h3(u["e1"][:]), browb3, self.colb(bcol), OP.subtract),
                reads=["browb", "slab" + P], writes=["e1"])
            b.emit("vector", lambda v: v.tensor_scalar_min(u["e1"][:], u["e1"][:], 0.0),
                reads=["e1"], writes=["e1"])
            b.emit("scalar", lambda s: s.activation(u["e1"][:], u["e1"][:], AF.Exp),
                reads=["e1"], writes=["e1"])
            b.emit("vector", lambda v: v.tensor_mul(u["t1"][:], u["e1"][:], u["PSG"][:, 0:H * C]),
                reads=["e1", "PSG"], writes=["t1"])
            b.emit("vector", lambda v: v.tensor_tensor(
                self.h3(u["NTpow" + P][:]), self.h3(u["t1"][:]), self.maskb(u["umask_neg"][:]),
                OP.mult),
                reads=["t1", "umask_neg"], writes=["NTpow" + P])
            b.emit("vector", lambda v: v.tensor_tensor(
                self.h3(u["GT" + P][:]), self.h3(u["NTpow" + P][:]), self.maskb(u["identb"][:]),
                OP.add),
                reads=["NTpow" + P, "identb"], writes=["GT" + P])

            for h in range(H):
                b.emit("tensor", lambda t, h=h, knT3=knT3, qnT3=qnT3: t.matmul(
                    u["PSG"][:, h * C:(h + 1) * C], knT3[:, h, :], qnT3[:, h, :],
                    start=(h == 0), stop=(h == H - 1)),
                    reads=["knT" + P, "qnT" + P, "PSG"], writes=["PSG"])
            b.emit("vector", lambda v, bcol=bcol: v.tensor_tensor(
                self.h3(u["e1"][:]), brow3, self.colb(bcol), OP.subtract),
                reads=["brow", "slab" + P], writes=["e1"])
            b.emit("vector", lambda v: v.tensor_scalar_min(u["e1"][:], u["e1"][:], 0.0),
                reads=["e1"], writes=["e1"])
            b.emit("scalar", lambda s: s.activation(u["e1"][:], u["e1"][:], AF.Exp),
                reads=["e1"], writes=["e1"])
            b.emit("vector", lambda v: v.tensor_mul(u["t1"][:], u["e1"][:], u["PSG"][:, 0:H * C]),
                reads=["e1", "PSG"], writes=["t1"])
            b.emit("vector", lambda v: v.tensor_tensor(
                self.h3(u["Sup" + P][:]), self.h3(u["t1"][:]), self.maskb(u["umask_incl"][:]),
                OP.mult),
                reads=["t1", "umask_incl"], writes=["Sup" + P])

        return [s0, s1, s2, s3, s4, s5, s6]

    def p1b(self, par):
        b, u, H, C = self.b, self.u, self.H, self.C
        P = str(par)
        stages = []
        for lev in range(1, self.NLEV):
            def ql(lev=lev):
                for h in range(H):
                    b.emit("tensor", lambda t, h=h: t.matmul(
                        u["PSQ"][:, h * C:(h + 1) * C],
                        self.h3(u["NTpow" + P][:])[:, h, :],
                        self.h3(u["Npow" + P][:])[:, h, :],
                        start=(h == 0), stop=(h == H - 1)),
                        reads=["NTpow" + P, "Npow" + P, "PSQ"], writes=["PSQ"])
                if lev < self.NLEV - 1:
                    for h in range(H):
                        b.emit("tensor", lambda t, h=h: t.matmul(
                            u["PSR"][:, h * C:(h + 1) * C],
                            self.h3(u["Npow" + P][:])[:, h, :],
                            self.h3(u["NTpow" + P][:])[:, h, :],
                            start=(h == 0), stop=(h == H - 1)),
                            reads=["NTpow" + P, "Npow" + P, "PSR"], writes=["PSR"])
                b.emit("scalar", lambda s: s.copy(u["Npow" + P][:], u["PSQ"][:, 0:H * C]),
                    reads=["PSQ"], writes=["Npow" + P])
                if lev < self.NLEV - 1:
                    b.emit("scalar", lambda s: s.copy(u["NTpow" + P][:], u["PSR"][:, 0:H * C]),
                        reads=["PSR"], writes=["NTpow" + P])
                for h in range(H):
                    b.emit("tensor", lambda t, h=h: t.matmul(
                        u["PSA"][:, h * C:(h + 1) * C],
                        self.h3(u["Npow" + P][:])[:, h, :],
                        self.h3(u["GT" + P][:])[:, h, :],
                        start=(h == 0), stop=(h == H - 1)),
                        reads=["Npow" + P, "GT" + P, "PSA"], writes=["PSA"])
                b.emit("vector", lambda v: v.tensor_add(
                    u["GT" + P][:], u["GT" + P][:], u["PSA"][:, 0:H * C]),
                    reads=["GT" + P, "PSA"], writes=["GT" + P])
            stages.append(ql)

        def wind():
            for h in range(H):
                b.emit("tensor", lambda t, h=h: t.matmul(
                    u["PSR"][:, h * C:(h + 1) * C],
                    self.h3(u["GT" + P][:])[:, h, :],
                    self.h3(u["rhs1" + P][:])[:, h, :],
                    start=(h == 0), stop=(h == H - 1)),
                    reads=["GT" + P, "rhs1" + P, "PSR"], writes=["PSR"])
            b.emit("scalar", lambda s: s.copy(u["Wind"][:], u["PSR"][:, 0:H * C]),
                reads=["PSR"], writes=["Wind"])
        stages.append(wind)
        return stages

    def p23_chain(self, par):
        b, u, H, C = self.b, self.u, self.H, self.C
        P = str(par)
        kcb, gam, bkap = u["kcb" + P], u["gam" + P], u["bkap" + P]
        knT3 = self.h3(u["knT" + P][:])
        qnT3 = self.h3(u["qnT" + P][:])
        gt3 = self.h3(u["GT" + P][:])

        def c0():
            for h in range(H):
                b.emit("tensor", lambda t, h=h: t.matmul(
                    u["PSK"][:, h * C:(h + 1) * C], knT3[:, h, :],
                    self.h3(u["M_bf"][:])[:, h, :],
                    start=(h == 0), stop=(h == H - 1)),
                    reads=["knT" + P, "M_bf", "PSK"], writes=["PSK"])
            for h in range(H):
                b.emit("tensor", lambda t, h=h: t.matmul(
                    u["PSA"][:, h * C:(h + 1) * C], qnT3[:, h, :],
                    self.h3(u["M_bf"][:])[:, h, :],
                    start=(h == 0), stop=(h == H - 1)),
                    reads=["qnT" + P, "M_bf", "PSA"], writes=["PSA"])

        def c1():
            b.emit("vector", lambda v: v.tensor_tensor(
                self.h3(u["X"][:]), self.h3(u["PSK"][:, 0:H * C]),
                self.colb(bkap[:]), OP.mult),
                reads=["PSK", "bkap" + P], writes=["X"])

        def c2():
            for h in range(H):
                b.emit("tensor", lambda t, h=h: t.matmul(
                    u["PSK"][:, h * C:(h + 1) * C], gt3[:, h, :],
                    self.h3(u["X"][:])[:, h, :],
                    start=(h == 0), stop=(h == H - 1)),
                    reads=["GT" + P, "X", "PSK"], writes=["PSK"])

        def c3():
            b.emit("vector", lambda v: v.tensor_sub(
                u["Wf"][:], u["Wind"][:], u["PSK"][:, 0:H * C]),
                reads=["Wind", "PSK"], writes=["Wf"])

        def c4():
            b.emit("vector", lambda v: v.tensor_tensor(
                self.h3(u["Wg"][:]), self.h3(u["Wf"][:]), self.colb(gam[:]),
                OP.mult), reads=["Wf", "gam" + P], writes=["Wg"])

        def c5():
            for h in range(H):
                b.emit("tensor", lambda t, h=h: t.matmul(
                    u["PSK"][:, h * C:(h + 1) * C],
                    self.h3(u["k_tm" + P][:])[:, h, :],
                    self.h3(u["Wg"][:])[:, h, :],
                    start=(h == 0), stop=(h == H - 1)),
                    reads=["k_tm" + P, "Wg", "PSK"], writes=["PSK"])

        def c6():
            b.emit("vector", lambda v: v.tensor_tensor(
                self.h3(u["M_f32"][:]), self.h3(u["M_f32"][:]),
                self.colb(kcb[:]), OP.mult),
                reads=["M_f32", "kcb" + P], writes=["M_f32"])

        def c7():
            b.emit("vector", lambda v: v.tensor_add(
                u["M_f32"][:], u["M_f32"][:], u["PSK"][:, 0:H * C]),
                reads=["M_f32", "PSK"], writes=["M_f32"])

        def c8():
            b.emit("scalar", lambda s: s.copy(u["M_bf"][:], u["M_f32"][:]),
                reads=["M_f32"], writes=["M_bf"])

        return [c0, c1, c2, c3, c4, c5, c6, c7, c8]

    def p23_out(self, par, zsl, og_cb, z_key):
        b, u, H, C = self.b, self.u, self.H, self.C
        P = str(par)
        slab2 = u["slab2" + P]
        PST, pstk = u["PST" + P], "PST" + P
        kappa = slab2[:, H:2 * H]

        def o1a():
            b.emit("vector", lambda v: v.tensor_tensor(
                self.h3(u["t2"][:]), self.h3(u["PSA"][:, 0:H * C]),
                self.colb(kappa), OP.mult),
                reads=["PSA", "slab2" + P], writes=["t2"])

        def o1b():
            b.emit("scalar", lambda s: s.copy(u["W_bf"][:], u["Wf"][:]),
                reads=["Wf"], writes=["W_bf"])
            for h in range(H):
                b.emit("tensor", lambda t, h=h: t.matmul(
                    u["PSS"][:, h * C:(h + 1) * C],
                    self.h3(u["Sup" + P][:])[:, h, :],
                    self.h3(u["W_bf"][:])[:, h, :],
                    start=(h == 0), stop=(h == H - 1)),
                    reads=["Sup" + P, "W_bf", "PSS"], writes=["PSS"])
            b.emit("vector", lambda v: v.tensor_add(
                u["O"][:], u["t2"][:], u["PSS"][:, 0:H * C]),
                reads=["t2", "PSS"], writes=["O"])

        def o2():
            b.emit("scalar", lambda s: s.activation(u["t2"][:], u["O"][:], AF.Square),
                reads=["O"], writes=["t2"])
            b.emit("vector", lambda v: v.tensor_reduce(
                u["s4"][:].unsqueeze(2), self.h3(u["t2"][:]), AX.X, OP.add),
                reads=["t2"], writes=["s4"])
            b.emit("scalar", lambda s: s.activation(u["s4"][:], u["s4"][:], AF.Ln,
                                                    bias=u["c_eps6"][:], scale=1.0 / 128.0),
                reads=["s4", "c_eps6"], writes=["s4"])
            b.emit("scalar", lambda s: s.activation(u["s4"][:], u["s4"][:], AF.Exp, scale=-0.5),
                reads=["s4"], writes=["s4"])
            b.emit("vector", lambda v: v.tensor_tensor(
                self.h3(u["t2"][:]), self.h3(u["O"][:]), self.colb(u["s4"][:]), OP.mult),
                reads=["O", "s4"], writes=["t2"])
            b.emit("vector", lambda v, z=zsl: v.tensor_tensor(
                self.h3(u["og_tm"][:]), self.h3(u["t2"][:]), z, OP.mult),
                reads=["t2", z_key], writes=["og_tm"])

        def o3():
            for h in range(H):
                b.emit("tensor", lambda t, h=h, PST=PST: t.matmul(
                    PST[:, h * C:(h + 1) * C], self.h3(u["og_tm"][:])[:, h, :],
                    u["identb"][:], start=(h == 0), stop=(h == H - 1), is_transpose=True),
                    reads=["og_tm", "identb", pstk], writes=[pstk])
            b.emit("scalar", lambda s, PST=PST: s.copy(u["ogT"][:], PST[:, 0:H * C]),
                reads=[pstk], writes=["ogT"])
            og_cb()

        return [o1a, o1b, o2, o3]


def alloc_scan_bufs(nc, ctx, H, C):
    u = {}

    def sb(name, shape, dt=F32):
        u[name] = ctx.enter_context(nc.sbuf_tensor("scan_" + name, shape, dt))

    def ps(name, shape, dt=F32):
        u[name] = ctx.enter_context(nc.psum_tensor("scan_" + name, shape, dt))

    HC = H * C
    ps("PSG", [128, 512]); ps("PSS", [128, 512]); ps("PSK", [128, 512])
    ps("PSA", [128, 512]); ps("PSQ", [128, 512]); ps("PSR", [128, 512])
    ps("PST0", [128, 1024], BF16); ps("PST1", [128, 1024], BF16)
    sb("b_scr", [H, C]); sb("blnb_scr", [H, C]); sb("zero_row", [H, 1])
    sb("bc_row", [1, H])
    sb("brow", [128, HC]); sb("browb", [128, HC])
    sb("sq", [128, HC], BF16); sb("lncol", [128, 2 * H]); sb("rcol", [128, 2 * H], BF16)
    sb("q_tm", [128, HC], BF16); sb("v_tm", [128, HC], BF16)
    sb("e1", [128, HC]); sb("t1", [128, HC]); sb("t2", [128, HC])
    sb("X", [128, HC], BF16); sb("O", [128, HC])
    sb("s4", [128, H]); sb("og_tm", [128, HC], BF16); sb("ogT", [128, HC], BF16)
    sb("Wg", [128, HC], BF16)
    sb("Wf", [128, HC]); sb("W_bf", [128, HC], BF16)
    sb("M_f32", [128, HC]); sb("M_bf", [128, HC], BF16)
    for P in ("0", "1"):
        sb("slab" + P, [128, 2 * H]); sb("slab2" + P, [128, 2 * H])
        sb("kcb" + P, [128, H]); sb("gam" + P, [128, H], BF16); sb("bkap" + P, [128, H])
        sb("rhs1" + P, [128, HC], BF16)
        sb("knT" + P, [128, HC], BF16); sb("qnT" + P, [128, HC], BF16)
        sb("k_tm" + P, [128, HC], BF16)
        sb("Npow" + P, [128, HC], BF16); sb("NTpow" + P, [128, HC], BF16)
        sb("GT" + P, [128, HC], BF16); sb("Sup" + P, [128, HC], BF16)
    sb("Wind", [128, HC])
    sb("identH", [H, H]); sb("identb", [128, 128], BF16); sb("ident_f", [128, 128])
    sb("ones1", [1, 128]); sb("ones128", [128, 1], BF16)
    sb("c_eps6", [128, 1]); sb("c_eps5", [128, 1]); sb("c_lnq", [128, 1]); sb("c_one", [128, 1])
    sb("lmask_neg", [C, C]); sb("umask_neg", [C, C]); sb("umask_incl", [C, C])
    for h in range(H):
        sb("sel" + str(h), [H, 128])
    return u



def scan_const_arrays(H, C):
    c = {}
    c["identH"] = np.eye(H, dtype=np.float32)
    c["ident_f"] = np.eye(128, dtype=np.float32)
    c["identb"] = np.eye(128, dtype=np.float32).astype(BF)
    c["ones1"] = np.ones((1, 128), np.float32)
    c["ones128"] = np.ones((128, 1), np.float32).astype(BF)
    c["c_eps6"] = np.full((128, 1), 1e-6, np.float32)
    c["c_eps5"] = np.full((128, 1), 1e-5, np.float32)
    c["c_lnq"] = np.full((128, 1), math.log(1.0 / math.sqrt(128.0)), np.float32)
    c["c_one"] = np.ones((128, 1), np.float32)
    c["lmask_neg"] = -np.tril(np.ones((C, C), np.float32), -1)
    c["umask_neg"] = -np.triu(np.ones((C, C), np.float32), 1)
    c["umask_incl"] = np.triu(np.ones((C, C), np.float32), 0)
    for h in range(H):
        s = np.zeros((H, 128), np.float32)
        s[h, :] = 1.0
        c["sel" + str(h)] = s
    c["zero_row"] = np.zeros((H, 1), np.float32)
    return c


def emit_const_loads(b, u, consts, nc, dma_eng="sync"):
    for name, arr in consts.items():
        t = nc.inline_tensor(np.ascontiguousarray(arr), "const_" + name)
        b.emit(dma_eng, lambda E, t=t, name=name: E.dma_start(out=u[name][:], in_=t.ap()),
               writes=[name], dma="xload")



def default_cfg():
    return dict(D=2048, L=4096, TILE=512, H=4, HG=16, C=128, NLEV=7,
                IFF=1408, NE=8, TB=1024, KCONV=4, IBMAX=256)


def mini_cfg():
    return dict(D=512, L=512, TILE=256, H=1, HG=4, C=128, NLEV=7,
                IFF=256, NE=8, TB=128, KCONV=4, IBMAX=256)


class CoreProgram:

    def __init__(self, nc, cfg):
        self.nc = nc
        self.cfg = cfg
        self.b = Builder(nc)

    def const_into(self, u, stack, name, arr):
        arr = np.ascontiguousarray(arr)
        u[name] = stack.enter_context(
            self.nc.sbuf_tensor("c_" + name, list(arr.shape),
                                mybir.dt.from_np(arr.dtype)))
        t = self.nc.inline_tensor(arr, "ct_" + name)
        self.b.emit("sync", lambda E, t=t, name=name: E.dma_start(
            out=u[name][:], in_=t.ap()), writes=[name], dma="xload")

    def barrier(self):
        lasts = {}
        for e in ENGINES:
            for kind, item in reversed(self.b.events[e]):
                if kind == "e":
                    lasts[e] = item
                    break
        targets = list(lasts.values()) + list(self.b.last_dma.values())
        for ce in ENGINES:
            for ev in targets:
                if ev.eng != ce or ev.amt == 16:
                    self.b._wait(ce, ev)

    def build(self):
        nc, cfg, b = self.nc, self.cfg, self.b
        D, L, TILE, H, C = cfg["D"], cfg["L"], cfg["TILE"], cfg["H"], cfg["C"]
        HG, IFF, NE, TB, KC = cfg["HG"], cfg["IFF"], cfg["NE"], cfg["TB"], cfg["KCONV"]
        ND, NT, NG = D // 128, L // TILE, TILE // 128
        NQKV = 3 * H
        VD = HG * 128
        NVC = VD // 128
        NTB = TB // 128
        NIB = IFF // 128
        NHOB = D // 512
        NSPL = cfg.get("NSPL", 2 if TB >= 1024 else 1)
        THB = TB // NSPL

        SH = IFF // 8
        IFFT = IFF + SH
        NIC = (IFFT + 127) // 128
        IFFP = NIC * 128
        TBL = min(512, TB)
        NGB = TBL // 128
        NBR = TB // TBL
        NTBL = 8 * NBR
        NCHK = NBR

        dt_in = lambda n, s, dt=F32: nc.dram_tensor(n, s, dt, kind="ExternalInput").ap()
        x_attn = dt_in("x_attn", [L, D])
        x_res = dt_in("x_res", [TB, D])
        wqkvz = dt_in("wqkvz", [D, (NQKV + H) * 128])
        wba = dt_in("wba", [D, 2 * H])
        convw = dt_in("convw", [NQKV * 128, KC])
        alog4 = dt_in("alog4", [H, 1])
        dtb4 = dt_in("dtb4", [H, 1])
        normw = dt_in("normw", [128, 1])
        wout = dt_in("wout", [VD, D])
        routerw = dt_in("routerw", [NE, D])
        ewgT = dt_in("ewgT", [D, IFFT])
        ewuT = dt_in("ewuT", [D, IFFT])
        ewdT = dt_in("ewdT", [IFFP, D])
        y_out = nc.dram_tensor("y", [TB, D], F32, kind="ExternalOutput").ap()

        agin = nc.dram_tensor("agin", [4, H * 128, L // 4], BF16)
        agout = nc.dram_tensor("agout", [4, VD, L // 4], BF16)
        x1_dram = nc.dram_tensor("x1_dram", [TB, D], F32)
        agin2 = nc.dram_tensor("agin2", [2, D, TB // 2], BF16)
        agout2 = nc.dram_tensor("agout2", [2, 8 * D, TB // 2], BF16)
        ag3in = nc.dram_tensor("ag3in", [NE * NTB, 128], F32)
        ag3out = nc.dram_tensor("ag3out", [8 * NE * NTB, 128], F32)
        rs_in = [nc.dram_tensor(f"rs_in{k}", [8 * TBL, D], BF16)
                 for k in range(NCHK)]
        rs_out = [nc.dram_tensor(f"rs_out{k}", [TBL, D], BF16)
                  for k in range(NCHK)]

        QW = (NQKV + H) * 128 // 4
        NCH_T = TILE // C
        with ExitStack() as actx:
            uA = {}
            sb = lambda n, s, dt=F32: uA.__setitem__(
                n, actx.enter_context(nc.sbuf_tensor(n, s, dt)))
            uA_scan = alloc_scan_bufs(nc, actx, H, C)
            uA.update(uA_scan)
            emit_const_loads(b, uA, scan_const_arrays(H, C), nc)

            sb("xt0", [128, D]); sb("xt1", [128, D])
            sb("xt2", [128, D]); sb("xt3", [128, D])
            sb("sq_ln", [128, D])
            sb("mu0", [128, 2]); sb("mu1", [128, 2])
            sb("rstd0", [128, 1]); sb("rstd1", [128, 1])
            sb("xn0", [128, D]); sb("xn1", [128, D])
            sb("lnxT", [128, ND, TILE], BF16)
            sb("wq0", [128, ND, QW], BF16)
            sb("wq1", [128, ND, QW], BF16)
            sb("wba_s", [128, ND, 2 * H], BF16)
            sb("qkv_raw", [128, NQKV, TILE + KC - 1], BF16)
            sb("silbuf", [128, NQKV, TILE], BF16)
            sb("z_sil0", [128, NCH_T, H * 128])
            sb("z_sil1", [128, NCH_T, H * 128])
            sb("diag8", [128, 2 * KC, 128], BF16)
            sb("convw_s", [128, NQKV, KC])
            sb("balog", [H, TILE]); sb("galog", [H, TILE]); sb("lnb_r", [H, TILE])
            sb("arows_scr", [H, TILE])
            sb("alog_c", [H, 1]); sb("dtb_c", [H, 1])

            b.emit("gpsimd", lambda E: E.dma_start(
                out=uA["wba_s"][:],
                in_=wba.rearrange("(dc p) c -> p dc c", p=128)),
                writes=["wba_s"], dma="cast")
            b.emit("sync", lambda E: E.dma_start(
                out=uA["convw_s"][:],
                in_=convw.rearrange("(cc p) k -> p cc k", p=128)),
                writes=["convw_s"], dma="xload")
            b.emit("sync", lambda E: E.dma_start(out=uA["alog_c"][:], in_=alog4),
                   writes=["alog_c"], dma="xload")
            b.emit("sync", lambda E: E.dma_start(out=uA["dtb_c"][:], in_=dtb4),
                   writes=["dtb_c"], dma="xload")

            h3q = lambda nm, q: uA[nm][:]
            b.emit("scalar", lambda s: s.activation(uA["alog_c"][:], uA["alog_c"][:], AF.Exp),
                   reads=["alog_c"], writes=["alog_c"])
            b.emit("vector", lambda v: v.memset(uA["M_f32"][:], 0.0), writes=["M_f32"])
            b.emit("vector", lambda v: v.memset(uA["M_bf"][:], 0.0), writes=["M_bf"])
            b.emit("vector", lambda v: v.memset(
                h3q("qkv_raw", NQKV)[:, :, 0:KC - 1], 0.0), writes=["qkv_raw"])

            em = ScanEmitter(b, uA, H, C, cfg["NLEV"])

            def prep_xload(ti):
                t0 = ti * TILE
                for g in range(NG):
                    b.emit("sync", lambda E, t0=t0, g=g: E.dma_start(
                        out=uA["xt" + str(g)][:],
                        in_=x_attn[t0 + g * 128:t0 + (g + 1) * 128, :]),
                        reads=["xt" + str(g)], writes=["xt" + str(g)],
                        dma="xl" + str(g))

            def prep_ln(ti, pair):
                jobs = [(uA["xt" + str(g)][:], "xt" + str(g), g, str(g % 2))
                        for g in range(2 * pair, min(2 * pair + 2, NG))]
                self.emit_ln_pair(uA, jobs, "lnxT", ND, D, "PSG")

            def prep_q_load(ti, q):
                wpar = q % 2
                wq = uA["wq" + str(wpar)]
                half = QW // 2
                for hh in range(2):
                    b.emit("gpsimd", lambda E, q=q, wq=wq, hh=hh: E.dma_start(
                        out=wq[:][:, :, hh * half:(hh + 1) * half],
                        in_=wqkvz[:, q * QW + hh * half:q * QW + (hh + 1) * half]
                        .rearrange("(dc p) c -> p dc c", p=128)),
                        reads=[("wq", wpar, hh)],
                        writes=[("wq", wpar, hh)],
                        dma="wq" + str(wpar) + str(hh))

            def prep_q_mm(ti, q):
                tp = str(ti % 2)
                zs = "z_sil" + tp
                lnxT3 = h3q("lnxT", ND)
                wpar = q % 2
                wq = uA["wq" + str(wpar)]
                for j in range(QW // 128):
                    cc = q * (QW // 128) + j
                    hh = j // ((QW // 128) // 2)
                    if cc < NQKV:
                        ps = "PSS"
                        for dc in range(ND):
                            b.emit("tensor", lambda t, j=j, dc=dc, ps=ps, wq=wq: t.matmul(
                                uA[ps][:, 0:TILE], wq[:][:, dc, j * 128:(j + 1) * 128],
                                lnxT3[:, dc, :], start=(dc == 0), stop=(dc == ND - 1)),
                                reads=[("wq", wpar, hh), "lnxT", ps], writes=[ps])
                        b.emit("scalar", lambda s, cc=cc, ps=ps: s.copy(
                            h3q("qkv_raw", NQKV)[:, cc, KC - 1:KC - 1 + TILE],
                            uA[ps][:, 0:TILE]),
                            reads=[ps], writes=["qkv_raw"])
                if q == 3:
                    for g in range(NG):
                        ps = "PSS"
                        for dc in range(ND):
                            b.emit("tensor", lambda t, g=g, dc=dc, ps=ps, wq=wq: t.matmul(
                                uA[ps][:, 0:H * 128], lnxT3[:, dc, g * 128:(g + 1) * 128],
                                wq[:][:, dc, QW - H * 128:QW],
                                start=(dc == 0), stop=(dc == ND - 1)),
                                reads=[("wq", wpar, 0), ("wq", wpar, 1), "lnxT", ps],
                                writes=[ps])
                        b.emit("scalar", lambda s, g=g, ps=ps, zs=zs: s.activation(
                            h3q(zs, NG)[:, g, :], uA[ps][:, 0:H * 128], AF.Silu),
                            reads=[ps], writes=[zs])
            def prep_s3(ti):
                lnxT3 = h3q("lnxT", ND)
                for which, lo in (("balog", 0), ("galog", H)):
                    for dc in range(ND):
                        b.emit("tensor", lambda t, dc=dc, lo=lo: t.matmul(
                            uA["PSS"][0:H, 0:TILE],
                            h3q("wba_s", ND)[:, dc, lo:lo + H], lnxT3[:, dc, :],
                            start=(dc == 0), stop=(dc == ND - 1)),
                            reads=["wba_s", "lnxT", "PSS"], writes=["PSS"])
                    b.emit("vector", lambda v, which=which: v.tensor_copy(
                        uA[which][:], uA["PSS"][0:H, 0:TILE]),
                        reads=["PSS"], writes=[which])
                for cc in range(NQKV):
                    pc = cc % 2
                    ps = "PSS" if pc == 0 else "PSG"
                    for j in range(KC):
                        b.emit("vector", lambda v, cc=cc, j=j, pc=pc: v.tensor_scalar_mul(
                            h3q("diag8", 2 * KC)[:, pc * KC + j, :], uA["identb"][:],
                            h3q("convw_s", NQKV)[:, cc, j:j + 1]),
                            reads=["identb", "convw_s", ("diag", pc)],
                            writes=[("diag", pc)])
                    for j in range(KC):
                        b.emit("tensor", lambda t, cc=cc, j=j, ps=ps, pc=pc: t.matmul(
                            uA[ps][:, 0:TILE], h3q("diag8", 2 * KC)[:, pc * KC + j, :],
                            h3q("qkv_raw", NQKV)[:, cc, j:j + TILE],
                            start=(j == 0), stop=(j == KC - 1)),
                            reads=[("diag", pc), "qkv_raw", ps], writes=[ps])
                    b.emit("scalar", lambda s, cc=cc, ps=ps: s.activation(
                        h3q("silbuf", NQKV)[:, cc, :], uA[ps][:, 0:TILE], AF.Silu),
                        reads=[ps], writes=["silbuf"])
                b.emit("vector", lambda v: v.tensor_copy(
                    h3q("qkv_raw", NQKV)[:, :, 0:KC - 1],
                    h3q("qkv_raw", NQKV)[:, :, TILE:TILE + KC - 1]),
                    reads=["qkv_raw"], writes=["qkv_raw"])
                bsl = uA["balog"][:]
                asl = uA["galog"][:]
                b.emit("scalar", lambda s, bsl=bsl: s.activation(
                    uA["arows_scr"][:], bsl, AF.Tanh, scale=0.5),
                    reads=["balog"], writes=["arows_scr"])
                b.emit("vector", lambda v, bsl=bsl: v.tensor_scalar(
                    bsl, uA["arows_scr"][:], 0.5, 0.5, OP.mult, OP.add),
                    reads=["arows_scr"], writes=["balog"])
                b.emit("vector", lambda v, asl=asl: v.tensor_scalar_add(
                    asl, asl, uA["dtb_c"][:]), reads=["galog", "dtb_c"], writes=["galog"])
                b.emit("scalar", lambda s, asl=asl: s.activation(
                    uA["arows_scr"][:], asl, AF.Abs), reads=["galog"], writes=["arows_scr"])
                b.emit("scalar", lambda s: s.activation(
                    uA["arows_scr"][:], uA["arows_scr"][:], AF.Exp, scale=-1.0),
                    reads=["arows_scr"], writes=["arows_scr"])
                b.emit("scalar", lambda s: s.activation(
                    uA["arows_scr"][:], uA["arows_scr"][:], AF.Ln, bias=uA["c_one"][0:H, :]),
                    reads=["arows_scr", "c_one"], writes=["arows_scr"])
                b.emit("vector", lambda v, asl=asl: v.tensor_scalar_max(
                    asl, asl, 0.0), reads=["galog"], writes=["galog"])
                b.emit("vector", lambda v, asl=asl: v.tensor_add(
                    asl, asl, uA["arows_scr"][:]), reads=["galog", "arows_scr"],
                    writes=["galog"])
                b.emit("vector", lambda v, asl=asl: v.tensor_scalar(
                    asl, asl, uA["alog_c"][:], -1.0, OP.mult, OP.mult),
                    reads=["galog", "alog_c"], writes=["galog"])
                b.emit("scalar", lambda s, bsl=bsl: s.activation(
                    uA["lnb_r"][:], bsl, AF.Ln),
                    reads=["balog"], writes=["lnb_r"])

            def prep_head(ti, ci):
                if ci == 0:
                    prep_xload(ti)
                    prep_q_load(ti, 0)
                    prep_q_load(ti, 1)
                elif ci == 2:
                    prep_q_load(ti, 2)
                    prep_q_load(ti, 3)

            def prep_tail(ti, ci):
                if ci == 0:
                    prep_ln(ti, 0)
                    prep_ln(ti, 1)
                elif ci == 1:
                    prep_q_mm(ti, 0)
                    prep_q_mm(ti, 1)
                elif ci == 2:
                    prep_q_mm(ti, 2)
                    prep_q_mm(ti, 3)
                else:
                    prep_s3(ti)

            def prep_all(ti):
                prep_xload(ti)
                prep_q_load(ti, 0); prep_q_load(ti, 1)
                prep_ln(ti, 0); prep_ln(ti, 1)
                prep_q_mm(ti, 0); prep_q_mm(ti, 1)
                prep_q_load(ti, 2); prep_q_load(ti, 3)
                prep_q_mm(ti, 2); prep_q_mm(ti, 3)
                prep_s3(ti)

            def scan_p1(ti, ci):
                par = (ti * NCH_T + ci) % 2
                return em.p1_stages(
                    h3q("silbuf", NQKV)[:, H:2 * H, ci * C:(ci + 1) * C],
                    h3q("silbuf", NQKV)[:, 0:H, ci * C:(ci + 1) * C],
                    h3q("silbuf", NQKV)[:, 2 * H:3 * H, ci * C:(ci + 1) * C],
                    uA["galog"][:, ci * C:(ci + 1) * C],
                    uA["lnb_r"][:, ci * C:(ci + 1) * C],
                    uA["balog"][:, ci * C:(ci + 1) * C],
                    par,
                    kq_keys=("silbuf", "silbuf", "silbuf"),
                    row_keys=("galog", "lnb_r", "balog"))

            def scan_p23_stages(ti, ci):
                if cfg.get("NOSCAN"):
                    return [], []
                par = (ti * NCH_T + ci) % 2
                tp = str(ti % 2)
                zs = "z_sil" + tp

                def og_cb(ti=ti, ci=ci):
                    tt = ti * TILE + ci * C
                    q, toff = tt // (L // 4), tt % (L // 4)
                    b.emit("scalar", lambda E, q=q, toff=toff: E.dma_start(
                        out=agin.ap()[q].rearrange("(h p) t -> p h t", p=128)
                        [:, :, toff:toff + C],
                        in_=uA["ogT"][:].rearrange("p (h c) -> p h c", h=H)),
                        reads=["ogT"], writes=[("agin_dram", (ti * NCH_T + ci) // 8)],
                        dma="og")
                ch = em.p23_chain(par)
                out = em.p23_out(
                    par, h3q(zs, NCH_T)[:, ci, :].rearrange("p (h c) -> p h c", h=H),
                    og_cb, zs)
                return ch, out

            NWIN = NT * NCH_T

            def fire_ag(q):
                b.emit("gpsimd", lambda E, q=q: E.collective_compute(
                    "AllGather", OP.bypass,
                    replica_groups=[[0, 1, 2, 3], [4, 5, 6, 7]],
                    ins=[agin.ap()[q]], outs=[agout.ap()[q]]),
                    reads=[("agin_dram", q)], writes=[("agout_dram", q)],
                    collective=True)

            if cfg.get("NOPREP"):
                def prep_all(ti):
                    pass
            prep_all(0)
            for w in range(NWIN + 1):
                ch, out, qs = [], [], []
                if w >= 1 and not cfg.get("NOSCAN"):
                    ch, out = scan_p23_stages((w - 1) // NCH_T, (w - 1) % NCH_T)
                    qs = em.p1b((w - 1) % 2)
                p1s = []
                if w < NWIN and not cfg.get("NOSCAN"):
                    p1s = scan_p1(w // NCH_T, w % NCH_T)
                if w < NWIN and (w // NCH_T) + 1 < NT and not cfg.get("NOPREP"):
                    prep_head((w // NCH_T) + 1, w % NCH_T)
                nop = lambda: None
                g = lambda lst, i: lst[i] if i < len(lst) else nop
                sched = [
                    g(ch, 0),
                    g(out, 0),
                    g(p1s, 0),
                    g(qs, 0),
                    g(ch, 1),
                    g(p1s, 1),
                    g(qs, 1),
                    g(p1s, 2),
                    g(qs, 2), g(qs, 3),
                    g(p1s, 3),
                    g(qs, 4),
                    g(p1s, 4),
                    g(qs, 5), g(qs, 6),
                    g(ch, 2),
                    g(p1s, 5),
                    g(ch, 3), g(ch, 4),
                    g(p1s, 6),
                    g(ch, 5),
                    g(ch, 6), g(ch, 7), g(ch, 8),
                    g(out, 1),
                    g(out, 2),
                    g(out, 3),
                ]
                for f in sched:
                    f()
                if w < NWIN:
                    cti, cci = w // NCH_T, w % NCH_T
                    if cti + 1 < NT and not cfg.get("NOPREP"):
                        prep_tail(cti + 1, cci)
                if not cfg.get("NOSCAN"):
                    if w == 24:
                        fire_ag(0)
                    elif w == 28:
                        fire_ag(1)
                    elif w == 30:
                        fire_ag(2)

            if not cfg.get("NOSCAN"):
                fire_ag(3)
            self.barrier()

        NVC2 = NVC // 2
        assert NVC % 2 == 0
        with ExitStack() as bctx:
            uB = {}
            for nm in ("PS0", "PS1", "PS2", "PS3", "PS4", "PS5"):
                uB[nm] = bctx.enter_context(nc.psum_tensor("b_" + nm, [128, 512], F32))

            with ExitStack() as b1ctx:
                u1 = dict(uB)
                sb1 = lambda n, s, dt=F32: u1.__setitem__(
                    n, b1ctx.enter_context(nc.sbuf_tensor("B1" + n, s, dt)))
                self.const_into(u1, b1ctx, "ident_f", np.eye(128, dtype=np.float32))
                self.const_into(u1, b1ctx, "identE", np.eye(NE, dtype=np.float32))
                self.const_into(u1, b1ctx, "c_eps5",
                                np.full((128, 1), EPS_LN, np.float32))
                sb1("x1", [128, NTB, D])
                sb1("ln2xT", [128, ND, TB], BF16)
                sb1("woutb0", [128, NVC2, D], BF16)
                sb1("woutb1", [128, NVC2, D], BF16)
                sb1("ogTg0", [128, NVC2, THB], BF16)
                sb1("sq_ln", [128, D])
                sb1("mu0", [128, 2]); sb1("mu1", [128, 2])
                sb1("rstd0", [128, 1]); sb1("rstd1", [128, 1])
                sb1("xn0", [128, D]); sb1("xn1", [128, D])
                sb1("normw_c", [128, 1])
                sb1("routw_s", [NE, D])
                sb1("routT", [128, ND * NE], BF16)
                sb1("wts", [128, NTB, NE])
                sb1("comb", [128, NTB, NE])
                sb1("ctb", [NTB * NE, 128])
                sb1("m12", [128, 2])
                sb1("gpk", [1, 8])

                b.emit("sync", lambda E: E.dma_start(out=u1["normw_c"][:], in_=normw),
                       writes=["normw_c"], dma="xload")
                b.emit("sync", lambda E: E.dma_start(
                    out=u1["x1"][:],
                    in_=x_res.rearrange("(g p) d -> p g d", p=128)),
                    writes=["x1"], dma="xload")
                b.emit("sync", lambda E: E.dma_start(out=u1["routw_s"][:], in_=routerw),
                       writes=["routw_s"], dma="xload")
                assert ND * NE <= 512
                for dc in range(ND):
                    b.emit("tensor", lambda t, dc=dc: t.matmul(
                        uB["PS0"][:, dc * NE:(dc + 1) * NE],
                        u1["routw_s"][:, dc * 128:(dc + 1) * 128], u1["identE"][:],
                        start=(dc == 0), stop=(dc == ND - 1), is_transpose=True),
                        reads=["routw_s", "identE", "PS0"], writes=["PS0"])
                b.emit("scalar", lambda s: s.copy(u1["routT"][:], uB["PS0"][:, 0:ND * NE]),
                       reads=["PS0"], writes=["routT"])
                pid_cell = {}
                def rowbase_of(E, vh):
                    if vh not in pid_cell:
                        if "q" not in pid_cell:
                            pid_cell["q"] = (E.partition_id() % 4) * VD
                        pid_cell[vh] = pid_cell["q"] + vh * (NVC2 * 128)
                    return pid_cell[vh]
                HT = NTB // 2
                for hf in range(NSPL):
                    for vh in range(2):
                        wb = "woutb" + str(vh)
                        og = "ogTg0"
                        if hf == 0:
                            for vc in range(NVC2):
                                gvc = vh * NVC2 + vc
                                b.emit("gpsimd", lambda E, vc=vc, gvc=gvc, wb=wb:
                                       E.dma_start(
                                    out=u1[wb][:][:, vc, :],
                                    in_=wout[gvc * 128:(gvc + 1) * 128, :]),
                                    reads=[(wb, vc)], writes=[(wb, vc)],
                                    dma="wout" + str(vh))
                                b.emit("vector", lambda v, vc=vc, wb=wb:
                                       v.tensor_scalar_mul(
                                    u1[wb][:][:, vc, :], u1[wb][:][:, vc, :],
                                    u1["normw_c"][:]),
                                    reads=[(wb, vc), "normw_c"], writes=[(wb, vc)])
                        b.emit("sync", lambda E, hf=hf, vh=vh, og=og: E.dma_start(
                            out=u1[og][:],
                            in_=agout.ap().rearrange("q v t -> (q v) t")
                            [bass.ds(rowbase_of(E, vh), NVC2 * 128),
                             hf * THB:(hf + 1) * THB]
                            .rearrange("(vc p) t -> p vc t", p=128)),
                            reads=[("agout_dram", 0), ("agout_dram", 1),
                                   ("agout_dram", 2), ("agout_dram", 3), og],
                            writes=[og], dma="xload")
                        for tg in range(THB // 128):
                            gtg = hf * (THB // 128) + tg
                            for db in range(NHOB):
                                ps = ("PS0", "PS1", "PS2", "PS3")[db % 4]
                                for vc in range(NVC2):
                                    b.emit("tensor", lambda t, tg=tg, db=db, vc=vc,
                                           ps=ps, wb=wb, og=og: t.matmul(
                                        uB[ps][:],
                                        u1[og][:][:, vc, tg * 128:(tg + 1) * 128],
                                        u1[wb][:][:, vc, db * 512:(db + 1) * 512],
                                        start=(vc == 0), stop=(vc == NVC2 - 1)),
                                        reads=[og] + [(wb, vc) for vc in range(NVC2)]
                                        + [ps], writes=[ps])
                                b.emit("vector", lambda v, gtg=gtg, db=db, ps=ps:
                                       v.tensor_add(
                                    u1["x1"][:][:, gtg, db * 512:(db + 1) * 512],
                                    u1["x1"][:][:, gtg, db * 512:(db + 1) * 512],
                                    uB[ps][:]),
                                    reads=[ps, "x1"], writes=["x1"])
                    for gt0 in range(hf * HT, (hf + 1) * HT, 2):
                        self.emit_ln_pair(
                            u1, [(u1["x1"][:][:, g, :], "x1", g, str(g % 2))
                                 for g in range(gt0, min(gt0 + 2, NTB))],
                            "ln2xT", ND, D, psname="PS4", add_norm_to_src=True)
                    clo, chi = hf * HT * 128, (hf + 1) * HT * 128
                    b.emit("gpsimd", lambda E, hf=hf, clo=clo, chi=chi: E.dma_start(
                        out=agin2.ap()[hf].rearrange("(dc p) t -> p dc t", p=128),
                        in_=u1["ln2xT"][:][:, :, clo:chi]),
                        reads=["ln2xT"], writes=[("agin2_dram", hf)], dma="bout")
                    b.emit("gpsimd", lambda E, hf=hf: E.collective_compute(
                        "AllGather", OP.bypass,
                        replica_groups=[list(range(8))],
                        ins=[agin2.ap()[hf]], outs=[agout2.ap()[hf]]),
                        reads=[("agin2_dram", hf)], writes=[("agout2_dram", hf)],
                        collective=True)
                b.emit("gpsimd", lambda E: E.dma_start(
                    out=x1_dram.ap().rearrange("(g p) d -> p g d", p=128),
                    in_=u1["x1"][:]), reads=["x1"], writes=["x1_dram"], dma="bout")
                ln2xT3 = u1["ln2xT"][:]
                for gtg in range(NTB):
                    for dc in range(ND):
                        b.emit("tensor", lambda t, gtg=gtg, dc=dc: t.matmul(
                            uB["PS4"][:, 0:NE], ln2xT3[:, dc, gtg * 128:(gtg + 1) * 128],
                            u1["routT"][:, dc * NE:(dc + 1) * NE],
                            start=(dc == 0), stop=(dc == ND - 1)),
                            reads=["ln2xT", "routT", "PS4"], writes=["PS4"])
                    w3 = u1["wts"][:][:, gtg, :]
                    c3 = u1["comb"][:][:, gtg, :]
                    b.emit("vector", lambda v: v.reduce_max(
                        u1["m12"][:, 0:1], uB["PS4"][:, 0:NE], AX.X),
                        reads=["PS4"], writes=["m12"])
                    b.emit("vector", lambda v, w3=w3: v.tensor_scalar_sub(
                        w3, uB["PS4"][:, 0:NE], u1["m12"][:, 0:1]),
                        reads=["PS4", "m12"], writes=["wts"])
                    b.emit("scalar", lambda s, w3=w3: s.activation(w3, w3, AF.Exp),
                        reads=["wts"], writes=["wts"])
                    b.emit("vector", lambda v, w3=w3: v.reduce_sum(
                        u1["m12"][:, 0:1], w3, AX.X), reads=["wts"], writes=["m12"])
                    b.emit("vector", lambda v: v.reciprocal(
                        u1["m12"][:, 1:2], u1["m12"][:, 0:1]), reads=["m12"], writes=["m12"])
                    b.emit("vector", lambda v, w3=w3: v.tensor_scalar_mul(
                        w3, w3, u1["m12"][:, 1:2]), reads=["m12", "wts"], writes=["wts"])
                    b.emit("vector", lambda v, w3=w3: v.reduce_max(
                        u1["m12"][:, 0:1], w3, AX.X), reads=["wts"], writes=["m12"])
                    b.emit("vector", lambda v, w3=w3, c3=c3: v.tensor_scalar(
                        c3, w3, u1["m12"][:, 0:1], -1e30, OP.is_ge, OP.mult),
                        reads=["wts", "m12"], writes=["comb"])
                    b.emit("vector", lambda v, w3=w3, c3=c3: v.tensor_add(
                        c3, c3, w3), reads=["comb", "wts"], writes=["comb"])
                    b.emit("vector", lambda v, c3=c3: v.reduce_max(
                        u1["m12"][:, 1:2], c3, AX.X), reads=["comb"], writes=["m12"])
                    b.emit("vector", lambda v, w3=w3, c3=c3: v.tensor_scalar(
                        c3, w3, u1["m12"][:, 1:2], None, OP.is_ge),
                        reads=["wts", "m12"], writes=["comb"])
                    b.emit("vector", lambda v, w3=w3, c3=c3: v.tensor_mul(
                        c3, c3, w3), reads=["comb", "wts"], writes=["comb"])
                if cfg.get("DBG"):
                    comb_dbg = nc.dram_tensor("comb_dbg", [128, NTB, NE], F32,
                                              kind="ExternalOutput").ap()
                    b.emit("gpsimd", lambda E: E.dma_start(
                        out=comb_dbg, in_=u1["comb"][:]),
                        reads=["comb"], writes=["combdbg"], dma=True)
                b.emit("tensor", lambda t: t.matmul(
                    uB["PS4"][0:NTB * NE, 0:128],
                    u1["comb"][:].rearrange("p a b -> p (a b)"), u1["ident_f"][:],
                    start=True, stop=True, is_transpose=True),
                    reads=["comb", "ident_f", "PS4"], writes=["PS4"])
                b.emit("vector", lambda v: v.tensor_copy(
                    u1["ctb"][:], uB["PS4"][0:NTB * NE, 0:128]),
                    reads=["PS4"], writes=["ctb"])
                for g in range(NTB):
                    b.emit("gpsimd", lambda E, g=g: E.dma_start(
                        out=ag3in.ap().rearrange("(e g) c -> e g c", g=NTB)[:, g, :],
                        in_=u1["ctb"][g * NE:(g + 1) * NE, :]),
                        reads=["ctb"], writes=["ag3in_dram"], dma="bout")
                b.emit("gpsimd", lambda E: E.collective_compute(
                    "AllGather", OP.bypass,
                    replica_groups=[list(range(8))],
                    ins=[ag3in.ap()], outs=[ag3out.ap()]),
                    reads=["ag3in_dram"], writes=["ag3out_dram"], collective=True)
                b.emit("gpsimd", lambda E: E.memset(u1["gpk"][:], 0.0),
                       writes=["gpk"])
                self.barrier()

            with ExitStack() as b4ctx:
                u4 = dict(uB)
                sb4 = lambda n, s, dt=F32: u4.__setitem__(
                    n, b4ctx.enter_context(nc.sbuf_tensor("B4" + n, s, dt)))
                sb4("guTg", [128, ND, IFFP], BF16)
                sb4("guTu", [128, ND, IFFP], BF16)
                sb4("wdT", [128, NIC, D], BF16)
                sb4("strm0", [128, ND, TBL], BF16)
                sb4("strm1", [128, ND, TBL], BF16)
                sb4("hbuf", [128, NIC, TBL], BF16)
                sb4("sgs", [128, TBL])
                sb4("cbc", [128, TBL])
                sb4("rsst0", [128, 512], BF16)
                sb4("rsst1", [128, 512], BF16)
                if IFFP > IFFT:
                    for wnm in ("guTg", "guTu"):
                        b.emit("vector", lambda v, wnm=wnm: v.memset(
                            u4[wnm][:][:, :, IFFT:IFFP], 0.0),
                            writes=[wnm + "_s"])
                wl_seq = [0]

                def emit_wld(src_ap, dst_ap, dkey):
                    i = wl_seq[0] % 8
                    wl_seq[0] += 1
                    b.emit("gpsimd", lambda E, src_ap=src_ap, dst_ap=dst_ap:
                        E.dma_start(out=dst_ap, in_=src_ap),
                        writes=[dkey], dma="wld" + str(i))

                for dc in range(ND):
                    for src, wnm in ((ewgT, "guTg"), (ewuT, "guTu")):
                        emit_wld(src[dc * 128:(dc + 1) * 128, 0:IFFT],
                                 u4[wnm][:][:, dc, 0:IFFT], wnm + "_s")
                for ic in range(NIC):
                    for hh in range(2):
                        emit_wld(
                            ewdT[ic * 128:(ic + 1) * 128,
                                 hh * (D // 2):(hh + 1) * (D // 2)],
                            u4["wdT"][:][:, ic, hh * (D // 2):(hh + 1) * (D // 2)],
                            "wdT_s")
                pid4 = {}
                def rreg(E, r):
                    if "p" not in pid4:
                        pid4["p"] = E.partition_id() % 8
                    if r not in pid4:
                        pid4[r] = pid4["p"] + r * NE
                    return pid4[r]
                ev_seq = [0]
                for tb in range(NTBL):
                    par = tb % 2
                    strm = u4["strm" + str(par)]
                    hf, r = tb // 8, tb % 8
                    b.emit("sync", lambda E, r=r, hf=hf, strm=strm: E.dma_start(
                        out=strm[:],
                        in_=agout2.ap()[hf][r * D:(r + 1) * D, :]
                        .rearrange("(dc p) t -> p dc t", p=128)),
                        reads=[("agout2_dram", hf), ("strm", par)],
                        writes=[("strm", par)], dma="str")
                    b.emit("scalar", lambda E, r=r, hf=hf: E.dma_start(
                        out=u4["cbc"][:].rearrange("p (g c) -> p g c", g=NGB),
                        in_=ag3out.ap().rearrange("(q g) c -> q g c", g=NTB)
                        [bass.ds(rreg(E, r), 1), hf * NGB:hf * NGB + NGB, :]
                        .broadcast_to([128, NGB, 128])),
                        reads=["ag3out_dram", "cbc"], writes=["cbc"], dma="cmb")
                    for ic in range(NIC):
                        for wnm, psn in (("guTg", "PS0"), ("guTu", "PS1")):
                            for dc in range(ND):
                                b.emit("tensor", lambda t, wnm=wnm, psn=psn, dc=dc,
                                       ic=ic, strm=strm: t.matmul(
                                    uB[psn][:, 0:TBL],
                                    u4[wnm][:][:, dc, ic * 128:(ic + 1) * 128],
                                    strm[:][:, dc, :],
                                    start=(dc == 0), stop=(dc == ND - 1)),
                                    reads=[wnm + "_s", ("strm", par), psn],
                                    writes=[psn])
                        b.emit("scalar", lambda s: s.activation(
                            u4["sgs"][:], uB["PS0"][:, 0:TBL], AF.Sigmoid),
                            reads=["PS0", "sgs"], writes=["sgs"])
                        if ic * 128 < IFF:
                            b.emit("vector", lambda v: v.tensor_mul(
                                u4["sgs"][:], u4["sgs"][:], u4["cbc"][:]),
                                reads=["sgs", "cbc"], writes=["sgs"])
                        b.emit("vector", lambda v, ic=ic: v.tensor_mul(
                            u4["hbuf"][:][:, ic, :], u4["sgs"][:], uB["PS1"][:, 0:TBL]),
                            reads=["sgs", "PS1", ("hbuf", ic)], writes=[("hbuf", ic)])
                    for tg in range(NGB):
                        gt = (r * NBR + hf) * NGB + tg
                        for hob in range(NHOB):
                            ps = ("PS2", "PS3", "PS4", "PS5")[(tg * NHOB + hob) % 4]
                            for ic in range(NIC):
                                b.emit("tensor", lambda t, tg=tg, hob=hob, ic=ic,
                                       ps=ps: t.matmul(
                                    uB[ps][:],
                                    u4["hbuf"][:][:, ic, tg * 128:(tg + 1) * 128],
                                    u4["wdT"][:][:, ic, hob * 512:(hob + 1) * 512],
                                    start=(ic == 0), stop=(ic == NIC - 1)),
                                    reads=[("hbuf", ic), "wdT_s", ps], writes=[ps])
                            par2 = ev_seq[0] % 2
                            ev_seq[0] += 1
                            rsst = u4["rsst" + str(par2)]
                            b.emit("vector", lambda v, ps=ps, rsst=rsst: v.tensor_copy(
                                rsst[:], uB[ps][:]),
                                reads=[ps, ("rsst", par2)], writes=[("rsst", par2)])
                            gr = (r * NGB + tg) * 128
                            b.emit("gpsimd", lambda E, gr=gr, hob=hob, hf=hf,
                                   rsst=rsst: E.dma_start(
                                out=rs_in[hf].ap()[gr:gr + 128,
                                                   hob * 512:(hob + 1) * 512],
                                in_=rsst[:]),
                                reads=[("rsst", par2)], writes=["rs_dram"], dma="rso")
                    if r == 7:
                        b.emit("gpsimd", lambda E, hf=hf: E.collective_compute(
                            "ReduceScatter", OP.add,
                            replica_groups=[list(range(8))],
                            ins=[rs_in[hf].ap()], outs=[rs_out[hf].ap()]),
                            reads=["rs_dram"], writes=[("rsout", hf)],
                            collective=True)
                sb4("gpk2", [1, 8])
                b.emit("gpsimd", lambda E: E.memset(u4["gpk2"][:], 0.0),
                       writes=["gpk2"])
                self.barrier()

            with ExitStack() as b5ctx:
                u5 = dict(uB)
                sb5 = lambda n, s, dt=F32: u5.__setitem__(
                    n, b5ctx.enter_context(nc.sbuf_tensor("B5" + n, s, dt)))
                sb5("x1t", [128, NTB, D])
                sb5("rst", [128, NTB, D], BF16)
                b.emit("sync", lambda E: E.dma_start(
                    out=u5["x1t"][:],
                    in_=x1_dram.ap().rearrange("(g p) d -> p g d", p=128)),
                    reads=["x1_dram"], writes=["x1t"], dma="xload")
                gpc = NTB // NCHK
                st = None
                for k in range(NCHK):
                    b.emit("sync", lambda E, k=k: E.dma_start(
                        out=u5["rst"][:][:, k * gpc:(k + 1) * gpc, :],
                        in_=rs_out[k].ap().rearrange("(g p) d -> p g d", p=128)),
                        reads=[("rsout", k)], writes=[("rst", k)], dma="xload")
                    for g in range(k * gpc, (k + 1) * gpc):
                        b.emit("vector", lambda v, g=g: v.tensor_add(
                            u5["x1t"][:][:, g, :], u5["x1t"][:][:, g, :],
                            u5["rst"][:][:, g, :]),
                            reads=["x1t", ("rst", k)], writes=["x1t", ("yadd", k)])
                    st = b.emit("sync", lambda E, k=k: E.dma_start(
                        out=y_out[k * (TB // NCHK):(k + 1) * (TB // NCHK), :]
                        .rearrange("(g p) d -> p g d", p=128),
                        in_=u5["x1t"][:][:, k * gpc:(k + 1) * gpc, :]),
                        reads=[("yadd", k)], dma="yout")
                b._wait("sync", st)
                b._wait("gpsimd", st)
        return b

    def emit_ln_group(self, u, xg_ap, src_key, g, dst_name, ND, D, psname="PS6",
                      add_norm_to_src=False, par=""):
        self.emit_ln_pair(u, [(xg_ap, src_key, g, par)], dst_name, ND, D,
                          psname, add_norm_to_src)

    def emit_ln_pair(self, u, jobs, dst_name, ND, D, psname="PS6",
                     add_norm_to_src=False, ident="ident_f"):
        b = self.b
        J = [(xg_ap, sk, g, "mu" + par, "rstd" + par, "xn" + par)
             for (xg_ap, sk, g, par) in jobs]
        for xg_ap, sk, g, mu, rstd, xn in J:
            b.emit("vector", lambda v, xg_ap=xg_ap, mu=mu: v.reduce_sum(
                u[mu][:, 0:1], xg_ap, AX.X), reads=[sk], writes=[mu])
        for xg_ap, sk, g, mu, rstd, xn in J:
            b.emit("scalar", lambda s, xg_ap=xg_ap, mu=mu: s.activation(
                u["sq_ln"][:], xg_ap, AF.Square, accum_out=u[mu][:, 1:2]),
                reads=[sk, mu, "sq_ln"], writes=["sq_ln", mu])
        for xg_ap, sk, g, mu, rstd, xn in J:
            b.emit("vector", lambda v, mu=mu: v.tensor_scalar_mul(
                u[mu][:], u[mu][:], 1.0 / D), reads=[mu], writes=[mu])
        for xg_ap, sk, g, mu, rstd, xn in J:
            b.emit("vector", lambda v, mu=mu, rstd=rstd: v.tensor_tensor(
                u[rstd][:], u[mu][:, 0:1], u[mu][:, 0:1], OP.mult),
                reads=[mu], writes=[rstd])
        for xg_ap, sk, g, mu, rstd, xn in J:
            b.emit("vector", lambda v, mu=mu, rstd=rstd: v.tensor_sub(
                u[rstd][:], u[mu][:, 1:2], u[rstd][:]),
                reads=[mu, rstd], writes=[rstd])
        for xg_ap, sk, g, mu, rstd, xn in J:
            b.emit("scalar", lambda s, rstd=rstd: s.activation(
                u[rstd][:], u[rstd][:], AF.Ln, bias=u["c_eps5"][:]),
                reads=[rstd, "c_eps5"], writes=[rstd])
        for xg_ap, sk, g, mu, rstd, xn in J:
            b.emit("scalar", lambda s, rstd=rstd: s.activation(
                u[rstd][:], u[rstd][:], AF.Exp, scale=-0.5),
                reads=[rstd], writes=[rstd])
        for xg_ap, sk, g, mu, rstd, xn in J:
            b.emit("vector", lambda v, xg_ap=xg_ap, mu=mu, rstd=rstd, xn=xn:
                   v.tensor_scalar(
                u[xn][:], xg_ap, u[mu][:, 0:1], u[rstd][:],
                OP.subtract, OP.mult),
                reads=[sk, mu, rstd, xn], writes=[xn])
        if add_norm_to_src:
            for xg_ap, sk, g, mu, rstd, xn in J:
                b.emit("vector", lambda v, xg_ap=xg_ap, xn=xn: v.tensor_add(
                    xg_ap, xg_ap, u[xn][:]), reads=[sk, xn], writes=[sk])
        for xg_ap, sk, g, mu, rstd, xn in J:
            for dc in range(ND):
                b.emit("tensor", lambda t, dc=dc, xn=xn: t.matmul(
                    u[psname][:, (dc % 4) * 128:(dc % 4) * 128 + 128],
                    u[xn][:, dc * 128:(dc + 1) * 128], u[ident][:],
                    start=(dc % 4 == 0), stop=(dc % 4 == 3 or dc == ND - 1),
                    is_transpose=True),
                    reads=[xn, ident, psname], writes=[psname])
                if dc % 4 == 3 or dc == ND - 1:
                    lo = (dc // 4) * 4
                    n_in = dc - lo + 1
                    b.emit("scalar", lambda s, lo=lo, n_in=n_in, g=g: s.copy(
                        u[dst_name][:][:, lo:lo + n_in, g * 128:(g + 1) * 128],
                        u[psname][:, 0:n_in * 128]
                        .rearrange("p (dc t) -> p dc t", dc=n_in)),
                        reads=[psname], writes=[dst_name])



def make_core_inputs(cfg, c, x, Wqkv, Wz, Wb, Wa, conv_w, A_log, dt_bias, norm_w,
                     Wout, router_w, shg, shu, shd, wg, wu, wd):
    D, L, H, HG, TB = cfg["D"], cfg["L"], cfg["H"], cfg["HG"], cfg["TB"]
    KEY = HG * 128
    bc, hg = c // 4, c % 4
    h0 = hg * H
    cw = H * 128
    qcols = Wqkv[:, h0 * 128:h0 * 128 + cw]
    kcols = Wqkv[:, KEY + h0 * 128:KEY + h0 * 128 + cw]
    vcols = Wqkv[:, 2 * KEY + h0 * 128:2 * KEY + h0 * 128 + cw]
    zcols = Wz[:, h0 * 128:h0 * 128 + cw]
    wqkvz = np.ascontiguousarray(np.concatenate([qcols, kcols, vcols, zcols], axis=1))
    wba = np.ascontiguousarray(np.concatenate(
        [Wb[:, h0:h0 + H], Wa[:, h0:h0 + H]], axis=1))
    convw = np.ascontiguousarray(np.concatenate(
        [conv_w[h0 * 128:h0 * 128 + cw, 0, :],
         conv_w[KEY + h0 * 128:KEY + h0 * 128 + cw, 0, :],
         conv_w[2 * KEY + h0 * 128:2 * KEY + h0 * 128 + cw, 0, :]], axis=0))
    flat = x.reshape(-1, D)
    IFF = cfg["IFF"]
    SH = IFF // 8
    IFFT = IFF + SH
    NIC = (IFFT + 127) // 128
    IFFP = NIC * 128
    ewgT = np.concatenate([wg[c].T, shg[c * SH:(c + 1) * SH].T], axis=1)
    ewuT = np.concatenate([wu[c].T, shu[c * SH:(c + 1) * SH].T], axis=1)
    ewdT = np.zeros((IFFP, D), np.float32)
    ewdT[:IFF] = wd[c].T
    ewdT[IFF:IFFT] = shd[:, c * SH:(c + 1) * SH].T
    return {
        "x_attn": np.ascontiguousarray(x[bc]),
        "x_res": np.ascontiguousarray(flat[c * TB:(c + 1) * TB]),
        "wqkvz": wqkvz,
        "wba": wba,
        "convw": convw,
        "alog4": np.ascontiguousarray(A_log[h0:h0 + H].reshape(H, 1)),
        "dtb4": np.ascontiguousarray(dt_bias[h0:h0 + H].reshape(H, 1)),
        "normw": np.ascontiguousarray(norm_w.reshape(128, 1)),
        "wout": np.ascontiguousarray(Wout),
        "routerw": np.ascontiguousarray(router_w),
        "ewgT": np.ascontiguousarray(ewgT),
        "ewuT": np.ascontiguousarray(ewuT),
        "ewdT": ewdT,
    }


def build_program(cfg):
    nc = bass.Bass("TRN2", target_bir_lowering=False)
    prog = CoreProgram(nc, cfg)
    with ExitStack() as sems_ctx:
        with nc.Block() as block:
            b = prog.build()
            b.finalize()
            b.sem_handles = {s: sems_ctx.enter_context(nc.semaphore(s))
                             for s in sorted(b.used_sems)}
            block.sync(lambda E: b.replay_engine("sync", E))
            block.tensor(lambda E: b.replay_engine("tensor", E))
            block.vector(lambda E: b.replay_engine("vector", E))
            block.scalar(lambda E: b.replay_engine("scalar", E))
            block.gpsimd(lambda E: b.replay_engine("gpsimd", E))
    return nc


def kernel(L0_attn_x, L0_attn_Wqkv, L0_attn_Wz, L0_attn_Wb, L0_attn_Wa,
           L0_attn_conv_w, L0_attn_A_log, L0_attn_dt_bias, L0_attn_norm_w,
           L0_attn_Wout, L0_ffn_router_w, L0_ffn_shared_gate, L0_ffn_shared_up,
           L0_ffn_shared_down, L0_ffn_w_gate, L0_ffn_w_up, L0_ffn_w_down):
    cfg = default_cfg()
    args = [np.asarray(a) for a in
            (L0_attn_x, L0_attn_Wqkv, L0_attn_Wz, L0_attn_Wb, L0_attn_Wa,
             L0_attn_conv_w, L0_attn_A_log, L0_attn_dt_bias, L0_attn_norm_w,
             L0_attn_Wout, L0_ffn_router_w, L0_ffn_shared_gate, L0_ffn_shared_up,
             L0_ffn_shared_down, L0_ffn_w_gate, L0_ffn_w_up, L0_ffn_w_down)]
    in_maps = [make_core_inputs(cfg, c, *args) for c in range(8)]
    nc = build_program(cfg)
    res = run_bass_kernel_spmd(nc, in_maps, core_ids=list(range(8)))
    y = np.concatenate([np.asarray(r["y"]) for r in res.results], axis=0)
    return y.reshape(L0_attn_x.shape)


# version 42
# speedup vs baseline: 1.1344x; 1.0031x over previous
import math
from contextlib import ExitStack

import numpy as np
import ml_dtypes
import concourse.bass as bass
import concourse.mybir as mybir
from concourse.bass_utils import run_bass_kernel_spmd

F32 = mybir.dt.float32
BF16 = mybir.dt.bfloat16
BF = ml_dtypes.bfloat16
AF = mybir.ActivationFunctionType
OP = mybir.AluOpType
AX = mybir.AxisListType

ENGINES = ("sync", "tensor", "vector", "scalar", "gpsimd")
ESEMS = {
    "sync": (None, "d_sync"),
    "tensor": ("c_pe", None),
    "vector": ("c_dve", None),
    "scalar": ("c_act", "d_act"),
    "gpsimd": ("c_gp", "d_gp"),
}
BASE_SEM_NAMES = ["c_pe", "c_dve", "c_act", "c_gp", "c_cc"]

EPS_L2 = 1e-6
EPS_RMS = 1e-6
EPS_LN = 1e-5


class Ev:
    __slots__ = ("fn", "sem", "amt", "num", "eng", "idx")

    def __init__(self, eng, fn, idx):
        self.eng = eng
        self.fn = fn
        self.sem = None
        self.amt = 0
        self.num = None
        self.idx = idx


class Builder:

    def __init__(self, nc):
        self.nc = nc
        self.events = {e: [] for e in ENGINES}
        self.res = {}
        self.sem_handles = {}
        self.n_waits = 0
        self.n_events = 0
        self.used_sems = set(BASE_SEM_NAMES)
        self.last_dma = {}
        self.maxw = {}
        self.eidx = dict.fromkeys(ENGINES, 0)

    def _rec(self, key):
        r = self.res.get(key)
        if r is None:
            r = {"w": None, "r": []}
            self.res[key] = r
        return r

    def _wait(self, eng, d):
        psem = d.sem if d.sem is not None else ESEMS[d.eng][0]
        if self.maxw.get((eng, psem), -1) >= d.idx:
            return
        self._mark(d)
        self.events[eng].append(("w", d))
        self.maxw[(eng, psem)] = d.idx
        self.n_waits += 1

    def dma_sem(self, eng, tag):
        nm = f"d_{eng}_{tag}"
        self.used_sems.add(nm)
        return nm

    def emit(self, eng, fn, reads=(), writes=(), dma=False, collective=False):
        ev = Ev(eng, fn, self.eidx[eng])
        self.eidx[eng] += 1
        deps = []
        for k in reads:
            r = self._rec(k)
            if r["w"] is not None:
                deps.append(r["w"])
        for k in writes:
            r = self._rec(k)
            if r["w"] is not None:
                deps.append(r["w"])
            deps.extend(r["r"])
        seen = set()
        for d in deps:
            if id(d) in seen:
                continue
            seen.add(id(d))
            if d.eng == eng == "tensor":
                continue
            if d.amt == 16:
                d2 = self.last_dma[d.sem]
                if d2 is not d:
                    if id(d2) in seen:
                        continue
                    seen.add(id(d2))
                    d = d2
                if d.eng != eng:
                    self._wait(d.eng, d)
            self._wait(eng, d)
        if dma:
            tag = dma if isinstance(dma, str) else "main"
            ev.sem = self.dma_sem(eng, tag)
            ev.amt = 16
            self.last_dma[ev.sem] = ev
        elif collective:
            ev.sem = "c_cc"
            ev.amt = 1
        self.events[eng].append(("e", ev))
        self.n_events += 1
        for k in writes:
            r = self._rec(k)
            r["w"] = ev
            r["r"] = []
        for k in reads:
            self._rec(k)["r"].append(ev)
        return ev

    def _mark(self, ev):
        if ev.sem is None:
            ev.sem = ESEMS[ev.eng][0]
            assert ev.sem is not None
            ev.amt = 1

    def finalize(self):
        totals = dict.fromkeys(self.used_sems, 0)
        for e in ENGINES:
            for kind, item in self.events[e]:
                if kind == "e" and item.sem is not None:
                    totals[item.sem] += item.amt
                    item.num = totals[item.sem]
        return totals

    def replay_engine(self, name, E):
        for kind, item in self.events[name]:
            if kind == "w":
                assert item.num is not None, "wait on unfinalized event"
                E.wait_ge(self.sem_handles[item.sem], item.num)
            else:
                ins = item.fn(E)
                if item.sem is not None:
                    assert ins is not None, "milestone event produced no instruction"
                    ins.then_inc(self.sem_handles[item.sem], item.amt)


class ScanEmitter:

    def __init__(self, b: Builder, bufs: dict, H: int, C: int, NLEV: int = 7):
        self.b = b
        self.u = bufs
        self.H = H
        self.C = C
        self.NLEV = NLEV

    def h3(self, ap):
        return ap.rearrange("p (h c) -> p h c", h=self.H)

    def colb(self, ap_col):
        return ap_col.unsqueeze(2).broadcast_to([ap_col.shape[0], self.H, self.C])

    def maskb(self, ap_mask):
        return ap_mask.unsqueeze(1).broadcast_to([self.C, self.H, self.C])

    def p1_stages(self, khat, qhat, vhat, g_row, lnb_row, beta_row, par,
                  kq_keys=("kqv_in", "kqv_in", "kqv_in"),
                  row_keys=("rows_in", "rows_in", "rows_in")):
        b, u, H, C = self.b, self.u, self.H, self.C
        kk, qk, vk = kq_keys
        gk, lbk, bk_ = row_keys
        P = str(par)
        slab, slab2 = u["slab" + P], u["slab2" + P]
        kcb, gam, bkap = u["kcb" + P], u["gam" + P], u["bkap" + P]
        PST, pstk = u["PST" + P], "PST" + P
        bcol = slab[:, 0:H]
        betacol = slab[:, H:2 * H]
        brow3 = self.h3(u["brow"][:])
        browb3 = self.h3(u["browb"][:])
        knT3 = self.h3(u["knT" + P][:])
        qnT3 = self.h3(u["qnT" + P][:])

        def s0():
            b.emit("vector", lambda v, g=g_row: v.tensor_tensor_scan(
                u["b_scr"][:], g, u["zero_row"][:].broadcast_to([H, C]), 0.0, OP.add, OP.add),
                reads=[gk, "zero_row"], writes=["b_scr"])
            b.emit("vector", lambda v, l=lnb_row: v.tensor_add(u["blnb_scr"][:], u["b_scr"][:], l),
                reads=["b_scr", lbk], writes=["blnb_scr"])

            b.emit("tensor", lambda t: t.matmul(
                u["PSG"][:, 0:H], u["b_scr"][:], u["identH"][:], start=True, stop=False,
                is_transpose=True),
                reads=["b_scr", "identH", "PSG"], writes=["PSG"])
            b.emit("tensor", lambda t, br=beta_row: t.matmul(
                u["PSG"][:, H:2 * H], br, u["identH"][:], start=False, stop=True,
                is_transpose=True),
                reads=[bk_, "identH", "PSG"], writes=["PSG"])
            b.emit("vector", lambda v, slab=slab: v.tensor_copy(slab[:], u["PSG"][:, 0:2 * H]),
                reads=["PSG"], writes=["slab" + P])

            b.emit("scalar", lambda s, slab2=slab2, bcol=bcol: s.activation(
                slab2[:, H:2 * H], bcol, AF.Exp),
                reads=["slab" + P], writes=["slab2" + P])
            b.emit("tensor", lambda t: t.matmul(
                u["PSG"][0:1, 2 * H:3 * H], u["b_scr"][:, C - 1:C], u["identH"][:],
                start=True, stop=True, is_transpose=True),
                reads=["b_scr", "identH", "PSG"], writes=["PSG"])
            b.emit("vector", lambda v: v.tensor_copy(u["bc_row"][0:1, :], u["PSG"][0:1, 2 * H:3 * H]),
                reads=["PSG"], writes=["bc_row"])
            b.emit("tensor", lambda t: t.matmul(
                u["PSG"][:, 3 * H:4 * H], u["ones1"][0:1, :], u["bc_row"][0:1, :],
                start=True, stop=True),
                reads=["bc_row", "ones1", "PSG"], writes=["PSG"])
            b.emit("scalar", lambda s, kcb=kcb: s.activation(kcb[:], u["PSG"][:, 3 * H:4 * H], AF.Exp),
                reads=["PSG"], writes=["kcb" + P])
            b.emit("vector", lambda v, slab2=slab2, bcol=bcol: v.tensor_tensor(
                slab2[:, 0:H], u["PSG"][:, 3 * H:4 * H], bcol, OP.subtract),
                reads=["PSG", "slab" + P], writes=["slab2" + P])
            b.emit("scalar", lambda s, gam=gam, slab2=slab2: s.activation(
                gam[:], slab2[:, 0:H], AF.Exp),
                reads=["slab2" + P], writes=["gam" + P])
            b.emit("vector", lambda v, bkap=bkap, betacol=betacol, slab2=slab2: v.tensor_mul(
                bkap[:], betacol, slab2[:, H:2 * H]),
                reads=["slab" + P, "slab2" + P], writes=["bkap" + P])


        def s1():
            for nm, srcb, srck in (("brow", u["b_scr"], "b_scr"),
                                   ("browb", u["blnb_scr"], "blnb_scr")):
                for h in range(H):
                    b.emit("tensor", lambda t, h=h, srcb=srcb: t.matmul(
                        u["PSG"][:, h * C:(h + 1) * C], u["sel" + str(h)][:], srcb[:],
                        start=(h == 0), stop=(h == H - 1)),
                        reads=[srck, "sel" + str(h), "PSG"], writes=["PSG"])
                b.emit("vector", lambda v, nm=nm: v.tensor_copy(u[nm][:], u["PSG"][:, 0:H * C]),
                    reads=["PSG"], writes=[nm])

        def s2():
            b.emit("scalar", lambda s, k=khat: s.activation(self.h3(u["sq"][:]), k, AF.Square),
                reads=[kk], writes=["sq"])
            for h in range(H):
                b.emit("tensor", lambda t, h=h: t.matmul(
                    u["PSG"][:, 4 * H + h:4 * H + h + 1], self.h3(u["sq"][:])[:, h, :],
                    u["ones128"][:, 0:1], start=(h == 0), stop=False),
                    reads=["sq", "ones128", "PSG"], writes=["PSG"])
            b.emit("scalar", lambda s, q=qhat: s.activation(self.h3(u["sq"][:]), q, AF.Square),
                reads=[qk, "PSG"], writes=["sq"])
            for h in range(H):
                b.emit("tensor", lambda t, h=h: t.matmul(
                    u["PSG"][:, 5 * H + h:5 * H + h + 1], self.h3(u["sq"][:])[:, h, :],
                    u["ones128"][:, 0:1], start=False, stop=(h == H - 1)),
                    reads=["sq", "ones128", "PSG"], writes=["PSG"])
            b.emit("scalar", lambda s: s.activation(u["lncol"][:], u["PSG"][:, 4 * H:6 * H], AF.Ln,
                                                    bias=u["c_eps6"][:]),
                reads=["PSG", "c_eps6"], writes=["lncol"])
            b.emit("scalar", lambda s: s.activation(u["rcol"][:, 0:H], u["lncol"][:, 0:H],
                                                    AF.Exp, scale=-0.5),
                reads=["lncol"], writes=["rcol"])
            b.emit("scalar", lambda s: s.activation(u["rcol"][:, H:2 * H], u["lncol"][:, H:2 * H],
                                                    AF.Exp, scale=-0.5, bias=u["c_lnq"][:]),
                reads=["lncol", "c_lnq"], writes=["rcol"])


        def kq(nm, src, srck, rlo):
            for h in range(H):
                b.emit("tensor", lambda t, h=h, src=src, PST=PST: t.matmul(
                    PST[:, h * C:(h + 1) * C], src[:, h, :], u["identb"][:],
                    start=(h == 0), stop=(h == H - 1), is_transpose=True),
                    reads=[srck, "identb", pstk], writes=[pstk])
            tmk = (nm + "_tm" + P) if nm == "k" else "q_tm"
            tm = u[tmk]
            b.emit("vector", lambda v, tm=tm, rlo=rlo, PST=PST: v.tensor_tensor(
                self.h3(tm[:]), self.h3(PST[:, 0:H * C]),
                self.colb(u["rcol"][:, rlo:rlo + H]), OP.mult),
                reads=[pstk, "rcol"], writes=[tmk])
            for h in range(H):
                b.emit("tensor", lambda t, h=h, tm=tm, PST=PST: t.matmul(
                    PST[:, H * C + h * C:H * C + (h + 1) * C], self.h3(tm[:])[:, h, :],
                    u["identb"][:],
                    start=(h == 0), stop=(h == H - 1), is_transpose=True),
                    reads=[tmk, "identb", pstk], writes=[pstk])
            nt = u[nm + "nT" + P]
            b.emit("scalar", lambda s, nt=nt, PST=PST: s.copy(
                nt[:], PST[:, H * C:2 * H * C]),
                reads=[pstk], writes=[nm + "nT" + P])

        def s3():
            kq("k", khat, kk, 0)

        def s4():
            kq("q", qhat, qk, H)
            for h in range(H):
                b.emit("tensor", lambda t, h=h, v_=vhat, PST=PST: t.matmul(
                    PST[:, h * C:(h + 1) * C], v_[:, h, :], u["identb"][:],
                    start=(h == 0), stop=(h == H - 1), is_transpose=True),
                    reads=[vk, "identb", pstk], writes=[pstk])
            b.emit("scalar", lambda s, PST=PST: s.copy(u["v_tm"][:], PST[:, 0:H * C]),
                reads=[pstk], writes=["v_tm"])
            b.emit("vector", lambda v, betacol=betacol: v.tensor_tensor(
                self.h3(u["rhs1" + P][:]), self.h3(u["v_tm"][:]), self.colb(betacol), OP.mult),
                reads=["v_tm", "slab" + P], writes=["rhs1" + P])

        def s5():
            for h in range(H):
                b.emit("tensor", lambda t, h=h, knT3=knT3: t.matmul(
                    u["PSG"][:, h * C:(h + 1) * C], knT3[:, h, :], knT3[:, h, :],
                    start=(h == 0), stop=(h == H - 1)),
                    reads=["knT" + P, "PSG"], writes=["PSG"])

            b.emit("vector", lambda v, bcol=bcol: v.tensor_tensor(
                self.h3(u["e1"][:]), self.colb(bcol), brow3, OP.subtract),
                reads=["slab" + P, "brow"], writes=["e1"])
            b.emit("vector", lambda v: v.tensor_scalar_min(u["e1"][:], u["e1"][:], 0.0),
                reads=["e1"], writes=["e1"])
            b.emit("scalar", lambda s: s.activation(u["e1"][:], u["e1"][:], AF.Exp),
                reads=["e1"], writes=["e1"])
            b.emit("vector", lambda v: v.tensor_mul(u["t1"][:], u["e1"][:], u["PSG"][:, 0:H * C]),
                reads=["e1", "PSG"], writes=["t1"])
            b.emit("vector", lambda v: v.tensor_tensor(
                self.h3(u["t1"][:]), self.h3(u["t1"][:]), self.maskb(u["lmask_neg"][:]), OP.mult),
                reads=["t1", "lmask_neg"], writes=["t1"])
            b.emit("vector", lambda v, betacol=betacol: v.tensor_tensor(
                self.h3(u["Npow" + P][:]), self.h3(u["t1"][:]), self.colb(betacol), OP.mult),
                reads=["t1", "slab" + P], writes=["Npow" + P])

        def s6():
            b.emit("vector", lambda v, bcol=bcol: v.tensor_tensor(
                self.h3(u["e1"][:]), browb3, self.colb(bcol), OP.subtract),
                reads=["browb", "slab" + P], writes=["e1"])
            b.emit("vector", lambda v: v.tensor_scalar_min(u["e1"][:], u["e1"][:], 0.0),
                reads=["e1"], writes=["e1"])
            b.emit("scalar", lambda s: s.activation(u["e1"][:], u["e1"][:], AF.Exp),
                reads=["e1"], writes=["e1"])
            b.emit("vector", lambda v: v.tensor_mul(u["t1"][:], u["e1"][:], u["PSG"][:, 0:H * C]),
                reads=["e1", "PSG"], writes=["t1"])
            b.emit("vector", lambda v: v.tensor_tensor(
                self.h3(u["NTpow" + P][:]), self.h3(u["t1"][:]), self.maskb(u["umask_neg"][:]),
                OP.mult),
                reads=["t1", "umask_neg"], writes=["NTpow" + P])
            b.emit("vector", lambda v: v.tensor_tensor(
                self.h3(u["GT" + P][:]), self.h3(u["NTpow" + P][:]), self.maskb(u["identb"][:]),
                OP.add),
                reads=["NTpow" + P, "identb"], writes=["GT" + P])

            for h in range(H):
                b.emit("tensor", lambda t, h=h, knT3=knT3, qnT3=qnT3: t.matmul(
                    u["PSG"][:, h * C:(h + 1) * C], knT3[:, h, :], qnT3[:, h, :],
                    start=(h == 0), stop=(h == H - 1)),
                    reads=["knT" + P, "qnT" + P, "PSG"], writes=["PSG"])
            b.emit("vector", lambda v, bcol=bcol: v.tensor_tensor(
                self.h3(u["e1"][:]), brow3, self.colb(bcol), OP.subtract),
                reads=["brow", "slab" + P], writes=["e1"])
            b.emit("vector", lambda v: v.tensor_scalar_min(u["e1"][:], u["e1"][:], 0.0),
                reads=["e1"], writes=["e1"])
            b.emit("scalar", lambda s: s.activation(u["e1"][:], u["e1"][:], AF.Exp),
                reads=["e1"], writes=["e1"])
            b.emit("vector", lambda v: v.tensor_mul(u["t1"][:], u["e1"][:], u["PSG"][:, 0:H * C]),
                reads=["e1", "PSG"], writes=["t1"])
            b.emit("vector", lambda v: v.tensor_tensor(
                self.h3(u["Sup" + P][:]), self.h3(u["t1"][:]), self.maskb(u["umask_incl"][:]),
                OP.mult),
                reads=["t1", "umask_incl"], writes=["Sup" + P])

        return [s0, s1, s2, s3, s4, s5, s6]

    def p1b(self, par):
        b, u, H, C = self.b, self.u, self.H, self.C
        P = str(par)
        stages = []
        for lev in range(1, self.NLEV):
            def ql(lev=lev):
                for h in range(H):
                    b.emit("tensor", lambda t, h=h: t.matmul(
                        u["PSQ"][:, h * C:(h + 1) * C],
                        self.h3(u["NTpow" + P][:])[:, h, :],
                        self.h3(u["Npow" + P][:])[:, h, :],
                        start=(h == 0), stop=(h == H - 1)),
                        reads=["NTpow" + P, "Npow" + P, "PSQ"], writes=["PSQ"])
                if lev < self.NLEV - 1:
                    for h in range(H):
                        b.emit("tensor", lambda t, h=h: t.matmul(
                            u["PSR"][:, h * C:(h + 1) * C],
                            self.h3(u["Npow" + P][:])[:, h, :],
                            self.h3(u["NTpow" + P][:])[:, h, :],
                            start=(h == 0), stop=(h == H - 1)),
                            reads=["NTpow" + P, "Npow" + P, "PSR"], writes=["PSR"])
                b.emit("scalar", lambda s: s.copy(u["Npow" + P][:], u["PSQ"][:, 0:H * C]),
                    reads=["PSQ"], writes=["Npow" + P])
                if lev < self.NLEV - 1:
                    b.emit("scalar", lambda s: s.copy(u["NTpow" + P][:], u["PSR"][:, 0:H * C]),
                        reads=["PSR"], writes=["NTpow" + P])
                for h in range(H):
                    b.emit("tensor", lambda t, h=h: t.matmul(
                        u["PSA"][:, h * C:(h + 1) * C],
                        self.h3(u["Npow" + P][:])[:, h, :],
                        self.h3(u["GT" + P][:])[:, h, :],
                        start=(h == 0), stop=(h == H - 1)),
                        reads=["Npow" + P, "GT" + P, "PSA"], writes=["PSA"])
                b.emit("vector", lambda v: v.tensor_add(
                    u["GT" + P][:], u["GT" + P][:], u["PSA"][:, 0:H * C]),
                    reads=["GT" + P, "PSA"], writes=["GT" + P])
            stages.append(ql)

        def wind():
            for h in range(H):
                b.emit("tensor", lambda t, h=h: t.matmul(
                    u["PSR"][:, h * C:(h + 1) * C],
                    self.h3(u["GT" + P][:])[:, h, :],
                    self.h3(u["rhs1" + P][:])[:, h, :],
                    start=(h == 0), stop=(h == H - 1)),
                    reads=["GT" + P, "rhs1" + P, "PSR"], writes=["PSR"])
            b.emit("scalar", lambda s: s.copy(u["Wind"][:], u["PSR"][:, 0:H * C]),
                reads=["PSR"], writes=["Wind"])
        stages.append(wind)
        return stages

    def p23_chain(self, par):
        b, u, H, C = self.b, self.u, self.H, self.C
        P = str(par)
        kcb, gam, bkap = u["kcb" + P], u["gam" + P], u["bkap" + P]
        knT3 = self.h3(u["knT" + P][:])
        qnT3 = self.h3(u["qnT" + P][:])
        gt3 = self.h3(u["GT" + P][:])

        def c0():
            for h in range(H):
                b.emit("tensor", lambda t, h=h: t.matmul(
                    u["PSK"][:, h * C:(h + 1) * C], knT3[:, h, :],
                    self.h3(u["M_bf"][:])[:, h, :],
                    start=(h == 0), stop=(h == H - 1)),
                    reads=["knT" + P, "M_bf", "PSK"], writes=["PSK"])
            for h in range(H):
                b.emit("tensor", lambda t, h=h: t.matmul(
                    u["PSA"][:, h * C:(h + 1) * C], qnT3[:, h, :],
                    self.h3(u["M_bf"][:])[:, h, :],
                    start=(h == 0), stop=(h == H - 1)),
                    reads=["qnT" + P, "M_bf", "PSA"], writes=["PSA"])

        def c1():
            b.emit("vector", lambda v: v.tensor_tensor(
                self.h3(u["X"][:]), self.h3(u["PSK"][:, 0:H * C]),
                self.colb(bkap[:]), OP.mult),
                reads=["PSK", "bkap" + P], writes=["X"])

        def c2():
            for h in range(H):
                b.emit("tensor", lambda t, h=h: t.matmul(
                    u["PSK"][:, h * C:(h + 1) * C], gt3[:, h, :],
                    self.h3(u["X"][:])[:, h, :],
                    start=(h == 0), stop=(h == H - 1)),
                    reads=["GT" + P, "X", "PSK"], writes=["PSK"])

        def c3():
            b.emit("vector", lambda v: v.tensor_sub(
                u["Wf"][:], u["Wind"][:], u["PSK"][:, 0:H * C]),
                reads=["Wind", "PSK"], writes=["Wf"])

        def c4():
            b.emit("vector", lambda v: v.tensor_tensor(
                self.h3(u["Wg"][:]), self.h3(u["Wf"][:]), self.colb(gam[:]),
                OP.mult), reads=["Wf", "gam" + P], writes=["Wg"])

        def c5():
            for h in range(H):
                b.emit("tensor", lambda t, h=h: t.matmul(
                    u["PSK"][:, h * C:(h + 1) * C],
                    self.h3(u["k_tm" + P][:])[:, h, :],
                    self.h3(u["Wg"][:])[:, h, :],
                    start=(h == 0), stop=(h == H - 1)),
                    reads=["k_tm" + P, "Wg", "PSK"], writes=["PSK"])

        def c6():
            b.emit("vector", lambda v: v.tensor_tensor(
                self.h3(u["M_f32"][:]), self.h3(u["M_f32"][:]),
                self.colb(kcb[:]), OP.mult),
                reads=["M_f32", "kcb" + P], writes=["M_f32"])

        def c7():
            b.emit("vector", lambda v: v.tensor_add(
                u["M_f32"][:], u["M_f32"][:], u["PSK"][:, 0:H * C]),
                reads=["M_f32", "PSK"], writes=["M_f32"])

        def c8():
            b.emit("scalar", lambda s: s.copy(u["M_bf"][:], u["M_f32"][:]),
                reads=["M_f32"], writes=["M_bf"])

        return [c0, c1, c2, c3, c4, c5, c6, c7, c8]

    def p23_out(self, par, zsl, og_cb, z_key):
        b, u, H, C = self.b, self.u, self.H, self.C
        P = str(par)
        slab2 = u["slab2" + P]
        PST, pstk = u["PST" + P], "PST" + P
        kappa = slab2[:, H:2 * H]

        def o1a():
            b.emit("vector", lambda v: v.tensor_tensor(
                self.h3(u["t2"][:]), self.h3(u["PSA"][:, 0:H * C]),
                self.colb(kappa), OP.mult),
                reads=["PSA", "slab2" + P], writes=["t2"])

        def o1b():
            b.emit("scalar", lambda s: s.copy(u["W_bf"][:], u["Wf"][:]),
                reads=["Wf"], writes=["W_bf"])
            for h in range(H):
                b.emit("tensor", lambda t, h=h: t.matmul(
                    u["PSS"][:, h * C:(h + 1) * C],
                    self.h3(u["Sup" + P][:])[:, h, :],
                    self.h3(u["W_bf"][:])[:, h, :],
                    start=(h == 0), stop=(h == H - 1)),
                    reads=["Sup" + P, "W_bf", "PSS"], writes=["PSS"])
            b.emit("vector", lambda v: v.tensor_add(
                u["O"][:], u["t2"][:], u["PSS"][:, 0:H * C]),
                reads=["t2", "PSS"], writes=["O"])

        def o2():
            b.emit("scalar", lambda s: s.activation(u["t2"][:], u["O"][:], AF.Square),
                reads=["O"], writes=["t2"])
            b.emit("vector", lambda v: v.tensor_reduce(
                u["s4"][:].unsqueeze(2), self.h3(u["t2"][:]), AX.X, OP.add),
                reads=["t2"], writes=["s4"])
            b.emit("scalar", lambda s: s.activation(u["s4"][:], u["s4"][:], AF.Ln,
                                                    bias=u["c_eps6"][:], scale=1.0 / 128.0),
                reads=["s4", "c_eps6"], writes=["s4"])
            b.emit("scalar", lambda s: s.activation(u["s4"][:], u["s4"][:], AF.Exp, scale=-0.5),
                reads=["s4"], writes=["s4"])
            b.emit("vector", lambda v: v.tensor_tensor(
                self.h3(u["t2"][:]), self.h3(u["O"][:]), self.colb(u["s4"][:]), OP.mult),
                reads=["O", "s4"], writes=["t2"])
            b.emit("vector", lambda v, z=zsl: v.tensor_tensor(
                self.h3(u["og_tm"][:]), self.h3(u["t2"][:]), z, OP.mult),
                reads=["t2", z_key], writes=["og_tm"])

        def o3():
            for h in range(H):
                b.emit("tensor", lambda t, h=h, PST=PST: t.matmul(
                    PST[:, h * C:(h + 1) * C], self.h3(u["og_tm"][:])[:, h, :],
                    u["identb"][:], start=(h == 0), stop=(h == H - 1), is_transpose=True),
                    reads=["og_tm", "identb", pstk], writes=[pstk])
            b.emit("scalar", lambda s, PST=PST: s.copy(u["ogT"][:], PST[:, 0:H * C]),
                reads=[pstk], writes=["ogT"])
            og_cb()

        return [o1a, o1b, o2, o3]


def alloc_scan_bufs(nc, ctx, H, C):
    u = {}

    def sb(name, shape, dt=F32):
        u[name] = ctx.enter_context(nc.sbuf_tensor("scan_" + name, shape, dt))

    def ps(name, shape, dt=F32):
        u[name] = ctx.enter_context(nc.psum_tensor("scan_" + name, shape, dt))

    HC = H * C
    ps("PSG", [128, 512]); ps("PSS", [128, 512]); ps("PSK", [128, 512])
    ps("PSA", [128, 512]); ps("PSQ", [128, 512]); ps("PSR", [128, 512])
    ps("PST0", [128, 1024], BF16); ps("PST1", [128, 1024], BF16)
    sb("b_scr", [H, C]); sb("blnb_scr", [H, C]); sb("zero_row", [H, 1])
    sb("bc_row", [1, H])
    sb("brow", [128, HC]); sb("browb", [128, HC])
    sb("sq", [128, HC], BF16); sb("lncol", [128, 2 * H]); sb("rcol", [128, 2 * H], BF16)
    sb("q_tm", [128, HC], BF16); sb("v_tm", [128, HC], BF16)
    sb("e1", [128, HC]); sb("t1", [128, HC]); sb("t2", [128, HC])
    sb("X", [128, HC], BF16); sb("O", [128, HC])
    sb("s4", [128, H]); sb("og_tm", [128, HC], BF16); sb("ogT", [128, HC], BF16)
    sb("Wg", [128, HC], BF16)
    sb("Wf", [128, HC]); sb("W_bf", [128, HC], BF16)
    sb("M_f32", [128, HC]); sb("M_bf", [128, HC], BF16)
    for P in ("0", "1"):
        sb("slab" + P, [128, 2 * H]); sb("slab2" + P, [128, 2 * H])
        sb("kcb" + P, [128, H]); sb("gam" + P, [128, H], BF16); sb("bkap" + P, [128, H])
        sb("rhs1" + P, [128, HC], BF16)
        sb("knT" + P, [128, HC], BF16); sb("qnT" + P, [128, HC], BF16)
        sb("k_tm" + P, [128, HC], BF16)
        sb("Npow" + P, [128, HC], BF16); sb("NTpow" + P, [128, HC], BF16)
        sb("GT" + P, [128, HC], BF16); sb("Sup" + P, [128, HC], BF16)
    sb("Wind", [128, HC])
    sb("identH", [H, H]); sb("identb", [128, 128], BF16); sb("ident_f", [128, 128])
    sb("ones1", [1, 128]); sb("ones128", [128, 1], BF16)
    sb("c_eps6", [128, 1]); sb("c_eps5", [128, 1]); sb("c_lnq", [128, 1]); sb("c_one", [128, 1])
    sb("lmask_neg", [C, C]); sb("umask_neg", [C, C]); sb("umask_incl", [C, C])
    for h in range(H):
        sb("sel" + str(h), [H, 128])
    return u



def scan_const_arrays(H, C):
    c = {}
    c["identH"] = np.eye(H, dtype=np.float32)
    c["ident_f"] = np.eye(128, dtype=np.float32)
    c["identb"] = np.eye(128, dtype=np.float32).astype(BF)
    c["ones1"] = np.ones((1, 128), np.float32)
    c["ones128"] = np.ones((128, 1), np.float32).astype(BF)
    c["c_eps6"] = np.full((128, 1), 1e-6, np.float32)
    c["c_eps5"] = np.full((128, 1), 1e-5, np.float32)
    c["c_lnq"] = np.full((128, 1), math.log(1.0 / math.sqrt(128.0)), np.float32)
    c["c_one"] = np.ones((128, 1), np.float32)
    c["lmask_neg"] = -np.tril(np.ones((C, C), np.float32), -1)
    c["umask_neg"] = -np.triu(np.ones((C, C), np.float32), 1)
    c["umask_incl"] = np.triu(np.ones((C, C), np.float32), 0)
    for h in range(H):
        s = np.zeros((H, 128), np.float32)
        s[h, :] = 1.0
        c["sel" + str(h)] = s
    c["zero_row"] = np.zeros((H, 1), np.float32)
    return c


def emit_const_loads(b, u, consts, nc, dma_eng="sync"):
    for name, arr in consts.items():
        t = nc.inline_tensor(np.ascontiguousarray(arr), "const_" + name)
        b.emit(dma_eng, lambda E, t=t, name=name: E.dma_start(out=u[name][:], in_=t.ap()),
               writes=[name], dma="xload")



def default_cfg():
    return dict(D=2048, L=4096, TILE=512, H=4, HG=16, C=128, NLEV=7,
                IFF=1408, NE=8, TB=1024, KCONV=4, IBMAX=256)


def mini_cfg():
    return dict(D=512, L=512, TILE=256, H=1, HG=4, C=128, NLEV=7,
                IFF=256, NE=8, TB=128, KCONV=4, IBMAX=256)


class CoreProgram:

    def __init__(self, nc, cfg):
        self.nc = nc
        self.cfg = cfg
        self.b = Builder(nc)

    def const_into(self, u, stack, name, arr):
        arr = np.ascontiguousarray(arr)
        u[name] = stack.enter_context(
            self.nc.sbuf_tensor("c_" + name, list(arr.shape),
                                mybir.dt.from_np(arr.dtype)))
        t = self.nc.inline_tensor(arr, "ct_" + name)
        self.b.emit("sync", lambda E, t=t, name=name: E.dma_start(
            out=u[name][:], in_=t.ap()), writes=[name], dma="xload")

    def barrier(self):
        lasts = {}
        for e in ENGINES:
            for kind, item in reversed(self.b.events[e]):
                if kind == "e":
                    lasts[e] = item
                    break
        targets = list(lasts.values()) + list(self.b.last_dma.values())
        for ce in ENGINES:
            for ev in targets:
                if ev.eng != ce or ev.amt == 16:
                    self.b._wait(ce, ev)

    def build(self):
        nc, cfg, b = self.nc, self.cfg, self.b
        D, L, TILE, H, C = cfg["D"], cfg["L"], cfg["TILE"], cfg["H"], cfg["C"]
        HG, IFF, NE, TB, KC = cfg["HG"], cfg["IFF"], cfg["NE"], cfg["TB"], cfg["KCONV"]
        ND, NT, NG = D // 128, L // TILE, TILE // 128
        NQKV = 3 * H
        VD = HG * 128
        NVC = VD // 128
        NTB = TB // 128
        NIB = IFF // 128
        NHOB = D // 512
        NSPL = cfg.get("NSPL", 2 if TB >= 1024 else 1)
        THB = TB // NSPL

        SH = IFF // 8
        IFFT = IFF + SH
        NIC = (IFFT + 127) // 128
        IFFP = NIC * 128
        TBL = min(512, TB)
        NGB = TBL // 128
        NBR = TB // TBL
        NTBL = 8 * NBR
        NCHK = NBR

        dt_in = lambda n, s, dt=F32: nc.dram_tensor(n, s, dt, kind="ExternalInput").ap()
        x_attn = dt_in("x_attn", [L, D])
        x_res = dt_in("x_res", [TB, D])
        wqkvz = dt_in("wqkvz", [D, (NQKV + H) * 128])
        wba = dt_in("wba", [D, 2 * H])
        convw = dt_in("convw", [NQKV * 128, KC])
        alog4 = dt_in("alog4", [H, 1])
        dtb4 = dt_in("dtb4", [H, 1])
        normw = dt_in("normw", [128, 1])
        wout = dt_in("wout", [VD, D])
        routerw = dt_in("routerw", [NE, D])
        ewgT = dt_in("ewgT", [D, IFFT])
        ewuT = dt_in("ewuT", [D, IFFT])
        ewdT = dt_in("ewdT", [IFFP, D])
        y_out = nc.dram_tensor("y", [TB, D], F32, kind="ExternalOutput").ap()

        agin = nc.dram_tensor("agin", [4, H * 128, L // 4], BF16)
        agout = nc.dram_tensor("agout", [4, VD, L // 4], BF16)
        x1_dram = nc.dram_tensor("x1_dram", [TB, D], F32)
        agin2 = nc.dram_tensor("agin2", [2, D, TB // 2], BF16)
        agout2 = nc.dram_tensor("agout2", [2, 8 * D, TB // 2], BF16)
        ag3in = nc.dram_tensor("ag3in", [NE * NTB, 128], F32)
        ag3out = nc.dram_tensor("ag3out", [8 * NE * NTB, 128], F32)
        rs_in = [nc.dram_tensor(f"rs_in{k}", [8 * TBL, D], BF16)
                 for k in range(NCHK)]
        rs_out = [nc.dram_tensor(f"rs_out{k}", [TBL, D], BF16)
                  for k in range(NCHK)]

        QW = (NQKV + H) * 128 // 4
        NCH_T = TILE // C
        with ExitStack() as actx:
            uA = {}
            sb = lambda n, s, dt=F32: uA.__setitem__(
                n, actx.enter_context(nc.sbuf_tensor(n, s, dt)))
            uA_scan = alloc_scan_bufs(nc, actx, H, C)
            uA.update(uA_scan)
            emit_const_loads(b, uA, scan_const_arrays(H, C), nc)

            sb("xt0", [128, D]); sb("xt1", [128, D])
            sb("xt2", [128, D]); sb("xt3", [128, D])
            sb("sq_ln", [128, D])
            sb("mu0", [128, 2]); sb("mu1", [128, 2])
            sb("rstd0", [128, 1]); sb("rstd1", [128, 1])
            sb("xn0", [128, D]); sb("xn1", [128, D])
            sb("lnxT", [128, ND, TILE], BF16)
            sb("wq0", [128, ND, QW], BF16)
            sb("wq1", [128, ND, QW], BF16)
            sb("wba_s", [128, ND, 2 * H], BF16)
            sb("qkv_raw", [128, NQKV, TILE + KC - 1], BF16)
            sb("silbuf", [128, NQKV, TILE], BF16)
            sb("z_sil0", [128, NCH_T, H * 128])
            sb("z_sil1", [128, NCH_T, H * 128])
            sb("diag8", [128, 2 * KC, 128], BF16)
            sb("convw_s", [128, NQKV, KC])
            sb("balog", [H, TILE]); sb("galog", [H, TILE]); sb("lnb_r", [H, TILE])
            sb("arows_scr", [H, TILE])
            sb("alog_c", [H, 1]); sb("dtb_c", [H, 1])

            b.emit("gpsimd", lambda E: E.dma_start(
                out=uA["wba_s"][:],
                in_=wba.rearrange("(dc p) c -> p dc c", p=128)),
                writes=["wba_s"], dma="cast")
            b.emit("sync", lambda E: E.dma_start(
                out=uA["convw_s"][:],
                in_=convw.rearrange("(cc p) k -> p cc k", p=128)),
                writes=["convw_s"], dma="xload")
            b.emit("sync", lambda E: E.dma_start(out=uA["alog_c"][:], in_=alog4),
                   writes=["alog_c"], dma="xload")
            b.emit("sync", lambda E: E.dma_start(out=uA["dtb_c"][:], in_=dtb4),
                   writes=["dtb_c"], dma="xload")

            h3q = lambda nm, q: uA[nm][:]
            b.emit("scalar", lambda s: s.activation(uA["alog_c"][:], uA["alog_c"][:], AF.Exp),
                   reads=["alog_c"], writes=["alog_c"])
            b.emit("vector", lambda v: v.memset(uA["M_f32"][:], 0.0), writes=["M_f32"])
            b.emit("vector", lambda v: v.memset(uA["M_bf"][:], 0.0), writes=["M_bf"])
            b.emit("vector", lambda v: v.memset(
                h3q("qkv_raw", NQKV)[:, :, 0:KC - 1], 0.0), writes=["qkv_raw"])

            em = ScanEmitter(b, uA, H, C, cfg["NLEV"])

            def prep_xload(ti):
                t0 = ti * TILE
                for g in range(NG):
                    b.emit("sync", lambda E, t0=t0, g=g: E.dma_start(
                        out=uA["xt" + str(g)][:],
                        in_=x_attn[t0 + g * 128:t0 + (g + 1) * 128, :]),
                        reads=["xt" + str(g)], writes=["xt" + str(g)],
                        dma="xl" + str(g))

            def prep_ln(ti, pair):
                jobs = [(uA["xt" + str(g)][:], "xt" + str(g), g, str(g % 2))
                        for g in range(2 * pair, min(2 * pair + 2, NG))]
                self.emit_ln_pair(uA, jobs, "lnxT", ND, D, "PSG")

            def prep_q_load(ti, q):
                wpar = q % 2
                wq = uA["wq" + str(wpar)]
                half = QW // 2
                for hh in range(2):
                    b.emit("gpsimd", lambda E, q=q, wq=wq, hh=hh: E.dma_start(
                        out=wq[:][:, :, hh * half:(hh + 1) * half],
                        in_=wqkvz[:, q * QW + hh * half:q * QW + (hh + 1) * half]
                        .rearrange("(dc p) c -> p dc c", p=128)),
                        reads=[("wq", wpar, hh)],
                        writes=[("wq", wpar, hh)],
                        dma="wq" + str(wpar) + str(hh))

            def prep_q_mm(ti, q):
                tp = str(ti % 2)
                zs = "z_sil" + tp
                lnxT3 = h3q("lnxT", ND)
                wpar = q % 2
                wq = uA["wq" + str(wpar)]
                for j in range(QW // 128):
                    cc = q * (QW // 128) + j
                    hh = j // ((QW // 128) // 2)
                    if cc < NQKV:
                        ps = "PSS"
                        for dc in range(ND):
                            b.emit("tensor", lambda t, j=j, dc=dc, ps=ps, wq=wq: t.matmul(
                                uA[ps][:, 0:TILE], wq[:][:, dc, j * 128:(j + 1) * 128],
                                lnxT3[:, dc, :], start=(dc == 0), stop=(dc == ND - 1)),
                                reads=[("wq", wpar, hh), "lnxT", ps], writes=[ps])
                        b.emit("scalar", lambda s, cc=cc, ps=ps: s.copy(
                            h3q("qkv_raw", NQKV)[:, cc, KC - 1:KC - 1 + TILE],
                            uA[ps][:, 0:TILE]),
                            reads=[ps], writes=["qkv_raw"])
                if q == 3:
                    for g in range(NG):
                        ps = "PSS"
                        for dc in range(ND):
                            b.emit("tensor", lambda t, g=g, dc=dc, ps=ps, wq=wq: t.matmul(
                                uA[ps][:, 0:H * 128], lnxT3[:, dc, g * 128:(g + 1) * 128],
                                wq[:][:, dc, QW - H * 128:QW],
                                start=(dc == 0), stop=(dc == ND - 1)),
                                reads=[("wq", wpar, 0), ("wq", wpar, 1), "lnxT", ps],
                                writes=[ps])
                        b.emit("scalar", lambda s, g=g, ps=ps, zs=zs: s.activation(
                            h3q(zs, NG)[:, g, :], uA[ps][:, 0:H * 128], AF.Silu),
                            reads=[ps], writes=[zs])
            def prep_s3(ti):
                lnxT3 = h3q("lnxT", ND)
                for which, lo in (("balog", 0), ("galog", H)):
                    for dc in range(ND):
                        b.emit("tensor", lambda t, dc=dc, lo=lo: t.matmul(
                            uA["PSS"][0:H, 0:TILE],
                            h3q("wba_s", ND)[:, dc, lo:lo + H], lnxT3[:, dc, :],
                            start=(dc == 0), stop=(dc == ND - 1)),
                            reads=["wba_s", "lnxT", "PSS"], writes=["PSS"])
                    b.emit("vector", lambda v, which=which: v.tensor_copy(
                        uA[which][:], uA["PSS"][0:H, 0:TILE]),
                        reads=["PSS"], writes=[which])
                for cc in range(NQKV):
                    pc = cc % 2
                    ps = "PSS" if pc == 0 else "PSG"
                    for j in range(KC):
                        b.emit("vector", lambda v, cc=cc, j=j, pc=pc: v.tensor_scalar_mul(
                            h3q("diag8", 2 * KC)[:, pc * KC + j, :], uA["identb"][:],
                            h3q("convw_s", NQKV)[:, cc, j:j + 1]),
                            reads=["identb", "convw_s", ("diag", pc)],
                            writes=[("diag", pc)])
                    for j in range(KC):
                        b.emit("tensor", lambda t, cc=cc, j=j, ps=ps, pc=pc: t.matmul(
                            uA[ps][:, 0:TILE], h3q("diag8", 2 * KC)[:, pc * KC + j, :],
                            h3q("qkv_raw", NQKV)[:, cc, j:j + TILE],
                            start=(j == 0), stop=(j == KC - 1)),
                            reads=[("diag", pc), "qkv_raw", ps], writes=[ps])
                    b.emit("scalar", lambda s, cc=cc, ps=ps: s.activation(
                        h3q("silbuf", NQKV)[:, cc, :], uA[ps][:, 0:TILE], AF.Silu),
                        reads=[ps], writes=["silbuf"])
                b.emit("vector", lambda v: v.tensor_copy(
                    h3q("qkv_raw", NQKV)[:, :, 0:KC - 1],
                    h3q("qkv_raw", NQKV)[:, :, TILE:TILE + KC - 1]),
                    reads=["qkv_raw"], writes=["qkv_raw"])
                bsl = uA["balog"][:]
                asl = uA["galog"][:]
                b.emit("scalar", lambda s, bsl=bsl: s.activation(
                    uA["arows_scr"][:], bsl, AF.Tanh, scale=0.5),
                    reads=["balog"], writes=["arows_scr"])
                b.emit("vector", lambda v, bsl=bsl: v.tensor_scalar(
                    bsl, uA["arows_scr"][:], 0.5, 0.5, OP.mult, OP.add),
                    reads=["arows_scr"], writes=["balog"])
                b.emit("vector", lambda v, asl=asl: v.tensor_scalar_add(
                    asl, asl, uA["dtb_c"][:]), reads=["galog", "dtb_c"], writes=["galog"])
                b.emit("scalar", lambda s, asl=asl: s.activation(
                    uA["arows_scr"][:], asl, AF.Abs), reads=["galog"], writes=["arows_scr"])
                b.emit("scalar", lambda s: s.activation(
                    uA["arows_scr"][:], uA["arows_scr"][:], AF.Exp, scale=-1.0),
                    reads=["arows_scr"], writes=["arows_scr"])
                b.emit("scalar", lambda s: s.activation(
                    uA["arows_scr"][:], uA["arows_scr"][:], AF.Ln, bias=uA["c_one"][0:H, :]),
                    reads=["arows_scr", "c_one"], writes=["arows_scr"])
                b.emit("vector", lambda v, asl=asl: v.tensor_scalar_max(
                    asl, asl, 0.0), reads=["galog"], writes=["galog"])
                b.emit("vector", lambda v, asl=asl: v.tensor_add(
                    asl, asl, uA["arows_scr"][:]), reads=["galog", "arows_scr"],
                    writes=["galog"])
                b.emit("vector", lambda v, asl=asl: v.tensor_scalar(
                    asl, asl, uA["alog_c"][:], -1.0, OP.mult, OP.mult),
                    reads=["galog", "alog_c"], writes=["galog"])
                b.emit("scalar", lambda s, bsl=bsl: s.activation(
                    uA["lnb_r"][:], bsl, AF.Ln),
                    reads=["balog"], writes=["lnb_r"])

            def prep_head(ti, ci):
                if ci == 0:
                    prep_xload(ti)
                    prep_q_load(ti, 0)
                    prep_q_load(ti, 1)
                elif ci == 2:
                    prep_q_load(ti, 2)
                    prep_q_load(ti, 3)

            def prep_tail(ti, ci):
                if ci == 0:
                    prep_ln(ti, 0)
                    prep_ln(ti, 1)
                elif ci == 1:
                    prep_q_mm(ti, 0)
                    prep_q_mm(ti, 1)
                elif ci == 2:
                    prep_q_mm(ti, 2)
                    prep_q_mm(ti, 3)
                else:
                    prep_s3(ti)

            def prep_all(ti):
                prep_xload(ti)
                prep_q_load(ti, 0); prep_q_load(ti, 1)
                prep_ln(ti, 0); prep_ln(ti, 1)
                prep_q_mm(ti, 0); prep_q_mm(ti, 1)
                prep_q_load(ti, 2); prep_q_load(ti, 3)
                prep_q_mm(ti, 2); prep_q_mm(ti, 3)
                prep_s3(ti)

            def scan_p1(ti, ci):
                par = (ti * NCH_T + ci) % 2
                return em.p1_stages(
                    h3q("silbuf", NQKV)[:, H:2 * H, ci * C:(ci + 1) * C],
                    h3q("silbuf", NQKV)[:, 0:H, ci * C:(ci + 1) * C],
                    h3q("silbuf", NQKV)[:, 2 * H:3 * H, ci * C:(ci + 1) * C],
                    uA["galog"][:, ci * C:(ci + 1) * C],
                    uA["lnb_r"][:, ci * C:(ci + 1) * C],
                    uA["balog"][:, ci * C:(ci + 1) * C],
                    par,
                    kq_keys=("silbuf", "silbuf", "silbuf"),
                    row_keys=("galog", "lnb_r", "balog"))

            def scan_p23_stages(ti, ci):
                if cfg.get("NOSCAN"):
                    return [], []
                par = (ti * NCH_T + ci) % 2
                tp = str(ti % 2)
                zs = "z_sil" + tp

                def og_cb(ti=ti, ci=ci):
                    tt = ti * TILE + ci * C
                    q, toff = tt // (L // 4), tt % (L // 4)
                    b.emit("scalar", lambda E, q=q, toff=toff: E.dma_start(
                        out=agin.ap()[q].rearrange("(h p) t -> p h t", p=128)
                        [:, :, toff:toff + C],
                        in_=uA["ogT"][:].rearrange("p (h c) -> p h c", h=H)),
                        reads=["ogT"], writes=[("agin_dram", (ti * NCH_T + ci) // 8)],
                        dma="og")
                ch = em.p23_chain(par)
                out = em.p23_out(
                    par, h3q(zs, NCH_T)[:, ci, :].rearrange("p (h c) -> p h c", h=H),
                    og_cb, zs)
                return ch, out

            NWIN = NT * NCH_T

            def fire_ag(q):
                b.emit("gpsimd", lambda E, q=q: E.collective_compute(
                    "AllGather", OP.bypass,
                    replica_groups=[[0, 1, 2, 3], [4, 5, 6, 7]],
                    ins=[agin.ap()[q]], outs=[agout.ap()[q]]),
                    reads=[("agin_dram", q)], writes=[("agout_dram", q)],
                    collective=True)

            if cfg.get("NOPREP"):
                def prep_all(ti):
                    pass
            prep_all(0)
            for w in range(NWIN + 1):
                ch, out, qs = [], [], []
                if w >= 1 and not cfg.get("NOSCAN"):
                    ch, out = scan_p23_stages((w - 1) // NCH_T, (w - 1) % NCH_T)
                    qs = em.p1b((w - 1) % 2)
                p1s = []
                if w < NWIN and not cfg.get("NOSCAN"):
                    p1s = scan_p1(w // NCH_T, w % NCH_T)
                if w < NWIN and (w // NCH_T) + 1 < NT and not cfg.get("NOPREP"):
                    prep_head((w // NCH_T) + 1, w % NCH_T)
                nop = lambda: None
                g = lambda lst, i: lst[i] if i < len(lst) else nop
                sched = [
                    g(ch, 0),
                    g(out, 0),
                    g(p1s, 0),
                    g(qs, 0),
                    g(ch, 1),
                    g(p1s, 1),
                    g(qs, 1),
                    g(p1s, 2),
                    g(qs, 2), g(qs, 3),
                    g(p1s, 3),
                    g(qs, 4),
                    g(p1s, 4),
                    g(qs, 5), g(qs, 6),
                    g(ch, 2),
                    g(p1s, 5),
                    g(ch, 3), g(ch, 4),
                    g(p1s, 6),
                    g(ch, 5),
                    g(ch, 6), g(ch, 7), g(ch, 8),
                    g(out, 1),
                    g(out, 2),
                    g(out, 3),
                ]
                for f in sched:
                    f()
                if w < NWIN:
                    cti, cci = w // NCH_T, w % NCH_T
                    if cti + 1 < NT and not cfg.get("NOPREP"):
                        prep_tail(cti + 1, cci)
                if not cfg.get("NOSCAN"):
                    if w == 24:
                        fire_ag(0)
                    elif w == 28:
                        fire_ag(1)
                    elif w == 30:
                        fire_ag(2)

            if not cfg.get("NOSCAN"):
                fire_ag(3)
            self.barrier()

        NVC2 = NVC // 2
        assert NVC % 2 == 0
        with ExitStack() as bctx:
            uB = {}
            for nm in ("PS0", "PS1", "PS2", "PS3", "PS4", "PS5"):
                uB[nm] = bctx.enter_context(nc.psum_tensor("b_" + nm, [128, 512], F32))

            with ExitStack() as b1ctx:
                u1 = dict(uB)
                sb1 = lambda n, s, dt=F32: u1.__setitem__(
                    n, b1ctx.enter_context(nc.sbuf_tensor("B1" + n, s, dt)))
                self.const_into(u1, b1ctx, "ident_f", np.eye(128, dtype=np.float32))
                self.const_into(u1, b1ctx, "identE", np.eye(NE, dtype=np.float32))
                self.const_into(u1, b1ctx, "c_eps5",
                                np.full((128, 1), EPS_LN, np.float32))
                sb1("x1", [128, NTB, D])
                sb1("ln2xT", [128, ND, TB], BF16)
                sb1("woutb0", [128, NVC2, D], BF16)
                sb1("woutb1", [128, NVC2, D], BF16)
                sb1("ogTg0", [128, NVC2, THB], BF16)
                sb1("sq_ln", [128, D])
                sb1("mu0", [128, 2]); sb1("mu1", [128, 2])
                sb1("rstd0", [128, 1]); sb1("rstd1", [128, 1])
                sb1("xn0", [128, D]); sb1("xn1", [128, D])
                sb1("normw_c", [128, 1])
                sb1("routw_s", [NE, D])
                sb1("routT", [128, ND * NE], BF16)
                sb1("wts", [128, NTB, NE])
                sb1("comb", [128, NTB, NE])
                sb1("ctb", [NTB * NE, 128])
                sb1("m12", [128, 2])
                sb1("gpk", [1, 8])

                b.emit("sync", lambda E: E.dma_start(out=u1["normw_c"][:], in_=normw),
                       writes=["normw_c"], dma="xload")
                b.emit("sync", lambda E: E.dma_start(
                    out=u1["x1"][:],
                    in_=x_res.rearrange("(g p) d -> p g d", p=128)),
                    writes=["x1"], dma="xload")
                b.emit("sync", lambda E: E.dma_start(out=u1["routw_s"][:], in_=routerw),
                       writes=["routw_s"], dma="xload")
                assert ND * NE <= 512
                for dc in range(ND):
                    b.emit("tensor", lambda t, dc=dc: t.matmul(
                        uB["PS0"][:, dc * NE:(dc + 1) * NE],
                        u1["routw_s"][:, dc * 128:(dc + 1) * 128], u1["identE"][:],
                        start=(dc == 0), stop=(dc == ND - 1), is_transpose=True),
                        reads=["routw_s", "identE", "PS0"], writes=["PS0"])
                b.emit("scalar", lambda s: s.copy(u1["routT"][:], uB["PS0"][:, 0:ND * NE]),
                       reads=["PS0"], writes=["routT"])
                pid_cell = {}
                def rowbase_of(E, vh):
                    if vh not in pid_cell:
                        if "q" not in pid_cell:
                            pid_cell["q"] = (E.partition_id() % 4) * VD
                        pid_cell[vh] = pid_cell["q"] + vh * (NVC2 * 128)
                    return pid_cell[vh]
                HT = NTB // 2
                for hf in range(NSPL):
                    for vh in range(2):
                        wb = "woutb" + str(vh)
                        og = "ogTg0"
                        if hf == 0:
                            for vc in range(NVC2):
                                gvc = vh * NVC2 + vc
                                b.emit("gpsimd", lambda E, vc=vc, gvc=gvc, wb=wb:
                                       E.dma_start(
                                    out=u1[wb][:][:, vc, :],
                                    in_=wout[gvc * 128:(gvc + 1) * 128, :]),
                                    reads=[(wb, vc)], writes=[(wb, vc)],
                                    dma="wout" + str(vh))
                                b.emit("vector", lambda v, vc=vc, wb=wb:
                                       v.tensor_scalar_mul(
                                    u1[wb][:][:, vc, :], u1[wb][:][:, vc, :],
                                    u1["normw_c"][:]),
                                    reads=[(wb, vc), "normw_c"], writes=[(wb, vc)])
                        b.emit("sync", lambda E, hf=hf, vh=vh, og=og: E.dma_start(
                            out=u1[og][:],
                            in_=agout.ap().rearrange("q v t -> (q v) t")
                            [bass.ds(rowbase_of(E, vh), NVC2 * 128),
                             hf * THB:(hf + 1) * THB]
                            .rearrange("(vc p) t -> p vc t", p=128)),
                            reads=[("agout_dram", 0), ("agout_dram", 1),
                                   ("agout_dram", 2), ("agout_dram", 3), og],
                            writes=[og], dma="xload")
                        for tg in range(THB // 128):
                            gtg = hf * (THB // 128) + tg
                            for db in range(NHOB):
                                ps = ("PS0", "PS1", "PS2", "PS3")[db % 4]
                                for vc in range(NVC2):
                                    b.emit("tensor", lambda t, tg=tg, db=db, vc=vc,
                                           ps=ps, wb=wb, og=og: t.matmul(
                                        uB[ps][:],
                                        u1[og][:][:, vc, tg * 128:(tg + 1) * 128],
                                        u1[wb][:][:, vc, db * 512:(db + 1) * 512],
                                        start=(vc == 0), stop=(vc == NVC2 - 1)),
                                        reads=[og] + [(wb, vc) for vc in range(NVC2)]
                                        + [ps], writes=[ps])
                                b.emit("vector", lambda v, gtg=gtg, db=db, ps=ps:
                                       v.tensor_add(
                                    u1["x1"][:][:, gtg, db * 512:(db + 1) * 512],
                                    u1["x1"][:][:, gtg, db * 512:(db + 1) * 512],
                                    uB[ps][:]),
                                    reads=[ps, "x1"], writes=["x1"])
                    for gt0 in range(hf * HT, (hf + 1) * HT, 2):
                        self.emit_ln_pair(
                            u1, [(u1["x1"][:][:, g, :], "x1", g, str(g % 2))
                                 for g in range(gt0, min(gt0 + 2, NTB))],
                            "ln2xT", ND, D, psname="PS4", add_norm_to_src=True)
                    clo, chi = hf * HT * 128, (hf + 1) * HT * 128
                    b.emit("gpsimd", lambda E, hf=hf, clo=clo, chi=chi: E.dma_start(
                        out=agin2.ap()[hf].rearrange("(dc p) t -> p dc t", p=128),
                        in_=u1["ln2xT"][:][:, :, clo:chi]),
                        reads=["ln2xT"], writes=[("agin2_dram", hf)], dma="bout")

                    def fire_ag2(hf=hf):
                        b.emit("gpsimd", lambda E, hf=hf: E.collective_compute(
                            "AllGather", OP.bypass,
                            replica_groups=[list(range(8))],
                            ins=[agin2.ap()[hf]], outs=[agout2.ap()[hf]]),
                            reads=[("agin2_dram", hf)],
                            writes=[("agout2_dram", hf)], collective=True)
                    if hf == 0:
                        fire_ag2()
                    else:
                        fire_ag2_half1 = fire_ag2
                b.emit("gpsimd", lambda E: E.dma_start(
                    out=x1_dram.ap().rearrange("(g p) d -> p g d", p=128),
                    in_=u1["x1"][:]), reads=["x1"], writes=["x1_dram"], dma="bout")
                ln2xT3 = u1["ln2xT"][:]
                for gtg in range(NTB):
                    for dc in range(ND):
                        b.emit("tensor", lambda t, gtg=gtg, dc=dc: t.matmul(
                            uB["PS4"][:, 0:NE], ln2xT3[:, dc, gtg * 128:(gtg + 1) * 128],
                            u1["routT"][:, dc * NE:(dc + 1) * NE],
                            start=(dc == 0), stop=(dc == ND - 1)),
                            reads=["ln2xT", "routT", "PS4"], writes=["PS4"])
                    w3 = u1["wts"][:][:, gtg, :]
                    c3 = u1["comb"][:][:, gtg, :]
                    b.emit("vector", lambda v: v.reduce_max(
                        u1["m12"][:, 0:1], uB["PS4"][:, 0:NE], AX.X),
                        reads=["PS4"], writes=["m12"])
                    b.emit("vector", lambda v, w3=w3: v.tensor_scalar_sub(
                        w3, uB["PS4"][:, 0:NE], u1["m12"][:, 0:1]),
                        reads=["PS4", "m12"], writes=["wts"])
                    b.emit("scalar", lambda s, w3=w3: s.activation(w3, w3, AF.Exp),
                        reads=["wts"], writes=["wts"])
                    b.emit("vector", lambda v, w3=w3: v.reduce_sum(
                        u1["m12"][:, 0:1], w3, AX.X), reads=["wts"], writes=["m12"])
                    b.emit("vector", lambda v: v.reciprocal(
                        u1["m12"][:, 1:2], u1["m12"][:, 0:1]), reads=["m12"], writes=["m12"])
                    b.emit("vector", lambda v, w3=w3: v.tensor_scalar_mul(
                        w3, w3, u1["m12"][:, 1:2]), reads=["m12", "wts"], writes=["wts"])
                    b.emit("vector", lambda v, w3=w3: v.reduce_max(
                        u1["m12"][:, 0:1], w3, AX.X), reads=["wts"], writes=["m12"])
                    b.emit("vector", lambda v, w3=w3, c3=c3: v.tensor_scalar(
                        c3, w3, u1["m12"][:, 0:1], -1e30, OP.is_ge, OP.mult),
                        reads=["wts", "m12"], writes=["comb"])
                    b.emit("vector", lambda v, w3=w3, c3=c3: v.tensor_add(
                        c3, c3, w3), reads=["comb", "wts"], writes=["comb"])
                    b.emit("vector", lambda v, c3=c3: v.reduce_max(
                        u1["m12"][:, 1:2], c3, AX.X), reads=["comb"], writes=["m12"])
                    b.emit("vector", lambda v, w3=w3, c3=c3: v.tensor_scalar(
                        c3, w3, u1["m12"][:, 1:2], None, OP.is_ge),
                        reads=["wts", "m12"], writes=["comb"])
                    b.emit("vector", lambda v, w3=w3, c3=c3: v.tensor_mul(
                        c3, c3, w3), reads=["comb", "wts"], writes=["comb"])
                if cfg.get("DBG"):
                    comb_dbg = nc.dram_tensor("comb_dbg", [128, NTB, NE], F32,
                                              kind="ExternalOutput").ap()
                    b.emit("gpsimd", lambda E: E.dma_start(
                        out=comb_dbg, in_=u1["comb"][:]),
                        reads=["comb"], writes=["combdbg"], dma=True)
                b.emit("tensor", lambda t: t.matmul(
                    uB["PS4"][0:NTB * NE, 0:128],
                    u1["comb"][:].rearrange("p a b -> p (a b)"), u1["ident_f"][:],
                    start=True, stop=True, is_transpose=True),
                    reads=["comb", "ident_f", "PS4"], writes=["PS4"])
                b.emit("vector", lambda v: v.tensor_copy(
                    u1["ctb"][:], uB["PS4"][0:NTB * NE, 0:128]),
                    reads=["PS4"], writes=["ctb"])
                for g in range(NTB):
                    b.emit("gpsimd", lambda E, g=g: E.dma_start(
                        out=ag3in.ap().rearrange("(e g) c -> e g c", g=NTB)[:, g, :],
                        in_=u1["ctb"][g * NE:(g + 1) * NE, :]),
                        reads=["ctb"], writes=["ag3in_dram"], dma="bout")
                b.emit("gpsimd", lambda E: E.collective_compute(
                    "AllGather", OP.bypass,
                    replica_groups=[list(range(8))],
                    ins=[ag3in.ap()], outs=[ag3out.ap()]),
                    reads=["ag3in_dram"], writes=["ag3out_dram"], collective=True)
                fire_ag2_half1()
                b.emit("gpsimd", lambda E: E.memset(u1["gpk"][:], 0.0),
                       writes=["gpk"])
                self.barrier()

            with ExitStack() as b4ctx:
                u4 = dict(uB)
                sb4 = lambda n, s, dt=F32: u4.__setitem__(
                    n, b4ctx.enter_context(nc.sbuf_tensor("B4" + n, s, dt)))
                sb4("guTg", [128, ND, IFFP], BF16)
                sb4("guTu", [128, ND, IFFP], BF16)
                sb4("wdT", [128, NIC, D], BF16)
                sb4("strm0", [128, ND, TBL], BF16)
                sb4("strm1", [128, ND, TBL], BF16)
                sb4("hbuf", [128, NIC, TBL], BF16)
                sb4("sgs", [128, TBL])
                sb4("cbc", [128, TBL])
                sb4("rsst0", [128, 512], BF16)
                sb4("rsst1", [128, 512], BF16)
                if IFFP > IFFT:
                    for wnm in ("guTg", "guTu"):
                        b.emit("vector", lambda v, wnm=wnm: v.memset(
                            u4[wnm][:][:, :, IFFT:IFFP], 0.0),
                            writes=[wnm + "_s"])
                wl_seq = [0]

                def emit_wld(src_ap, dst_ap, dkey):
                    i = wl_seq[0] % 8
                    wl_seq[0] += 1
                    b.emit("gpsimd", lambda E, src_ap=src_ap, dst_ap=dst_ap:
                        E.dma_start(out=dst_ap, in_=src_ap),
                        writes=[dkey], dma="wld" + str(i))

                for dc in range(ND):
                    for src, wnm in ((ewgT, "guTg"), (ewuT, "guTu")):
                        emit_wld(src[dc * 128:(dc + 1) * 128, 0:IFFT],
                                 u4[wnm][:][:, dc, 0:IFFT], wnm + "_s")
                for ic in range(NIC):
                    for hh in range(2):
                        emit_wld(
                            ewdT[ic * 128:(ic + 1) * 128,
                                 hh * (D // 2):(hh + 1) * (D // 2)],
                            u4["wdT"][:][:, ic, hh * (D // 2):(hh + 1) * (D // 2)],
                            "wdT_s")
                pid4 = {}
                def rreg(E, r):
                    if "p" not in pid4:
                        pid4["p"] = E.partition_id() % 8
                    if r not in pid4:
                        pid4[r] = pid4["p"] + r * NE
                    return pid4[r]
                ev_seq = [0]
                for tb in range(NTBL):
                    par = tb % 2
                    strm = u4["strm" + str(par)]
                    hf, r = tb // 8, tb % 8
                    b.emit("sync", lambda E, r=r, hf=hf, strm=strm: E.dma_start(
                        out=strm[:],
                        in_=agout2.ap()[hf][r * D:(r + 1) * D, :]
                        .rearrange("(dc p) t -> p dc t", p=128)),
                        reads=[("agout2_dram", hf), ("strm", par)],
                        writes=[("strm", par)], dma="str")
                    b.emit("scalar", lambda E, r=r, hf=hf: E.dma_start(
                        out=u4["cbc"][:].rearrange("p (g c) -> p g c", g=NGB),
                        in_=ag3out.ap().rearrange("(q g) c -> q g c", g=NTB)
                        [bass.ds(rreg(E, r), 1), hf * NGB:hf * NGB + NGB, :]
                        .broadcast_to([128, NGB, 128])),
                        reads=["ag3out_dram", "cbc"], writes=["cbc"], dma="cmb")
                    for ic in range(NIC):
                        for wnm, psn in (("guTg", "PS0"), ("guTu", "PS1")):
                            for dc in range(ND):
                                b.emit("tensor", lambda t, wnm=wnm, psn=psn, dc=dc,
                                       ic=ic, strm=strm: t.matmul(
                                    uB[psn][:, 0:TBL],
                                    u4[wnm][:][:, dc, ic * 128:(ic + 1) * 128],
                                    strm[:][:, dc, :],
                                    start=(dc == 0), stop=(dc == ND - 1)),
                                    reads=[wnm + "_s", ("strm", par), psn],
                                    writes=[psn])
                        b.emit("scalar", lambda s: s.activation(
                            u4["sgs"][:], uB["PS0"][:, 0:TBL], AF.Sigmoid),
                            reads=["PS0", "sgs"], writes=["sgs"])
                        if ic * 128 < IFF:
                            b.emit("vector", lambda v: v.tensor_mul(
                                u4["sgs"][:], u4["sgs"][:], u4["cbc"][:]),
                                reads=["sgs", "cbc"], writes=["sgs"])
                        b.emit("vector", lambda v, ic=ic: v.tensor_mul(
                            u4["hbuf"][:][:, ic, :], u4["sgs"][:], uB["PS1"][:, 0:TBL]),
                            reads=["sgs", "PS1", ("hbuf", ic)], writes=[("hbuf", ic)])
                    for tg in range(NGB):
                        gt = (r * NBR + hf) * NGB + tg
                        for hob in range(NHOB):
                            ps = ("PS2", "PS3", "PS4", "PS5")[(tg * NHOB + hob) % 4]
                            for ic in range(NIC):
                                b.emit("tensor", lambda t, tg=tg, hob=hob, ic=ic,
                                       ps=ps: t.matmul(
                                    uB[ps][:],
                                    u4["hbuf"][:][:, ic, tg * 128:(tg + 1) * 128],
                                    u4["wdT"][:][:, ic, hob * 512:(hob + 1) * 512],
                                    start=(ic == 0), stop=(ic == NIC - 1)),
                                    reads=[("hbuf", ic), "wdT_s", ps], writes=[ps])
                            par2 = ev_seq[0] % 2
                            ev_seq[0] += 1
                            rsst = u4["rsst" + str(par2)]
                            b.emit("vector", lambda v, ps=ps, rsst=rsst: v.tensor_copy(
                                rsst[:], uB[ps][:]),
                                reads=[ps, ("rsst", par2)], writes=[("rsst", par2)])
                            gr = (r * NGB + tg) * 128
                            b.emit("gpsimd", lambda E, gr=gr, hob=hob, hf=hf,
                                   rsst=rsst: E.dma_start(
                                out=rs_in[hf].ap()[gr:gr + 128,
                                                   hob * 512:(hob + 1) * 512],
                                in_=rsst[:]),
                                reads=[("rsst", par2)], writes=["rs_dram"], dma="rso")
                    if r == 7:
                        b.emit("gpsimd", lambda E, hf=hf: E.collective_compute(
                            "ReduceScatter", OP.add,
                            replica_groups=[list(range(8))],
                            ins=[rs_in[hf].ap()], outs=[rs_out[hf].ap()]),
                            reads=["rs_dram"], writes=[("rsout", hf)],
                            collective=True)
                sb4("gpk2", [1, 8])
                b.emit("gpsimd", lambda E: E.memset(u4["gpk2"][:], 0.0),
                       writes=["gpk2"])
                self.barrier()

            with ExitStack() as b5ctx:
                u5 = dict(uB)
                sb5 = lambda n, s, dt=F32: u5.__setitem__(
                    n, b5ctx.enter_context(nc.sbuf_tensor("B5" + n, s, dt)))
                sb5("x1t", [128, NTB, D])
                sb5("rst", [128, NTB, D], BF16)
                b.emit("sync", lambda E: E.dma_start(
                    out=u5["x1t"][:],
                    in_=x1_dram.ap().rearrange("(g p) d -> p g d", p=128)),
                    reads=["x1_dram"], writes=["x1t"], dma="xload")
                gpc = NTB // NCHK
                st = None
                for k in range(NCHK):
                    b.emit("sync", lambda E, k=k: E.dma_start(
                        out=u5["rst"][:][:, k * gpc:(k + 1) * gpc, :],
                        in_=rs_out[k].ap().rearrange("(g p) d -> p g d", p=128)),
                        reads=[("rsout", k)], writes=[("rst", k)], dma="xload")
                    for g in range(k * gpc, (k + 1) * gpc):
                        b.emit("vector", lambda v, g=g: v.tensor_add(
                            u5["x1t"][:][:, g, :], u5["x1t"][:][:, g, :],
                            u5["rst"][:][:, g, :]),
                            reads=["x1t", ("rst", k)], writes=["x1t", ("yadd", k)])
                    st = b.emit("sync", lambda E, k=k: E.dma_start(
                        out=y_out[k * (TB // NCHK):(k + 1) * (TB // NCHK), :]
                        .rearrange("(g p) d -> p g d", p=128),
                        in_=u5["x1t"][:][:, k * gpc:(k + 1) * gpc, :]),
                        reads=[("yadd", k)], dma="yout")
                b._wait("sync", st)
                b._wait("gpsimd", st)
        return b

    def emit_ln_group(self, u, xg_ap, src_key, g, dst_name, ND, D, psname="PS6",
                      add_norm_to_src=False, par=""):
        self.emit_ln_pair(u, [(xg_ap, src_key, g, par)], dst_name, ND, D,
                          psname, add_norm_to_src)

    def emit_ln_pair(self, u, jobs, dst_name, ND, D, psname="PS6",
                     add_norm_to_src=False, ident="ident_f"):
        b = self.b
        J = [(xg_ap, sk, g, "mu" + par, "rstd" + par, "xn" + par)
             for (xg_ap, sk, g, par) in jobs]
        for xg_ap, sk, g, mu, rstd, xn in J:
            b.emit("vector", lambda v, xg_ap=xg_ap, mu=mu: v.reduce_sum(
                u[mu][:, 0:1], xg_ap, AX.X), reads=[sk], writes=[mu])
        for xg_ap, sk, g, mu, rstd, xn in J:
            b.emit("scalar", lambda s, xg_ap=xg_ap, mu=mu: s.activation(
                u["sq_ln"][:], xg_ap, AF.Square, accum_out=u[mu][:, 1:2]),
                reads=[sk, mu, "sq_ln"], writes=["sq_ln", mu])
        for xg_ap, sk, g, mu, rstd, xn in J:
            b.emit("vector", lambda v, mu=mu: v.tensor_scalar_mul(
                u[mu][:], u[mu][:], 1.0 / D), reads=[mu], writes=[mu])
        for xg_ap, sk, g, mu, rstd, xn in J:
            b.emit("vector", lambda v, mu=mu, rstd=rstd: v.tensor_tensor(
                u[rstd][:], u[mu][:, 0:1], u[mu][:, 0:1], OP.mult),
                reads=[mu], writes=[rstd])
        for xg_ap, sk, g, mu, rstd, xn in J:
            b.emit("vector", lambda v, mu=mu, rstd=rstd: v.tensor_sub(
                u[rstd][:], u[mu][:, 1:2], u[rstd][:]),
                reads=[mu, rstd], writes=[rstd])
        for xg_ap, sk, g, mu, rstd, xn in J:
            b.emit("scalar", lambda s, rstd=rstd: s.activation(
                u[rstd][:], u[rstd][:], AF.Ln, bias=u["c_eps5"][:]),
                reads=[rstd, "c_eps5"], writes=[rstd])
        for xg_ap, sk, g, mu, rstd, xn in J:
            b.emit("scalar", lambda s, rstd=rstd: s.activation(
                u[rstd][:], u[rstd][:], AF.Exp, scale=-0.5),
                reads=[rstd], writes=[rstd])
        for xg_ap, sk, g, mu, rstd, xn in J:
            b.emit("vector", lambda v, xg_ap=xg_ap, mu=mu, rstd=rstd, xn=xn:
                   v.tensor_scalar(
                u[xn][:], xg_ap, u[mu][:, 0:1], u[rstd][:],
                OP.subtract, OP.mult),
                reads=[sk, mu, rstd, xn], writes=[xn])
        if add_norm_to_src:
            for xg_ap, sk, g, mu, rstd, xn in J:
                b.emit("vector", lambda v, xg_ap=xg_ap, xn=xn: v.tensor_add(
                    xg_ap, xg_ap, u[xn][:]), reads=[sk, xn], writes=[sk])
        for xg_ap, sk, g, mu, rstd, xn in J:
            for dc in range(ND):
                b.emit("tensor", lambda t, dc=dc, xn=xn: t.matmul(
                    u[psname][:, (dc % 4) * 128:(dc % 4) * 128 + 128],
                    u[xn][:, dc * 128:(dc + 1) * 128], u[ident][:],
                    start=(dc % 4 == 0), stop=(dc % 4 == 3 or dc == ND - 1),
                    is_transpose=True),
                    reads=[xn, ident, psname], writes=[psname])
                if dc % 4 == 3 or dc == ND - 1:
                    lo = (dc // 4) * 4
                    n_in = dc - lo + 1
                    b.emit("scalar", lambda s, lo=lo, n_in=n_in, g=g: s.copy(
                        u[dst_name][:][:, lo:lo + n_in, g * 128:(g + 1) * 128],
                        u[psname][:, 0:n_in * 128]
                        .rearrange("p (dc t) -> p dc t", dc=n_in)),
                        reads=[psname], writes=[dst_name])



def make_core_inputs(cfg, c, x, Wqkv, Wz, Wb, Wa, conv_w, A_log, dt_bias, norm_w,
                     Wout, router_w, shg, shu, shd, wg, wu, wd):
    D, L, H, HG, TB = cfg["D"], cfg["L"], cfg["H"], cfg["HG"], cfg["TB"]
    KEY = HG * 128
    bc, hg = c // 4, c % 4
    h0 = hg * H
    cw = H * 128
    qcols = Wqkv[:, h0 * 128:h0 * 128 + cw]
    kcols = Wqkv[:, KEY + h0 * 128:KEY + h0 * 128 + cw]
    vcols = Wqkv[:, 2 * KEY + h0 * 128:2 * KEY + h0 * 128 + cw]
    zcols = Wz[:, h0 * 128:h0 * 128 + cw]
    wqkvz = np.ascontiguousarray(np.concatenate([qcols, kcols, vcols, zcols], axis=1))
    wba = np.ascontiguousarray(np.concatenate(
        [Wb[:, h0:h0 + H], Wa[:, h0:h0 + H]], axis=1))
    convw = np.ascontiguousarray(np.concatenate(
        [conv_w[h0 * 128:h0 * 128 + cw, 0, :],
         conv_w[KEY + h0 * 128:KEY + h0 * 128 + cw, 0, :],
         conv_w[2 * KEY + h0 * 128:2 * KEY + h0 * 128 + cw, 0, :]], axis=0))
    flat = x.reshape(-1, D)
    IFF = cfg["IFF"]
    SH = IFF // 8
    IFFT = IFF + SH
    NIC = (IFFT + 127) // 128
    IFFP = NIC * 128
    ewgT = np.concatenate([wg[c].T, shg[c * SH:(c + 1) * SH].T], axis=1)
    ewuT = np.concatenate([wu[c].T, shu[c * SH:(c + 1) * SH].T], axis=1)
    ewdT = np.zeros((IFFP, D), np.float32)
    ewdT[:IFF] = wd[c].T
    ewdT[IFF:IFFT] = shd[:, c * SH:(c + 1) * SH].T
    return {
        "x_attn": np.ascontiguousarray(x[bc]),
        "x_res": np.ascontiguousarray(flat[c * TB:(c + 1) * TB]),
        "wqkvz": wqkvz,
        "wba": wba,
        "convw": convw,
        "alog4": np.ascontiguousarray(A_log[h0:h0 + H].reshape(H, 1)),
        "dtb4": np.ascontiguousarray(dt_bias[h0:h0 + H].reshape(H, 1)),
        "normw": np.ascontiguousarray(norm_w.reshape(128, 1)),
        "wout": np.ascontiguousarray(Wout),
        "routerw": np.ascontiguousarray(router_w),
        "ewgT": np.ascontiguousarray(ewgT),
        "ewuT": np.ascontiguousarray(ewuT),
        "ewdT": ewdT,
    }


def build_program(cfg):
    nc = bass.Bass("TRN2", target_bir_lowering=False)
    prog = CoreProgram(nc, cfg)
    with ExitStack() as sems_ctx:
        with nc.Block() as block:
            b = prog.build()
            b.finalize()
            b.sem_handles = {s: sems_ctx.enter_context(nc.semaphore(s))
                             for s in sorted(b.used_sems)}
            block.sync(lambda E: b.replay_engine("sync", E))
            block.tensor(lambda E: b.replay_engine("tensor", E))
            block.vector(lambda E: b.replay_engine("vector", E))
            block.scalar(lambda E: b.replay_engine("scalar", E))
            block.gpsimd(lambda E: b.replay_engine("gpsimd", E))
    return nc


def kernel(L0_attn_x, L0_attn_Wqkv, L0_attn_Wz, L0_attn_Wb, L0_attn_Wa,
           L0_attn_conv_w, L0_attn_A_log, L0_attn_dt_bias, L0_attn_norm_w,
           L0_attn_Wout, L0_ffn_router_w, L0_ffn_shared_gate, L0_ffn_shared_up,
           L0_ffn_shared_down, L0_ffn_w_gate, L0_ffn_w_up, L0_ffn_w_down):
    cfg = default_cfg()
    args = [np.asarray(a) for a in
            (L0_attn_x, L0_attn_Wqkv, L0_attn_Wz, L0_attn_Wb, L0_attn_Wa,
             L0_attn_conv_w, L0_attn_A_log, L0_attn_dt_bias, L0_attn_norm_w,
             L0_attn_Wout, L0_ffn_router_w, L0_ffn_shared_gate, L0_ffn_shared_up,
             L0_ffn_shared_down, L0_ffn_w_gate, L0_ffn_w_up, L0_ffn_w_down)]
    in_maps = [make_core_inputs(cfg, c, *args) for c in range(8)]
    nc = build_program(cfg)
    res = run_bass_kernel_spmd(nc, in_maps, core_ids=list(range(8)))
    y = np.concatenate([np.asarray(r["y"]) for r in res.results], axis=0)
    return y.reshape(L0_attn_x.shape)
